# version 10
# baseline (speedup 1.0000x reference)
import os
import contextlib
import numpy as np
import ml_dtypes
import concourse.bass as bass
import concourse.mybir as mybir
from concourse.bass_utils import run_bass_kernel_spmd

F32 = mybir.dt.float32
BF16 = mybir.dt.bfloat16
U32 = mybir.dt.uint32
ALU = mybir.AluOpType
ACT = mybir.ActivationFunctionType
AX = mybir.AxisListType

import os
NSC_LIM = int(os.environ.get("KNSC", "16"))
NO_CC = os.environ.get("KNOCC", "0") == "1"
NO_IG = os.environ.get("KNOIG", "0") == "1"
NO_MX = os.environ.get("KNOMX", "0") == "1"
KPHASE = int(os.environ.get("KPHASE", "4"))
KSELCUT = int(os.environ.get("KSELCUT", "99"))
KDBG = os.environ.get("KDBG", "0") == "1"
KDBG2 = os.environ.get("KDBG2", "0") == "1"
T = 4096
D = 1024
L = 128
NCH = T // L
SC = 256
NSC = T // SC
H = 32
P = 64
DIN = 2048
DIP = 4384
HID = 256
SD = 64
NPOOL = 50
NSEL = 56
NCORES = 8
ENGS = ("tensor", "vector", "scalar", "gpsimd", "sync")


class Sched:

    class Op:
        __slots__ = ("eng", "fn", "reads", "writes", "idx", "deps", "inc",
                     "is_dma", "dma_sem", "dma_seq", "dma_amt", "tag",
                     "need_drain", "need_local")

        def __init__(self, eng, fn, reads, writes, idx, tag):
            self.eng, self.fn = eng, fn
            self.reads, self.writes = reads, writes
            self.idx = idx
            self.deps = []
            self.inc = False
            self.need_drain = False
            self.need_local = False
            self.is_dma = False
            self.dma_sem = None
            self.dma_seq = 0
            self.dma_amt = 16
            self.tag = tag

    def __init__(self, n_dma_slots=None):
        if n_dma_slots is None:
            n_dma_slots = int(os.environ.get('KSLOTS', '64'))
        self.ops = {e: [] for e in ENGS}
        self.last_writer = {}
        self.readers = {}
        self.n_dma_slots = n_dma_slots
        self.slot_seq = [0] * n_dma_slots
        self.slot_last = [None] * n_dma_slots
        self.chan_slots = {}
        self.chan_uses = {}
        self._next_slot = 0
        self.oplimit = None

    def _muted(self):
        if self.oplimit is None:
            return False
        self.oplimit -= 1
        if self.oplimit < 0:
            return True
        return False

    def _record(self, eng, fn, reads, writes, tag):
        if self._muted():
            return Sched.Op(eng, None, (), (), -1, "muted")
        op = Sched.Op(eng, fn, tuple(reads), tuple(writes), len(self.ops[eng]), tag)
        deps = []
        for k in op.reads:
            w = self.last_writer.get(k)
            if w is not None:
                deps.append(w)
        for k in op.writes:
            w = self.last_writer.get(k)
            if w is not None:
                deps.append(w)
            for r in self.readers.get(k, ()):
                deps.append(r)
        seen = set()
        for d in deps:
            key = (d.eng, d.idx)
            if key not in seen:
                seen.add(key)
                op.deps.append(d)
        for k in op.writes:
            self.last_writer[k] = op
            self.readers[k] = []
        for k in op.reads:
            self.readers.setdefault(k, []).append(op)
        self.ops[eng].append(op)
        return op

    def op(self, eng, fn, reads=(), writes=(), tag=""):
        return self._record(eng, fn, reads, writes, tag)

    def dma_op(self, eng, fn, reads=(), writes=(), chan="d", nslots=2, tag="dmaop", amt=16):
        slots = self.chan_slots.setdefault(chan, [])
        if len(slots) < nslots and self._next_slot < self.n_dma_slots:
            slots.append(self._next_slot)
            self._next_slot += 1
        use = self.chan_uses.get(chan, 0)
        self.chan_uses[chan] = use + 1
        slot = slots[use % len(slots)]
        op = self._record(eng, fn, reads, writes, tag)
        if op.idx < 0:
            return op
        op.is_dma = True
        op.dma_sem = slot
        op.dma_amt = amt
        prev = self.slot_last[slot]
        if prev is not None:
            op.deps.append(prev)
        self.slot_seq[slot] += 1
        op.dma_seq = self.slot_seq[slot]
        self.slot_last[slot] = op
        return op

    def dma(self, eng, out_ap, in_ap, reads=(), writes=(), chan="d", nslots=4, tag="dma"):
        nslots = min(nslots, int(os.environ.get('KNSLOT', '4')))
        slots = self.chan_slots.setdefault(chan, [])
        if len(slots) < nslots and self._next_slot < self.n_dma_slots:
            slots.append(self._next_slot)
            self._next_slot += 1
        use = self.chan_uses.get(chan, 0)
        self.chan_uses[chan] = use + 1
        slot = slots[use % len(slots)]

        def fn(e, out_ap=out_ap, in_ap=in_ap):
            return e.dma_start(out=out_ap, in_=in_ap)

        op = self._record(eng, fn, reads, writes, tag)
        if op.idx < 0:
            return op
        op.is_dma = True
        op.dma_sem = slot
        prev = self.slot_last[slot]
        if prev is not None:
            op.deps.append(prev)
        self.slot_seq[slot] += 1
        op.dma_seq = self.slot_seq[slot]
        self.slot_last[slot] = op
        return op

    def barrier(self):
        marks = {}
        for e in ENGS:
            m = None
            for op in reversed(self.ops[e]):
                if op.fn is not None and not op.is_dma:
                    m = op
                    break
            if m is not None:
                m.inc = True
                m.need_drain = True
            marks[e] = m
        for e in ENGS:
            w = self._record(e, None, (), (), "barrier_w")
            for o in ENGS:
                if o != e and marks[o] is not None:
                    w.deps.append(marks[o])
            for s in range(self.n_dma_slots):
                if self.slot_last[s] is not None:
                    w.deps.append(self.slot_last[s])

    def finalize(self):
        w = self._record("sync", (lambda en: en.nop()), (), (), "final")
        for s in range(self.n_dma_slots):
            if self.slot_last[s] is not None:
                w.deps.append(self.slot_last[s])

    def emit(self, block, sems, dma_sems):
        paranoid = os.environ.get("KPAR", "0") == "1"
        audit = os.environ.get("KAUDIT", "0") == "1"
        if audit:
            paranoid = True
        for e in ENGS:
            for op in self.ops[e]:
                if paranoid and not op.is_dma and op.fn is not None:
                    op.inc = True
                    if audit:
                        op.need_drain = True
                for d in op.deps:
                    cross = d.eng != e or op.is_dma
                    if (not d.is_dma) and (cross or paranoid):
                        d.inc = True
                    if (not d.is_dma) and cross:
                        d.need_drain = True
                    elif not d.is_dma:
                        d.need_local = True
        inccount = {}
        for e in ENGS:
            c = 0
            for op in self.ops[e]:
                if op.inc and not op.is_dma:
                    c += 1
                inccount[(e, op.idx)] = c

        def make_section(ename):
            ops_list = self.ops[ename]

            def section(eng):
                waited = {}
                for op in ops_list:
                    want = {}
                    for d in op.deps:
                        if d.is_dma:
                            k = ("dma", d.dma_sem)
                            want[k] = max(want.get(k, 0), d.dma_amt * d.dma_seq)
                        elif d.eng != ename or paranoid or op.is_dma:
                            want[d.eng] = max(want.get(d.eng, 0), inccount[(d.eng, d.idx)])
                    for k, v in sorted(want.items(), key=str):
                        if waited.get(k, -1) < v:
                            if isinstance(k, tuple):
                                eng.wait_ge(dma_sems[k[1]], v)
                            else:
                                eng.wait_ge(sems[k], v)
                            waited[k] = v
                    if op.fn is None:
                        continue
                    instr = op.fn(eng)
                    if op.is_dma:
                        instr.then_inc(dma_sems[op.dma_sem], op.dma_amt)
                    elif op.inc:
                        if op.need_drain or op.need_local:
                            eng.maybe_drain_then_inc((sems[ename], 1), fusable=True)
                        else:
                            if instr is None or ename == "gpsimd":
                                try:
                                    instr = eng.engine_nop()
                                except Exception:
                                    instr = eng.nop()
                            instr.then_inc(sems[ename], 1)
                    elif op.need_local or op.need_drain:
                        eng.drain(fusable=True)
            return section

        for ename in ENGS:
            getattr(block, ename)(make_section(ename))
        return {e: len(self.ops[e]) for e in ENGS} | {
            "incs": sum(1 for e in ENGS for o in self.ops[e] if o.inc)}


def build_program():
    nc = bass.Bass("TRN2", target_bir_lowering=False,
                   detect_race_conditions=os.environ.get("KRACE", "0") == "1")
    sch = Sched()

    def dramin(name, shape, dt=F32):
        return nc.dram_tensor(name, list(shape), dt, kind="ExternalInput").ap()

    x_h = dramin("x", (T, D))
    w1_h = dramin("w_inproj", (D, DIP))
    wo_h = dramin("w_out", (DIN, D))
    wsc1_h = dramin("w_sc1", (D, HID))
    wsc2_h = dramin("w_sc2", (HID, 1))
    wsm_h = dramin("w_summ", (D, SD))
    wq_h = dramin("w_q", (D, SD))
    wk_h = dramin("w_k", (SD, SD))
    wv_h = dramin("w_v", (SD, D))
    wg_h = dramin("w_gate", (DIN, D))
    wfcT_h = dramin("w_fcT", (D, 4096))
    wprT_h = dramin("w_projT", (4096, D))
    cwv_h = dramin("conv_w_t", (128, 72))
    ccb_h = dramin("conv_b_t", (128, 18))
    nw_h = dramin("norm_w_t", (128, 8))
    sw_h = dramin("ssm_w_t", (128, 16))
    lnw_h = dramin("ln_w_t", (128, 32))
    lnb_h = dramin("ln_b_t", (128, 32))
    dtb_h = dramin("dt_bias_t", (32, 1))
    alog_h = dramin("a_log_t", (32, 1))
    dfm_h = dramin("d_fm_t", (128, 16))
    ident_h = dramin("ident", (128, 128), BF16)
    identf_h = dramin("identf", (128, 128), F32)
    mask_h = dramin("mask_sl", (128, 128))

    out_h = nc.dram_tensor("out", [T, D], F32, kind="ExternalOutput").ap()

    y_spill = nc.dram_tensor("y_spill", [NCH, 128, D], F32).ap()
    dbg_h = {}
    if KDBG:
        for nm, shp, dt_ in [("xntok", [128, D], BF16), ("xnf", [128, 2048], BF16),
                             ("sz", [128, 16 * SC], BF16), ("xbcc", [128, 18 * SC], BF16),
                             ("dtv", [32, SC], F32), ("csv", [32, L], F32), ("statc", [128, 20], F32),
                             ("xdt", [128, 2048], BF16), ("yfmsc", [128, 2048], BF16),
                             ("yop", [128, 2048], BF16), ("dsv", [32, L], F32),
                             ("expcs32", [32, L], F32), ("sall", [128, 2048], BF16),
                             ("gmask", [128, 128], BF16), ("mtb0", [128, 128], BF16),
                             ("xdt2", [128, 2048], BF16), ("dtT16", [128, 32], BF16),
                             ("mtbh2", [128, 128], BF16), ("ssegh2", [128, 128], F32),
                             ("identd", [128, 128], BF16), ("identfd", [128, 128], F32),
                             ("constsd", [128, 28], F32),
                             ("csbh2", [128, 512], F32), ("csTd", [128, 32], F32)]:
            dbg_h[nm] = nc.dram_tensor("dbg_" + nm, shp, dt_, kind="ExternalOutput").ap()
    yfm_spill = nc.dram_tensor("yfm_spill", [NCH, 128, 1024], BF16).ap()
    xmid_spill = nc.dram_tensor("xmid_spill", [NCH, 128, D], F32).ap()
    wqfc_hbm = nc.dram_tensor("wqfc_hbm", [8, 128, 4096], BF16).ap()
    wqpr_hbm = nc.dram_tensor("wqpr_hbm", [32, 128, D], BF16).ap()
    sm_in = nc.dram_tensor("sm_in", [T + 64, SD], F32)
    sm_out = nc.dram_tensor("sm_out", [8 * (T + 64), SD], F32)
    c_hbm = nc.dram_tensor("c_hbm", [2, 1024], F32).ap()

    A = nc.alloc_sbuf_tensor
    consts = A("constssb", [128, 28], F32)
    (C_HALF, C_EPS, C_NEG1, C_NEGH, C_I2048, C_I4096, C_ONE, C_TAUS,
     C_1E9, C_N1E9, C_128, C_QUART, C_IFC, C_ZERO, C_1E6M, C_256, C_N128,
     C_T1, C_T2, C_T3, C_T4, C_T5, C_T6, C_T7, C_64) = range(25)
    CONST_VALS = {C_HALF: 0.5, C_EPS: 1e-5, C_NEG1: -1.0, C_NEGH: -0.5,
                  C_I2048: 1.0 / 2048, C_I4096: 1.0 / 4096, C_ONE: 1.0,
                  C_TAUS: 0.3 * 32768.0, C_1E9: 1e9, C_N1E9: -1e9,
                  C_128: 128.0, C_QUART: 0.25, C_IFC: float(2.0 ** -22),
                  C_ZERO: 0.0, C_1E6M: 1e-6, C_256: 256.0, C_N128: -128.0,
                  C_T1: 4096.0, C_T2: 8192.0, C_T3: 12288.0, C_T4: 16384.0,
                  C_T5: 20480.0, C_T6: 24576.0, C_T7: 28672.0, C_64: 64.0}
    ident = A("identsb", [128, 128], BF16)
    identf = A("identfsb", [128, 128], F32)
    masksl = A("masksl", [128, 128], F32)
    onescol = A("onescol", [128, 1], F32)
    onescol16 = A("onescol16", [128, 1], BF16)
    ones32 = A("ones32", [32, 128], F32)
    onecol32 = A("onecol32", [32, 1], F32)
    piota_f = A("piota_f", [128, 1], F32)
    piota_u = A("piota_u", [128, 1], U32)
    scrow = A("scrow", [1, 128], F32)
    thetas = A("thetas", [1, 64], F32)
    kkT = A("kkT", [64, NSEL], BF16)
    vvtok = A("vvtok", [NSEL, D], BF16)
    mb128 = A("mb128", [128, NPOOL], F32)
    bb128 = A("bb128", [128, NPOOL], F32)
    condcol = A("condcol", [128, 1], F32)
    ghalf = A("ghalf", [128, 2], F32)
    gnegh = A("gnegh", [128, 2], F32)
    lnw16 = A("lnw16", [128, 32], BF16)
    lnwf32 = A("lnwf32", [128, 32], F32)
    lnb16 = A("lnb16", [128, 32], BF16)
    gacc = A("gacc", [128, 4], F32)
    grow_sb = A("grow_sb", [1, 4], F32)

    PB = [nc.alloc_psum_tensor(f"pb{i}", [128, 512], F32) for i in range(6)]
    PT = [nc.alloc_psum_tensor(f"pt{i}", [128, 1024], BF16) for i in range(2)]
    KPB = [f"PB{i}" for i in range(6)]
    KPT = [f"PT{i}" for i in range(2)]

    def cst(i, p=128):
        return consts[0:p, i:i + 1]

    def v_tt(out, a, b, op, reads, writes, tag="tt"):
        sch.op("vector", lambda e: e.tensor_tensor(out, a, b, op), reads, writes, tag)

    def v_ts(out, in0, s1, s2, op0, op1, reads, writes, tag="ts"):
        sch.op("vector", lambda e: e.tensor_scalar(out, in0, s1, s2, op0, op1), reads, writes, tag)

    def v_stt(out, in0, s, in1, op0, op1, reads, writes, tag="stt"):
        sch.op("vector", lambda e: e.scalar_tensor_tensor(out, in0, s, in1, op0, op1), reads, writes, tag)

    def v_copy(out, in_, reads, writes, tag="cp"):
        sch.op("vector", lambda e: e.tensor_copy(out, in_), reads, writes, tag)

    def v_red(out, in_, op, reads, writes, negate=False, absval=False, tag="red"):
        sch.op("vector", lambda e: e.tensor_reduce(out, in_, AX.X, op,
               apply_absolute_value=absval, negate=negate), reads, writes, tag)

    def v_scan(out, d0, d1, init, op0, op1, reads, writes, tag="scan"):
        sch.op("vector", lambda e: e.tensor_tensor_scan(out, d0, d1, init, op0, op1), reads, writes, tag)

    def v_memset(ap_, val, writes, tag="memset"):
        sch.op("vector", lambda e: e.memset(ap_, val), (), writes, tag)

    def v_recip(out, in_, reads, writes, tag="recip"):
        sch.op("vector", lambda e: e.reciprocal(out, in_), reads, writes, tag)

    def v_max(out, in_, reads, writes, tag="max"):
        sch.op("vector", lambda e: e.max(out, in_), reads, writes, tag)

    def v_maxidx(out, in_max, in_vals, reads, writes, tag="mxi"):
        sch.op("vector", lambda e: e.max_index(out, in_max, in_vals), reads, writes, tag)

    def v_matchrep(out, rep, vals, imm, reads, writes, tag="mr"):
        sch.op("vector", lambda e: e.match_replace(out, in_to_replace=rep, in_values=vals,
               imm_value=imm), reads, writes, tag)

    def s_act(out, in_, func, reads, writes, bias=None, scale=None, accum=None, tag="act"):
        kw = {}
        if bias is not None:
            kw["bias"] = bias
        if scale is not None:
            kw["scale"] = scale
        if accum is not None:
            kw["accum_out"] = accum
        sch.op("scalar", lambda e: e.activation(out, in_, func, **kw), reads, writes, tag)

    def s_copy(out, in_, reads, writes, tag="scp"):
        sch.op("scalar", lambda e: e.copy(out, in_), reads, writes, tag)

    def t_mm(out, lhsT, rhs, start, stop, reads, writes, tag="mm", tile_position=None):
        sch.op("tensor", lambda e: e.matmul(out, lhsT, rhs, start=start, stop=stop,
               tile_position=tile_position), reads, writes, tag)

    def t_tr(out, in_, pin, reads, writes, tag="tr"):
        sch.op("tensor", lambda e: e.transpose(out, in_, ident[0:pin, 0:pin]),
               tuple(reads) + ("ident",), writes, tag)

    def t_trf(out, in_, pin, reads, writes, tag="trf"):
        sch.op("tensor", lambda e: e.transpose(out, in_, identf[0:pin, 0:pin]),
               tuple(reads) + ("identf",), writes, tag)

    def bcast(dst, src_row, n, reads, writes, chan="f"):
        sch.dma("gpsimd", dst, src_row.unsqueeze(1).broadcast_to((1, 128, n)),
                reads, writes, chan=chan, tag="bcast")

    def rsqrt2(out, in_, scratch, p, reads, writes):
        s_act(scratch, in_, ACT.Ln, reads, ("rsq_t",) , tag="rln")
        s_act(out, scratch, ACT.Exp, ("rsq_t", "consts"), writes,
              scale=cst(C_NEGH, p), tag="rexp")

    v_memset(consts[:], 0.0, ("consts",), tag="c0")
    for ci2, v in CONST_VALS.items():
        v_memset(consts[:, ci2:ci2 + 1], v, ("consts",), tag="c")
    v_memset(onescol[:], 1.0, ("onescol",))
    v_memset(onescol16[:], 1.0, ("onescol16",))
    v_memset(ones32[:], 1.0, ("ones32",))
    v_memset(onecol32[:], 1.0, ("onecol32",))
    v_memset(gacc[:], 0.0, ("gacc",))
    sch.op("gpsimd", lambda e: e.iota(piota_u[:], pattern=[[0, 1]], base=0,
           channel_multiplier=1), (), ("piota_u",), "iota")
    v_copy(piota_f[:], piota_u[:], ("piota_u",), ("piota_f",))
    sch.dma("sync", ident[:], ident_h, (), ("ident",), chan="w")
    sch.dma("sync", identf[:], identf_h, (), ("identf",), chan="w")
    sch.dma("sync", masksl[:], mask_h, (), ("masksl",), chan="w")

    stackA = contextlib.ExitStack()
    SB = lambda name, shape, dt: stackA.enter_context(nc.sbuf_tensor(name, shape, dt))
    cw1 = SB("cw1", [128, 8 * DIP], BF16)
    cwo = SB("cwo", [128, 16 * 1024], BF16)
    csc1 = SB("csc1", [128, 8 * HID], BF16)
    csc2 = SB("csc2", [128, 2], BF16)
    csum = SB("csum", [128, 8 * SD], BF16)
    cwcv = SB("cwcv", [128, 72], F32)
    ccb = SB("ccb", [128, 18], F32)
    nwcols = SB("nwcols", [128, 8], F32)
    swcols = SB("swcols", [128, 16], F32)
    dtbcol = SB("dtbcol", [32, 1], F32)
    acol = SB("acol", [32, 1], F32)
    acol2 = SB("acol2", [32, 1], F32)
    dfm = SB("dfm", [128, 16], F32)
    stackP0 = contextlib.ExitStack()
    wtmp = stackP0.enter_context(nc.sbuf_tensor("wtmp", [128, DIP], F32, side="right"))
    wtmpG = SB("wtmpG", [128, 1024], F32)
    lnwf = SB("lnwf", [128, 32], F32)
    lnbf = SB("lnbf", [128, 32], F32)

    sch.dma("sync", cwcv[:], cwv_h, (), ("cwcv",), chan="w")
    sch.dma("sync", ccb[:], ccb_h, (), ("ccb",), chan="w")
    sch.dma("sync", nwcols[:], nw_h, (), ("nwcols",), chan="w")
    sch.dma("sync", swcols[:], sw_h, (), ("swcols",), chan="w")
    sch.dma("sync", dtbcol[:], dtb_h, (), ("dtbcol",), chan="w")
    sch.dma("sync", dfm[:], dfm_h, (), ("dfm",), chan="w")
    sch.dma("sync", lnwf[:], lnw_h, (), ("lnwf",), chan="w")
    sch.dma("sync", lnbf[:], lnb_h, (), ("lnbf",), chan="w")
    v_copy(lnw16[:], lnwf[:], ("lnwf",), ("lnw16",))
    v_copy(lnwf32[:], lnwf[:], ("lnwf",), ("lnwf32",))
    v_copy(lnb16[:], lnbf[:], ("lnbf",), ("lnb16",))
    sch.dma("sync", acol2[:], alog_h, (), ("acol_r",), chan="w")
    s_act(acol[:], acol2[:], ACT.Exp, ("acol_r",), ("acol_e",))
    v_ts(acol2[:], acol[:], cst(C_NEG1, 32), None, ALU.mult, ALU.bypass,
         ("acol_e", "consts"), ("acol",))
    acol_ap = acol2

    for k in range(8):
        wt, wkey = wtmp, "wtmpA"
        sch.dma("sync", wt[:], w1_h[k * 128:(k + 1) * 128, :], (), (wkey,), chan="w")
        v_ts(cw1[:, k * DIP:(k + 1) * DIP], wt[:], nwcols[:, k:k + 1], None,
             ALU.mult, ALU.bypass, (wkey, "nwcols"), (f"cw1_{k}",))
    for k in range(16):
        wt, wkey = wtmp, "wtmpA"
        sch.dma("sync", wt[:, 0:1024], wo_h[k * 128:(k + 1) * 128, :], (), (wkey,), chan="w")
        v_ts(cwo[:, k * 1024:(k + 1) * 1024], wt[:, 0:1024], swcols[:, k:k + 1], None,
             ALU.mult, ALU.bypass, (wkey, "swcols"), (f"cwo_{k}",))
    for k in range(8):
        wt, wkey = wtmp, "wtmpA"
        sch.dma("sync", wt[:, 0:HID], wsc1_h[k * 128:(k + 1) * 128, :], (), (wkey,), chan="w")
        v_copy(csc1[:, k * HID:(k + 1) * HID], wt[:, 0:HID], (wkey,), (f"csc1_{k}",))
    for k in range(2):
        wt, wkey = wtmp, "wtmpA"
        sch.dma("sync", wt[:, 0:1], wsc2_h[k * 128:(k + 1) * 128, :], (), (wkey,), chan="w")
        v_copy(csc2[:, k:k + 1], wt[:, 0:1], (wkey,), (f"csc2_{k}",))
    for k in range(8):
        wt, wkey = wtmp, "wtmpA"
        sch.dma("sync", wt[:, 0:SD], wsm_h[k * 128:(k + 1) * 128, :], (), (wkey,), chan="w")
        v_copy(csum[:, k * SD:(k + 1) * SD], wt[:, 0:SD], (wkey,), (f"csum_{k}",))
    stackP0.close()

    xtok = SB("xtok", [128, D], F32)
    xntok = SB("xntok", [128, D], BF16)
    xnf = SB("xnf", [128, 2048], BF16)
    statc = SB("statc", [128, 32], F32)
    sz = SB("sz", [128, 16 * SC], BF16)
    xbcr = SB("xbcr", [128, 18 * (SC + 4)], BF16)
    xbcc = SB("xbcc", [128, 18 * SC], BF16)
    cacc = SB("cacc", [128, 2 * SC], F32)
    dte = SB("dte", [32, SC], F32)
    dtv = SB("dtv", [32, SC], F32)
    dav = SB("dav", [32, SC], F32)
    csv = SB("csv", [32, SC], F32)
    ndsv = SB("ndsv", [32, L], F32)
    dsv = SB("dsv", [32, L], F32)
    expcs32 = SB("expcs32", [32, L], F32)
    expcs16 = SB("expcs16", [32, L], BF16)
    csflatq = [SB(f"csflatq{i}", [1, 8 * L], F32) for i in range(2)]
    expflatq = [SB(f"expflatq{i}", [1, 8 * L], BF16) for i in range(2)]
    lamrow = SB("lamrow", [1, 32], F32)
    lamraw = SB("lamraw", [1, 32], F32)
    lamb = SB("lamb", [128, 32], F32)
    csT = SB("csT", [128, 32], F32)
    dtT16 = SB("dtT16", [128, 32], BF16)
    dsT16 = SB("dsT16", [128, 32], BF16)
    csbq = [SB(f"csbq{i}", [128, 512], F32) for i in range(2)]
    expq = [SB(f"expq{i}", [128, 512], BF16) for i in range(2)]
    xdt = SB("xdt", [128, 2048], BF16)
    xdwg = SB("xdwg", [128, 512], BF16)
    bmtok = SB("bmtok", [128, 128], BF16)
    gmask = SB("gmask", [128, 128], BF16)
    sseg = [SB(f"sseg{i}", [128, 128], F32) for i in range(2)]
    eseg = [SB(f"eseg{i}", [128, 128], BF16) for i in range(2)]
    mtb = [SB(f"mtb{i}", [128, 128], BF16) for i in range(2)]
    cmsb = [SB(f"cmsb{i}", [128, 128], BF16) for i in range(2)]
    sall = SB("sall", [128, 2048], BF16)
    stmp = SB("stmp", [128, 512], BF16)
    dxsb = SB("dxsb", [128, 512], BF16)
    yfmsc = SB("yfmsc", [128, 16 * L], BF16)
    sqb = SB("sqb", [128, 2 * L], F32)
    rsr1 = SB("rsr1", [1, L], F32)
    rsr2 = SB("rsr2", [1, L], F32)
    rsrow = SB("rsrow", [1, L], F32)
    rsb = SB("rsb", [128, L], F32)
    ytok = SB("ytok", [128, D], F32)
    yfm16 = SB("yfm16", [128, 1024], BF16)
    s1b = SB("s1b", [128, 2 * HID], BF16)
    summT = SB("summT", [64, 128], BF16)
    summtk = SB("summtk", [128, SD], F32)

    v_memset(sall[:], 0.0, ("S0", "S1", "S2", "S3"))
    v_memset(xbcr[:], 0.0, tuple(f"tail{j}" for j in range(18)))

    GAMMA_TILES = [("fc", k) for k in range(8)] + [("pr", k) for k in range(32)]
    gamma_i = [0]
    gacc_par = [0, 0]

    def emit_gamma_tile():
        if gamma_i[0] >= len(GAMMA_TILES):
            return
        which, k = GAMMA_TILES[gamma_i[0]]
        gamma_i[0] += 1
        npiece = 4 if which == "fc" else 1
        srcw = wfcT_h if which == "fc" else wprT_h
        ci3 = 0 if which == "fc" else 1
        for pc in range(npiece):
            sch.dma("sync", wtmpG[:],
                    srcw[k * 128:(k + 1) * 128, pc * 1024:(pc + 1) * 1024], (), ("wtmpG",),
                    chan="g", nslots=2)
            v_red(statc[:, 30:31], wtmpG[:], ALU.add, ("wtmpG",), ("gred",), absval=True)
            par = gacc_par[ci3]
            v_tt(gacc[:, ci3 + 2 * (1 - par): ci3 + 2 * (1 - par) + 1],
                 gacc[:, ci3 + 2 * par: ci3 + 2 * par + 1], statc[:, 30:31], ALU.add,
                 ("gacc", "gred"), ("gacc",), tag="gacc")
            gacc_par[ci3] = 1 - par

    def ln_tile(ti, xkey, xview, outview, outkey, stat, skeys):
        k0 = skeys
        sch.op("vector", lambda e: e.bn_stats(stat[:, 0:6], xview[:, 0:512]), (xkey,), (k0 + "a",))
        sch.op("vector", lambda e: e.bn_stats(stat[:, 6:12], xview[:, 512:1024]), (xkey,), (k0 + "a",))
        sch.op("vector", lambda e: e.bn_aggr(stat[:, 12:14], stat[:, 0:12]), (k0 + "a",), (k0 + "b",))
        mean = stat[:, 12:13]
        var = stat[:, 13:14]
        veps = stat[:, 14:15]
        inv = stat[:, 15:16]
        v2 = stat[:, 16:17]
        r2 = stat[:, 17:18]
        stot = stat[:, 18:19]
        lnscr = stat[:, 19:20]
        v_ts(veps, var, cst(C_EPS), None, ALU.add, ALU.bypass, (k0 + "b", "consts"), (k0 + "c",))
        rsqrt2(inv, veps, lnscr, 128, (k0 + "c",), (k0 + "d",))
        v_ts(v2, var, inv, inv, ALU.mult, ALU.mult, (k0 + "b", k0 + "d"), (k0 + "e",))
        v_ts(r2, v2, cst(C_EPS), None, ALU.add, ALU.bypass, (k0 + "e", "consts"), (k0 + "f",))
        rsqrt2(v2, r2, lnscr, 128, (k0 + "f",), (k0 + "g",))
        v_ts(stot, inv, v2, None, ALU.mult, ALU.bypass, (k0 + "d", k0 + "g"), (k0 + "h",))
        v_ts(outview, xview, mean, stot, ALU.subtract, ALU.mult,
             (xkey, k0 + "b", k0 + "h"), (outkey,))

    sch.barrier()

    for sc_i in range(min(NSC, NSC_LIM)):
        t0 = sc_i * SC
        for ti in range(2):
            xk = "xtokA"
            sch.dma("sync", xtok[:],
                    x_h[t0 + ti * 128: t0 + (ti + 1) * 128, :], (), (xk,), chan="x")
            ln_tile(ti, xk, xtok[:], xntok[:], "xn1", statc, "lnA")
            for k in range(8):
                t_tr(PT[0][0:128, 0:128], xntok[:, k * 128:(k + 1) * 128],
                     128, ("xn1",), (KPT[0],))
                v_copy(xnf[:, k * 256 + ti * 128: k * 256 + (ti + 1) * 128],
                       PT[0][0:128, 0:128], (KPT[0],), (f"xnf{k}",))
        if KDBG and sc_i == 0:
            sch.dma("sync", dbg_h["xnf"], xnf[:], tuple(f"xnf{k}" for k in range(8)), (), chan="oo")
            sch.dma("sync", dbg_h["identd"], ident[:], ("ident",), (), chan="oo")
            sch.dma("sync", dbg_h["identfd"], identf[:], ("identf",), (), chan="oo")
            sch.dma("sync", dbg_h["constsd"], consts[:], ("consts",), (), chan="oo")
        for m in range(34):
            pb, pkey = (PB[m % 2], KPB[m % 2])
            for k in range(8):
                t_mm(pb[:, 0:SC], cw1[:, k * DIP + m * 128: k * DIP + (m + 1) * 128],
                     xnf[:, k * 256:(k + 1) * 256], k == 0, k == 7,
                     (f"cw1_{k}", f"xnf{k}"), (pkey,))
            if m < 16:
                s_act(sz[:, m * SC:(m + 1) * SC], pb[:, 0:SC], ACT.Silu, (pkey,), (f"sz{m}",))
            else:
                j = m - 16
                s_copy(xbcr[:, j * (SC + 4) + 4: (j + 1) * (SC + 4)], pb[:, 0:SC],
                       (pkey, f"tail{j}"), (f"xbcr{j}",))
        pbdt = PB[2]
        for k in range(8):
            t_mm(pbdt[0:32, 0:SC], cw1[:, k * DIP + 34 * 128: k * DIP + 34 * 128 + 32],
                 xnf[:, k * 256:(k + 1) * 256], k == 0, k == 7,
                 (f"cw1_{k}", f"xnf{k}"), (KPB[2],))
        for j in range(18):
            base = j * (SC + 4)
            a0, a1 = cacc[:, 0:SC], cacc[:, SC:2 * SC]
            v_ts(a0, xbcr[:, base + 1: base + 1 + SC], cwcv[:, j * 4: j * 4 + 1], None,
                 ALU.mult, ALU.bypass, (f"xbcr{j}", f"tail{j}", "cwcv"), ("cacc0",))
            v_stt(a1, xbcr[:, base + 2: base + 2 + SC], cwcv[:, j * 4 + 1: j * 4 + 2], a0,
                  ALU.mult, ALU.add, (f"xbcr{j}", f"tail{j}", "cacc0", "cwcv"), ("cacc1",))
            v_stt(a0, xbcr[:, base + 3: base + 3 + SC], cwcv[:, j * 4 + 2: j * 4 + 3], a1,
                  ALU.mult, ALU.add, (f"xbcr{j}", f"tail{j}", "cacc1", "cwcv"), ("cacc0",))
            v_stt(a1, xbcr[:, base + 4: base + 4 + SC], cwcv[:, j * 4 + 3: j * 4 + 4], a0,
                  ALU.mult, ALU.add, (f"xbcr{j}", "cacc0", "cwcv"), ("cacc1",))
            s_act(xbcc[:, j * SC:(j + 1) * SC], a1, ACT.Silu, ("cacc1", "ccb"), (f"xbcc{j}",),
                  bias=ccb[:, j:j + 1])
            v_copy(xbcr[:, base + 1: base + 4], xbcr[:, base + SC + 1: base + SC + 4],
                   (f"xbcr{j}",), (f"tail{j}",))
        s_act(dte[:], pbdt[0:32, 0:SC], ACT.Exp, (KPB[2], "dtbcol"), ("dte",), bias=dtbcol[:])
        s_act(dtv[:], dte[:], ACT.Ln, ("dte", "onecol32"), ("dtv",), bias=onecol32[:])
        v_ts(dav[:], dtv[:], acol_ap[:], None, ALU.mult, ALU.bypass, ("dtv", "acol"), ("dav",))

        emit_gamma_tile()
        emit_gamma_tile()
        emit_gamma_tile()

        for ci in range(2):
            ch = sc_i * 2 + ci
            cs_ = csv[:, ci * L:(ci + 1) * L]
            v_scan(cs_, ones32[:], dav[:, ci * L:(ci + 1) * L], 0.0, ALU.mult, ALU.add,
                   ("dav", "ones32"), (f"cs{ci}",))
            nds_ = ndsv[:]
            v_ts(nds_, cs_, cs_[:, L - 1:L], None, ALU.subtract, ALU.bypass,
                 (f"cs{ci}",), ("ndsv",))
            s_act(dsv[:], nds_, ACT.Exp, ("ndsv", "consts"),
                  ("dsv",), scale=cst(C_NEG1, 32))
            s_act(expcs32[:], cs_, ACT.Exp, (f"cs{ci}",), ("ecsv",))
            v_copy(expcs16[:], expcs32[:], ("ecsv",), ("ecs16",))
            for qf in range(4):
                qb2 = qf % 2
                sch.dma("gpsimd", csflatq[qb2][0:1, :].rearrange("o (h l) -> o h l", h=8),
                        cs_[qf * 8:(qf + 1) * 8, :], (f"cs{ci}",), (f"csflat{qb2}",),
                        chan="f", tag="csflat")
                sch.dma("gpsimd", expflatq[qb2][0:1, :].rearrange("o (h l) -> o h l", h=8),
                        expcs16[qf * 8:(qf + 1) * 8, :],
                        ("ecs16",), (f"expflat{qb2}",), chan="f", tag="expflat")
                s_act(lamrow[0:1, qf * 8:(qf + 1) * 8], csflatq[qb2][0:1, L - 1::L],
                      ACT.Exp, (f"csflat{qb2}",), ("lamrow",))
            bcast(lamb[:], lamrow[:], 32, ("lamrow",), ("lamb",))
            t_trf(PB[5][0:128, 0:32], cs_, 32, (f"cs{ci}",), (KPB[5],))
            v_copy(csT[:], PB[5][0:128, 0:32], (KPB[5],), ("csT",))
            t_trf(PB[5][0:128, 32:64], dtv[:, ci * L:(ci + 1) * L], 32, ("dtv",), (KPB[5],))
            v_copy(dtT16[:], PB[5][0:128, 32:64], (KPB[5],), ("dtT16",))
            t_trf(PB[5][0:128, 64:96], dsv[:], 32, ("dsv",), (KPB[5],))
            v_copy(dsT16[:], PB[5][0:128, 64:96], (KPB[5],), ("dsT16",))

            for j in range(16):
                t_tr(PT[1][0:128, 0:128], xbcc[:, j * SC + ci * L: j * SC + (ci + 1) * L],
                     128, (f"xbcc{j}",), (KPT[1],))
                v_tt(xdt[:, j * 128:(j + 1) * 128].rearrange("p (h q) -> p h q", h=2),
                     PT[1][0:128, 0:128].rearrange("p (h q) -> p h q", h=2),
                     dtT16[:, 2 * j:2 * j + 2].unsqueeze(2).broadcast_to((128, 2, 64)),
                     ALU.mult, (KPT[1], "dtT16"), ("xdt",))
            t_tr(PT[1][0:128, 0:128], xbcc[:, 16 * SC + ci * L: 16 * SC + (ci + 1) * L],
                 128, ("xbcc16",), (KPT[1],))
            v_copy(bmtok[:], PT[1][0:128, 0:128], (KPT[1],), ("bmtok",))

            if KDBG and sc_i == 0 and ci == 0:
                sch.dma("sync", dbg_h["xdt2"], xdt[:], ("xdt",), (), chan="oo")
                sch.dma("sync", dbg_h["dtT16"], dtT16[:], ("dtT16",), (), chan="oo")
            t_mm(PB[2][:, 0:128], xbcc[:, 16 * SC + ci * L: 16 * SC + (ci + 1) * L],
                 xbcc[:, 17 * SC + ci * L: 17 * SC + (ci + 1) * L], True, True,
                 ("xbcc16", "xbcc17"), (KPB[2],))
            v_tt(gmask[:], PB[2][:, 0:128], masksl[:], ALU.mult, (KPB[2], "masksl"), ("gmask",))

            for h in range(H):
                q, hb, hh = h // 8, h % 2, h % 8
                qb = (h // 4) % 2
                if hh == 0:
                    qb2 = q % 2
                    sch.dma("gpsimd", csflatq[qb2][0:1, :].rearrange("o (h2 l) -> o h2 l", h2=8),
                            cs_[q * 8:(q + 1) * 8, :], (f"cs{ci}",), (f"csflat{qb2}",),
                            chan="f", tag="csflat")
                    sch.dma("gpsimd", expflatq[qb2][0:1, :].rearrange("o (h2 l) -> o h2 l", h2=8),
                            expcs16[q * 8:(q + 1) * 8, :], ("ecs16",), (f"expflat{qb2}",),
                            chan="f", tag="expflat")
                if h % 4 == 0:
                    half4 = (h % 8) // 4
                    sch.dma("gpsimd", csbq[qb][:],
                            csflatq[q % 2][0:1, half4 * 512:(half4 + 1) * 512].unsqueeze(1).broadcast_to((1, 128, 512)),
                            (f"csflat{q % 2}",), (f"csbq{qb}",), chan="b")
                    sch.dma("gpsimd", expq[qb][:],
                            expflatq[q % 2][0:1, half4 * 512:(half4 + 1) * 512].unsqueeze(1).broadcast_to((1, 128, 512)),
                            (f"expflat{q % 2}",), (f"expq{qb}",), chan="b")
                v_ts(sseg[hb][:], csbq[qb][:, (h % 4) * 128:(h % 4 + 1) * 128], csT[:, h:h + 1],
                     cst(C_ZERO), ALU.subtract, ALU.min,
                     (f"csbq{qb}", "csT", "consts"), (f"sseg{hb}",))
                s_act(eseg[hb][:], sseg[hb][:], ACT.Exp, (f"sseg{hb}",), (f"eseg{hb}",))
                v_tt(mtb[hb][:], eseg[hb][:], gmask[:], ALU.mult,
                     (f"eseg{hb}", "gmask"), (f"mtb{hb}",))
                if KDBG and sc_i == 0 and ci == 0 and h == 2:
                    sch.dma("sync", dbg_h["mtbh2"], mtb[hb][:], (f"mtb{hb}",), (), chan="oo")
                    sch.dma("sync", dbg_h["ssegh2"], sseg[hb][:], (f"sseg{hb}",), (), chan="oo")
                    sch.dma("sync", dbg_h["csbh2"], csbq[qb][:], (f"csbq{qb}",), (), chan="oo")
                    sch.dma("sync", dbg_h["csTd"], csT[:], ("csT",), (), chan="oo")
                v_tt(cmsb[hb][:], xbcc[:, 17 * SC + ci * L: 17 * SC + (ci + 1) * L],
                     expq[qb][:, (h % 4) * 128:(h % 4 + 1) * 128], ALU.mult,
                     ("xbcc17", f"expq{qb}"), (f"cmsb{hb}",))
                bank = PB[3 + (q % 2)]
                bkey = KPB[3 + (q % 2)]
                pp, ff = (h % 2) * 64, ((h // 2) % 4) * 128
                t_mm(bank[pp:pp + 64, ff:ff + 128], xdt[:, h * P:(h + 1) * P], mtb[hb][:],
                     True, ch == 0, ("xdt", f"mtb{hb}"), (bkey,), tag="ydiag",
                     tile_position=(0, pp))
                if ch > 0:
                    t_mm(bank[pp:pp + 64, ff:ff + 128], sall[:, h * P:(h + 1) * P], cmsb[hb][:],
                         False, True, (f"S{q}", f"cmsb{hb}"), (bkey,), tag="yoff",
                         tile_position=(0, pp))
                if hh == 7:
                    g = q
                    for jj in range(4):
                        j = g * 4 + jj
                        v_ts(dxsb[:, jj * 128:(jj + 1) * 128],
                             xbcc[:, j * SC + ci * L: j * SC + (ci + 1) * L],
                             dfm[:, j:j + 1], None, ALU.mult, ALU.bypass,
                             (f"xbcc{j}", "dfm"), ("dxsb",))
                    v_tt(yfmsc[:, g * 512:(g + 1) * 512],
                         bank[:, 0:512], dxsb[:], ALU.add,
                         (bkey, "dxsb"), (f"yfm{g}",), tag="yevac")
                    v_tt(xdwg[:].rearrange("p (hq q) -> p hq q", hq=8),
                         xdt[:, g * 512:(g + 1) * 512].rearrange("p (hq q) -> p hq q", hq=8),
                         dsT16[:, g * 8:(g + 1) * 8].unsqueeze(2).broadcast_to((128, 8, 64)),
                         ALU.mult, ("xdt", "dsT16"), ("xdwg",))
                    t_mm(PB[5][:, 0:512], bmtok[:], xdwg[:], True, True,
                         ("bmtok", "xdwg"), (KPB[5],), tag="states")
                    v_tt(stmp[:].rearrange("p (hq q) -> p hq q", hq=8),
                         sall[:, g * 512:(g + 1) * 512].rearrange("p (hq q) -> p hq q", hq=8),
                         lamb[:, g * 8:(g + 1) * 8].unsqueeze(2).broadcast_to((128, 8, 64)),
                         ALU.mult, (f"S{g}", "lamb"), ("stmp",), tag="supd")
                    v_tt(sall[:, g * 512:(g + 1) * 512], stmp[:], PB[5][:, 0:512], ALU.add,
                         ("stmp", KPB[5]), (f"S{g}",), tag="supd2")

            yzb = xdt
            yop = xnf
            yview = yfmsc[:].rearrange("p (j l) -> p j l", l=L)
            zview = sz[:].rearrange("p (j l) -> p j l", l=SC)[:, :, ci * L:(ci + 1) * L]
            yzview = yzb[:].rearrange("p (j l) -> p j l", l=L)
            ykeys = tuple(f"yfm{g}" for g in range(4))
            zkeys = tuple(f"sz{m}" for m in range(16))
            v_tt(yzview, yview, zview, ALU.mult, ykeys + zkeys, ("yz", "xdt"), tag="yz")
            for j in range(16):
                sq = sqb[:, (j % 2) * L:(j % 2 + 1) * L]
                v_tt(sq, yzb[:, j * L:(j + 1) * L], yzb[:, j * L:(j + 1) * L], ALU.mult,
                     ("yz", "xdt"), (f"sq{j % 2}",), tag="sq")
                t_mm(PB[2][0:1, 128:256], onescol[:], sq, j == 0, j == 15,
                     (f"sq{j % 2}", "onescol"), (KPB[2],), tag="sqmm")
            v_ts(rsr1[:], PB[2][0:1, 128:256], cst(C_I2048, 1), cst(C_EPS, 1),
                 ALU.mult, ALU.add, (KPB[2], "consts"), ("rsr1",))
            s_act(rsr2[:], rsr1[:], ACT.Ln, ("rsr1",), ("rsr2",))
            s_act(rsrow[:], rsr2[:], ACT.Exp, ("rsr2", "consts"), ("rsrow",),
                  scale=cst(C_NEGH, 1))
            bcast(rsb[:], rsrow[:], 128, ("rsrow",), ("rsb",))
            v_tt(yop[:].rearrange("p (j l) -> p j l", l=128), yzview,
                 rsb[:].unsqueeze(1).broadcast_to((128, 16, 128)), ALU.mult,
                 ("yz", "xdt", "rsb"), ("yop",) + tuple(f"xnf{k}" for k in range(8)), tag="yop")
            if KDBG and sc_i == 0 and ci == 0:
                sch.dma("sync", dbg_h["statc"], statc[:, 0:20], ("lnAb", "lnAh"), (), chan="oo")
                for nm, ap_, keys in [("xntok", xntok[:], ("xn1",)),
                                      ("sz", sz[:], tuple(f"sz{m}" for m in range(16))),
                                      ("xbcc", xbcc[:], tuple(f"xbcc{j}" for j in range(18))),
                                      ("dtv", dtv[:], ("dtv",)), ("csv", csv[:, 0:L], ("cs0",)), 
                                      ("xdt", xdt[:], ("yz", "xdt")), ("yfmsc", yfmsc[:], tuple(f"yfm{g}" for g in range(4))),
                                      ("yop", yop[:], ("yop",)), ("dsv", dsv[:], ("dsv",)),
                                      ("expcs32", expcs32[:], ("ecsv",)), ("sall", sall[:], ("S0", "S1", "S2", "S3")),
                                      ("gmask", gmask[:], ("gmask",)), ("mtb0", mtb[1][:], ("mtb1",))]:
                    sch.dma("sync", dbg_h[nm], ap_, keys, (), chan="oo")
            for half in range(2):
                pbo, pkey = PB[half], KPB[half]
                for j in range(16):
                    t_mm(pbo[:, 0:512], yop[:, j * 128:(j + 1) * 128],
                         cwo[:, j * 1024 + half * 512: j * 1024 + (half + 1) * 512],
                         j == 0, j == 15,
                         ("yop", f"xnf{j // 2}", f"cwo_{j}"), (pkey,), tag="outproj")
                v_copy(ytok[:, half * 512:(half + 1) * 512],
                       pbo[:, 0:512], (pkey,), ("ytokk",))
            sch.dma("sync", y_spill[ch, :, :], ytok[:],
                    ("ytokk",), (), chan="ys")
            for k in range(8):
                t_trf(PB[5][0:128, 96:224], ytok[:, k * 128:(k + 1) * 128],
                      128, ("ytokk",), (KPB[5],))
                v_copy(yfm16[:, k * 128:(k + 1) * 128],
                       PB[5][0:128, 96:224], (KPB[5],), ("yfm16k",))
            sch.dma("sync", yfm_spill[ch, :, :], yfm16[:],
                    ("yfm16k",), (), chan="ys")
            for m2 in range(2):
                for k in range(8):
                    t_mm(PB[2][:, 256:384], csc1[:, k * HID + m2 * 128: k * HID + (m2 + 1) * 128],
                         yfm16[:, k * 128:(k + 1) * 128],
                         k == 0, k == 7, (f"csc1_{k}", "yfm16k"), (KPB[2],), tag="sc1")
                v_ts(s1b[:, m2 * 128:(m2 + 1) * 128], PB[2][:, 256:384], 0.0, None,
                     ALU.max, ALU.bypass, (KPB[2],), (f"s1b{m2}",), tag="relu")
            for k2 in range(2):
                t_mm(PB[2][0:1, 384:512], csc2[:, k2:k2 + 1], s1b[:, k2 * 128:(k2 + 1) * 128],
                     k2 == 0, k2 == 1, (f"csc2_{k2}", "s1b0", "s1b1"), (KPB[2],), tag="sc2")
            s_act(rsr1[:], PB[2][0:1, 384:512], ACT.Exp, (KPB[2], "consts", "rsr1"), ("rsr1",),
                  scale=cst(C_NEG1, 1))
            v_ts(rsr2[:], rsr1[:], cst(C_ONE, 1), None, ALU.add, ALU.bypass,
                 ("rsr1", "consts", "rsr2"), ("rsr2",))
            v_recip(scrow[:], rsr2[:], ("rsr2",), ("scrow",))
            sch.dma("sync", sm_in.ap()[T + 2 * ch: T + 2 * ch + 2, :],
                    scrow[:].rearrange("o (a b) -> o a b", a=2),
                    ("scrow",), ("sc_in",), chan="ys")
            for k in range(8):
                t_mm(PB[5][0:64, 224:352], csum[:, k * SD:(k + 1) * SD],
                     yfm16[:, k * 128:(k + 1) * 128],
                     k == 0, k == 7, (f"csum_{k}", "yfm16k"), (KPB[5],), tag="summ")
            v_copy(summT[:], PB[5][0:64, 224:352], (KPB[5],), ("summT",))
            t_tr(PT[1][0:128, 128:192], summT[:], 64, ("summT",), (KPT[1],))
            v_copy(summtk[:], PT[1][0:128, 128:192], (KPT[1],), ("summtk",))
            sch.dma("sync", sm_in.ap()[ch * 128:(ch + 1) * 128, :], summtk[:],
                    ("summtk",), (), chan="ys")

    if KDBG2:
        d3 = {}
        for nm, shp, dt_ in [("csc2", [128, 2], BF16), ("csum", [128, 8 * SD], BF16),
                             ("s1b", [128, 2 * HID], BF16), ("yfm16", [128, 1024], BF16),
                             ("summT", [64, 128], BF16), ("scrow", [1, 128], F32),
                             ("summtk", [128, SD], F32)]:
            d3[nm] = nc.dram_tensor("dbg3_" + nm, shp, dt_, kind="ExternalOutput").ap()
        sch.dma("sync", d3["csc2"], csc2[:], ("csc2_0", "csc2_1"), (), chan="oo")
        sch.dma("sync", d3["csum"], csum[:], tuple(f"csum_{k}" for k in range(8)), (), chan="oo")
        sch.dma("sync", d3["s1b"], s1b[:], ("s1b0", "s1b1"), (), chan="oo")
        sch.dma("sync", d3["yfm16"], yfm16[:], ("yfm16k",), (), chan="oo")
        sch.dma("sync", d3["summT"], summT[:], ("summT",), (), chan="oo")
        sch.dma("sync", d3["scrow"], scrow[:], ("scrow",), (), chan="oo")
        sch.dma("sync", d3["summtk"], summtk[:], ("summtk",), (), chan="oo")
        d3y = nc.dram_tensor("dbg3_ysp", [4 * 128, D], F32, kind="ExternalOutput").ap()
        sch.dma("sync", d3y, y_spill[0:4, :, :].rearrange("a b c -> (a b) c"), (), (), chan="oo")
        d3yf = nc.dram_tensor("dbg3_yfsp", [2 * 128, 1024], BF16, kind="ExternalOutput").ap()
        sch.dma("sync", d3yf, yfm_spill[0:2, :, :].rearrange("a b c -> (a b) c"), (), (), chan="oo")
    sch.oplimit = None
    while gamma_i[0] < len(GAMMA_TILES):
        emit_gamma_tile()
    for ci3 in range(2):
        last_col = ci3 + 2 * gacc_par[ci3]
        t_mm(PB[2][0:1, ci3:ci3 + 1], gacc[:, last_col:last_col + 1], onescol[:],
             True, True, ("gacc", "onescol"), (KPB[2],), tag="gfin")
    v_ts(grow_sb[0:1, 0:2], PB[2][0:1, 0:2], cst(C_IFC, 1), cst(C_1E6M, 1),
         ALU.mult, ALU.add, (KPB[2], "consts"), ("grow",))
    v_ts(grow_sb[0:1, 2:4], grow_sb[0:1, 0:2], cst(C_HALF, 1), None, ALU.mult, ALU.bypass,
         ("grow", "consts"), ("ghrow",))
    bcast(ghalf[:], grow_sb[0:1, 2:4], 2, ("ghrow",), ("ghalf",))
    v_ts(gnegh[:], ghalf[:], cst(C_NEG1), None, ALU.mult, ALU.bypass,
         ("ghalf", "consts"), ("gnegh",))

    sch.barrier()
    stackA.close()

    if KPHASE <= 1:
        dummy = nc.alloc_sbuf_tensor("dummy1", [128, D], F32)
        sch.op("vector", lambda e: e.memset(dummy[:], 0.0), (), ("dummyk",), "dm")
        for chd in range(NCH):
            sch.dma("sync", out_h[chd * 128:(chd + 1) * 128, :], dummy[:], ("dummyk",), (), chan="oo")
        sch.finalize()
        with contextlib.ExitStack() as es:
            sems = {}
            for e in ENGS:
                sems[e] = es.enter_context(nc.semaphore(f"sem_{e}"))
            dma_sems = [es.enter_context(nc.semaphore(f"dsem{i}")) for i in range(sch.n_dma_slots)]
            block = es.enter_context(nc.Block())
            stats = sch.emit(block, sems, dma_sems)
        return nc, stats

    if KSELCUT < 99:
        sch.oplimit = KSELCUT
    if os.environ.get("KVFIRST", "0") == "1":
        pass
    stackS = contextlib.ExitStack()
    SBs = lambda name, shape, dt: stackS.enter_context(nc.sbuf_tensor(name, shape, dt))
    ssc = SBs("ssc", [128, 256], F32)
    sscraw = SBs("sscraw", [128, 256], F32)
    fhib = SBs("fhib", [128, 8], F32)
    flob = SBs("flob", [128, 8], F32)
    vcmp = SBs("vcmp", [128, 256], F32)
    redc = SBs("redc", [128, 2], F32)
    scal2 = SBs("scal2", [1, 8], F32)
    mx8 = SBs("mx8", [128, 8], F32)
    cnd8 = SBs("cnd8", [8, 128], F32)
    cand = SBs("cand", [1, 1024], F32)
    candB = SBs("candB", [1, 1024], F32)
    th8b = SBs("th8b", [128, 8], F32)
    mi8 = SBs("mi8", [128, 8], U32)
    mif = SBs("mif", [128, 8], F32)
    gpos = SBs("gpos", [128, 8], F32)
    gpos2 = SBs("gpos2", [128, 8], F32)
    gtr = SBs("gtr", [8, 128], F32)
    gall = SBs("gall", [8, 8], F32)
    gcolf = SBs("gcolf", [NSEL, 1], F32)
    gcolu = SBs("gcolu", [NSEL, 1], U32)
    gcolf2 = SBs("gcolf2", [NSEL, 1], F32)
    radj = SBs("radj", [NSEL, 1], F32)
    radj2 = SBs("radj2", [NSEL, 1], F32)
    rc1 = SBs("rc1", [NSEL, 1], F32)
    poolraw = SBs("poolraw", [NSEL, SD], F32)
    poolfm = SBs("poolfm", [64, NSEL], BF16)
    vmrow = SBs("vmrow", [1, NPOOL], F32)
    mrow = SBs("mrow", [1, NPOOL], F32)
    brow = SBs("brow", [1, NPOOL], F32)
    ckw = SBs("ckw", [64, 64], BF16)
    cvw = SBs("cvw", [64, 1024], BF16)
    wk_f = SBs("wk_f", [64, 64], F32)
    wv_f = SBs("wv_f", [64, 1024], F32)

    ccsm = sch.dma_op("gpsimd", lambda e: e.collective_compute(
        "AllGather", ALU.bypass, replica_groups=[list(range(NCORES))],
        ins=[sm_in.ap().opt()], outs=[sm_out.ap().opt()]),
        ("sc_in",), ("sm_out_cc",), chan="cc", nslots=1, tag="ccsm", amt=1)
    for s in sch.chan_slots.get("ys", []):
        if sch.slot_last[s] is not None and sch.slot_last[s] is not ccsm:
            ccsm.deps.append(sch.slot_last[s])

    for rr in range(8):
        sch.dma("sync", sscraw[rr * 16:(rr + 1) * 16, :],
                sm_out.ap()[rr * (T + 64) + T: (rr + 1) * (T + 64), :]
                .rearrange("a b -> (a b)").rearrange("(p f) -> p f", p=16),
                ("sm_out_cc",), ("sscraw",), chan="e")
    for hblk in range(2):
        t_trf(PB[3][0:128, 0:128], sscraw[:, hblk * 128:(hblk + 1) * 128], 128,
              ("sscraw",), (KPB[3],))
        v_copy(ssc[:, hblk * 128:(hblk + 1) * 128], PB[3][0:128, 0:128], (KPB[3],), ("ssc",))
    v_ts(vcmp[:], ssc[:], cst(C_HALF), None, ALU.is_gt, ALU.bypass,
         ("ssc", "consts"), ("vcmp",))
    v_red(redc[:, 0:1], vcmp[:], ALU.add, ("vcmp",), ("redc",))
    v_red(redc[:, 1:2], ssc[:], ALU.add, ("ssc",), ("redc",))
    t_mm(PB[2][0:1, 0:1], redc[:, 0:1], onescol[:], True, True,
         ("redc", "onescol"), (KPB[2],), tag="cnt")
    t_mm(PB[2][0:1, 1:2], redc[:, 1:2], onescol[:], True, True,
         ("redc", "onescol"), (KPB[2],), tag="sum")
    v_copy(scal2[0:1, 0:2], PB[2][0:1, 0:2], (KPB[2],), ("scal2",))
    v_ts(scal2[0:1, 2:3], scal2[0:1, 1:2], cst(C_TAUS, 1), None, ALU.is_gt, ALU.bypass,
         ("scal2", "consts"), ("sc3",))
    v_ts(scal2[0:1, 3:4], scal2[0:1, 0:1], cst(C_ZERO, 1), None, ALU.is_gt, ALU.bypass,
         ("scal2", "consts"), ("sc4",))
    v_tt(scal2[0:1, 4:5], scal2[0:1, 2:3], scal2[0:1, 3:4], ALU.mult,
         ("sc3", "sc4"), ("cond1",))
    bcast(condcol[:], scal2[0:1, 4:5], 1, ("cond1",), ("condcol",))
    v_max(mx8[:], ssc[:], ("ssc",), ("mx8",))
    t_trf(PB[2][0:8, 128:256], mx8[:], 128, ("mx8",), (KPB[2],))
    v_copy(cnd8[:], PB[2][0:8, 128:256], (KPB[2],), ("cnd8",))
    sch.dma("gpsimd", cand[0:1, :].rearrange("o (j p) -> o j p", j=8), cnd8[:],
            ("cnd8",), ("cand",), chan="ec", nslots=1)
    for r in range(7):
        src, skey = (cand, "cand") if r % 2 == 0 else (candB, "candB")
        dst, dkey = (candB, "candB") if r % 2 == 0 else (cand, "cand")
        v_max(thetas[0:1, r * 8:(r + 1) * 8], src[0:1, :], (skey,), ("thetas",))
        v_matchrep(dst[0:1, :], thetas[0:1, r * 8:(r + 1) * 8], src[0:1, :], -1e30,
                   ("thetas", skey), (dkey,))
    for r in range(7):
        bcast(th8b[:], thetas[0:1, r * 8:(r + 1) * 8], 8, ("thetas",), ("th8b",))
        v_maxidx(mi8[:], th8b[:], ssc[:], ("th8b", "ssc"), ("mi8",))
        v_copy(mif[:], mi8[:], ("mi8",), ("mif",))
        v_ts(fhib[:], mif[:], cst(C_128), None, ALU.is_ge, ALU.bypass,
             ("mif", "consts"), ("fhib",))
        v_stt(flob[:], fhib[:], cst(C_N128), mif[:], ALU.mult, ALU.add,
              ("fhib", "mif", "consts"), ("flob",))
        v_ts(gpos2[:], flob[:], cst(C_256), piota_f[:], ALU.mult, ALU.add,
             ("flob", "consts", "piota_f"), ("gpos1",))
        v_stt(gpos[:], fhib[:], cst(C_128), gpos2[:], ALU.mult, ALU.add,
              ("gpos1", "fhib", "consts"), ("gpos",))
        t_trf(PB[2][0:8, 256:384], gpos[:], 128, ("gpos",), (KPB[2],))
        v_copy(gtr[:], PB[2][0:8, 256:384], (KPB[2],), ("gtr",))
        v_red(gall[:, r:r + 1], gtr[:], ALU.min, ("gtr",), ("gall",))
    for r in range(7):
        sch.dma("gpsimd", gcolf[r * 8:(r + 1) * 8, 0:1], gall[0:8, r:r + 1],
                ("gall",), ("gcolf",), chan=f"eg{r}", nslots=1)
    v_memset(radj[:], 0.0, ("radj",))
    for kk7 in range(1, 8):
        v_ts(rc1[:], gcolf[:], consts[0:NSEL, C_T1 + kk7 - 1: C_T1 + kk7], None,
             ALU.is_ge, ALU.bypass, ("gcolf", "consts"), ("rc1",))
        v_tt(radj2[:], radj[:], rc1[:], ALU.add, ("radj", "rc1"), ("radj",), tag="radd")
        radj, radj2 = radj2, radj
    v_stt(gcolf2[:], radj[:], cst(C_64, NSEL), gcolf[:], ALU.mult, ALU.add,
          ("radj", "gcolf", "consts"), ("gcolf2",))
    v_copy(gcolu[:], gcolf2[:], ("gcolf2",), ("gcolu",))
    sch.dma_op("gpsimd", lambda e: e.indirect_dma_start(
        out=poolraw[:], out_offset=None, in_=sm_out.ap(),
        in_offset=bass.IndirectOffsetOnAxis(ap=gcolu[:], axis=0)),
        ("gcolu", "sm_out_cc"), ("poolraw",), chan="ig", tag="gather")
    t_trf(PB[3][0:64, 128:128 + NSEL], poolraw[:], 56, ("poolraw",), (KPB[3],))
    v_copy(poolfm[:], PB[3][0:64, 128:128 + NSEL], (KPB[3],), ("poolfm",))
    v_ts(vmrow[:], thetas[0:1, 0:NPOOL], cst(C_HALF, 1), None, ALU.is_gt, ALU.bypass,
         ("thetas", "consts"), ("vmrow",))
    v_ts(mrow[:], vmrow[:], cst(C_QUART, 1), None, ALU.mult, ALU.bypass,
         ("vmrow", "consts"), ("mrow",))
    v_ts(brow[:], vmrow[:], cst(C_1E9, 1), cst(C_N1E9, 1), ALU.mult, ALU.add,
         ("vmrow", "consts"), ("brow",))
    bcast(mb128[:], mrow[:], NPOOL, ("mrow",), ("mb128",))
    bcast(bb128[:], brow[:], NPOOL, ("brow",), ("bb128",))
    sch.dma("sync", wk_f[:], wk_h, (), ("wk_f",), chan="e")
    sch.dma("sync", wv_f[:], wv_h, (), ("wv_f",), chan="e")
    v_copy(ckw[:], wk_f[:], ("wk_f",), ("ckw",))
    v_copy(cvw[:], wv_f[:], ("wv_f",), ("cvw",))
    t_mm(PB[3][0:64, 0:NSEL], ckw[:], poolfm[:], True, True, ("ckw", "poolfm"), (KPB[3],))
    v_copy(kkT[:], PB[3][0:64, 0:NSEL], (KPB[3],), ("kkT",))
    for half in range(2):
        t_mm(PB[4][0:NSEL, 0:512], poolfm[:], cvw[:, half * 512:(half + 1) * 512],
             True, True, ("poolfm", "cvw"), (KPB[4],))
        v_copy(vvtok[:, half * 512:(half + 1) * 512], PB[4][0:NSEL, 0:512],
               (KPB[4],), ("vvtok",))

    sch.oplimit = None
    if KDBG2:
        d2 = {}
        smin_dbg = nc.dram_tensor("dbg2_smin", [64, 64], F32, kind="ExternalOutput").ap()
        smsum_dbg = nc.dram_tensor("dbg2_smsum", [128, 64], F32, kind="ExternalOutput").ap()
        smout_dbg = nc.dram_tensor("dbg2_smout", [64, 64], F32, kind="ExternalOutput").ap()
        sscraw_dbg = nc.dram_tensor("dbg2_sscraw", [128, 256], F32, kind="ExternalOutput").ap()
        sch.dma("sync", smin_dbg, sm_in.ap()[T:T + 64, :], ("sc_in",), (), chan="oo")
        sch.dma("sync", smsum_dbg, sm_in.ap()[0:128, :], (), (), chan="oo")
        dbgcp = sch.dma("sync", smout_dbg, sm_out.ap()[T:T + 64, :], ("sm_out_cc",), (), chan="oo")
        sch.dma("sync", sscraw_dbg, sscraw[:], ("sscraw",), (), chan="oo")
        for nm, shp, dt_ in [("ssc", [128, 256], F32), ("cand", [1, 1024], F32),
                             ("thetas", [1, 64], F32), ("gcolf", [NSEL, 1], F32),
                             ("gcolf2", [NSEL, 1], F32), ("poolraw", [NSEL, SD], F32),
                             ("kkT", [64, NSEL], BF16), ("vvtok", [NSEL, D], BF16),
                             ("scal2", [1, 8], F32), ("mrow", [1, NPOOL], F32),
                             ("gall", [8, 8], F32)]:
            d2[nm] = nc.dram_tensor("dbg2_" + nm, shp, dt_, kind="ExternalOutput").ap()
        for nm, ap_, keys in [("ssc", ssc[:], ("ssc",)), ("cand", cand[0:1, :], ("cand",)),
                              ("thetas", thetas[:], ("thetas",)), ("gcolf", gcolf[:], ("gcolf",)),
                              ("gcolf2", gcolf2[:], ("gcolf2",)), ("poolraw", poolraw[:], ("poolraw",)),
                              ("kkT", kkT[:], ("kkT",)), ("vvtok", vvtok[:], ("vvtok",)),
                              ("scal2", scal2[:], ("scal2", "cond1", "sc3", "sc4")),
                              ("mrow", mrow[:], ("mrow",)), ("gall", gall[:], ("gall",))]:
            sch.dma("sync", d2[nm], ap_, keys, (), chan="oo")
    sch.barrier()
    stackS.close()

    if KPHASE <= 2:
        dummy = nc.alloc_sbuf_tensor("dummy2", [128, D], F32)
        sch.op("vector", lambda e: e.memset(dummy[:], 0.0), (), ("dummyk",), "dm")
        for chd in range(NCH):
            sch.dma("sync", out_h[chd * 128:(chd + 1) * 128, :], dummy[:], ("dummyk",), (), chan="oo")
        sch.finalize()
        with contextlib.ExitStack() as es:
            sems = {}
            for e in ENGS:
                sems[e] = es.enter_context(nc.semaphore(f"sem_{e}"))
            dma_sems = [es.enter_context(nc.semaphore(f"dsem{i}")) for i in range(sch.n_dma_slots)]
            block = es.enter_context(nc.Block())
            stats = sch.emit(block, sems, dma_sems)
        return nc, stats

    stackB1 = contextlib.ExitStack()
    SB1 = lambda name, shape, dt: stackB1.enter_context(nc.sbuf_tensor(name, shape, dt))
    cgate = SB1("cgate", [128, 16 * 1024], BF16)
    cq = SB1("cq", [128, 8 * SD], BF16)
    ytokB = SB1("ytokB", [128, 4 * D], F32)
    yfmB = SB1("yfmB", [128, 4 * 1024], BF16)
    xB = SB1("xB", [128, 4 * D], F32)
    qfm = SB1("qfm", [64, 512], BF16)
    lgb = SB1("lgb", [128, 2 * NPOOL], F32)
    ebuf = SB1("ebuf", [128, NPOOL], F32)
    esum = SB1("esum", [128, 2], F32)
    smb = SB1("smb", [128, NPOOL], BF16)
    smTb = SB1("smTb", [64, 4 * 128], BF16)
    retB = SB1("retB", [128, 8 * 512], BF16)
    gb = SB1("gb", [128, 512], BF16)
    t2m = SB1("t2m", [128, 512], BF16)
    t2B = SB1("t2B", [128, 8 * 512], BF16)
    t2tok = SB1("t2tok", [128, 4 * 1024], BF16)
    statB = SB1("statB", [128, 24], F32)
    xpxB = SB1("xpxB", [128, D], F32)
    ysumB = SB1("ysumB", [128, D], F32)
    xmoB = SB1("xmoB", [128, D], F32)
    wtq = SB1("wtq", [128, 4096], F32)
    wqt = SB1("wqt", [128, 4096], BF16)
    wqneg = SB1("wqneg", [128, 4096], BF16)
    wqt2 = SB1("wqt2", [128, 4096], BF16)
    posb = SB1("posb", [128, 4096], F32)
    wtmpB = SB1("wtmpB", [128, SD], F32)
    lnbw2 = SB1("lnbw2", [128, 64], BF16)

    for k in range(16):
        sch.dma("sync", wtq[:, 0:1024], wg_h[k * 128:(k + 1) * 128, :], (), ("wtq",), chan="w")
        v_copy(cgate[:, k * 1024:(k + 1) * 1024], wtq[:, 0:1024], ("wtq",), (f"cg_{k}",))
    for k in range(8):
        sch.dma("sync", wtmpB[:], wq_h[k * 128:(k + 1) * 128, :], (), ("wtqq",), chan="w")
        v_copy(cq[:, k * SD:(k + 1) * SD], wtmpB[:], ("wtqq",), (f"cq_{k}",))
    for k in range(32):
        v_copy(lnbw2[:, 2 * k:2 * k + 1], lnb16[:, k:k + 1], ("lnb16",), ("lnbw2",))
        v_copy(lnbw2[:, 2 * k + 1:2 * k + 2], lnw16[:, k:k + 1], ("lnw16",), ("lnbw2",))

    QUANT_TILES = [("fc", k) for k in range(8)] + [("pr", k) for k in range(32)]
    quant_i = [0]

    def emit_quant_tile():
        if quant_i[0] >= len(QUANT_TILES):
            return
        which, k = QUANT_TILES[quant_i[0]]
        quant_i[0] += 1
        n = 4096 if which == "fc" else 1024
        src = wfcT_h if which == "fc" else wprT_h
        dst = wqfc_hbm if which == "fc" else wqpr_hbm
        gc = 0 if which == "fc" else 1
        sch.dma("sync", wtq[:, 0:n], src[k * 128:(k + 1) * 128, :], (), ("wtq",),
                chan="g", nslots=2)
        v_ts(posb[:, 0:n], wtq[:, 0:n], ghalf[:, gc:gc + 1], None, ALU.is_gt, ALU.bypass,
             ("wtq", "ghalf"), ("posb",))
        v_ts(wqneg[:, 0:n], wtq[:, 0:n], gnegh[:, gc:gc + 1], None, ALU.is_lt, ALU.bypass,
             ("wtq", "gnegh"), ("wqnegk",))
        v_tt(wqt[:, 0:n], posb[:, 0:n], wqneg[:, 0:n], ALU.subtract,
             ("posb", "wqnegk"), ("wqt",))
        if which == "pr":
            for half in range(2):
                t_mm(PB[4 + half][0:2, 0:512], lnbw2[:, 2 * k:2 * k + 2],
                     wqt[:, half * 512:(half + 1) * 512], k == 0, k == 31,
                     ("wqt", "lnbw2"), (KPB[4 + half],), tag="cmm")
            v_ts(wqt2[:, 0:n], wqt[:, 0:n], lnwf32[:, k:k + 1], None, ALU.mult, ALU.bypass,
                 ("wqt", "lnwf32"), ("wqt2",))
            sch.dma("sync", dst[k, :, :], wqt2[:, 0:n], ("wqt2",), (), chan="qo")
        else:
            sch.dma("sync", dst[k, :, :], wqt[:, 0:n], ("wqt",), (), chan="qo")

    for i2 in range(min(8, NSC_LIM)):
        ch0 = i2 * 4
        for tt in range(4):
            ch = ch0 + tt
            sch.dma("sync", ytokB[:, tt * D:(tt + 1) * D], y_spill[ch, :, :], (), (f"ytB{tt}",), chan="x")
            sch.dma("sync", yfmB[:, tt * 1024:(tt + 1) * 1024], yfm_spill[ch, :, :], (), (f"yfB{tt}",), chan="x")
            sch.dma("sync", xB[:, tt * D:(tt + 1) * D], x_h[ch * 128:(ch + 1) * 128, :], (), (f"xB{tt}",), chan="x")
        ykeys = tuple(f"yfB{tt}" for tt in range(4))
        for k in range(8):
            t_mm(PB[2][0:64, 0:512], cq[:, k * SD:(k + 1) * SD],
                 yfmB[:].rearrange("p (t4 n) -> p t4 n", t4=4)[:, :, k * 128:(k + 1) * 128],
                 k == 0, k == 7, (f"cq_{k}",) + ykeys, (KPB[2],), tag="q")
        v_copy(qfm[:], PB[2][0:64, 0:512], (KPB[2],), ("qfm",))
        for tt in range(4):
            t_mm(PB[3][:, 0:NPOOL], qfm[:, tt * 128:(tt + 1) * 128], kkT[0:64, 0:NPOOL],
                 True, True, ("qfm", "kkT"), (KPB[3],), tag="lg")
            v_tt(lgb[:, 0:NPOOL], PB[3][:, 0:NPOOL], mb128[:], ALU.mult,
                 (KPB[3], "mb128"), ("lg1",))
            v_tt(lgb[:, NPOOL:2 * NPOOL], lgb[:, 0:NPOOL], bb128[:], ALU.add,
                 ("lg1", "bb128"), ("lg2",))
            v_red(statB[:, 0:1], lgb[:, NPOOL:2 * NPOOL], ALU.max, ("lg2",), ("negmx",), negate=True)
            s_act(ebuf[:], lgb[:, NPOOL:2 * NPOOL], ACT.Exp, ("lg2", "negmx"), ("ebuf",),
                  bias=statB[:, 0:1], accum=esum[:, 0:1])
            v_recip(esum[:, 1:2], esum[:, 0:1], ("ebuf",), ("esr",))
            v_ts(smb[:], ebuf[:], esum[:, 1:2], None, ALU.mult, ALU.bypass,
                 ("ebuf", "esr"), ("smb",))
            t_tr(PT[0][0:NPOOL, 0:128], smb[:], 128, ("smb",), (KPT[0],))
            v_copy(smTb[0:NPOOL, tt * 128:(tt + 1) * 128], PT[0][0:NPOOL, 0:128],
                   (KPT[0],), ("smTb",))
        for k in range(8):
            t_mm(PB[k % 2][:, 0:512], vvtok[0:NPOOL, k * 128:(k + 1) * 128],
                 smTb[0:NPOOL, 0:512], True, True, ("vvtok", "smTb"), (KPB[k % 2],), tag="ret")
            v_copy(retB[:, k * 512:(k + 1) * 512], PB[k % 2][:, 0:512],
                   (KPB[k % 2],), (f"ret{k}",))
        for m in range(8):
            pb, pkey = PB[m % 2], KPB[m % 2]
            for k in range(8):
                t_mm(pb[:, 0:512], cgate[:, k * 1024 + m * 128: k * 1024 + m * 128 + 128],
                     yfmB[:].rearrange("p (t4 n) -> p t4 n", t4=4)[:, :, k * 128:(k + 1) * 128],
                     k == 0, False, (f"cg_{k}",) + ykeys, (pkey,), tag="gate")
            for k in range(8):
                t_mm(pb[:, 0:512], cgate[:, (8 + k) * 1024 + m * 128: (8 + k) * 1024 + m * 128 + 128],
                     retB[:, k * 512:(k + 1) * 512], False, k == 7,
                     (f"cg_{8 + k}", f"ret{k}"), (pkey,), tag="gate2")
            s_act(gb[:], pb[:, 0:512], ACT.Sigmoid, (pkey,), ("gb",))
            v_tt(t2m[:], gb[:], retB[:, m * 512:(m + 1) * 512], ALU.mult,
                 ("gb", f"ret{m}"), ("t2m",))
            v_ts(t2B[:, m * 512:(m + 1) * 512], t2m[:], condcol[:], None, ALU.mult, ALU.bypass,
                 ("t2m", "condcol"), (f"t2B{m}",))
        for m in range(8):
            for tt in range(4):
                t_tr(PT[1][0:128, 0:128], t2B[:, m * 512 + tt * 128: m * 512 + (tt + 1) * 128],
                     128, (f"t2B{m}",), (KPT[1],))
                v_copy(t2tok[:, tt * 1024 + m * 128: tt * 1024 + (m + 1) * 128],
                       PT[1][0:128, 0:128], (KPT[1],), (f"t2tok{tt}",))
        for tt in range(4):
            ch = ch0 + tt
            sch.op("vector", lambda e, tt=tt: e.bn_stats(statB[:, 2:8], xB[:, tt * D: tt * D + 512]),
                   (f"xB{tt}",), ("stB",))
            sch.op("vector", lambda e, tt=tt: e.bn_stats(statB[:, 8:14], xB[:, tt * D + 512:(tt + 1) * D]),
                   (f"xB{tt}",), ("stB",))
            sch.op("vector", lambda e: e.bn_aggr(statB[:, 14:16], statB[:, 2:14]),
                   ("stB",), ("stB",))
            mean, var = statB[:, 14:15], statB[:, 15:16]
            veps, inv = statB[:, 16:17], statB[:, 17:18]
            q1, q2, lnscr = statB[:, 18:19], statB[:, 19:20], statB[:, 20:21]
            v_ts(veps, var, cst(C_EPS), None, ALU.add, ALU.bypass, ("stB", "consts"), ("stB1",))
            rsqrt2(inv, veps, lnscr, 128, ("stB1",), ("stB2",))
            v_ts(q1, inv, cst(C_ONE), None, ALU.add, ALU.bypass, ("stB2", "consts"), ("stB3",))
            v_ts(q2, mean, inv, None, ALU.mult, ALU.bypass, ("stB", "stB2"), ("stB4",))
            v_ts(xpxB[:], xB[:, tt * D:(tt + 1) * D], q1, q2,
                 ALU.mult, ALU.subtract, (f"xB{tt}", "stB3", "stB4"), ("xpxB",))
            v_tt(ysumB[:], ytokB[:, tt * D:(tt + 1) * D], t2tok[:, tt * 1024:(tt + 1) * 1024],
                 ALU.add, (f"ytB{tt}", f"t2tok{tt}"), ("ysumB",))
            v_tt(xmoB[:], xpxB[:], ysumB[:], ALU.add, ("xpxB", "ysumB"), ("xmoB",))
            sch.dma("sync", xmid_spill[ch, :, :], xmoB[:], ("xmoB",), (), chan="xo")
        for _ in range(5):
            emit_quant_tile()

    while quant_i[0] < len(QUANT_TILES):
        emit_quant_tile()
    crow_st = SB1("crow_st", [2, 1024], F32)
    v_copy(crow_st[0:2, 0:512], PB[4][0:2, 0:512], (KPB[4],), ("c1row",))
    v_copy(crow_st[0:2, 512:1024], PB[5][0:2, 0:512], (KPB[5],), ("c2row",))
    sch.dma("sync", c_hbm[:, :], crow_st[:], ("c1row", "c2row"), ("c_hbm",), chan="xo")

    sch.barrier()
    stackB1.close()

    if KPHASE <= 3:
        dummy = nc.alloc_sbuf_tensor("dummy3", [128, D], F32)
        sch.op("vector", lambda e: e.memset(dummy[:], 0.0), (), ("dummyk",), "dm")
        for chd in range(NCH):
            sch.dma("sync", out_h[chd * 128:(chd + 1) * 128, :], dummy[:], ("dummyk",), (), chan="oo")
        sch.finalize()
        with contextlib.ExitStack() as es:
            sems = {}
            for e in ENGS:
                sems[e] = es.enter_context(nc.semaphore(f"sem_{e}"))
            dma_sems = [es.enter_context(nc.semaphore(f"dsem{i}")) for i in range(sch.n_dma_slots)]
            block = es.enter_context(nc.Block())
            stats = sch.emit(block, sems, dma_sems)
        return nc, stats

    stackB2 = contextlib.ExitStack()
    SB2 = lambda name, shape, dt: stackB2.enter_context(nc.sbuf_tensor(name, shape, dt))
    wqfcS = SB2("wqfcS", [128, 8 * 4096], BF16)
    wqprS = SB2("wqprS", [128, 32 * 1024], BF16)
    c1b = SB2("c1b", [128, 1024], F32)
    c2b = SB2("c2b", [128, 1024], F32)
    xmB2 = SB2("xmB2", [128, 2 * D], F32)
    x2tok = SB2("x2tok", [128, 2 * D], BF16)
    x2fm = SB2("x2fm", [128, 8 * 256], BF16)
    gbuf = SB2("gbuf", [128, 32 * 256], BF16)
    sqbB = SB2("sqbB", [128, 2 * 256], BF16)
    atmp = SB2("atmp", [128, 2 * 256], BF16)
    statB2 = SB2("statB2", [128, 24], F32)
    murow = SB2("murow", [1, 256], F32)
    m2row = SB2("m2row", [1, 256], F32)
    varrow = SB2("varrow", [1, 256], F32)
    var2row = SB2("var2row", [1, 256], F32)
    lnsrow = SB2("lnsrow", [1, 256], F32)
    rsrow2 = SB2("rsrow2", [1, 256], F32)
    muarow = SB2("muarow", [1, 256], F32)
    nmuarow = SB2("nmuarow", [1, 256], F32)
    rsb2 = SB2("rsb2", [128, 256], F32)
    nmacol = SB2("nmacol", [128, 2], F32)
    corrb = SB2("corrb", [128, 512], F32)
    corr2b = SB2("corr2b", [128, 512], F32)
    outsb = SB2("outsb", [128, 2 * D], F32)

    for k in range(8):
        sch.dma("sync", wqfcS[:, k * 4096:(k + 1) * 4096], wqfc_hbm[k, :, :], (), (f"wf_{k}",), chan="w")
    for k in range(32):
        sch.dma("sync", wqprS[:, k * 1024:(k + 1) * 1024], wqpr_hbm[k, :, :], (), (f"wp_{k}",), chan="w")
    sch.dma("gpsimd", c1b[:], c_hbm[0:1, :].unsqueeze(1).broadcast_to((1, 128, 1024)),
            ("c_hbm",), ("c1b",), chan="f")
    sch.dma("gpsimd", c2b[:], c_hbm[1:2, :].unsqueeze(1).broadcast_to((1, 128, 1024)),
            ("c_hbm",), ("c2b",), chan="f")

    for i3 in range(min(16, NSC_LIM)):
        ch0 = i3 * 2
        for tt2 in range(2):
            sch.dma("sync", xmB2[:, tt2 * D:(tt2 + 1) * D], xmid_spill[ch0 + tt2, :, :],
                    (), (f"xm2_{tt2}",), chan="x")
            sch.op("vector", lambda e, tt2=tt2: e.bn_stats(statB2[:, 0:6],
                   xmB2[:, tt2 * D: tt2 * D + 512]), (f"xm2_{tt2}",), ("st2",))
            sch.op("vector", lambda e, tt2=tt2: e.bn_stats(statB2[:, 6:12],
                   xmB2[:, tt2 * D + 512:(tt2 + 1) * D]), (f"xm2_{tt2}",), ("st2",))
            sch.op("vector", lambda e: e.bn_aggr(statB2[:, 12:14], statB2[:, 0:12]),
                   ("st2",), ("st2",))
            mean, var = statB2[:, 12:13], statB2[:, 13:14]
            veps, inv, lnscr = statB2[:, 14:15], statB2[:, 15:16], statB2[:, 16:17]
            v_ts(veps, var, cst(C_EPS), None, ALU.add, ALU.bypass, ("st2", "consts"), ("st2a",))
            rsqrt2(inv, veps, lnscr, 128, ("st2a",), ("st2b",))
            v_ts(x2tok[:, tt2 * D:(tt2 + 1) * D], xmB2[:, tt2 * D:(tt2 + 1) * D], mean, inv,
                 ALU.subtract, ALU.mult, (f"xm2_{tt2}", "st2", "st2b"), (f"x2_{tt2}",))
            for k in range(8):
                t_tr(PT[0][0:128, 0:128], x2tok[:, tt2 * D + k * 128: tt2 * D + (k + 1) * 128],
                     128, (f"x2_{tt2}",), (KPT[0],))
                v_copy(x2fm[:, k * 256 + tt2 * 128: k * 256 + (tt2 + 1) * 128],
                       PT[0][0:128, 0:128], (KPT[0],), (f"x2f{k}",))
        x2keys = tuple(f"x2f{k}" for k in range(8))
        for j in range(32):
            pb, pkey = PB[j % 2], KPB[j % 2]
            for k in range(8):
                t_mm(pb[:, 0:256], wqfcS[:, k * 4096 + j * 128: k * 4096 + (j + 1) * 128],
                     x2fm[:, k * 256:(k + 1) * 256], k == 0, k == 7,
                     (f"wf_{k}",) + x2keys, (pkey,), tag="hmm")
            s_act(gbuf[:, j * 256:(j + 1) * 256], pb[:, 0:256], ACT.Gelu, (pkey,), (f"g{j}",))
            sq = sqbB[:, (j % 2) * 256:(j % 2 + 1) * 256]
            v_tt(sq, gbuf[:, j * 256:(j + 1) * 256], gbuf[:, j * 256:(j + 1) * 256],
                 ALU.mult, (f"g{j}",), (f"sqB{j % 2}",), tag="sqB")
            t_mm(PB[4][0:1, 0:256], onescol16[:], gbuf[:, j * 256:(j + 1) * 256],
                 j == 0, j == 31, (f"g{j}", "onescol16"), (KPB[4],), tag="sgmm")
            t_mm(PB[5][0:1, 0:256], onescol16[:], sq, j == 0, j == 31,
                 (f"sqB{j % 2}", "onescol16"), (KPB[5],), tag="sqmm2")
        v_ts(murow[:], PB[4][0:1, 0:256], cst(C_I4096, 1), None, ALU.mult, ALU.bypass,
             (KPB[4], "consts"), ("murow",))
        v_tt(m2row[:], murow[:], murow[:], ALU.mult, ("murow",), ("m2row",))
        v_stt(varrow[:], PB[5][0:1, 0:256], cst(C_I4096, 1), m2row[:], ALU.mult, ALU.subtract,
              (KPB[5], "m2row", "consts"), ("varrow",))
        v_ts(var2row[:], varrow[:], cst(C_EPS, 1), None, ALU.add, ALU.bypass,
             ("varrow", "consts"), ("var2row",))
        s_act(lnsrow[:], var2row[:], ACT.Ln, ("var2row",), ("lnsrow",))
        s_act(rsrow2[:], lnsrow[:], ACT.Exp, ("lnsrow", "consts"), ("rsrow2",),
              scale=cst(C_NEGH, 1))
        v_tt(muarow[:], murow[:], rsrow2[:], ALU.mult, ("murow", "rsrow2"), ("muarow",))
        v_ts(nmuarow[:], muarow[:], cst(C_NEG1, 1), None, ALU.mult, ALU.bypass,
             ("muarow", "consts"), ("nmuarow",))
        bcast(rsb2[:], rsrow2[:], 256, ("rsrow2",), ("rsb2",))
        for tt2 in range(2):
            t_trf(PB[2][0:128, 0:1], nmuarow[0:1, tt2 * 128:(tt2 + 1) * 128], 1,
                  ("nmuarow",), (KPB[2],))
            v_copy(nmacol[:, tt2:tt2 + 1], PB[2][0:128, 0:1], (KPB[2],), ("nmacol",))
        for j in range(32):
            at = atmp[:, (j % 2) * 256:(j % 2 + 1) * 256]
            v_tt(at, gbuf[:, j * 256:(j + 1) * 256], rsb2[:], ALU.mult,
                 (f"g{j}", "rsb2"), (f"at{j % 2}",), tag="aA")
            for tt2 in range(2):
                for dh in range(2):
                    pb, pkey = PB[tt2 * 2 + dh], KPB[tt2 * 2 + dh]
                    t_mm(pb[:, 0:512], at[:, tt2 * 128:(tt2 + 1) * 128],
                         wqprS[:, j * 1024 + dh * 512: j * 1024 + (dh + 1) * 512],
                         j == 0, j == 31, (f"at{j % 2}", f"wp_{j}"), (pkey,), tag="pmm")
        for tt2 in range(2):
            for dh in range(2):
                pb, pkey = PB[tt2 * 2 + dh], KPB[tt2 * 2 + dh]
                v_stt(corrb[:], c2b[:, dh * 512:(dh + 1) * 512], nmacol[:, tt2:tt2 + 1],
                      c1b[:, dh * 512:(dh + 1) * 512], ALU.mult, ALU.add,
                      ("c2b", "c1b", "nmacol"), ("corrb",))
                v_tt(corr2b[:], corrb[:], pb[:, 0:512], ALU.add, ("corrb", pkey), ("corr2b",))
                v_tt(outsb[:, tt2 * D + dh * 512: tt2 * D + (dh + 1) * 512], corr2b[:],
                     xmB2[:, tt2 * D + dh * 512: tt2 * D + (dh + 1) * 512], ALU.add,
                     ("corr2b", f"xm2_{tt2}"), (f"osb{tt2}",))
            sch.dma("sync", out_h[(ch0 + tt2) * 128:(ch0 + tt2 + 1) * 128, :],
                    outsb[:, tt2 * D:(tt2 + 1) * D], (f"osb{tt2}",), (), chan="oo")

    sch.finalize()
    stackB2.close()

    with contextlib.ExitStack() as es:
        sems = {}
        for e in ENGS:
            sems[e] = es.enter_context(nc.semaphore(f"sem_{e}"))
        dma_sems = [es.enter_context(nc.semaphore(f"dsem{i}")) for i in range(sch.n_dma_slots)]
        block = es.enter_context(nc.Block())
        stats = sch.emit(block, sems, dma_sems)
    return nc, stats


_CACHE = {}


def _get_program():
    if "nc" not in _CACHE:
        nc, stats = build_program()
        _CACHE["nc"] = nc
        _CACHE["stats"] = stats
    return _CACHE["nc"]


def stage_inputs(**inputs):
    x = np.asarray(inputs["L0_attn_x"], dtype=np.float32)
    conv_w = np.asarray(inputs["L0_attn_conv_w"], dtype=np.float32)
    conv_b = np.asarray(inputs["L0_attn_conv_b"], dtype=np.float32)
    norm_w = np.asarray(inputs["L0_attn_norm_w"], dtype=np.float32)
    ssm_w = np.asarray(inputs["L0_attn_ssm_norm_w"], dtype=np.float32)
    ln_w = np.asarray(inputs["L0_ffn_ln_w"], dtype=np.float32)
    ln_b = np.asarray(inputs["L0_ffn_ln_b"], dtype=np.float32)
    Dh = np.asarray(inputs["L0_attn_D"], dtype=np.float32)

    common = {
        "w_inproj": np.ascontiguousarray(inputs["L0_attn_in_proj_w"], dtype=np.float32),
        "w_out": np.ascontiguousarray(inputs["L0_attn_out_proj_w"], dtype=np.float32),
        "w_sc1": np.ascontiguousarray(inputs["L0_attn_scorer_w1"], dtype=np.float32),
        "w_sc2": np.ascontiguousarray(inputs["L0_attn_scorer_w2"], dtype=np.float32),
        "w_summ": np.ascontiguousarray(inputs["L0_attn_summ_w"], dtype=np.float32),
        "w_q": np.ascontiguousarray(inputs["L0_attn_q_w"], dtype=np.float32),
        "w_k": np.ascontiguousarray(inputs["L0_attn_k_w"], dtype=np.float32),
        "w_v": np.ascontiguousarray(inputs["L0_attn_v_w"], dtype=np.float32),
        "w_gate": np.ascontiguousarray(inputs["L0_attn_gate_w"], dtype=np.float32),
        "w_fcT": np.ascontiguousarray(np.asarray(inputs["L0_ffn_W_fc"], dtype=np.float32).T),
        "w_projT": np.ascontiguousarray(np.asarray(inputs["L0_ffn_W_proj"], dtype=np.float32).T),
        "conv_w_t": np.ascontiguousarray(
            conv_w.reshape(18, 128, 4).transpose(1, 0, 2).reshape(128, 72)),
        "conv_b_t": np.ascontiguousarray(conv_b.reshape(18, 128).T),
        "norm_w_t": np.ascontiguousarray(norm_w.reshape(8, 128).T),
        "ssm_w_t": np.ascontiguousarray(ssm_w.reshape(16, 128).T),
        "ln_w_t": np.ascontiguousarray(ln_w.reshape(32, 128).T),
        "ln_b_t": np.ascontiguousarray(ln_b.reshape(32, 128).T),
        "dt_bias_t": np.ascontiguousarray(
            np.asarray(inputs["L0_attn_dt_bias"], dtype=np.float32).reshape(32, 1)),
        "a_log_t": np.ascontiguousarray(
            np.asarray(inputs["L0_attn_A_log"], dtype=np.float32).reshape(32, 1)),
        "d_fm_t": np.ascontiguousarray(np.concatenate(
            [np.broadcast_to(Dh[0::2], (64, 16)), np.broadcast_to(Dh[1::2], (64, 16))], axis=0)),
        "ident": np.eye(128, dtype=ml_dtypes.bfloat16),
        "identf": np.eye(128, dtype=np.float32),
        "mask_sl": np.ascontiguousarray(np.triu(np.ones((128, 128), dtype=np.float32))),
    }
    in_maps = [dict(common, x=np.ascontiguousarray(x[b])) for b in range(NCORES)]
    return in_maps


def kernel(**inputs):
    in_maps = stage_inputs(**inputs)
    nc = _get_program()
    res = run_bass_kernel_spmd(nc, in_maps, core_ids=list(range(NCORES)))
    out = np.stack([np.asarray(r["out"], dtype=np.float32) for r in res.results], axis=0)
    return out


# version 11
# speedup vs baseline: 1.3370x; 1.3370x over previous
import os
import contextlib
import numpy as np
import ml_dtypes
import concourse.bass as bass
import concourse.mybir as mybir
from concourse.bass_utils import run_bass_kernel_spmd

F32 = mybir.dt.float32
BF16 = mybir.dt.bfloat16
U32 = mybir.dt.uint32
ALU = mybir.AluOpType
ACT = mybir.ActivationFunctionType
AX = mybir.AxisListType

import os
NSC_LIM = int(os.environ.get("KNSC", "16"))
NO_CC = os.environ.get("KNOCC", "0") == "1"
NO_IG = os.environ.get("KNOIG", "0") == "1"
NO_MX = os.environ.get("KNOMX", "0") == "1"
KPHASE = int(os.environ.get("KPHASE", "4"))
KSELCUT = int(os.environ.get("KSELCUT", "99"))
KDBG = os.environ.get("KDBG", "0") == "1"
KDBG2 = os.environ.get("KDBG2", "0") == "1"
T = 4096
D = 1024
L = 128
NCH = T // L
SC = 256
NSC = T // SC
H = 32
P = 64
DIN = 2048
DIP = 4384
HID = 256
SD = 64
NPOOL = 50
NSEL = 56
NCORES = 8
ENGS = ("tensor", "vector", "scalar", "gpsimd", "sync")


class Sched:

    class Op:
        __slots__ = ("eng", "fn", "reads", "writes", "idx", "deps", "inc",
                     "is_dma", "dma_sem", "dma_seq", "dma_amt", "tag",
                     "need_drain", "need_local", "drain_before")

        def __init__(self, eng, fn, reads, writes, idx, tag):
            self.eng, self.fn = eng, fn
            self.reads, self.writes = reads, writes
            self.idx = idx
            self.deps = []
            self.inc = False
            self.need_drain = False
            self.need_local = False
            self.drain_before = False
            self.is_dma = False
            self.dma_sem = None
            self.dma_seq = 0
            self.dma_amt = 16
            self.tag = tag

    def __init__(self, n_dma_slots=None):
        if n_dma_slots is None:
            n_dma_slots = int(os.environ.get('KSLOTS', '64'))
        self.ops = {e: [] for e in ENGS}
        self.last_writer = {}
        self.readers = {}
        self.n_dma_slots = n_dma_slots
        self.slot_seq = [0] * n_dma_slots
        self.slot_last = [None] * n_dma_slots
        self.chan_slots = {}
        self.chan_uses = {}
        self._next_slot = 0
        self.oplimit = None

    def _muted(self):
        if self.oplimit is None:
            return False
        self.oplimit -= 1
        if self.oplimit < 0:
            return True
        return False

    def _record(self, eng, fn, reads, writes, tag):
        if self._muted():
            return Sched.Op(eng, None, (), (), -1, "muted")
        op = Sched.Op(eng, fn, tuple(reads), tuple(writes), len(self.ops[eng]), tag)
        deps = []
        for k in op.reads:
            w = self.last_writer.get(k)
            if w is not None:
                deps.append((w, 0))
        for k in op.writes:
            w = self.last_writer.get(k)
            if w is not None:
                deps.append((w, 1))
            for r in self.readers.get(k, ()):
                deps.append((r, 2))
        seen = {}
        for d, kind in deps:
            key = (d.eng, d.idx)
            if key not in seen:
                seen[key] = len(op.deps)
                op.deps.append((d, kind))
            else:
                i = seen[key]
                if kind < op.deps[i][1]:
                    op.deps[i] = (d, kind)
        for k in op.writes:
            self.last_writer[k] = op
            self.readers[k] = []
        for k in op.reads:
            self.readers.setdefault(k, []).append(op)
        self.ops[eng].append(op)
        return op

    def op(self, eng, fn, reads=(), writes=(), tag=""):
        return self._record(eng, fn, reads, writes, tag)

    def dma_op(self, eng, fn, reads=(), writes=(), chan="d", nslots=2, tag="dmaop", amt=16):
        slots = self.chan_slots.setdefault(chan, [])
        if len(slots) < nslots and self._next_slot < self.n_dma_slots:
            slots.append(self._next_slot)
            self._next_slot += 1
        use = self.chan_uses.get(chan, 0)
        self.chan_uses[chan] = use + 1
        slot = slots[use % len(slots)]
        op = self._record(eng, fn, reads, writes, tag)
        if op.idx < 0:
            return op
        op.is_dma = True
        op.dma_sem = slot
        op.dma_amt = amt
        prev = self.slot_last[slot]
        if prev is not None:
            op.deps.append((prev, False))
        self.slot_seq[slot] += 1
        op.dma_seq = self.slot_seq[slot]
        self.slot_last[slot] = op
        return op

    def dma(self, eng, out_ap, in_ap, reads=(), writes=(), chan="d", nslots=4, tag="dma"):
        nslots = min(nslots, int(os.environ.get('KNSLOT', '4')))
        slots = self.chan_slots.setdefault(chan, [])
        if len(slots) < nslots and self._next_slot < self.n_dma_slots:
            slots.append(self._next_slot)
            self._next_slot += 1
        use = self.chan_uses.get(chan, 0)
        self.chan_uses[chan] = use + 1
        slot = slots[use % len(slots)]

        def fn(e, out_ap=out_ap, in_ap=in_ap):
            return e.dma_start(out=out_ap, in_=in_ap)

        op = self._record(eng, fn, reads, writes, tag)
        if op.idx < 0:
            return op
        op.is_dma = True
        op.dma_sem = slot
        prev = self.slot_last[slot]
        if prev is not None:
            op.deps.append((prev, False))
        self.slot_seq[slot] += 1
        op.dma_seq = self.slot_seq[slot]
        self.slot_last[slot] = op
        return op

    def barrier(self):
        marks = {}
        for e in ENGS:
            m = None
            for op in reversed(self.ops[e]):
                if op.fn is not None and not op.is_dma:
                    m = op
                    break
            if m is not None:
                m.inc = True
                m.need_drain = True
            marks[e] = m
        for e in ENGS:
            w = self._record(e, None, (), (), "barrier_w")
            for o in ENGS:
                if o != e and marks[o] is not None:
                    w.deps.append((marks[o], False))
            for s in range(self.n_dma_slots):
                if self.slot_last[s] is not None:
                    w.deps.append((self.slot_last[s], False))

    def finalize(self):
        w = self._record("sync", (lambda en: en.nop()), (), (), "final")
        for s in range(self.n_dma_slots):
            if self.slot_last[s] is not None:
                w.deps.append((self.slot_last[s], False))

    def emit(self, block, sems, dma_sems):
        paranoid = os.environ.get("KPAR", "0") == "1"
        audit = os.environ.get("KAUDIT", "0") == "1"
        if audit:
            paranoid = True
        local_edges = {e: [] for e in ENGS}
        for e in ENGS:
            for op in self.ops[e]:
                if paranoid and not op.is_dma and op.fn is not None:
                    op.inc = True
                    if audit:
                        op.need_drain = True
                for d, kind in op.deps:
                    cross = d.eng != e or op.is_dma
                    if (not d.is_dma) and (cross or paranoid):
                        d.inc = True
                    if kind == 2 or d.is_dma:
                        continue
                    if cross:
                        d.need_drain = True
                    elif kind == 0:
                        local_edges[e].append((d.idx, op.idx))
        for e in ENGS:
            pts = sorted(o.idx + 0.5 for o in self.ops[e]
                         if o.need_drain or (audit and o.inc))
            import bisect as _bi
            for p, c in sorted(local_edges[e], key=lambda pc: pc[1]):
                i = _bi.bisect_right(pts, p)
                if i < len(pts) and pts[i] < c:
                    continue
                self.ops[e][c].drain_before = True
                _bi.insort(pts, c - 0.5)
        inccount = {}
        for e in ENGS:
            c = 0
            for op in self.ops[e]:
                if op.inc and not op.is_dma:
                    c += 1
                inccount[(e, op.idx)] = c

        def make_section(ename):
            ops_list = self.ops[ename]

            def section(eng):
                waited = {}
                for op in ops_list:
                    want = {}
                    for d, kind in op.deps:
                        if d.is_dma:
                            k = ("dma", d.dma_sem)
                            want[k] = max(want.get(k, 0), d.dma_amt * d.dma_seq)
                        elif d.eng != ename or paranoid or op.is_dma:
                            want[d.eng] = max(want.get(d.eng, 0), inccount[(d.eng, d.idx)])
                    for k, v in sorted(want.items(), key=str):
                        if waited.get(k, -1) < v:
                            if isinstance(k, tuple):
                                eng.wait_ge(dma_sems[k[1]], v)
                            else:
                                eng.wait_ge(sems[k], v)
                            waited[k] = v
                    if op.fn is None:
                        continue
                    if op.drain_before:
                        eng.drain(fusable=True)
                    instr = op.fn(eng)
                    if op.is_dma:
                        instr.then_inc(dma_sems[op.dma_sem], op.dma_amt)
                    elif op.inc:
                        if op.need_drain:
                            eng.maybe_drain_then_inc((sems[ename], 1), fusable=True)
                        else:
                            if instr is None or ename == "gpsimd":
                                try:
                                    instr = eng.engine_nop()
                                except Exception:
                                    instr = eng.nop()
                            instr.then_inc(sems[ename], 1)
            return section

        for ename in ENGS:
            getattr(block, ename)(make_section(ename))
        return {e: len(self.ops[e]) for e in ENGS} | {
            "incs": sum(1 for e in ENGS for o in self.ops[e] if o.inc)}


def build_program():
    nc = bass.Bass("TRN2", target_bir_lowering=False,
                   detect_race_conditions=os.environ.get("KRACE", "0") == "1")
    sch = Sched()

    def dramin(name, shape, dt=F32):
        return nc.dram_tensor(name, list(shape), dt, kind="ExternalInput").ap()

    x_h = dramin("x", (T, D))
    w1_h = dramin("w_inproj", (D, DIP))
    wo_h = dramin("w_out", (DIN, D))
    wsc1_h = dramin("w_sc1", (D, HID))
    wsc2_h = dramin("w_sc2", (HID, 1))
    wsm_h = dramin("w_summ", (D, SD))
    wq_h = dramin("w_q", (D, SD))
    wk_h = dramin("w_k", (SD, SD))
    wv_h = dramin("w_v", (SD, D))
    wg_h = dramin("w_gate", (DIN, D))
    wfcT_h = dramin("w_fcT", (D, 4096))
    wprT_h = dramin("w_projT", (4096, D))
    cwv_h = dramin("conv_w_t", (128, 72))
    ccb_h = dramin("conv_b_t", (128, 18))
    nw_h = dramin("norm_w_t", (128, 8))
    sw_h = dramin("ssm_w_t", (128, 16))
    lnw_h = dramin("ln_w_t", (128, 32))
    lnb_h = dramin("ln_b_t", (128, 32))
    dtb_h = dramin("dt_bias_t", (32, 1))
    alog_h = dramin("a_log_t", (32, 1))
    dfm_h = dramin("d_fm_t", (128, 16))
    ident_h = dramin("ident", (128, 128), BF16)
    identf_h = dramin("identf", (128, 128), F32)
    mask_h = dramin("mask_sl", (128, 128))

    out_h = nc.dram_tensor("out", [T, D], F32, kind="ExternalOutput").ap()

    y_spill = nc.dram_tensor("y_spill", [NCH, 128, D], F32).ap()
    dbg_h = {}
    if KDBG:
        for nm, shp, dt_ in [("xntok", [128, D], BF16), ("xnf", [128, 2048], BF16),
                             ("sz", [128, 16 * SC], BF16), ("xbcc", [128, 18 * SC], BF16),
                             ("dtv", [32, SC], F32), ("csv", [32, L], F32), ("statc", [128, 20], F32),
                             ("xdt", [128, 2048], BF16), ("yfmsc", [128, 2048], BF16),
                             ("yop", [128, 2048], BF16), ("dsv", [32, L], F32),
                             ("expcs32", [32, L], F32), ("sall", [128, 2048], BF16),
                             ("gmask", [128, 128], BF16), ("mtb0", [128, 128], BF16),
                             ("xdt2", [128, 2048], BF16), ("dtT16", [128, 32], BF16),
                             ("mtbh2", [128, 128], BF16), ("ssegh2", [128, 128], F32),
                             ("identd", [128, 128], BF16), ("identfd", [128, 128], F32),
                             ("constsd", [128, 28], F32),
                             ("csbh2", [128, 512], F32), ("csTd", [128, 32], F32)]:
            dbg_h[nm] = nc.dram_tensor("dbg_" + nm, shp, dt_, kind="ExternalOutput").ap()
    yfm_spill = nc.dram_tensor("yfm_spill", [NCH, 128, 1024], BF16).ap()
    xmid_spill = nc.dram_tensor("xmid_spill", [NCH, 128, D], F32).ap()
    wqfc_hbm = nc.dram_tensor("wqfc_hbm", [8, 128, 4096], BF16).ap()
    wqpr_hbm = nc.dram_tensor("wqpr_hbm", [32, 128, D], BF16).ap()
    sm_in = nc.dram_tensor("sm_in", [T + 64, SD], F32)
    sm_out = nc.dram_tensor("sm_out", [8 * (T + 64), SD], F32)
    c_hbm = nc.dram_tensor("c_hbm", [2, 1024], F32).ap()

    A = nc.alloc_sbuf_tensor
    consts = A("constssb", [128, 28], F32)
    (C_HALF, C_EPS, C_NEG1, C_NEGH, C_I2048, C_I4096, C_ONE, C_TAUS,
     C_1E9, C_N1E9, C_128, C_QUART, C_IFC, C_ZERO, C_1E6M, C_256, C_N128,
     C_T1, C_T2, C_T3, C_T4, C_T5, C_T6, C_T7, C_64) = range(25)
    CONST_VALS = {C_HALF: 0.5, C_EPS: 1e-5, C_NEG1: -1.0, C_NEGH: -0.5,
                  C_I2048: 1.0 / 2048, C_I4096: 1.0 / 4096, C_ONE: 1.0,
                  C_TAUS: 0.3 * 32768.0, C_1E9: 1e9, C_N1E9: -1e9,
                  C_128: 128.0, C_QUART: 0.25, C_IFC: float(2.0 ** -22),
                  C_ZERO: 0.0, C_1E6M: 1e-6, C_256: 256.0, C_N128: -128.0,
                  C_T1: 4096.0, C_T2: 8192.0, C_T3: 12288.0, C_T4: 16384.0,
                  C_T5: 20480.0, C_T6: 24576.0, C_T7: 28672.0, C_64: 64.0}
    ident = A("identsb", [128, 128], BF16)
    identf = A("identfsb", [128, 128], F32)
    masksl = A("masksl", [128, 128], F32)
    onescol = A("onescol", [128, 1], F32)
    onescol16 = A("onescol16", [128, 1], BF16)
    ones32 = A("ones32", [32, 128], F32)
    onecol32 = A("onecol32", [32, 1], F32)
    piota_f = A("piota_f", [128, 1], F32)
    piota_u = A("piota_u", [128, 1], U32)
    scrow = A("scrow", [1, 128], F32)
    thetas = A("thetas", [1, 64], F32)
    kkT = A("kkT", [64, NSEL], BF16)
    vvtok = A("vvtok", [NSEL, D], BF16)
    mb128 = A("mb128", [128, NPOOL], F32)
    bb128 = A("bb128", [128, NPOOL], F32)
    condcol = A("condcol", [128, 1], F32)
    ghalf = A("ghalf", [128, 2], F32)
    gnegh = A("gnegh", [128, 2], F32)
    lnw16 = A("lnw16", [128, 32], BF16)
    lnwf32 = A("lnwf32", [128, 32], F32)
    lnb16 = A("lnb16", [128, 32], BF16)
    gacc = A("gacc", [128, 4], F32)
    grow_sb = A("grow_sb", [1, 4], F32)

    PB = [nc.alloc_psum_tensor(f"pb{i}", [128, 512], F32) for i in range(6)]
    PT = [nc.alloc_psum_tensor(f"pt{i}", [128, 1024], BF16) for i in range(2)]
    KPB = [f"PB{i}" for i in range(6)]
    KPT = [f"PT{i}" for i in range(2)]

    def cst(i, p=128):
        return consts[0:p, i:i + 1]

    def v_tt(out, a, b, op, reads, writes, tag="tt"):
        sch.op("vector", lambda e: e.tensor_tensor(out, a, b, op), reads, writes, tag)

    def v_ts(out, in0, s1, s2, op0, op1, reads, writes, tag="ts"):
        sch.op("vector", lambda e: e.tensor_scalar(out, in0, s1, s2, op0, op1), reads, writes, tag)

    def v_stt(out, in0, s, in1, op0, op1, reads, writes, tag="stt"):
        sch.op("vector", lambda e: e.scalar_tensor_tensor(out, in0, s, in1, op0, op1), reads, writes, tag)

    def v_copy(out, in_, reads, writes, tag="cp"):
        sch.op("vector", lambda e: e.tensor_copy(out, in_), reads, writes, tag)

    def v_red(out, in_, op, reads, writes, negate=False, absval=False, tag="red"):
        sch.op("vector", lambda e: e.tensor_reduce(out, in_, AX.X, op,
               apply_absolute_value=absval, negate=negate), reads, writes, tag)

    def v_scan(out, d0, d1, init, op0, op1, reads, writes, tag="scan"):
        sch.op("vector", lambda e: e.tensor_tensor_scan(out, d0, d1, init, op0, op1), reads, writes, tag)

    def v_memset(ap_, val, writes, tag="memset"):
        sch.op("vector", lambda e: e.memset(ap_, val), (), writes, tag)

    def v_recip(out, in_, reads, writes, tag="recip"):
        sch.op("vector", lambda e: e.reciprocal(out, in_), reads, writes, tag)

    def v_max(out, in_, reads, writes, tag="max"):
        sch.op("vector", lambda e: e.max(out, in_), reads, writes, tag)

    def v_maxidx(out, in_max, in_vals, reads, writes, tag="mxi"):
        sch.op("vector", lambda e: e.max_index(out, in_max, in_vals), reads, writes, tag)

    def v_matchrep(out, rep, vals, imm, reads, writes, tag="mr"):
        sch.op("vector", lambda e: e.match_replace(out, in_to_replace=rep, in_values=vals,
               imm_value=imm), reads, writes, tag)

    def s_act(out, in_, func, reads, writes, bias=None, scale=None, accum=None, tag="act"):
        kw = {}
        if bias is not None:
            kw["bias"] = bias
        if scale is not None:
            kw["scale"] = scale
        if accum is not None:
            kw["accum_out"] = accum
        sch.op("scalar", lambda e: e.activation(out, in_, func, **kw), reads, writes, tag)

    def s_copy(out, in_, reads, writes, tag="scp"):
        sch.op("scalar", lambda e: e.copy(out, in_), reads, writes, tag)

    def t_mm(out, lhsT, rhs, start, stop, reads, writes, tag="mm", tile_position=None):
        sch.op("tensor", lambda e: e.matmul(out, lhsT, rhs, start=start, stop=stop,
               tile_position=tile_position), reads, writes, tag)

    def t_tr(out, in_, pin, reads, writes, tag="tr"):
        sch.op("tensor", lambda e: e.transpose(out, in_, ident[0:pin, 0:pin]),
               tuple(reads) + ("ident",), writes, tag)

    def t_trf(out, in_, pin, reads, writes, tag="trf"):
        sch.op("tensor", lambda e: e.transpose(out, in_, identf[0:pin, 0:pin]),
               tuple(reads) + ("identf",), writes, tag)

    def bcast(dst, src_row, n, reads, writes, chan="f"):
        sch.dma("gpsimd", dst, src_row.unsqueeze(1).broadcast_to((1, 128, n)),
                reads, writes, chan=chan, tag="bcast")

    def rsqrt2(out, in_, scratch, p, reads, writes):
        s_act(scratch, in_, ACT.Ln, reads, ("rsq_t",) , tag="rln")
        s_act(out, scratch, ACT.Exp, ("rsq_t", "consts"), writes,
              scale=cst(C_NEGH, p), tag="rexp")

    v_memset(consts[:], 0.0, ("consts",), tag="c0")
    for ci2, v in CONST_VALS.items():
        v_memset(consts[:, ci2:ci2 + 1], v, ("consts",), tag="c")
    v_memset(onescol[:], 1.0, ("onescol",))
    v_memset(onescol16[:], 1.0, ("onescol16",))
    v_memset(ones32[:], 1.0, ("ones32",))
    v_memset(onecol32[:], 1.0, ("onecol32",))
    v_memset(gacc[:], 0.0, ("gacc",))
    sch.op("gpsimd", lambda e: e.iota(piota_u[:], pattern=[[0, 1]], base=0,
           channel_multiplier=1), (), ("piota_u",), "iota")
    v_copy(piota_f[:], piota_u[:], ("piota_u",), ("piota_f",))
    sch.dma("sync", ident[:], ident_h, (), ("ident",), chan="w")
    sch.dma("sync", identf[:], identf_h, (), ("identf",), chan="w")
    sch.dma("sync", masksl[:], mask_h, (), ("masksl",), chan="w")

    stackA = contextlib.ExitStack()
    SB = lambda name, shape, dt: stackA.enter_context(nc.sbuf_tensor(name, shape, dt))
    cw1 = SB("cw1", [128, 8 * DIP], BF16)
    cwo = SB("cwo", [128, 16 * 1024], BF16)
    csc1 = SB("csc1", [128, 8 * HID], BF16)
    csc2 = SB("csc2", [128, 2], BF16)
    csum = SB("csum", [128, 8 * SD], BF16)
    cwcv = SB("cwcv", [128, 72], F32)
    ccb = SB("ccb", [128, 18], F32)
    nwcols = SB("nwcols", [128, 8], F32)
    swcols = SB("swcols", [128, 16], F32)
    dtbcol = SB("dtbcol", [32, 1], F32)
    acol = SB("acol", [32, 1], F32)
    acol2 = SB("acol2", [32, 1], F32)
    dfm = SB("dfm", [128, 16], F32)
    stackP0 = contextlib.ExitStack()
    wtmp = stackP0.enter_context(nc.sbuf_tensor("wtmp", [128, DIP], F32, side="right"))
    wtmpG = SB("wtmpG", [128, 1024], F32)
    lnwf = SB("lnwf", [128, 32], F32)
    lnbf = SB("lnbf", [128, 32], F32)

    sch.dma("sync", cwcv[:], cwv_h, (), ("cwcv",), chan="w")
    sch.dma("sync", ccb[:], ccb_h, (), ("ccb",), chan="w")
    sch.dma("sync", nwcols[:], nw_h, (), ("nwcols",), chan="w")
    sch.dma("sync", swcols[:], sw_h, (), ("swcols",), chan="w")
    sch.dma("sync", dtbcol[:], dtb_h, (), ("dtbcol",), chan="w")
    sch.dma("sync", dfm[:], dfm_h, (), ("dfm",), chan="w")
    sch.dma("sync", lnwf[:], lnw_h, (), ("lnwf",), chan="w")
    sch.dma("sync", lnbf[:], lnb_h, (), ("lnbf",), chan="w")
    v_copy(lnw16[:], lnwf[:], ("lnwf",), ("lnw16",))
    v_copy(lnwf32[:], lnwf[:], ("lnwf",), ("lnwf32",))
    v_copy(lnb16[:], lnbf[:], ("lnbf",), ("lnb16",))
    sch.dma("sync", acol2[:], alog_h, (), ("acol_r",), chan="w")
    s_act(acol[:], acol2[:], ACT.Exp, ("acol_r",), ("acol_e",))
    v_ts(acol2[:], acol[:], cst(C_NEG1, 32), None, ALU.mult, ALU.bypass,
         ("acol_e", "consts"), ("acol",))
    acol_ap = acol2

    for k in range(8):
        wt, wkey = wtmp, "wtmpA"
        sch.dma("sync", wt[:], w1_h[k * 128:(k + 1) * 128, :], (), (wkey,), chan="w")
        v_ts(cw1[:, k * DIP:(k + 1) * DIP], wt[:], nwcols[:, k:k + 1], None,
             ALU.mult, ALU.bypass, (wkey, "nwcols"), (f"cw1_{k}",))
    for k in range(16):
        wt, wkey = wtmp, "wtmpA"
        sch.dma("sync", wt[:, 0:1024], wo_h[k * 128:(k + 1) * 128, :], (), (wkey,), chan="w")
        v_ts(cwo[:, k * 1024:(k + 1) * 1024], wt[:, 0:1024], swcols[:, k:k + 1], None,
             ALU.mult, ALU.bypass, (wkey, "swcols"), (f"cwo_{k}",))
    for k in range(8):
        wt, wkey = wtmp, "wtmpA"
        sch.dma("sync", wt[:, 0:HID], wsc1_h[k * 128:(k + 1) * 128, :], (), (wkey,), chan="w")
        v_copy(csc1[:, k * HID:(k + 1) * HID], wt[:, 0:HID], (wkey,), (f"csc1_{k}",))
    for k in range(2):
        wt, wkey = wtmp, "wtmpA"
        sch.dma("sync", wt[:, 0:1], wsc2_h[k * 128:(k + 1) * 128, :], (), (wkey,), chan="w")
        v_copy(csc2[:, k:k + 1], wt[:, 0:1], (wkey,), (f"csc2_{k}",))
    for k in range(8):
        wt, wkey = wtmp, "wtmpA"
        sch.dma("sync", wt[:, 0:SD], wsm_h[k * 128:(k + 1) * 128, :], (), (wkey,), chan="w")
        v_copy(csum[:, k * SD:(k + 1) * SD], wt[:, 0:SD], (wkey,), (f"csum_{k}",))
    stackP0.close()

    xtok = SB("xtok", [128, D], F32)
    xntok = SB("xntok", [128, D], BF16)
    xnf = SB("xnf", [128, 2048], BF16)
    statc = SB("statc", [128, 32], F32)
    sz = SB("sz", [128, 16 * SC], BF16)
    xbcr = SB("xbcr", [128, 18 * (SC + 4)], BF16)
    xbcc = SB("xbcc", [128, 18 * SC], BF16)
    cacc = SB("cacc", [128, 2 * SC], F32)
    dte = SB("dte", [32, SC], F32)
    dtv = SB("dtv", [32, SC], F32)
    dav = SB("dav", [32, SC], F32)
    csv = SB("csv", [32, SC], F32)
    ndsv = SB("ndsv", [32, L], F32)
    dsv = SB("dsv", [32, L], F32)
    expcs32 = SB("expcs32", [32, L], F32)
    expcs16 = SB("expcs16", [32, L], BF16)
    csflatq = [SB(f"csflatq{i}", [1, 8 * L], F32) for i in range(2)]
    expflatq = [SB(f"expflatq{i}", [1, 8 * L], BF16) for i in range(2)]
    lamrow = SB("lamrow", [1, 32], F32)
    lamraw = SB("lamraw", [1, 32], F32)
    lamb = SB("lamb", [128, 32], F32)
    csT = SB("csT", [128, 32], F32)
    dtT16 = SB("dtT16", [128, 32], BF16)
    dsT16 = SB("dsT16", [128, 32], BF16)
    csbq = [SB(f"csbq{i}", [128, 512], F32) for i in range(2)]
    expq = [SB(f"expq{i}", [128, 512], BF16) for i in range(2)]
    xdt = SB("xdt", [128, 2048], BF16)
    xdwg = SB("xdwg", [128, 512], BF16)
    bmtok = SB("bmtok", [128, 128], BF16)
    gmask = SB("gmask", [128, 128], BF16)
    sseg = [SB(f"sseg{i}", [128, 128], F32) for i in range(2)]
    eseg = [SB(f"eseg{i}", [128, 128], BF16) for i in range(2)]
    mtb = [SB(f"mtb{i}", [128, 128], BF16) for i in range(2)]
    cmsb = [SB(f"cmsb{i}", [128, 128], BF16) for i in range(2)]
    sall = SB("sall", [128, 2048], BF16)
    stmp = SB("stmp", [128, 512], BF16)
    dxsb = SB("dxsb", [128, 512], BF16)
    yfmsc = SB("yfmsc", [128, 16 * L], BF16)
    sqb = SB("sqb", [128, 2 * L], F32)
    rsr1 = SB("rsr1", [1, L], F32)
    rsr2 = SB("rsr2", [1, L], F32)
    rsrow = SB("rsrow", [1, L], F32)
    rsb = SB("rsb", [128, L], F32)
    ytok = SB("ytok", [128, D], F32)
    yfm16 = SB("yfm16", [128, 1024], BF16)
    s1b = SB("s1b", [128, 2 * HID], BF16)
    summT = SB("summT", [64, 128], BF16)
    summtk = SB("summtk", [128, SD], F32)

    v_memset(sall[:], 0.0, ("S0", "S1", "S2", "S3"))
    v_memset(xbcr[:], 0.0, tuple(f"tail{j}" for j in range(18)))

    GAMMA_TILES = [("fc", k) for k in range(8)] + [("pr", k) for k in range(32)]
    gamma_i = [0]
    gacc_par = [0, 0]

    def emit_gamma_tile():
        if gamma_i[0] >= len(GAMMA_TILES):
            return
        which, k = GAMMA_TILES[gamma_i[0]]
        gamma_i[0] += 1
        npiece = 4 if which == "fc" else 1
        srcw = wfcT_h if which == "fc" else wprT_h
        ci3 = 0 if which == "fc" else 1
        for pc in range(npiece):
            sch.dma("sync", wtmpG[:],
                    srcw[k * 128:(k + 1) * 128, pc * 1024:(pc + 1) * 1024], (), ("wtmpG",),
                    chan="g", nslots=2)
            v_red(statc[:, 30:31], wtmpG[:], ALU.add, ("wtmpG",), ("gred",), absval=True)
            par = gacc_par[ci3]
            v_tt(gacc[:, ci3 + 2 * (1 - par): ci3 + 2 * (1 - par) + 1],
                 gacc[:, ci3 + 2 * par: ci3 + 2 * par + 1], statc[:, 30:31], ALU.add,
                 ("gacc", "gred"), ("gacc",), tag="gacc")
            gacc_par[ci3] = 1 - par

    def ln_tile(ti, xkey, xview, outview, outkey, stat, skeys):
        k0 = skeys
        sch.op("vector", lambda e: e.bn_stats(stat[:, 0:6], xview[:, 0:512]), (xkey,), (k0 + "a",))
        sch.op("vector", lambda e: e.bn_stats(stat[:, 6:12], xview[:, 512:1024]), (xkey,), (k0 + "a",))
        sch.op("vector", lambda e: e.bn_aggr(stat[:, 12:14], stat[:, 0:12]), (k0 + "a",), (k0 + "b",))
        mean = stat[:, 12:13]
        var = stat[:, 13:14]
        veps = stat[:, 14:15]
        inv = stat[:, 15:16]
        v2 = stat[:, 16:17]
        r2 = stat[:, 17:18]
        stot = stat[:, 18:19]
        lnscr = stat[:, 19:20]
        v_ts(veps, var, cst(C_EPS), None, ALU.add, ALU.bypass, (k0 + "b", "consts"), (k0 + "c",))
        rsqrt2(inv, veps, lnscr, 128, (k0 + "c",), (k0 + "d",))
        v_ts(v2, var, inv, inv, ALU.mult, ALU.mult, (k0 + "b", k0 + "d"), (k0 + "e",))
        v_ts(r2, v2, cst(C_EPS), None, ALU.add, ALU.bypass, (k0 + "e", "consts"), (k0 + "f",))
        rsqrt2(v2, r2, lnscr, 128, (k0 + "f",), (k0 + "g",))
        v_ts(stot, inv, v2, None, ALU.mult, ALU.bypass, (k0 + "d", k0 + "g"), (k0 + "h",))
        v_ts(outview, xview, mean, stot, ALU.subtract, ALU.mult,
             (xkey, k0 + "b", k0 + "h"), (outkey,))

    sch.barrier()

    for sc_i in range(min(NSC, NSC_LIM)):
        t0 = sc_i * SC
        for ti in range(2):
            xk = "xtokA"
            sch.dma("sync", xtok[:],
                    x_h[t0 + ti * 128: t0 + (ti + 1) * 128, :], (), (xk,), chan="x")
            ln_tile(ti, xk, xtok[:], xntok[:], "xn1", statc, "lnA")
            for k in range(8):
                t_tr(PT[0][0:128, 0:128], xntok[:, k * 128:(k + 1) * 128],
                     128, ("xn1",), (KPT[0],))
                v_copy(xnf[:, k * 256 + ti * 128: k * 256 + (ti + 1) * 128],
                       PT[0][0:128, 0:128], (KPT[0],), (f"xnf{k}",))
        if KDBG and sc_i == 0:
            sch.dma("sync", dbg_h["xnf"], xnf[:], tuple(f"xnf{k}" for k in range(8)), (), chan="oo")
            sch.dma("sync", dbg_h["identd"], ident[:], ("ident",), (), chan="oo")
            sch.dma("sync", dbg_h["identfd"], identf[:], ("identf",), (), chan="oo")
            sch.dma("sync", dbg_h["constsd"], consts[:], ("consts",), (), chan="oo")
        for m in range(34):
            pb, pkey = (PB[m % 2], KPB[m % 2])
            for k in range(8):
                t_mm(pb[:, 0:SC], cw1[:, k * DIP + m * 128: k * DIP + (m + 1) * 128],
                     xnf[:, k * 256:(k + 1) * 256], k == 0, k == 7,
                     (f"cw1_{k}", f"xnf{k}"), (pkey,))
            if m < 16:
                s_act(sz[:, m * SC:(m + 1) * SC], pb[:, 0:SC], ACT.Silu, (pkey,), (f"sz{m}",))
            else:
                j = m - 16
                s_copy(xbcr[:, j * (SC + 4) + 4: (j + 1) * (SC + 4)], pb[:, 0:SC],
                       (pkey, f"tail{j}"), (f"xbcr{j}",))
        pbdt = PB[2]
        for k in range(8):
            t_mm(pbdt[0:32, 0:SC], cw1[:, k * DIP + 34 * 128: k * DIP + 34 * 128 + 32],
                 xnf[:, k * 256:(k + 1) * 256], k == 0, k == 7,
                 (f"cw1_{k}", f"xnf{k}"), (KPB[2],))
        for j in range(18):
            base = j * (SC + 4)
            a0, a1 = cacc[:, 0:SC], cacc[:, SC:2 * SC]
            v_ts(a0, xbcr[:, base + 1: base + 1 + SC], cwcv[:, j * 4: j * 4 + 1], None,
                 ALU.mult, ALU.bypass, (f"xbcr{j}", f"tail{j}", "cwcv"), ("cacc0",))
            v_stt(a1, xbcr[:, base + 2: base + 2 + SC], cwcv[:, j * 4 + 1: j * 4 + 2], a0,
                  ALU.mult, ALU.add, (f"xbcr{j}", f"tail{j}", "cacc0", "cwcv"), ("cacc1",))
            v_stt(a0, xbcr[:, base + 3: base + 3 + SC], cwcv[:, j * 4 + 2: j * 4 + 3], a1,
                  ALU.mult, ALU.add, (f"xbcr{j}", f"tail{j}", "cacc1", "cwcv"), ("cacc0",))
            v_stt(a1, xbcr[:, base + 4: base + 4 + SC], cwcv[:, j * 4 + 3: j * 4 + 4], a0,
                  ALU.mult, ALU.add, (f"xbcr{j}", "cacc0", "cwcv"), ("cacc1",))
            s_act(xbcc[:, j * SC:(j + 1) * SC], a1, ACT.Silu, ("cacc1", "ccb"), (f"xbcc{j}",),
                  bias=ccb[:, j:j + 1])
            v_copy(xbcr[:, base + 1: base + 4], xbcr[:, base + SC + 1: base + SC + 4],
                   (f"xbcr{j}",), (f"tail{j}",))
        s_act(dte[:], pbdt[0:32, 0:SC], ACT.Exp, (KPB[2], "dtbcol"), ("dte",), bias=dtbcol[:])
        s_act(dtv[:], dte[:], ACT.Ln, ("dte", "onecol32"), ("dtv",), bias=onecol32[:])
        v_ts(dav[:], dtv[:], acol_ap[:], None, ALU.mult, ALU.bypass, ("dtv", "acol"), ("dav",))

        emit_gamma_tile()
        emit_gamma_tile()
        emit_gamma_tile()

        for ci in range(2):
            ch = sc_i * 2 + ci
            cs_ = csv[:, ci * L:(ci + 1) * L]
            v_scan(cs_, ones32[:], dav[:, ci * L:(ci + 1) * L], 0.0, ALU.mult, ALU.add,
                   ("dav", "ones32"), (f"cs{ci}",))
            nds_ = ndsv[:]
            v_ts(nds_, cs_, cs_[:, L - 1:L], None, ALU.subtract, ALU.bypass,
                 (f"cs{ci}",), ("ndsv",))
            s_act(dsv[:], nds_, ACT.Exp, ("ndsv", "consts"),
                  ("dsv",), scale=cst(C_NEG1, 32))
            s_act(expcs32[:], cs_, ACT.Exp, (f"cs{ci}",), ("ecsv",))
            v_copy(expcs16[:], expcs32[:], ("ecsv",), ("ecs16",))
            for qf in range(4):
                qb2 = qf % 2
                sch.dma("gpsimd", csflatq[qb2][0:1, :].rearrange("o (h l) -> o h l", h=8),
                        cs_[qf * 8:(qf + 1) * 8, :], (f"cs{ci}",), (f"csflat{qb2}",),
                        chan="f", tag="csflat")
                sch.dma("gpsimd", expflatq[qb2][0:1, :].rearrange("o (h l) -> o h l", h=8),
                        expcs16[qf * 8:(qf + 1) * 8, :],
                        ("ecs16",), (f"expflat{qb2}",), chan="f", tag="expflat")
                s_act(lamrow[0:1, qf * 8:(qf + 1) * 8], csflatq[qb2][0:1, L - 1::L],
                      ACT.Exp, (f"csflat{qb2}",), ("lamrow",))
            bcast(lamb[:], lamrow[:], 32, ("lamrow",), ("lamb",))
            t_trf(PB[5][0:128, 0:32], cs_, 32, (f"cs{ci}",), (KPB[5],))
            v_copy(csT[:], PB[5][0:128, 0:32], (KPB[5],), ("csT",))
            t_trf(PB[5][0:128, 32:64], dtv[:, ci * L:(ci + 1) * L], 32, ("dtv",), (KPB[5],))
            v_copy(dtT16[:], PB[5][0:128, 32:64], (KPB[5],), ("dtT16",))
            t_trf(PB[5][0:128, 64:96], dsv[:], 32, ("dsv",), (KPB[5],))
            v_copy(dsT16[:], PB[5][0:128, 64:96], (KPB[5],), ("dsT16",))

            for j in range(16):
                t_tr(PT[1][0:128, 0:128], xbcc[:, j * SC + ci * L: j * SC + (ci + 1) * L],
                     128, (f"xbcc{j}",), (KPT[1],))
                v_tt(xdt[:, j * 128:(j + 1) * 128].rearrange("p (h q) -> p h q", h=2),
                     PT[1][0:128, 0:128].rearrange("p (h q) -> p h q", h=2),
                     dtT16[:, 2 * j:2 * j + 2].unsqueeze(2).broadcast_to((128, 2, 64)),
                     ALU.mult, (KPT[1], "dtT16"), ("xdt",))
            t_tr(PT[1][0:128, 0:128], xbcc[:, 16 * SC + ci * L: 16 * SC + (ci + 1) * L],
                 128, ("xbcc16",), (KPT[1],))
            v_copy(bmtok[:], PT[1][0:128, 0:128], (KPT[1],), ("bmtok",))

            if KDBG and sc_i == 0 and ci == 0:
                sch.dma("sync", dbg_h["xdt2"], xdt[:], ("xdt",), (), chan="oo")
                sch.dma("sync", dbg_h["dtT16"], dtT16[:], ("dtT16",), (), chan="oo")
            t_mm(PB[2][:, 0:128], xbcc[:, 16 * SC + ci * L: 16 * SC + (ci + 1) * L],
                 xbcc[:, 17 * SC + ci * L: 17 * SC + (ci + 1) * L], True, True,
                 ("xbcc16", "xbcc17"), (KPB[2],))
            v_tt(gmask[:], PB[2][:, 0:128], masksl[:], ALU.mult, (KPB[2], "masksl"), ("gmask",))

            for h in range(H):
                q, hb, hh = h // 8, h % 2, h % 8
                qb = (h // 4) % 2
                if hh == 0:
                    qb2 = q % 2
                    sch.dma("gpsimd", csflatq[qb2][0:1, :].rearrange("o (h2 l) -> o h2 l", h2=8),
                            cs_[q * 8:(q + 1) * 8, :], (f"cs{ci}",), (f"csflat{qb2}",),
                            chan="f", tag="csflat")
                    sch.dma("gpsimd", expflatq[qb2][0:1, :].rearrange("o (h2 l) -> o h2 l", h2=8),
                            expcs16[q * 8:(q + 1) * 8, :], ("ecs16",), (f"expflat{qb2}",),
                            chan="f", tag="expflat")
                if h % 4 == 0:
                    half4 = (h % 8) // 4
                    sch.dma("gpsimd", csbq[qb][:],
                            csflatq[q % 2][0:1, half4 * 512:(half4 + 1) * 512].unsqueeze(1).broadcast_to((1, 128, 512)),
                            (f"csflat{q % 2}",), (f"csbq{qb}",), chan="b")
                    sch.dma("gpsimd", expq[qb][:],
                            expflatq[q % 2][0:1, half4 * 512:(half4 + 1) * 512].unsqueeze(1).broadcast_to((1, 128, 512)),
                            (f"expflat{q % 2}",), (f"expq{qb}",), chan="b")
                v_ts(sseg[hb][:], csbq[qb][:, (h % 4) * 128:(h % 4 + 1) * 128], csT[:, h:h + 1],
                     cst(C_ZERO), ALU.subtract, ALU.min,
                     (f"csbq{qb}", "csT", "consts"), (f"sseg{hb}",))
                s_act(eseg[hb][:], sseg[hb][:], ACT.Exp, (f"sseg{hb}",), (f"eseg{hb}",))
                v_tt(mtb[hb][:], eseg[hb][:], gmask[:], ALU.mult,
                     (f"eseg{hb}", "gmask"), (f"mtb{hb}",))
                if KDBG and sc_i == 0 and ci == 0 and h == 2:
                    sch.dma("sync", dbg_h["mtbh2"], mtb[hb][:], (f"mtb{hb}",), (), chan="oo")
                    sch.dma("sync", dbg_h["ssegh2"], sseg[hb][:], (f"sseg{hb}",), (), chan="oo")
                    sch.dma("sync", dbg_h["csbh2"], csbq[qb][:], (f"csbq{qb}",), (), chan="oo")
                    sch.dma("sync", dbg_h["csTd"], csT[:], ("csT",), (), chan="oo")
                v_tt(cmsb[hb][:], xbcc[:, 17 * SC + ci * L: 17 * SC + (ci + 1) * L],
                     expq[qb][:, (h % 4) * 128:(h % 4 + 1) * 128], ALU.mult,
                     ("xbcc17", f"expq{qb}"), (f"cmsb{hb}",))
                bank = PB[3 + (q % 2)]
                bkey = KPB[3 + (q % 2)]
                pp, ff = (h % 2) * 64, ((h // 2) % 4) * 128
                t_mm(bank[pp:pp + 64, ff:ff + 128], xdt[:, h * P:(h + 1) * P], mtb[hb][:],
                     True, ch == 0, ("xdt", f"mtb{hb}"), (bkey,), tag="ydiag",
                     tile_position=(0, pp))
                if ch > 0:
                    t_mm(bank[pp:pp + 64, ff:ff + 128], sall[:, h * P:(h + 1) * P], cmsb[hb][:],
                         False, True, (f"S{q}", f"cmsb{hb}"), (bkey,), tag="yoff",
                         tile_position=(0, pp))
                if hh == 7:
                    g = q
                    for jj in range(4):
                        j = g * 4 + jj
                        v_ts(dxsb[:, jj * 128:(jj + 1) * 128],
                             xbcc[:, j * SC + ci * L: j * SC + (ci + 1) * L],
                             dfm[:, j:j + 1], None, ALU.mult, ALU.bypass,
                             (f"xbcc{j}", "dfm"), ("dxsb",))
                    v_tt(yfmsc[:, g * 512:(g + 1) * 512],
                         bank[:, 0:512], dxsb[:], ALU.add,
                         (bkey, "dxsb"), (f"yfm{g}",), tag="yevac")
                    v_tt(xdwg[:].rearrange("p (hq q) -> p hq q", hq=8),
                         xdt[:, g * 512:(g + 1) * 512].rearrange("p (hq q) -> p hq q", hq=8),
                         dsT16[:, g * 8:(g + 1) * 8].unsqueeze(2).broadcast_to((128, 8, 64)),
                         ALU.mult, ("xdt", "dsT16"), ("xdwg",))
                    t_mm(PB[5][:, 0:512], bmtok[:], xdwg[:], True, True,
                         ("bmtok", "xdwg"), (KPB[5],), tag="states")
                    v_tt(stmp[:].rearrange("p (hq q) -> p hq q", hq=8),
                         sall[:, g * 512:(g + 1) * 512].rearrange("p (hq q) -> p hq q", hq=8),
                         lamb[:, g * 8:(g + 1) * 8].unsqueeze(2).broadcast_to((128, 8, 64)),
                         ALU.mult, (f"S{g}", "lamb"), ("stmp",), tag="supd")
                    v_tt(sall[:, g * 512:(g + 1) * 512], stmp[:], PB[5][:, 0:512], ALU.add,
                         ("stmp", KPB[5]), (f"S{g}",), tag="supd2")

            yzb = xdt
            yop = xnf
            yview = yfmsc[:].rearrange("p (j l) -> p j l", l=L)
            zview = sz[:].rearrange("p (j l) -> p j l", l=SC)[:, :, ci * L:(ci + 1) * L]
            yzview = yzb[:].rearrange("p (j l) -> p j l", l=L)
            ykeys = tuple(f"yfm{g}" for g in range(4))
            zkeys = tuple(f"sz{m}" for m in range(16))
            v_tt(yzview, yview, zview, ALU.mult, ykeys + zkeys, ("yz", "xdt"), tag="yz")
            for j in range(16):
                sq = sqb[:, (j % 2) * L:(j % 2 + 1) * L]
                v_tt(sq, yzb[:, j * L:(j + 1) * L], yzb[:, j * L:(j + 1) * L], ALU.mult,
                     ("yz", "xdt"), (f"sq{j % 2}",), tag="sq")
                t_mm(PB[2][0:1, 128:256], onescol[:], sq, j == 0, j == 15,
                     (f"sq{j % 2}", "onescol"), (KPB[2],), tag="sqmm")
            v_ts(rsr1[:], PB[2][0:1, 128:256], cst(C_I2048, 1), cst(C_EPS, 1),
                 ALU.mult, ALU.add, (KPB[2], "consts"), ("rsr1",))
            s_act(rsr2[:], rsr1[:], ACT.Ln, ("rsr1",), ("rsr2",))
            s_act(rsrow[:], rsr2[:], ACT.Exp, ("rsr2", "consts"), ("rsrow",),
                  scale=cst(C_NEGH, 1))
            bcast(rsb[:], rsrow[:], 128, ("rsrow",), ("rsb",))
            v_tt(yop[:].rearrange("p (j l) -> p j l", l=128), yzview,
                 rsb[:].unsqueeze(1).broadcast_to((128, 16, 128)), ALU.mult,
                 ("yz", "xdt", "rsb"), ("yop",) + tuple(f"xnf{k}" for k in range(8)), tag="yop")
            if KDBG and sc_i == 0 and ci == 0:
                sch.dma("sync", dbg_h["statc"], statc[:, 0:20], ("lnAb", "lnAh"), (), chan="oo")
                for nm, ap_, keys in [("xntok", xntok[:], ("xn1",)),
                                      ("sz", sz[:], tuple(f"sz{m}" for m in range(16))),
                                      ("xbcc", xbcc[:], tuple(f"xbcc{j}" for j in range(18))),
                                      ("dtv", dtv[:], ("dtv",)), ("csv", csv[:, 0:L], ("cs0",)), 
                                      ("xdt", xdt[:], ("yz", "xdt")), ("yfmsc", yfmsc[:], tuple(f"yfm{g}" for g in range(4))),
                                      ("yop", yop[:], ("yop",)), ("dsv", dsv[:], ("dsv",)),
                                      ("expcs32", expcs32[:], ("ecsv",)), ("sall", sall[:], ("S0", "S1", "S2", "S3")),
                                      ("gmask", gmask[:], ("gmask",)), ("mtb0", mtb[1][:], ("mtb1",))]:
                    sch.dma("sync", dbg_h[nm], ap_, keys, (), chan="oo")
            for half in range(2):
                pbo, pkey = PB[half], KPB[half]
                for j in range(16):
                    t_mm(pbo[:, 0:512], yop[:, j * 128:(j + 1) * 128],
                         cwo[:, j * 1024 + half * 512: j * 1024 + (half + 1) * 512],
                         j == 0, j == 15,
                         ("yop", f"xnf{j // 2}", f"cwo_{j}"), (pkey,), tag="outproj")
                v_copy(ytok[:, half * 512:(half + 1) * 512],
                       pbo[:, 0:512], (pkey,), ("ytokk",))
            sch.dma("sync", y_spill[ch, :, :], ytok[:],
                    ("ytokk",), (), chan="ys")
            for k in range(8):
                t_trf(PB[5][0:128, 96:224], ytok[:, k * 128:(k + 1) * 128],
                      128, ("ytokk",), (KPB[5],))
                v_copy(yfm16[:, k * 128:(k + 1) * 128],
                       PB[5][0:128, 96:224], (KPB[5],), ("yfm16k",))
            sch.dma("sync", yfm_spill[ch, :, :], yfm16[:],
                    ("yfm16k",), (), chan="ys")
            for m2 in range(2):
                for k in range(8):
                    t_mm(PB[2][:, 256:384], csc1[:, k * HID + m2 * 128: k * HID + (m2 + 1) * 128],
                         yfm16[:, k * 128:(k + 1) * 128],
                         k == 0, k == 7, (f"csc1_{k}", "yfm16k"), (KPB[2],), tag="sc1")
                v_ts(s1b[:, m2 * 128:(m2 + 1) * 128], PB[2][:, 256:384], 0.0, None,
                     ALU.max, ALU.bypass, (KPB[2],), (f"s1b{m2}",), tag="relu")
            for k2 in range(2):
                t_mm(PB[2][0:1, 384:512], csc2[:, k2:k2 + 1], s1b[:, k2 * 128:(k2 + 1) * 128],
                     k2 == 0, k2 == 1, (f"csc2_{k2}", "s1b0", "s1b1"), (KPB[2],), tag="sc2")
            s_act(rsr1[:], PB[2][0:1, 384:512], ACT.Exp, (KPB[2], "consts", "rsr1"), ("rsr1",),
                  scale=cst(C_NEG1, 1))
            v_ts(rsr2[:], rsr1[:], cst(C_ONE, 1), None, ALU.add, ALU.bypass,
                 ("rsr1", "consts", "rsr2"), ("rsr2",))
            v_recip(scrow[:], rsr2[:], ("rsr2",), ("scrow",))
            sch.dma("sync", sm_in.ap()[T + 2 * ch: T + 2 * ch + 2, :],
                    scrow[:].rearrange("o (a b) -> o a b", a=2),
                    ("scrow",), ("sc_in",), chan="ys")
            for k in range(8):
                t_mm(PB[5][0:64, 224:352], csum[:, k * SD:(k + 1) * SD],
                     yfm16[:, k * 128:(k + 1) * 128],
                     k == 0, k == 7, (f"csum_{k}", "yfm16k"), (KPB[5],), tag="summ")
            v_copy(summT[:], PB[5][0:64, 224:352], (KPB[5],), ("summT",))
            t_tr(PT[1][0:128, 128:192], summT[:], 64, ("summT",), (KPT[1],))
            v_copy(summtk[:], PT[1][0:128, 128:192], (KPT[1],), ("summtk",))
            sch.dma("sync", sm_in.ap()[ch * 128:(ch + 1) * 128, :], summtk[:],
                    ("summtk",), (), chan="ys")

    if KDBG2:
        d3 = {}
        for nm, shp, dt_ in [("csc2", [128, 2], BF16), ("csum", [128, 8 * SD], BF16),
                             ("s1b", [128, 2 * HID], BF16), ("yfm16", [128, 1024], BF16),
                             ("summT", [64, 128], BF16), ("scrow", [1, 128], F32),
                             ("summtk", [128, SD], F32)]:
            d3[nm] = nc.dram_tensor("dbg3_" + nm, shp, dt_, kind="ExternalOutput").ap()
        sch.dma("sync", d3["csc2"], csc2[:], ("csc2_0", "csc2_1"), (), chan="oo")
        sch.dma("sync", d3["csum"], csum[:], tuple(f"csum_{k}" for k in range(8)), (), chan="oo")
        sch.dma("sync", d3["s1b"], s1b[:], ("s1b0", "s1b1"), (), chan="oo")
        sch.dma("sync", d3["yfm16"], yfm16[:], ("yfm16k",), (), chan="oo")
        sch.dma("sync", d3["summT"], summT[:], ("summT",), (), chan="oo")
        sch.dma("sync", d3["scrow"], scrow[:], ("scrow",), (), chan="oo")
        sch.dma("sync", d3["summtk"], summtk[:], ("summtk",), (), chan="oo")
        d3y = nc.dram_tensor("dbg3_ysp", [4 * 128, D], F32, kind="ExternalOutput").ap()
        sch.dma("sync", d3y, y_spill[0:4, :, :].rearrange("a b c -> (a b) c"), (), (), chan="oo")
        d3yf = nc.dram_tensor("dbg3_yfsp", [2 * 128, 1024], BF16, kind="ExternalOutput").ap()
        sch.dma("sync", d3yf, yfm_spill[0:2, :, :].rearrange("a b c -> (a b) c"), (), (), chan="oo")
    sch.oplimit = None
    while gamma_i[0] < len(GAMMA_TILES):
        emit_gamma_tile()
    for ci3 in range(2):
        last_col = ci3 + 2 * gacc_par[ci3]
        t_mm(PB[2][0:1, ci3:ci3 + 1], gacc[:, last_col:last_col + 1], onescol[:],
             True, True, ("gacc", "onescol"), (KPB[2],), tag="gfin")
    v_ts(grow_sb[0:1, 0:2], PB[2][0:1, 0:2], cst(C_IFC, 1), cst(C_1E6M, 1),
         ALU.mult, ALU.add, (KPB[2], "consts"), ("grow",))
    v_ts(grow_sb[0:1, 2:4], grow_sb[0:1, 0:2], cst(C_HALF, 1), None, ALU.mult, ALU.bypass,
         ("grow", "consts"), ("ghrow",))
    bcast(ghalf[:], grow_sb[0:1, 2:4], 2, ("ghrow",), ("ghalf",))
    v_ts(gnegh[:], ghalf[:], cst(C_NEG1), None, ALU.mult, ALU.bypass,
         ("ghalf", "consts"), ("gnegh",))

    sch.barrier()
    stackA.close()

    if KPHASE <= 1:
        dummy = nc.alloc_sbuf_tensor("dummy1", [128, D], F32)
        sch.op("vector", lambda e: e.memset(dummy[:], 0.0), (), ("dummyk",), "dm")
        for chd in range(NCH):
            sch.dma("sync", out_h[chd * 128:(chd + 1) * 128, :], dummy[:], ("dummyk",), (), chan="oo")
        sch.finalize()
        with contextlib.ExitStack() as es:
            sems = {}
            for e in ENGS:
                sems[e] = es.enter_context(nc.semaphore(f"sem_{e}"))
            dma_sems = [es.enter_context(nc.semaphore(f"dsem{i}")) for i in range(sch.n_dma_slots)]
            block = es.enter_context(nc.Block())
            stats = sch.emit(block, sems, dma_sems)
        return nc, stats

    if KSELCUT < 99:
        sch.oplimit = KSELCUT
    if os.environ.get("KVFIRST", "0") == "1":
        pass
    stackS = contextlib.ExitStack()
    SBs = lambda name, shape, dt: stackS.enter_context(nc.sbuf_tensor(name, shape, dt))
    ssc = SBs("ssc", [128, 256], F32)
    sscraw = SBs("sscraw", [128, 256], F32)
    fhib = SBs("fhib", [128, 8], F32)
    flob = SBs("flob", [128, 8], F32)
    vcmp = SBs("vcmp", [128, 256], F32)
    redc = SBs("redc", [128, 2], F32)
    scal2 = SBs("scal2", [1, 8], F32)
    mx8 = SBs("mx8", [128, 8], F32)
    cnd8 = SBs("cnd8", [8, 128], F32)
    cand = SBs("cand", [1, 1024], F32)
    candB = SBs("candB", [1, 1024], F32)
    th8b = SBs("th8b", [128, 8], F32)
    mi8 = SBs("mi8", [128, 8], U32)
    mif = SBs("mif", [128, 8], F32)
    gpos = SBs("gpos", [128, 8], F32)
    gpos2 = SBs("gpos2", [128, 8], F32)
    gtr = SBs("gtr", [8, 128], F32)
    gall = SBs("gall", [8, 8], F32)
    gcolf = SBs("gcolf", [NSEL, 1], F32)
    gcolu = SBs("gcolu", [NSEL, 1], U32)
    gcolf2 = SBs("gcolf2", [NSEL, 1], F32)
    radj = SBs("radj", [NSEL, 1], F32)
    radj2 = SBs("radj2", [NSEL, 1], F32)
    rc1 = SBs("rc1", [NSEL, 1], F32)
    poolraw = SBs("poolraw", [NSEL, SD], F32)
    poolfm = SBs("poolfm", [64, NSEL], BF16)
    vmrow = SBs("vmrow", [1, NPOOL], F32)
    mrow = SBs("mrow", [1, NPOOL], F32)
    brow = SBs("brow", [1, NPOOL], F32)
    ckw = SBs("ckw", [64, 64], BF16)
    cvw = SBs("cvw", [64, 1024], BF16)
    wk_f = SBs("wk_f", [64, 64], F32)
    wv_f = SBs("wv_f", [64, 1024], F32)

    ccsm = sch.dma_op("gpsimd", lambda e: e.collective_compute(
        "AllGather", ALU.bypass, replica_groups=[list(range(NCORES))],
        ins=[sm_in.ap().opt()], outs=[sm_out.ap().opt()]),
        ("sc_in",), ("sm_out_cc",), chan="cc", nslots=1, tag="ccsm", amt=1)
    for s in sch.chan_slots.get("ys", []):
        if sch.slot_last[s] is not None and sch.slot_last[s] is not ccsm:
            ccsm.deps.append((sch.slot_last[s], False))

    for rr in range(8):
        sch.dma("sync", sscraw[rr * 16:(rr + 1) * 16, :],
                sm_out.ap()[rr * (T + 64) + T: (rr + 1) * (T + 64), :]
                .rearrange("a b -> (a b)").rearrange("(p f) -> p f", p=16),
                ("sm_out_cc",), ("sscraw",), chan="e")
    for hblk in range(2):
        t_trf(PB[3][0:128, 0:128], sscraw[:, hblk * 128:(hblk + 1) * 128], 128,
              ("sscraw",), (KPB[3],))
        v_copy(ssc[:, hblk * 128:(hblk + 1) * 128], PB[3][0:128, 0:128], (KPB[3],), ("ssc",))
    v_ts(vcmp[:], ssc[:], cst(C_HALF), None, ALU.is_gt, ALU.bypass,
         ("ssc", "consts"), ("vcmp",))
    v_red(redc[:, 0:1], vcmp[:], ALU.add, ("vcmp",), ("redc",))
    v_red(redc[:, 1:2], ssc[:], ALU.add, ("ssc",), ("redc",))
    t_mm(PB[2][0:1, 0:1], redc[:, 0:1], onescol[:], True, True,
         ("redc", "onescol"), (KPB[2],), tag="cnt")
    t_mm(PB[2][0:1, 1:2], redc[:, 1:2], onescol[:], True, True,
         ("redc", "onescol"), (KPB[2],), tag="sum")
    v_copy(scal2[0:1, 0:2], PB[2][0:1, 0:2], (KPB[2],), ("scal2",))
    v_ts(scal2[0:1, 2:3], scal2[0:1, 1:2], cst(C_TAUS, 1), None, ALU.is_gt, ALU.bypass,
         ("scal2", "consts"), ("sc3",))
    v_ts(scal2[0:1, 3:4], scal2[0:1, 0:1], cst(C_ZERO, 1), None, ALU.is_gt, ALU.bypass,
         ("scal2", "consts"), ("sc4",))
    v_tt(scal2[0:1, 4:5], scal2[0:1, 2:3], scal2[0:1, 3:4], ALU.mult,
         ("sc3", "sc4"), ("cond1",))
    bcast(condcol[:], scal2[0:1, 4:5], 1, ("cond1",), ("condcol",))
    v_max(mx8[:], ssc[:], ("ssc",), ("mx8",))
    t_trf(PB[2][0:8, 128:256], mx8[:], 128, ("mx8",), (KPB[2],))
    v_copy(cnd8[:], PB[2][0:8, 128:256], (KPB[2],), ("cnd8",))
    sch.dma("gpsimd", cand[0:1, :].rearrange("o (j p) -> o j p", j=8), cnd8[:],
            ("cnd8",), ("cand",), chan="ec", nslots=1)
    for r in range(7):
        src, skey = (cand, "cand") if r % 2 == 0 else (candB, "candB")
        dst, dkey = (candB, "candB") if r % 2 == 0 else (cand, "cand")
        v_max(thetas[0:1, r * 8:(r + 1) * 8], src[0:1, :], (skey,), ("thetas",))
        v_matchrep(dst[0:1, :], thetas[0:1, r * 8:(r + 1) * 8], src[0:1, :], -1e30,
                   ("thetas", skey), (dkey,))
    for r in range(7):
        bcast(th8b[:], thetas[0:1, r * 8:(r + 1) * 8], 8, ("thetas",), ("th8b",))
        v_maxidx(mi8[:], th8b[:], ssc[:], ("th8b", "ssc"), ("mi8",))
        v_copy(mif[:], mi8[:], ("mi8",), ("mif",))
        v_ts(fhib[:], mif[:], cst(C_128), None, ALU.is_ge, ALU.bypass,
             ("mif", "consts"), ("fhib",))
        v_stt(flob[:], fhib[:], cst(C_N128), mif[:], ALU.mult, ALU.add,
              ("fhib", "mif", "consts"), ("flob",))
        v_ts(gpos2[:], flob[:], cst(C_256), piota_f[:], ALU.mult, ALU.add,
             ("flob", "consts", "piota_f"), ("gpos1",))
        v_stt(gpos[:], fhib[:], cst(C_128), gpos2[:], ALU.mult, ALU.add,
              ("gpos1", "fhib", "consts"), ("gpos",))
        t_trf(PB[2][0:8, 256:384], gpos[:], 128, ("gpos",), (KPB[2],))
        v_copy(gtr[:], PB[2][0:8, 256:384], (KPB[2],), ("gtr",))
        v_red(gall[:, r:r + 1], gtr[:], ALU.min, ("gtr",), ("gall",))
    for r in range(7):
        sch.dma("gpsimd", gcolf[r * 8:(r + 1) * 8, 0:1], gall[0:8, r:r + 1],
                ("gall",), ("gcolf",), chan=f"eg{r}", nslots=1)
    v_memset(radj[:], 0.0, ("radj",))
    for kk7 in range(1, 8):
        v_ts(rc1[:], gcolf[:], consts[0:NSEL, C_T1 + kk7 - 1: C_T1 + kk7], None,
             ALU.is_ge, ALU.bypass, ("gcolf", "consts"), ("rc1",))
        v_tt(radj2[:], radj[:], rc1[:], ALU.add, ("radj", "rc1"), ("radj",), tag="radd")
        radj, radj2 = radj2, radj
    v_stt(gcolf2[:], radj[:], cst(C_64, NSEL), gcolf[:], ALU.mult, ALU.add,
          ("radj", "gcolf", "consts"), ("gcolf2",))
    v_copy(gcolu[:], gcolf2[:], ("gcolf2",), ("gcolu",))
    sch.dma_op("gpsimd", lambda e: e.indirect_dma_start(
        out=poolraw[:], out_offset=None, in_=sm_out.ap(),
        in_offset=bass.IndirectOffsetOnAxis(ap=gcolu[:], axis=0)),
        ("gcolu", "sm_out_cc"), ("poolraw",), chan="ig", tag="gather")
    t_trf(PB[3][0:64, 128:128 + NSEL], poolraw[:], 56, ("poolraw",), (KPB[3],))
    v_copy(poolfm[:], PB[3][0:64, 128:128 + NSEL], (KPB[3],), ("poolfm",))
    v_ts(vmrow[:], thetas[0:1, 0:NPOOL], cst(C_HALF, 1), None, ALU.is_gt, ALU.bypass,
         ("thetas", "consts"), ("vmrow",))
    v_ts(mrow[:], vmrow[:], cst(C_QUART, 1), None, ALU.mult, ALU.bypass,
         ("vmrow", "consts"), ("mrow",))
    v_ts(brow[:], vmrow[:], cst(C_1E9, 1), cst(C_N1E9, 1), ALU.mult, ALU.add,
         ("vmrow", "consts"), ("brow",))
    bcast(mb128[:], mrow[:], NPOOL, ("mrow",), ("mb128",))
    bcast(bb128[:], brow[:], NPOOL, ("brow",), ("bb128",))
    sch.dma("sync", wk_f[:], wk_h, (), ("wk_f",), chan="e")
    sch.dma("sync", wv_f[:], wv_h, (), ("wv_f",), chan="e")
    v_copy(ckw[:], wk_f[:], ("wk_f",), ("ckw",))
    v_copy(cvw[:], wv_f[:], ("wv_f",), ("cvw",))
    t_mm(PB[3][0:64, 0:NSEL], ckw[:], poolfm[:], True, True, ("ckw", "poolfm"), (KPB[3],))
    v_copy(kkT[:], PB[3][0:64, 0:NSEL], (KPB[3],), ("kkT",))
    for half in range(2):
        t_mm(PB[4][0:NSEL, 0:512], poolfm[:], cvw[:, half * 512:(half + 1) * 512],
             True, True, ("poolfm", "cvw"), (KPB[4],))
        v_copy(vvtok[:, half * 512:(half + 1) * 512], PB[4][0:NSEL, 0:512],
               (KPB[4],), ("vvtok",))

    sch.oplimit = None
    if KDBG2:
        d2 = {}
        smin_dbg = nc.dram_tensor("dbg2_smin", [64, 64], F32, kind="ExternalOutput").ap()
        smsum_dbg = nc.dram_tensor("dbg2_smsum", [128, 64], F32, kind="ExternalOutput").ap()
        smout_dbg = nc.dram_tensor("dbg2_smout", [64, 64], F32, kind="ExternalOutput").ap()
        sscraw_dbg = nc.dram_tensor("dbg2_sscraw", [128, 256], F32, kind="ExternalOutput").ap()
        sch.dma("sync", smin_dbg, sm_in.ap()[T:T + 64, :], ("sc_in",), (), chan="oo")
        sch.dma("sync", smsum_dbg, sm_in.ap()[0:128, :], (), (), chan="oo")
        dbgcp = sch.dma("sync", smout_dbg, sm_out.ap()[T:T + 64, :], ("sm_out_cc",), (), chan="oo")
        sch.dma("sync", sscraw_dbg, sscraw[:], ("sscraw",), (), chan="oo")
        for nm, shp, dt_ in [("ssc", [128, 256], F32), ("cand", [1, 1024], F32),
                             ("thetas", [1, 64], F32), ("gcolf", [NSEL, 1], F32),
                             ("gcolf2", [NSEL, 1], F32), ("poolraw", [NSEL, SD], F32),
                             ("kkT", [64, NSEL], BF16), ("vvtok", [NSEL, D], BF16),
                             ("scal2", [1, 8], F32), ("mrow", [1, NPOOL], F32),
                             ("gall", [8, 8], F32)]:
            d2[nm] = nc.dram_tensor("dbg2_" + nm, shp, dt_, kind="ExternalOutput").ap()
        for nm, ap_, keys in [("ssc", ssc[:], ("ssc",)), ("cand", cand[0:1, :], ("cand",)),
                              ("thetas", thetas[:], ("thetas",)), ("gcolf", gcolf[:], ("gcolf",)),
                              ("gcolf2", gcolf2[:], ("gcolf2",)), ("poolraw", poolraw[:], ("poolraw",)),
                              ("kkT", kkT[:], ("kkT",)), ("vvtok", vvtok[:], ("vvtok",)),
                              ("scal2", scal2[:], ("scal2", "cond1", "sc3", "sc4")),
                              ("mrow", mrow[:], ("mrow",)), ("gall", gall[:], ("gall",))]:
            sch.dma("sync", d2[nm], ap_, keys, (), chan="oo")
    sch.barrier()
    stackS.close()

    if KPHASE <= 2:
        dummy = nc.alloc_sbuf_tensor("dummy2", [128, D], F32)
        sch.op("vector", lambda e: e.memset(dummy[:], 0.0), (), ("dummyk",), "dm")
        for chd in range(NCH):
            sch.dma("sync", out_h[chd * 128:(chd + 1) * 128, :], dummy[:], ("dummyk",), (), chan="oo")
        sch.finalize()
        with contextlib.ExitStack() as es:
            sems = {}
            for e in ENGS:
                sems[e] = es.enter_context(nc.semaphore(f"sem_{e}"))
            dma_sems = [es.enter_context(nc.semaphore(f"dsem{i}")) for i in range(sch.n_dma_slots)]
            block = es.enter_context(nc.Block())
            stats = sch.emit(block, sems, dma_sems)
        return nc, stats

    stackB1 = contextlib.ExitStack()
    SB1 = lambda name, shape, dt: stackB1.enter_context(nc.sbuf_tensor(name, shape, dt))
    cgate = SB1("cgate", [128, 16 * 1024], BF16)
    cq = SB1("cq", [128, 8 * SD], BF16)
    ytokB = SB1("ytokB", [128, 4 * D], F32)
    yfmB = SB1("yfmB", [128, 4 * 1024], BF16)
    xB = SB1("xB", [128, 4 * D], F32)
    qfm = SB1("qfm", [64, 512], BF16)
    lgb = SB1("lgb", [128, 2 * NPOOL], F32)
    ebuf = SB1("ebuf", [128, NPOOL], F32)
    esum = SB1("esum", [128, 2], F32)
    smb = SB1("smb", [128, NPOOL], BF16)
    smTb = SB1("smTb", [64, 4 * 128], BF16)
    retB = SB1("retB", [128, 8 * 512], BF16)
    gb = SB1("gb", [128, 512], BF16)
    t2m = SB1("t2m", [128, 512], BF16)
    t2B = SB1("t2B", [128, 8 * 512], BF16)
    t2tok = SB1("t2tok", [128, 4 * 1024], BF16)
    statB = SB1("statB", [128, 24], F32)
    xpxB = SB1("xpxB", [128, D], F32)
    ysumB = SB1("ysumB", [128, D], F32)
    xmoB = SB1("xmoB", [128, D], F32)
    wtq = SB1("wtq", [128, 4096], F32)
    wqt = SB1("wqt", [128, 4096], BF16)
    wqneg = SB1("wqneg", [128, 4096], BF16)
    wqt2 = SB1("wqt2", [128, 4096], BF16)
    posb = SB1("posb", [128, 4096], F32)
    wtmpB = SB1("wtmpB", [128, SD], F32)
    lnbw2 = SB1("lnbw2", [128, 64], BF16)

    for k in range(16):
        sch.dma("sync", wtq[:, 0:1024], wg_h[k * 128:(k + 1) * 128, :], (), ("wtq",), chan="w")
        v_copy(cgate[:, k * 1024:(k + 1) * 1024], wtq[:, 0:1024], ("wtq",), (f"cg_{k}",))
    for k in range(8):
        sch.dma("sync", wtmpB[:], wq_h[k * 128:(k + 1) * 128, :], (), ("wtqq",), chan="w")
        v_copy(cq[:, k * SD:(k + 1) * SD], wtmpB[:], ("wtqq",), (f"cq_{k}",))
    for k in range(32):
        v_copy(lnbw2[:, 2 * k:2 * k + 1], lnb16[:, k:k + 1], ("lnb16",), ("lnbw2",))
        v_copy(lnbw2[:, 2 * k + 1:2 * k + 2], lnw16[:, k:k + 1], ("lnw16",), ("lnbw2",))

    QUANT_TILES = [("fc", k) for k in range(8)] + [("pr", k) for k in range(32)]
    quant_i = [0]

    def emit_quant_tile():
        if quant_i[0] >= len(QUANT_TILES):
            return
        which, k = QUANT_TILES[quant_i[0]]
        quant_i[0] += 1
        n = 4096 if which == "fc" else 1024
        src = wfcT_h if which == "fc" else wprT_h
        dst = wqfc_hbm if which == "fc" else wqpr_hbm
        gc = 0 if which == "fc" else 1
        sch.dma("sync", wtq[:, 0:n], src[k * 128:(k + 1) * 128, :], (), ("wtq",),
                chan="g", nslots=2)
        v_ts(posb[:, 0:n], wtq[:, 0:n], ghalf[:, gc:gc + 1], None, ALU.is_gt, ALU.bypass,
             ("wtq", "ghalf"), ("posb",))
        v_ts(wqneg[:, 0:n], wtq[:, 0:n], gnegh[:, gc:gc + 1], None, ALU.is_lt, ALU.bypass,
             ("wtq", "gnegh"), ("wqnegk",))
        v_tt(wqt[:, 0:n], posb[:, 0:n], wqneg[:, 0:n], ALU.subtract,
             ("posb", "wqnegk"), ("wqt",))
        if which == "pr":
            for half in range(2):
                t_mm(PB[4 + half][0:2, 0:512], lnbw2[:, 2 * k:2 * k + 2],
                     wqt[:, half * 512:(half + 1) * 512], k == 0, k == 31,
                     ("wqt", "lnbw2"), (KPB[4 + half],), tag="cmm")
            v_ts(wqt2[:, 0:n], wqt[:, 0:n], lnwf32[:, k:k + 1], None, ALU.mult, ALU.bypass,
                 ("wqt", "lnwf32"), ("wqt2",))
            sch.dma("sync", dst[k, :, :], wqt2[:, 0:n], ("wqt2",), (), chan="qo")
        else:
            sch.dma("sync", dst[k, :, :], wqt[:, 0:n], ("wqt",), (), chan="qo")

    for i2 in range(min(8, NSC_LIM)):
        ch0 = i2 * 4
        for tt in range(4):
            ch = ch0 + tt
            sch.dma("sync", ytokB[:, tt * D:(tt + 1) * D], y_spill[ch, :, :], (), (f"ytB{tt}",), chan="x")
            sch.dma("sync", yfmB[:, tt * 1024:(tt + 1) * 1024], yfm_spill[ch, :, :], (), (f"yfB{tt}",), chan="x")
            sch.dma("sync", xB[:, tt * D:(tt + 1) * D], x_h[ch * 128:(ch + 1) * 128, :], (), (f"xB{tt}",), chan="x")
        ykeys = tuple(f"yfB{tt}" for tt in range(4))
        for k in range(8):
            t_mm(PB[2][0:64, 0:512], cq[:, k * SD:(k + 1) * SD],
                 yfmB[:].rearrange("p (t4 n) -> p t4 n", t4=4)[:, :, k * 128:(k + 1) * 128],
                 k == 0, k == 7, (f"cq_{k}",) + ykeys, (KPB[2],), tag="q")
        v_copy(qfm[:], PB[2][0:64, 0:512], (KPB[2],), ("qfm",))
        for tt in range(4):
            t_mm(PB[3][:, 0:NPOOL], qfm[:, tt * 128:(tt + 1) * 128], kkT[0:64, 0:NPOOL],
                 True, True, ("qfm", "kkT"), (KPB[3],), tag="lg")
            v_tt(lgb[:, 0:NPOOL], PB[3][:, 0:NPOOL], mb128[:], ALU.mult,
                 (KPB[3], "mb128"), ("lg1",))
            v_tt(lgb[:, NPOOL:2 * NPOOL], lgb[:, 0:NPOOL], bb128[:], ALU.add,
                 ("lg1", "bb128"), ("lg2",))
            v_red(statB[:, 0:1], lgb[:, NPOOL:2 * NPOOL], ALU.max, ("lg2",), ("negmx",), negate=True)
            s_act(ebuf[:], lgb[:, NPOOL:2 * NPOOL], ACT.Exp, ("lg2", "negmx"), ("ebuf",),
                  bias=statB[:, 0:1], accum=esum[:, 0:1])
            v_recip(esum[:, 1:2], esum[:, 0:1], ("ebuf",), ("esr",))
            v_ts(smb[:], ebuf[:], esum[:, 1:2], None, ALU.mult, ALU.bypass,
                 ("ebuf", "esr"), ("smb",))
            t_tr(PT[0][0:NPOOL, 0:128], smb[:], 128, ("smb",), (KPT[0],))
            v_copy(smTb[0:NPOOL, tt * 128:(tt + 1) * 128], PT[0][0:NPOOL, 0:128],
                   (KPT[0],), ("smTb",))
        for k in range(8):
            t_mm(PB[k % 2][:, 0:512], vvtok[0:NPOOL, k * 128:(k + 1) * 128],
                 smTb[0:NPOOL, 0:512], True, True, ("vvtok", "smTb"), (KPB[k % 2],), tag="ret")
            v_copy(retB[:, k * 512:(k + 1) * 512], PB[k % 2][:, 0:512],
                   (KPB[k % 2],), (f"ret{k}",))
        for m in range(8):
            pb, pkey = PB[m % 2], KPB[m % 2]
            for k in range(8):
                t_mm(pb[:, 0:512], cgate[:, k * 1024 + m * 128: k * 1024 + m * 128 + 128],
                     yfmB[:].rearrange("p (t4 n) -> p t4 n", t4=4)[:, :, k * 128:(k + 1) * 128],
                     k == 0, False, (f"cg_{k}",) + ykeys, (pkey,), tag="gate")
            for k in range(8):
                t_mm(pb[:, 0:512], cgate[:, (8 + k) * 1024 + m * 128: (8 + k) * 1024 + m * 128 + 128],
                     retB[:, k * 512:(k + 1) * 512], False, k == 7,
                     (f"cg_{8 + k}", f"ret{k}"), (pkey,), tag="gate2")
            s_act(gb[:], pb[:, 0:512], ACT.Sigmoid, (pkey,), ("gb",))
            v_tt(t2m[:], gb[:], retB[:, m * 512:(m + 1) * 512], ALU.mult,
                 ("gb", f"ret{m}"), ("t2m",))
            v_ts(t2B[:, m * 512:(m + 1) * 512], t2m[:], condcol[:], None, ALU.mult, ALU.bypass,
                 ("t2m", "condcol"), (f"t2B{m}",))
        for m in range(8):
            for tt in range(4):
                t_tr(PT[1][0:128, 0:128], t2B[:, m * 512 + tt * 128: m * 512 + (tt + 1) * 128],
                     128, (f"t2B{m}",), (KPT[1],))
                v_copy(t2tok[:, tt * 1024 + m * 128: tt * 1024 + (m + 1) * 128],
                       PT[1][0:128, 0:128], (KPT[1],), (f"t2tok{tt}",))
        for tt in range(4):
            ch = ch0 + tt
            sch.op("vector", lambda e, tt=tt: e.bn_stats(statB[:, 2:8], xB[:, tt * D: tt * D + 512]),
                   (f"xB{tt}",), ("stB",))
            sch.op("vector", lambda e, tt=tt: e.bn_stats(statB[:, 8:14], xB[:, tt * D + 512:(tt + 1) * D]),
                   (f"xB{tt}",), ("stB",))
            sch.op("vector", lambda e: e.bn_aggr(statB[:, 14:16], statB[:, 2:14]),
                   ("stB",), ("stB",))
            mean, var = statB[:, 14:15], statB[:, 15:16]
            veps, inv = statB[:, 16:17], statB[:, 17:18]
            q1, q2, lnscr = statB[:, 18:19], statB[:, 19:20], statB[:, 20:21]
            v_ts(veps, var, cst(C_EPS), None, ALU.add, ALU.bypass, ("stB", "consts"), ("stB1",))
            rsqrt2(inv, veps, lnscr, 128, ("stB1",), ("stB2",))
            v_ts(q1, inv, cst(C_ONE), None, ALU.add, ALU.bypass, ("stB2", "consts"), ("stB3",))
            v_ts(q2, mean, inv, None, ALU.mult, ALU.bypass, ("stB", "stB2"), ("stB4",))
            v_ts(xpxB[:], xB[:, tt * D:(tt + 1) * D], q1, q2,
                 ALU.mult, ALU.subtract, (f"xB{tt}", "stB3", "stB4"), ("xpxB",))
            v_tt(ysumB[:], ytokB[:, tt * D:(tt + 1) * D], t2tok[:, tt * 1024:(tt + 1) * 1024],
                 ALU.add, (f"ytB{tt}", f"t2tok{tt}"), ("ysumB",))
            v_tt(xmoB[:], xpxB[:], ysumB[:], ALU.add, ("xpxB", "ysumB"), ("xmoB",))
            sch.dma("sync", xmid_spill[ch, :, :], xmoB[:], ("xmoB",), (), chan="xo")
        for _ in range(5):
            emit_quant_tile()

    while quant_i[0] < len(QUANT_TILES):
        emit_quant_tile()
    crow_st = SB1("crow_st", [2, 1024], F32)
    v_copy(crow_st[0:2, 0:512], PB[4][0:2, 0:512], (KPB[4],), ("c1row",))
    v_copy(crow_st[0:2, 512:1024], PB[5][0:2, 0:512], (KPB[5],), ("c2row",))
    sch.dma("sync", c_hbm[:, :], crow_st[:], ("c1row", "c2row"), ("c_hbm",), chan="xo")

    sch.barrier()
    stackB1.close()

    if KPHASE <= 3:
        dummy = nc.alloc_sbuf_tensor("dummy3", [128, D], F32)
        sch.op("vector", lambda e: e.memset(dummy[:], 0.0), (), ("dummyk",), "dm")
        for chd in range(NCH):
            sch.dma("sync", out_h[chd * 128:(chd + 1) * 128, :], dummy[:], ("dummyk",), (), chan="oo")
        sch.finalize()
        with contextlib.ExitStack() as es:
            sems = {}
            for e in ENGS:
                sems[e] = es.enter_context(nc.semaphore(f"sem_{e}"))
            dma_sems = [es.enter_context(nc.semaphore(f"dsem{i}")) for i in range(sch.n_dma_slots)]
            block = es.enter_context(nc.Block())
            stats = sch.emit(block, sems, dma_sems)
        return nc, stats

    stackB2 = contextlib.ExitStack()
    SB2 = lambda name, shape, dt: stackB2.enter_context(nc.sbuf_tensor(name, shape, dt))
    wqfcS = SB2("wqfcS", [128, 8 * 4096], BF16)
    wqprS = SB2("wqprS", [128, 32 * 1024], BF16)
    c1b = SB2("c1b", [128, 1024], F32)
    c2b = SB2("c2b", [128, 1024], F32)
    xmB2 = SB2("xmB2", [128, 2 * D], F32)
    x2tok = SB2("x2tok", [128, 2 * D], BF16)
    x2fm = SB2("x2fm", [128, 8 * 256], BF16)
    gbuf = SB2("gbuf", [128, 32 * 256], BF16)
    sqbB = SB2("sqbB", [128, 2 * 256], BF16)
    atmp = SB2("atmp", [128, 2 * 256], BF16)
    statB2 = SB2("statB2", [128, 24], F32)
    murow = SB2("murow", [1, 256], F32)
    m2row = SB2("m2row", [1, 256], F32)
    varrow = SB2("varrow", [1, 256], F32)
    var2row = SB2("var2row", [1, 256], F32)
    lnsrow = SB2("lnsrow", [1, 256], F32)
    rsrow2 = SB2("rsrow2", [1, 256], F32)
    muarow = SB2("muarow", [1, 256], F32)
    nmuarow = SB2("nmuarow", [1, 256], F32)
    rsb2 = SB2("rsb2", [128, 256], F32)
    nmacol = SB2("nmacol", [128, 2], F32)
    corrb = SB2("corrb", [128, 512], F32)
    corr2b = SB2("corr2b", [128, 512], F32)
    outsb = SB2("outsb", [128, 2 * D], F32)

    for k in range(8):
        sch.dma("sync", wqfcS[:, k * 4096:(k + 1) * 4096], wqfc_hbm[k, :, :], (), (f"wf_{k}",), chan="w")
    for k in range(32):
        sch.dma("sync", wqprS[:, k * 1024:(k + 1) * 1024], wqpr_hbm[k, :, :], (), (f"wp_{k}",), chan="w")
    sch.dma("gpsimd", c1b[:], c_hbm[0:1, :].unsqueeze(1).broadcast_to((1, 128, 1024)),
            ("c_hbm",), ("c1b",), chan="f")
    sch.dma("gpsimd", c2b[:], c_hbm[1:2, :].unsqueeze(1).broadcast_to((1, 128, 1024)),
            ("c_hbm",), ("c2b",), chan="f")

    for i3 in range(min(16, NSC_LIM)):
        ch0 = i3 * 2
        for tt2 in range(2):
            sch.dma("sync", xmB2[:, tt2 * D:(tt2 + 1) * D], xmid_spill[ch0 + tt2, :, :],
                    (), (f"xm2_{tt2}",), chan="x")
            sch.op("vector", lambda e, tt2=tt2: e.bn_stats(statB2[:, 0:6],
                   xmB2[:, tt2 * D: tt2 * D + 512]), (f"xm2_{tt2}",), ("st2",))
            sch.op("vector", lambda e, tt2=tt2: e.bn_stats(statB2[:, 6:12],
                   xmB2[:, tt2 * D + 512:(tt2 + 1) * D]), (f"xm2_{tt2}",), ("st2",))
            sch.op("vector", lambda e: e.bn_aggr(statB2[:, 12:14], statB2[:, 0:12]),
                   ("st2",), ("st2",))
            mean, var = statB2[:, 12:13], statB2[:, 13:14]
            veps, inv, lnscr = statB2[:, 14:15], statB2[:, 15:16], statB2[:, 16:17]
            v_ts(veps, var, cst(C_EPS), None, ALU.add, ALU.bypass, ("st2", "consts"), ("st2a",))
            rsqrt2(inv, veps, lnscr, 128, ("st2a",), ("st2b",))
            v_ts(x2tok[:, tt2 * D:(tt2 + 1) * D], xmB2[:, tt2 * D:(tt2 + 1) * D], mean, inv,
                 ALU.subtract, ALU.mult, (f"xm2_{tt2}", "st2", "st2b"), (f"x2_{tt2}",))
            for k in range(8):
                t_tr(PT[0][0:128, 0:128], x2tok[:, tt2 * D + k * 128: tt2 * D + (k + 1) * 128],
                     128, (f"x2_{tt2}",), (KPT[0],))
                v_copy(x2fm[:, k * 256 + tt2 * 128: k * 256 + (tt2 + 1) * 128],
                       PT[0][0:128, 0:128], (KPT[0],), (f"x2f{k}",))
        x2keys = tuple(f"x2f{k}" for k in range(8))
        for j in range(32):
            pb, pkey = PB[j % 2], KPB[j % 2]
            for k in range(8):
                t_mm(pb[:, 0:256], wqfcS[:, k * 4096 + j * 128: k * 4096 + (j + 1) * 128],
                     x2fm[:, k * 256:(k + 1) * 256], k == 0, k == 7,
                     (f"wf_{k}",) + x2keys, (pkey,), tag="hmm")
            s_act(gbuf[:, j * 256:(j + 1) * 256], pb[:, 0:256], ACT.Gelu, (pkey,), (f"g{j}",))
            sq = sqbB[:, (j % 2) * 256:(j % 2 + 1) * 256]
            v_tt(sq, gbuf[:, j * 256:(j + 1) * 256], gbuf[:, j * 256:(j + 1) * 256],
                 ALU.mult, (f"g{j}",), (f"sqB{j % 2}",), tag="sqB")
            t_mm(PB[4][0:1, 0:256], onescol16[:], gbuf[:, j * 256:(j + 1) * 256],
                 j == 0, j == 31, (f"g{j}", "onescol16"), (KPB[4],), tag="sgmm")
            t_mm(PB[5][0:1, 0:256], onescol16[:], sq, j == 0, j == 31,
                 (f"sqB{j % 2}", "onescol16"), (KPB[5],), tag="sqmm2")
        v_ts(murow[:], PB[4][0:1, 0:256], cst(C_I4096, 1), None, ALU.mult, ALU.bypass,
             (KPB[4], "consts"), ("murow",))
        v_tt(m2row[:], murow[:], murow[:], ALU.mult, ("murow",), ("m2row",))
        v_stt(varrow[:], PB[5][0:1, 0:256], cst(C_I4096, 1), m2row[:], ALU.mult, ALU.subtract,
              (KPB[5], "m2row", "consts"), ("varrow",))
        v_ts(var2row[:], varrow[:], cst(C_EPS, 1), None, ALU.add, ALU.bypass,
             ("varrow", "consts"), ("var2row",))
        s_act(lnsrow[:], var2row[:], ACT.Ln, ("var2row",), ("lnsrow",))
        s_act(rsrow2[:], lnsrow[:], ACT.Exp, ("lnsrow", "consts"), ("rsrow2",),
              scale=cst(C_NEGH, 1))
        v_tt(muarow[:], murow[:], rsrow2[:], ALU.mult, ("murow", "rsrow2"), ("muarow",))
        v_ts(nmuarow[:], muarow[:], cst(C_NEG1, 1), None, ALU.mult, ALU.bypass,
             ("muarow", "consts"), ("nmuarow",))
        bcast(rsb2[:], rsrow2[:], 256, ("rsrow2",), ("rsb2",))
        for tt2 in range(2):
            t_trf(PB[2][0:128, 0:1], nmuarow[0:1, tt2 * 128:(tt2 + 1) * 128], 1,
                  ("nmuarow",), (KPB[2],))
            v_copy(nmacol[:, tt2:tt2 + 1], PB[2][0:128, 0:1], (KPB[2],), ("nmacol",))
        for j in range(32):
            at = atmp[:, (j % 2) * 256:(j % 2 + 1) * 256]
            v_tt(at, gbuf[:, j * 256:(j + 1) * 256], rsb2[:], ALU.mult,
                 (f"g{j}", "rsb2"), (f"at{j % 2}",), tag="aA")
            for tt2 in range(2):
                for dh in range(2):
                    pb, pkey = PB[tt2 * 2 + dh], KPB[tt2 * 2 + dh]
                    t_mm(pb[:, 0:512], at[:, tt2 * 128:(tt2 + 1) * 128],
                         wqprS[:, j * 1024 + dh * 512: j * 1024 + (dh + 1) * 512],
                         j == 0, j == 31, (f"at{j % 2}", f"wp_{j}"), (pkey,), tag="pmm")
        for tt2 in range(2):
            for dh in range(2):
                pb, pkey = PB[tt2 * 2 + dh], KPB[tt2 * 2 + dh]
                v_stt(corrb[:], c2b[:, dh * 512:(dh + 1) * 512], nmacol[:, tt2:tt2 + 1],
                      c1b[:, dh * 512:(dh + 1) * 512], ALU.mult, ALU.add,
                      ("c2b", "c1b", "nmacol"), ("corrb",))
                v_tt(corr2b[:], corrb[:], pb[:, 0:512], ALU.add, ("corrb", pkey), ("corr2b",))
                v_tt(outsb[:, tt2 * D + dh * 512: tt2 * D + (dh + 1) * 512], corr2b[:],
                     xmB2[:, tt2 * D + dh * 512: tt2 * D + (dh + 1) * 512], ALU.add,
                     ("corr2b", f"xm2_{tt2}"), (f"osb{tt2}",))
            sch.dma("sync", out_h[(ch0 + tt2) * 128:(ch0 + tt2 + 1) * 128, :],
                    outsb[:, tt2 * D:(tt2 + 1) * D], (f"osb{tt2}",), (), chan="oo")

    sch.finalize()
    stackB2.close()

    with contextlib.ExitStack() as es:
        sems = {}
        for e in ENGS:
            sems[e] = es.enter_context(nc.semaphore(f"sem_{e}"))
        dma_sems = [es.enter_context(nc.semaphore(f"dsem{i}")) for i in range(sch.n_dma_slots)]
        block = es.enter_context(nc.Block())
        stats = sch.emit(block, sems, dma_sems)
    return nc, stats


_CACHE = {}


def _get_program():
    if "nc" not in _CACHE:
        nc, stats = build_program()
        _CACHE["nc"] = nc
        _CACHE["stats"] = stats
    return _CACHE["nc"]


def stage_inputs(**inputs):
    x = np.asarray(inputs["L0_attn_x"], dtype=np.float32)
    conv_w = np.asarray(inputs["L0_attn_conv_w"], dtype=np.float32)
    conv_b = np.asarray(inputs["L0_attn_conv_b"], dtype=np.float32)
    norm_w = np.asarray(inputs["L0_attn_norm_w"], dtype=np.float32)
    ssm_w = np.asarray(inputs["L0_attn_ssm_norm_w"], dtype=np.float32)
    ln_w = np.asarray(inputs["L0_ffn_ln_w"], dtype=np.float32)
    ln_b = np.asarray(inputs["L0_ffn_ln_b"], dtype=np.float32)
    Dh = np.asarray(inputs["L0_attn_D"], dtype=np.float32)

    common = {
        "w_inproj": np.ascontiguousarray(inputs["L0_attn_in_proj_w"], dtype=np.float32),
        "w_out": np.ascontiguousarray(inputs["L0_attn_out_proj_w"], dtype=np.float32),
        "w_sc1": np.ascontiguousarray(inputs["L0_attn_scorer_w1"], dtype=np.float32),
        "w_sc2": np.ascontiguousarray(inputs["L0_attn_scorer_w2"], dtype=np.float32),
        "w_summ": np.ascontiguousarray(inputs["L0_attn_summ_w"], dtype=np.float32),
        "w_q": np.ascontiguousarray(inputs["L0_attn_q_w"], dtype=np.float32),
        "w_k": np.ascontiguousarray(inputs["L0_attn_k_w"], dtype=np.float32),
        "w_v": np.ascontiguousarray(inputs["L0_attn_v_w"], dtype=np.float32),
        "w_gate": np.ascontiguousarray(inputs["L0_attn_gate_w"], dtype=np.float32),
        "w_fcT": np.ascontiguousarray(np.asarray(inputs["L0_ffn_W_fc"], dtype=np.float32).T),
        "w_projT": np.ascontiguousarray(np.asarray(inputs["L0_ffn_W_proj"], dtype=np.float32).T),
        "conv_w_t": np.ascontiguousarray(
            conv_w.reshape(18, 128, 4).transpose(1, 0, 2).reshape(128, 72)),
        "conv_b_t": np.ascontiguousarray(conv_b.reshape(18, 128).T),
        "norm_w_t": np.ascontiguousarray(norm_w.reshape(8, 128).T),
        "ssm_w_t": np.ascontiguousarray(ssm_w.reshape(16, 128).T),
        "ln_w_t": np.ascontiguousarray(ln_w.reshape(32, 128).T),
        "ln_b_t": np.ascontiguousarray(ln_b.reshape(32, 128).T),
        "dt_bias_t": np.ascontiguousarray(
            np.asarray(inputs["L0_attn_dt_bias"], dtype=np.float32).reshape(32, 1)),
        "a_log_t": np.ascontiguousarray(
            np.asarray(inputs["L0_attn_A_log"], dtype=np.float32).reshape(32, 1)),
        "d_fm_t": np.ascontiguousarray(np.concatenate(
            [np.broadcast_to(Dh[0::2], (64, 16)), np.broadcast_to(Dh[1::2], (64, 16))], axis=0)),
        "ident": np.eye(128, dtype=ml_dtypes.bfloat16),
        "identf": np.eye(128, dtype=np.float32),
        "mask_sl": np.ascontiguousarray(np.triu(np.ones((128, 128), dtype=np.float32))),
    }
    in_maps = [dict(common, x=np.ascontiguousarray(x[b])) for b in range(NCORES)]
    return in_maps


def kernel(**inputs):
    in_maps = stage_inputs(**inputs)
    nc = _get_program()
    res = run_bass_kernel_spmd(nc, in_maps, core_ids=list(range(NCORES)))
    out = np.stack([np.asarray(r["out"], dtype=np.float32) for r in res.results], axis=0)
    return out


# version 12
# speedup vs baseline: 1.3690x; 1.0240x over previous
import os
import contextlib
import numpy as np
import ml_dtypes
import concourse.bass as bass
import concourse.mybir as mybir
from concourse.bass_utils import run_bass_kernel_spmd

F32 = mybir.dt.float32
BF16 = mybir.dt.bfloat16
U32 = mybir.dt.uint32
ALU = mybir.AluOpType
ACT = mybir.ActivationFunctionType
AX = mybir.AxisListType

import os
NSC_LIM = int(os.environ.get("KNSC", "16"))
NO_CC = os.environ.get("KNOCC", "0") == "1"
NO_IG = os.environ.get("KNOIG", "0") == "1"
NO_MX = os.environ.get("KNOMX", "0") == "1"
KPHASE = int(os.environ.get("KPHASE", "4"))
KSELCUT = int(os.environ.get("KSELCUT", "99"))
KDBG = os.environ.get("KDBG", "0") == "1"
KDBG2 = os.environ.get("KDBG2", "0") == "1"
T = 4096
D = 1024
L = 128
NCH = T // L
SC = 256
NSC = T // SC
H = 32
P = 64
DIN = 2048
DIP = 4384
HID = 256
SD = 64
NPOOL = 50
NSEL = 56
NCORES = 8
ENGS = ("tensor", "vector", "scalar", "gpsimd", "sync")


class Sched:

    class Op:
        __slots__ = ("eng", "fn", "reads", "writes", "idx", "deps", "inc",
                     "is_dma", "dma_sem", "dma_seq", "dma_amt", "tag",
                     "need_drain", "need_local", "drain_before")

        def __init__(self, eng, fn, reads, writes, idx, tag):
            self.eng, self.fn = eng, fn
            self.reads, self.writes = reads, writes
            self.idx = idx
            self.deps = []
            self.inc = False
            self.need_drain = False
            self.need_local = False
            self.drain_before = False
            self.is_dma = False
            self.dma_sem = None
            self.dma_seq = 0
            self.dma_amt = 16
            self.tag = tag

    def __init__(self, n_dma_slots=None):
        if n_dma_slots is None:
            n_dma_slots = int(os.environ.get('KSLOTS', '64'))
        self.ops = {e: [] for e in ENGS}
        self.last_writer = {}
        self.readers = {}
        self.n_dma_slots = n_dma_slots
        self.slot_seq = [0] * n_dma_slots
        self.slot_last = [None] * n_dma_slots
        self.chan_slots = {}
        self.chan_uses = {}
        self._next_slot = 0
        self.oplimit = None

    def _muted(self):
        if self.oplimit is None:
            return False
        self.oplimit -= 1
        if self.oplimit < 0:
            return True
        return False

    def _record(self, eng, fn, reads, writes, tag):
        if self._muted():
            return Sched.Op(eng, None, (), (), -1, "muted")
        op = Sched.Op(eng, fn, tuple(reads), tuple(writes), len(self.ops[eng]), tag)
        deps = []
        for k in op.reads:
            w = self.last_writer.get(k)
            if w is not None:
                deps.append((w, 0))
        for k in op.writes:
            w = self.last_writer.get(k)
            if w is not None:
                deps.append((w, 1))
            for r in self.readers.get(k, ()):
                deps.append((r, 2))
        seen = {}
        for d, kind in deps:
            key = (d.eng, d.idx)
            if key not in seen:
                seen[key] = len(op.deps)
                op.deps.append((d, kind))
            else:
                i = seen[key]
                if kind < op.deps[i][1]:
                    op.deps[i] = (d, kind)
        for k in op.writes:
            self.last_writer[k] = op
            self.readers[k] = []
        for k in op.reads:
            self.readers.setdefault(k, []).append(op)
        self.ops[eng].append(op)
        return op

    def op(self, eng, fn, reads=(), writes=(), tag=""):
        return self._record(eng, fn, reads, writes, tag)

    def dma_op(self, eng, fn, reads=(), writes=(), chan="d", nslots=2, tag="dmaop", amt=16):
        slots = self.chan_slots.setdefault(chan, [])
        if len(slots) < nslots and self._next_slot < self.n_dma_slots:
            slots.append(self._next_slot)
            self._next_slot += 1
        use = self.chan_uses.get(chan, 0)
        self.chan_uses[chan] = use + 1
        slot = slots[use % len(slots)]
        op = self._record(eng, fn, reads, writes, tag)
        if op.idx < 0:
            return op
        op.is_dma = True
        op.dma_sem = slot
        op.dma_amt = amt
        prev = self.slot_last[slot]
        if prev is not None:
            op.deps.append((prev, False))
        self.slot_seq[slot] += 1
        op.dma_seq = self.slot_seq[slot]
        self.slot_last[slot] = op
        return op

    def dma(self, eng, out_ap, in_ap, reads=(), writes=(), chan="d", nslots=4, tag="dma"):
        nslots = min(nslots, int(os.environ.get('KNSLOT', '4')))
        slots = self.chan_slots.setdefault(chan, [])
        if len(slots) < nslots and self._next_slot < self.n_dma_slots:
            slots.append(self._next_slot)
            self._next_slot += 1
        use = self.chan_uses.get(chan, 0)
        self.chan_uses[chan] = use + 1
        slot = slots[use % len(slots)]

        def fn(e, out_ap=out_ap, in_ap=in_ap):
            return e.dma_start(out=out_ap, in_=in_ap)

        op = self._record(eng, fn, reads, writes, tag)
        if op.idx < 0:
            return op
        op.is_dma = True
        op.dma_sem = slot
        prev = self.slot_last[slot]
        if prev is not None:
            op.deps.append((prev, False))
        self.slot_seq[slot] += 1
        op.dma_seq = self.slot_seq[slot]
        self.slot_last[slot] = op
        return op

    def barrier(self):
        marks = {}
        for e in ENGS:
            m = None
            for op in reversed(self.ops[e]):
                if op.fn is not None and not op.is_dma:
                    m = op
                    break
            if m is not None:
                m.inc = True
                m.need_drain = True
            marks[e] = m
        for e in ENGS:
            w = self._record(e, None, (), (), "barrier_w")
            for o in ENGS:
                if o != e and marks[o] is not None:
                    w.deps.append((marks[o], False))
            for s in range(self.n_dma_slots):
                if self.slot_last[s] is not None:
                    w.deps.append((self.slot_last[s], False))

    def finalize(self):
        w = self._record("sync", (lambda en: en.nop()), (), (), "final")
        for s in range(self.n_dma_slots):
            if self.slot_last[s] is not None:
                w.deps.append((self.slot_last[s], False))

    def emit(self, block, sems, dma_sems):
        paranoid = os.environ.get("KPAR", "0") == "1"
        audit = os.environ.get("KAUDIT", "0") == "1"
        if audit:
            paranoid = True
        local_edges = {e: [] for e in ENGS}
        for e in ENGS:
            for op in self.ops[e]:
                if paranoid and not op.is_dma and op.fn is not None:
                    op.inc = True
                    if audit:
                        op.need_drain = True
                for d, kind in op.deps:
                    cross = d.eng != e or op.is_dma
                    if (not d.is_dma) and (cross or paranoid):
                        d.inc = True
                    if kind == 2 or d.is_dma:
                        continue
                    if cross:
                        d.need_drain = True
                    elif kind == 0:
                        local_edges[e].append((d.idx, op.idx))
        for e in ENGS:
            pts = sorted(o.idx + 0.5 for o in self.ops[e]
                         if o.need_drain or (audit and o.inc))
            import bisect as _bi
            for p, c in sorted(local_edges[e], key=lambda pc: pc[1]):
                i = _bi.bisect_right(pts, p)
                if i < len(pts) and pts[i] < c:
                    continue
                self.ops[e][c].drain_before = True
                _bi.insort(pts, c - 0.5)
        inccount = {}
        for e in ENGS:
            c = 0
            for op in self.ops[e]:
                if op.inc and not op.is_dma:
                    c += 1
                inccount[(e, op.idx)] = c

        nodrain = set(os.environ.get("KND", "tensor,scalar").split(",")) - {""}

        def make_section(ename):
            ops_list = self.ops[ename]
            eng_nodrain = ename in nodrain

            def section(eng):
                waited = {}
                for op in ops_list:
                    want = {}
                    for d, kind in op.deps:
                        if d.is_dma:
                            k = ("dma", d.dma_sem)
                            want[k] = max(want.get(k, 0), d.dma_amt * d.dma_seq)
                        elif d.eng != ename or paranoid or op.is_dma:
                            want[d.eng] = max(want.get(d.eng, 0), inccount[(d.eng, d.idx)])
                    for k, v in sorted(want.items(), key=str):
                        if waited.get(k, -1) < v:
                            if isinstance(k, tuple):
                                eng.wait_ge(dma_sems[k[1]], v)
                            else:
                                eng.wait_ge(sems[k], v)
                            waited[k] = v
                    if op.fn is None:
                        continue
                    if op.drain_before:
                        eng.drain(fusable=True)
                    instr = op.fn(eng)
                    if op.is_dma:
                        instr.then_inc(dma_sems[op.dma_sem], op.dma_amt)
                    elif op.inc:
                        if op.need_drain and not eng_nodrain:
                            eng.maybe_drain_then_inc((sems[ename], 1), fusable=True)
                        else:
                            if instr is None or ename == "gpsimd":
                                try:
                                    instr = eng.engine_nop()
                                except Exception:
                                    instr = eng.nop()
                            instr.then_inc(sems[ename], 1)
            return section

        for ename in ENGS:
            getattr(block, ename)(make_section(ename))
        return {e: len(self.ops[e]) for e in ENGS} | {
            "incs": sum(1 for e in ENGS for o in self.ops[e] if o.inc)}


def build_program():
    nc = bass.Bass("TRN2", target_bir_lowering=False,
                   detect_race_conditions=os.environ.get("KRACE", "0") == "1")
    sch = Sched()

    def dramin(name, shape, dt=F32):
        return nc.dram_tensor(name, list(shape), dt, kind="ExternalInput").ap()

    x_h = dramin("x", (T, D))
    w1_h = dramin("w_inproj", (D, DIP))
    wo_h = dramin("w_out", (DIN, D))
    wsc1_h = dramin("w_sc1", (D, HID))
    wsc2_h = dramin("w_sc2", (HID, 1))
    wsm_h = dramin("w_summ", (D, SD))
    wq_h = dramin("w_q", (D, SD))
    wk_h = dramin("w_k", (SD, SD))
    wv_h = dramin("w_v", (SD, D))
    wg_h = dramin("w_gate", (DIN, D))
    wfcT_h = dramin("w_fcT", (D, 4096))
    wprT_h = dramin("w_projT", (4096, D))
    cwv_h = dramin("conv_w_t", (128, 72))
    ccb_h = dramin("conv_b_t", (128, 18))
    nw_h = dramin("norm_w_t", (128, 8))
    sw_h = dramin("ssm_w_t", (128, 16))
    lnw_h = dramin("ln_w_t", (128, 32))
    lnb_h = dramin("ln_b_t", (128, 32))
    dtb_h = dramin("dt_bias_t", (32, 1))
    alog_h = dramin("a_log_t", (32, 1))
    dfm_h = dramin("d_fm_t", (128, 16))
    ident_h = dramin("ident", (128, 128), BF16)
    identf_h = dramin("identf", (128, 128), F32)
    mask_h = dramin("mask_sl", (128, 128))

    out_h = nc.dram_tensor("out", [T, D], F32, kind="ExternalOutput").ap()

    y_spill = nc.dram_tensor("y_spill", [NCH, 128, D], F32).ap()
    dbg_h = {}
    if KDBG:
        for nm, shp, dt_ in [("xntok", [128, D], BF16), ("xnf", [128, 2048], BF16),
                             ("sz", [128, 16 * SC], BF16), ("xbcc", [128, 18 * SC], BF16),
                             ("dtv", [32, SC], F32), ("csv", [32, L], F32), ("statc", [128, 20], F32),
                             ("xdt", [128, 2048], BF16), ("yfmsc", [128, 2048], BF16),
                             ("yop", [128, 2048], BF16), ("dsv", [32, L], F32),
                             ("expcs32", [32, L], F32), ("sall", [128, 2048], BF16),
                             ("gmask", [128, 128], BF16), ("mtb0", [128, 128], BF16),
                             ("xdt2", [128, 2048], BF16), ("dtT16", [128, 32], BF16),
                             ("mtbh2", [128, 128], BF16), ("ssegh2", [128, 128], F32),
                             ("identd", [128, 128], BF16), ("identfd", [128, 128], F32),
                             ("constsd", [128, 28], F32),
                             ("csbh2", [128, 512], F32), ("csTd", [128, 32], F32)]:
            dbg_h[nm] = nc.dram_tensor("dbg_" + nm, shp, dt_, kind="ExternalOutput").ap()
    yfm_spill = nc.dram_tensor("yfm_spill", [NCH, 128, 1024], BF16).ap()
    xmid_spill = nc.dram_tensor("xmid_spill", [NCH, 128, D], F32).ap()
    wqfc_hbm = nc.dram_tensor("wqfc_hbm", [8, 128, 4096], BF16).ap()
    wqpr_hbm = nc.dram_tensor("wqpr_hbm", [32, 128, D], BF16).ap()
    sm_in = nc.dram_tensor("sm_in", [T + 64, SD], F32)
    sm_out = nc.dram_tensor("sm_out", [8 * (T + 64), SD], F32)
    c_hbm = nc.dram_tensor("c_hbm", [2, 1024], F32).ap()

    A = nc.alloc_sbuf_tensor
    consts = A("constssb", [128, 28], F32)
    (C_HALF, C_EPS, C_NEG1, C_NEGH, C_I2048, C_I4096, C_ONE, C_TAUS,
     C_1E9, C_N1E9, C_128, C_QUART, C_IFC, C_ZERO, C_1E6M, C_256, C_N128,
     C_T1, C_T2, C_T3, C_T4, C_T5, C_T6, C_T7, C_64) = range(25)
    CONST_VALS = {C_HALF: 0.5, C_EPS: 1e-5, C_NEG1: -1.0, C_NEGH: -0.5,
                  C_I2048: 1.0 / 2048, C_I4096: 1.0 / 4096, C_ONE: 1.0,
                  C_TAUS: 0.3 * 32768.0, C_1E9: 1e9, C_N1E9: -1e9,
                  C_128: 128.0, C_QUART: 0.25, C_IFC: float(2.0 ** -22),
                  C_ZERO: 0.0, C_1E6M: 1e-6, C_256: 256.0, C_N128: -128.0,
                  C_T1: 4096.0, C_T2: 8192.0, C_T3: 12288.0, C_T4: 16384.0,
                  C_T5: 20480.0, C_T6: 24576.0, C_T7: 28672.0, C_64: 64.0}
    ident = A("identsb", [128, 128], BF16)
    identf = A("identfsb", [128, 128], F32)
    masksl = A("masksl", [128, 128], F32)
    onescol = A("onescol", [128, 1], F32)
    onescol16 = A("onescol16", [128, 1], BF16)
    ones32 = A("ones32", [32, 128], F32)
    onecol32 = A("onecol32", [32, 1], F32)
    piota_f = A("piota_f", [128, 1], F32)
    piota_u = A("piota_u", [128, 1], U32)
    scrow = A("scrow", [1, 128], F32)
    thetas = A("thetas", [1, 64], F32)
    kkT = A("kkT", [64, NSEL], BF16)
    vvtok = A("vvtok", [NSEL, D], BF16)
    mb128 = A("mb128", [128, NPOOL], F32)
    bb128 = A("bb128", [128, NPOOL], F32)
    condcol = A("condcol", [128, 1], F32)
    ghalf = A("ghalf", [128, 2], F32)
    gnegh = A("gnegh", [128, 2], F32)
    lnw16 = A("lnw16", [128, 32], BF16)
    lnwf32 = A("lnwf32", [128, 32], F32)
    lnb16 = A("lnb16", [128, 32], BF16)
    gacc = A("gacc", [128, 4], F32)
    grow_sb = A("grow_sb", [1, 4], F32)

    PB = [nc.alloc_psum_tensor(f"pb{i}", [128, 512], F32) for i in range(6)]
    PT = [nc.alloc_psum_tensor(f"pt{i}", [128, 1024], BF16) for i in range(2)]
    KPB = [f"PB{i}" for i in range(6)]
    KPT = [f"PT{i}" for i in range(2)]

    def cst(i, p=128):
        return consts[0:p, i:i + 1]

    def v_tt(out, a, b, op, reads, writes, tag="tt"):
        sch.op("vector", lambda e: e.tensor_tensor(out, a, b, op), reads, writes, tag)

    def v_ts(out, in0, s1, s2, op0, op1, reads, writes, tag="ts"):
        sch.op("vector", lambda e: e.tensor_scalar(out, in0, s1, s2, op0, op1), reads, writes, tag)

    def v_stt(out, in0, s, in1, op0, op1, reads, writes, tag="stt"):
        sch.op("vector", lambda e: e.scalar_tensor_tensor(out, in0, s, in1, op0, op1), reads, writes, tag)

    def v_copy(out, in_, reads, writes, tag="cp"):
        sch.op("vector", lambda e: e.tensor_copy(out, in_), reads, writes, tag)

    def v_red(out, in_, op, reads, writes, negate=False, absval=False, tag="red"):
        sch.op("vector", lambda e: e.tensor_reduce(out, in_, AX.X, op,
               apply_absolute_value=absval, negate=negate), reads, writes, tag)

    def v_scan(out, d0, d1, init, op0, op1, reads, writes, tag="scan"):
        sch.op("vector", lambda e: e.tensor_tensor_scan(out, d0, d1, init, op0, op1), reads, writes, tag)

    def v_memset(ap_, val, writes, tag="memset"):
        sch.op("vector", lambda e: e.memset(ap_, val), (), writes, tag)

    def v_recip(out, in_, reads, writes, tag="recip"):
        sch.op("vector", lambda e: e.reciprocal(out, in_), reads, writes, tag)

    def v_max(out, in_, reads, writes, tag="max"):
        sch.op("vector", lambda e: e.max(out, in_), reads, writes, tag)

    def v_maxidx(out, in_max, in_vals, reads, writes, tag="mxi"):
        sch.op("vector", lambda e: e.max_index(out, in_max, in_vals), reads, writes, tag)

    def v_matchrep(out, rep, vals, imm, reads, writes, tag="mr"):
        sch.op("vector", lambda e: e.match_replace(out, in_to_replace=rep, in_values=vals,
               imm_value=imm), reads, writes, tag)

    def s_act(out, in_, func, reads, writes, bias=None, scale=None, accum=None, tag="act"):
        kw = {}
        if bias is not None:
            kw["bias"] = bias
        if scale is not None:
            kw["scale"] = scale
        if accum is not None:
            kw["accum_out"] = accum
        sch.op("scalar", lambda e: e.activation(out, in_, func, **kw), reads, writes, tag)

    def s_copy(out, in_, reads, writes, tag="scp"):
        sch.op("scalar", lambda e: e.copy(out, in_), reads, writes, tag)

    def t_mm(out, lhsT, rhs, start, stop, reads, writes, tag="mm", tile_position=None):
        sch.op("tensor", lambda e: e.matmul(out, lhsT, rhs, start=start, stop=stop,
               tile_position=tile_position), reads, writes, tag)

    def t_tr(out, in_, pin, reads, writes, tag="tr"):
        sch.op("tensor", lambda e: e.transpose(out, in_, ident[0:pin, 0:pin]),
               tuple(reads) + ("ident",), writes, tag)

    def t_trf(out, in_, pin, reads, writes, tag="trf"):
        sch.op("tensor", lambda e: e.transpose(out, in_, identf[0:pin, 0:pin]),
               tuple(reads) + ("identf",), writes, tag)

    def bcast(dst, src_row, n, reads, writes, chan="f"):
        sch.dma("gpsimd", dst, src_row.unsqueeze(1).broadcast_to((1, 128, n)),
                reads, writes, chan=chan, tag="bcast")

    def rsqrt2(out, in_, scratch, p, reads, writes):
        s_act(scratch, in_, ACT.Ln, reads, ("rsq_t",) , tag="rln")
        s_act(out, scratch, ACT.Exp, ("rsq_t", "consts"), writes,
              scale=cst(C_NEGH, p), tag="rexp")

    v_memset(consts[:], 0.0, ("consts",), tag="c0")
    for ci2, v in CONST_VALS.items():
        v_memset(consts[:, ci2:ci2 + 1], v, ("consts",), tag="c")
    v_memset(onescol[:], 1.0, ("onescol",))
    v_memset(onescol16[:], 1.0, ("onescol16",))
    v_memset(ones32[:], 1.0, ("ones32",))
    v_memset(onecol32[:], 1.0, ("onecol32",))
    v_memset(gacc[:], 0.0, ("gacc",))
    sch.op("gpsimd", lambda e: e.iota(piota_u[:], pattern=[[0, 1]], base=0,
           channel_multiplier=1), (), ("piota_u",), "iota")
    v_copy(piota_f[:], piota_u[:], ("piota_u",), ("piota_f",))
    sch.dma("sync", ident[:], ident_h, (), ("ident",), chan="w")
    sch.dma("sync", identf[:], identf_h, (), ("identf",), chan="w")
    sch.dma("sync", masksl[:], mask_h, (), ("masksl",), chan="w")

    stackA = contextlib.ExitStack()
    SB = lambda name, shape, dt: stackA.enter_context(nc.sbuf_tensor(name, shape, dt))
    cw1 = SB("cw1", [128, 8 * DIP], BF16)
    cwo = SB("cwo", [128, 16 * 1024], BF16)
    csc1 = SB("csc1", [128, 8 * HID], BF16)
    csc2 = SB("csc2", [128, 2], BF16)
    csum = SB("csum", [128, 8 * SD], BF16)
    cwcv = SB("cwcv", [128, 72], F32)
    ccb = SB("ccb", [128, 18], F32)
    nwcols = SB("nwcols", [128, 8], F32)
    swcols = SB("swcols", [128, 16], F32)
    dtbcol = SB("dtbcol", [32, 1], F32)
    acol = SB("acol", [32, 1], F32)
    acol2 = SB("acol2", [32, 1], F32)
    dfm = SB("dfm", [128, 16], F32)
    stackP0 = contextlib.ExitStack()
    wtmp = stackP0.enter_context(nc.sbuf_tensor("wtmp", [128, DIP], F32, side="right"))
    wtmpG = SB("wtmpG", [128, 1024], F32)
    lnwf = SB("lnwf", [128, 32], F32)
    lnbf = SB("lnbf", [128, 32], F32)

    sch.dma("sync", cwcv[:], cwv_h, (), ("cwcv",), chan="w")
    sch.dma("sync", ccb[:], ccb_h, (), ("ccb",), chan="w")
    sch.dma("sync", nwcols[:], nw_h, (), ("nwcols",), chan="w")
    sch.dma("sync", swcols[:], sw_h, (), ("swcols",), chan="w")
    sch.dma("sync", dtbcol[:], dtb_h, (), ("dtbcol",), chan="w")
    sch.dma("sync", dfm[:], dfm_h, (), ("dfm",), chan="w")
    sch.dma("sync", lnwf[:], lnw_h, (), ("lnwf",), chan="w")
    sch.dma("sync", lnbf[:], lnb_h, (), ("lnbf",), chan="w")
    v_copy(lnw16[:], lnwf[:], ("lnwf",), ("lnw16",))
    v_copy(lnwf32[:], lnwf[:], ("lnwf",), ("lnwf32",))
    v_copy(lnb16[:], lnbf[:], ("lnbf",), ("lnb16",))
    sch.dma("sync", acol2[:], alog_h, (), ("acol_r",), chan="w")
    s_act(acol[:], acol2[:], ACT.Exp, ("acol_r",), ("acol_e",))
    v_ts(acol2[:], acol[:], cst(C_NEG1, 32), None, ALU.mult, ALU.bypass,
         ("acol_e", "consts"), ("acol",))
    acol_ap = acol2

    for k in range(8):
        wt, wkey = wtmp, "wtmpA"
        sch.dma("sync", wt[:], w1_h[k * 128:(k + 1) * 128, :], (), (wkey,), chan="w")
        v_ts(cw1[:, k * DIP:(k + 1) * DIP], wt[:], nwcols[:, k:k + 1], None,
             ALU.mult, ALU.bypass, (wkey, "nwcols"), (f"cw1_{k}",))
    for k in range(16):
        wt, wkey = wtmp, "wtmpA"
        sch.dma("sync", wt[:, 0:1024], wo_h[k * 128:(k + 1) * 128, :], (), (wkey,), chan="w")
        v_ts(cwo[:, k * 1024:(k + 1) * 1024], wt[:, 0:1024], swcols[:, k:k + 1], None,
             ALU.mult, ALU.bypass, (wkey, "swcols"), (f"cwo_{k}",))
    for k in range(8):
        wt, wkey = wtmp, "wtmpA"
        sch.dma("sync", wt[:, 0:HID], wsc1_h[k * 128:(k + 1) * 128, :], (), (wkey,), chan="w")
        v_copy(csc1[:, k * HID:(k + 1) * HID], wt[:, 0:HID], (wkey,), (f"csc1_{k}",))
    for k in range(2):
        wt, wkey = wtmp, "wtmpA"
        sch.dma("sync", wt[:, 0:1], wsc2_h[k * 128:(k + 1) * 128, :], (), (wkey,), chan="w")
        v_copy(csc2[:, k:k + 1], wt[:, 0:1], (wkey,), (f"csc2_{k}",))
    for k in range(8):
        wt, wkey = wtmp, "wtmpA"
        sch.dma("sync", wt[:, 0:SD], wsm_h[k * 128:(k + 1) * 128, :], (), (wkey,), chan="w")
        v_copy(csum[:, k * SD:(k + 1) * SD], wt[:, 0:SD], (wkey,), (f"csum_{k}",))
    stackP0.close()

    xtok = SB("xtok", [128, D], F32)
    xntok = SB("xntok", [128, D], BF16)
    xnf = SB("xnf", [128, 2048], BF16)
    statc = SB("statc", [128, 32], F32)
    sz = SB("sz", [128, 16 * SC], BF16)
    xbcr = SB("xbcr", [128, 18 * (SC + 4)], BF16)
    xbcc = SB("xbcc", [128, 18 * SC], BF16)
    cacc = SB("cacc", [128, 2 * SC], F32)
    dte = SB("dte", [32, SC], F32)
    dtv = SB("dtv", [32, SC], F32)
    dav = SB("dav", [32, SC], F32)
    csv = SB("csv", [32, SC], F32)
    ndsv = SB("ndsv", [32, L], F32)
    dsv = SB("dsv", [32, L], F32)
    expcs32 = SB("expcs32", [32, L], F32)
    expcs16 = SB("expcs16", [32, L], BF16)
    csflatq = [SB(f"csflatq{i}", [1, 8 * L], F32) for i in range(2)]
    expflatq = [SB(f"expflatq{i}", [1, 8 * L], BF16) for i in range(2)]
    lamrow = SB("lamrow", [1, 32], F32)
    lamraw = SB("lamraw", [1, 32], F32)
    lamb = SB("lamb", [128, 32], F32)
    csT = SB("csT", [128, 32], F32)
    dtT16 = SB("dtT16", [128, 32], BF16)
    dsT16 = SB("dsT16", [128, 32], BF16)
    csbq = [SB(f"csbq{i}", [128, 512], F32) for i in range(2)]
    expq = [SB(f"expq{i}", [128, 512], BF16) for i in range(2)]
    xdt = SB("xdt", [128, 2048], BF16)
    xdwg = SB("xdwg", [128, 512], BF16)
    bmtok = SB("bmtok", [128, 128], BF16)
    gmask = SB("gmask", [128, 128], BF16)
    sseg = [SB(f"sseg{i}", [128, 128], F32) for i in range(2)]
    eseg = [SB(f"eseg{i}", [128, 128], BF16) for i in range(2)]
    mtb = [SB(f"mtb{i}", [128, 128], BF16) for i in range(2)]
    cmsb = [SB(f"cmsb{i}", [128, 128], BF16) for i in range(2)]
    sall = SB("sall", [128, 2048], BF16)
    stmp = SB("stmp", [128, 512], BF16)
    dxsb = SB("dxsb", [128, 512], BF16)
    yfmsc = SB("yfmsc", [128, 16 * L], BF16)
    sqb = SB("sqb", [128, 2 * L], F32)
    rsr1 = SB("rsr1", [1, L], F32)
    rsr2 = SB("rsr2", [1, L], F32)
    rsrow = SB("rsrow", [1, L], F32)
    rsb = SB("rsb", [128, L], F32)
    ytok = SB("ytok", [128, D], F32)
    yfm16 = SB("yfm16", [128, 1024], BF16)
    s1b = SB("s1b", [128, 2 * HID], BF16)
    summT = SB("summT", [64, 128], BF16)
    summtk = SB("summtk", [128, SD], F32)

    v_memset(sall[:], 0.0, ("S0", "S1", "S2", "S3"))
    v_memset(xbcr[:], 0.0, tuple(f"tail{j}" for j in range(18)))

    GAMMA_TILES = [("fc", k) for k in range(8)] + [("pr", k) for k in range(32)]
    gamma_i = [0]
    gacc_par = [0, 0]

    def emit_gamma_tile():
        if gamma_i[0] >= len(GAMMA_TILES):
            return
        which, k = GAMMA_TILES[gamma_i[0]]
        gamma_i[0] += 1
        npiece = 4 if which == "fc" else 1
        srcw = wfcT_h if which == "fc" else wprT_h
        ci3 = 0 if which == "fc" else 1
        for pc in range(npiece):
            sch.dma("sync", wtmpG[:],
                    srcw[k * 128:(k + 1) * 128, pc * 1024:(pc + 1) * 1024], (), ("wtmpG",),
                    chan="g", nslots=2)
            v_red(statc[:, 30:31], wtmpG[:], ALU.add, ("wtmpG",), ("gred",), absval=True)
            par = gacc_par[ci3]
            v_tt(gacc[:, ci3 + 2 * (1 - par): ci3 + 2 * (1 - par) + 1],
                 gacc[:, ci3 + 2 * par: ci3 + 2 * par + 1], statc[:, 30:31], ALU.add,
                 ("gacc", "gred"), ("gacc",), tag="gacc")
            gacc_par[ci3] = 1 - par

    def ln_tile(ti, xkey, xview, outview, outkey, stat, skeys):
        k0 = skeys
        sch.op("vector", lambda e: e.bn_stats(stat[:, 0:6], xview[:, 0:512]), (xkey,), (k0 + "a",))
        sch.op("vector", lambda e: e.bn_stats(stat[:, 6:12], xview[:, 512:1024]), (xkey,), (k0 + "a",))
        sch.op("vector", lambda e: e.bn_aggr(stat[:, 12:14], stat[:, 0:12]), (k0 + "a",), (k0 + "b",))
        mean = stat[:, 12:13]
        var = stat[:, 13:14]
        veps = stat[:, 14:15]
        inv = stat[:, 15:16]
        v2 = stat[:, 16:17]
        r2 = stat[:, 17:18]
        stot = stat[:, 18:19]
        lnscr = stat[:, 19:20]
        v_ts(veps, var, cst(C_EPS), None, ALU.add, ALU.bypass, (k0 + "b", "consts"), (k0 + "c",))
        rsqrt2(inv, veps, lnscr, 128, (k0 + "c",), (k0 + "d",))
        v_ts(v2, var, inv, inv, ALU.mult, ALU.mult, (k0 + "b", k0 + "d"), (k0 + "e",))
        v_ts(r2, v2, cst(C_EPS), None, ALU.add, ALU.bypass, (k0 + "e", "consts"), (k0 + "f",))
        rsqrt2(v2, r2, lnscr, 128, (k0 + "f",), (k0 + "g",))
        v_ts(stot, inv, v2, None, ALU.mult, ALU.bypass, (k0 + "d", k0 + "g"), (k0 + "h",))
        v_ts(outview, xview, mean, stot, ALU.subtract, ALU.mult,
             (xkey, k0 + "b", k0 + "h"), (outkey,))

    sch.barrier()

    for sc_i in range(min(NSC, NSC_LIM)):
        t0 = sc_i * SC
        for ti in range(2):
            xk = "xtokA"
            sch.dma("sync", xtok[:],
                    x_h[t0 + ti * 128: t0 + (ti + 1) * 128, :], (), (xk,), chan="x")
            ln_tile(ti, xk, xtok[:], xntok[:], "xn1", statc, "lnA")
            for k in range(8):
                t_tr(PT[0][0:128, 0:128], xntok[:, k * 128:(k + 1) * 128],
                     128, ("xn1",), (KPT[0],))
                v_copy(xnf[:, k * 256 + ti * 128: k * 256 + (ti + 1) * 128],
                       PT[0][0:128, 0:128], (KPT[0],), (f"xnf{k}",))
        if KDBG and sc_i == 0:
            sch.dma("sync", dbg_h["xnf"], xnf[:], tuple(f"xnf{k}" for k in range(8)), (), chan="oo")
            sch.dma("sync", dbg_h["identd"], ident[:], ("ident",), (), chan="oo")
            sch.dma("sync", dbg_h["identfd"], identf[:], ("identf",), (), chan="oo")
            sch.dma("sync", dbg_h["constsd"], consts[:], ("consts",), (), chan="oo")
        for m in range(34):
            pb, pkey = (PB[m % 2], KPB[m % 2])
            for k in range(8):
                t_mm(pb[:, 0:SC], cw1[:, k * DIP + m * 128: k * DIP + (m + 1) * 128],
                     xnf[:, k * 256:(k + 1) * 256], k == 0, k == 7,
                     (f"cw1_{k}", f"xnf{k}"), (pkey,))
            if m < 16:
                s_act(sz[:, m * SC:(m + 1) * SC], pb[:, 0:SC], ACT.Silu, (pkey,), (f"sz{m}",))
            else:
                j = m - 16
                s_copy(xbcr[:, j * (SC + 4) + 4: (j + 1) * (SC + 4)], pb[:, 0:SC],
                       (pkey, f"tail{j}"), (f"xbcr{j}",))
        pbdt = PB[2]
        for k in range(8):
            t_mm(pbdt[0:32, 0:SC], cw1[:, k * DIP + 34 * 128: k * DIP + 34 * 128 + 32],
                 xnf[:, k * 256:(k + 1) * 256], k == 0, k == 7,
                 (f"cw1_{k}", f"xnf{k}"), (KPB[2],))
        for j in range(18):
            base = j * (SC + 4)
            a0, a1 = cacc[:, 0:SC], cacc[:, SC:2 * SC]
            v_ts(a0, xbcr[:, base + 1: base + 1 + SC], cwcv[:, j * 4: j * 4 + 1], None,
                 ALU.mult, ALU.bypass, (f"xbcr{j}", f"tail{j}", "cwcv"), ("cacc0",))
            v_stt(a1, xbcr[:, base + 2: base + 2 + SC], cwcv[:, j * 4 + 1: j * 4 + 2], a0,
                  ALU.mult, ALU.add, (f"xbcr{j}", f"tail{j}", "cacc0", "cwcv"), ("cacc1",))
            v_stt(a0, xbcr[:, base + 3: base + 3 + SC], cwcv[:, j * 4 + 2: j * 4 + 3], a1,
                  ALU.mult, ALU.add, (f"xbcr{j}", f"tail{j}", "cacc1", "cwcv"), ("cacc0",))
            v_stt(a1, xbcr[:, base + 4: base + 4 + SC], cwcv[:, j * 4 + 3: j * 4 + 4], a0,
                  ALU.mult, ALU.add, (f"xbcr{j}", "cacc0", "cwcv"), ("cacc1",))
            s_act(xbcc[:, j * SC:(j + 1) * SC], a1, ACT.Silu, ("cacc1", "ccb"), (f"xbcc{j}",),
                  bias=ccb[:, j:j + 1])
            v_copy(xbcr[:, base + 1: base + 4], xbcr[:, base + SC + 1: base + SC + 4],
                   (f"xbcr{j}",), (f"tail{j}",))
        s_act(dte[:], pbdt[0:32, 0:SC], ACT.Exp, (KPB[2], "dtbcol"), ("dte",), bias=dtbcol[:])
        s_act(dtv[:], dte[:], ACT.Ln, ("dte", "onecol32"), ("dtv",), bias=onecol32[:])
        v_ts(dav[:], dtv[:], acol_ap[:], None, ALU.mult, ALU.bypass, ("dtv", "acol"), ("dav",))

        emit_gamma_tile()
        emit_gamma_tile()
        emit_gamma_tile()

        for ci in range(2):
            ch = sc_i * 2 + ci
            cs_ = csv[:, ci * L:(ci + 1) * L]
            v_scan(cs_, ones32[:], dav[:, ci * L:(ci + 1) * L], 0.0, ALU.mult, ALU.add,
                   ("dav", "ones32"), (f"cs{ci}",))
            nds_ = ndsv[:]
            v_ts(nds_, cs_, cs_[:, L - 1:L], None, ALU.subtract, ALU.bypass,
                 (f"cs{ci}",), ("ndsv",))
            s_act(dsv[:], nds_, ACT.Exp, ("ndsv", "consts"),
                  ("dsv",), scale=cst(C_NEG1, 32))
            s_act(expcs32[:], cs_, ACT.Exp, (f"cs{ci}",), ("ecsv",))
            v_copy(expcs16[:], expcs32[:], ("ecsv",), ("ecs16",))
            for qf in range(4):
                qb2 = qf % 2
                sch.dma("gpsimd", csflatq[qb2][0:1, :].rearrange("o (h l) -> o h l", h=8),
                        cs_[qf * 8:(qf + 1) * 8, :], (f"cs{ci}",), (f"csflat{qb2}",),
                        chan="f", tag="csflat")
                sch.dma("gpsimd", expflatq[qb2][0:1, :].rearrange("o (h l) -> o h l", h=8),
                        expcs16[qf * 8:(qf + 1) * 8, :],
                        ("ecs16",), (f"expflat{qb2}",), chan="f", tag="expflat")
                s_act(lamrow[0:1, qf * 8:(qf + 1) * 8], csflatq[qb2][0:1, L - 1::L],
                      ACT.Exp, (f"csflat{qb2}",), ("lamrow",))
            bcast(lamb[:], lamrow[:], 32, ("lamrow",), ("lamb",))
            t_trf(PB[5][0:128, 0:32], cs_, 32, (f"cs{ci}",), (KPB[5],))
            v_copy(csT[:], PB[5][0:128, 0:32], (KPB[5],), ("csT",))
            t_trf(PB[5][0:128, 32:64], dtv[:, ci * L:(ci + 1) * L], 32, ("dtv",), (KPB[5],))
            v_copy(dtT16[:], PB[5][0:128, 32:64], (KPB[5],), ("dtT16",))
            t_trf(PB[5][0:128, 64:96], dsv[:], 32, ("dsv",), (KPB[5],))
            v_copy(dsT16[:], PB[5][0:128, 64:96], (KPB[5],), ("dsT16",))

            for j in range(16):
                t_tr(PT[1][0:128, 0:128], xbcc[:, j * SC + ci * L: j * SC + (ci + 1) * L],
                     128, (f"xbcc{j}",), (KPT[1],))
                v_tt(xdt[:, j * 128:(j + 1) * 128].rearrange("p (h q) -> p h q", h=2),
                     PT[1][0:128, 0:128].rearrange("p (h q) -> p h q", h=2),
                     dtT16[:, 2 * j:2 * j + 2].unsqueeze(2).broadcast_to((128, 2, 64)),
                     ALU.mult, (KPT[1], "dtT16"), ("xdt",))
            t_tr(PT[1][0:128, 0:128], xbcc[:, 16 * SC + ci * L: 16 * SC + (ci + 1) * L],
                 128, ("xbcc16",), (KPT[1],))
            v_copy(bmtok[:], PT[1][0:128, 0:128], (KPT[1],), ("bmtok",))

            if KDBG and sc_i == 0 and ci == 0:
                sch.dma("sync", dbg_h["xdt2"], xdt[:], ("xdt",), (), chan="oo")
                sch.dma("sync", dbg_h["dtT16"], dtT16[:], ("dtT16",), (), chan="oo")
            t_mm(PB[2][:, 0:128], xbcc[:, 16 * SC + ci * L: 16 * SC + (ci + 1) * L],
                 xbcc[:, 17 * SC + ci * L: 17 * SC + (ci + 1) * L], True, True,
                 ("xbcc16", "xbcc17"), (KPB[2],))
            v_tt(gmask[:], PB[2][:, 0:128], masksl[:], ALU.mult, (KPB[2], "masksl"), ("gmask",))

            for h in range(H):
                q, hb, hh = h // 8, h % 2, h % 8
                qb = (h // 4) % 2
                if hh == 0:
                    qb2 = q % 2
                    sch.dma("gpsimd", csflatq[qb2][0:1, :].rearrange("o (h2 l) -> o h2 l", h2=8),
                            cs_[q * 8:(q + 1) * 8, :], (f"cs{ci}",), (f"csflat{qb2}",),
                            chan="f", tag="csflat")
                    sch.dma("gpsimd", expflatq[qb2][0:1, :].rearrange("o (h2 l) -> o h2 l", h2=8),
                            expcs16[q * 8:(q + 1) * 8, :], ("ecs16",), (f"expflat{qb2}",),
                            chan="f", tag="expflat")
                if h % 4 == 0:
                    half4 = (h % 8) // 4
                    sch.dma("gpsimd", csbq[qb][:],
                            csflatq[q % 2][0:1, half4 * 512:(half4 + 1) * 512].unsqueeze(1).broadcast_to((1, 128, 512)),
                            (f"csflat{q % 2}",), (f"csbq{qb}",), chan="b")
                    sch.dma("gpsimd", expq[qb][:],
                            expflatq[q % 2][0:1, half4 * 512:(half4 + 1) * 512].unsqueeze(1).broadcast_to((1, 128, 512)),
                            (f"expflat{q % 2}",), (f"expq{qb}",), chan="b")
                v_ts(sseg[hb][:], csbq[qb][:, (h % 4) * 128:(h % 4 + 1) * 128], csT[:, h:h + 1],
                     cst(C_ZERO), ALU.subtract, ALU.min,
                     (f"csbq{qb}", "csT", "consts"), (f"sseg{hb}",))
                s_act(eseg[hb][:], sseg[hb][:], ACT.Exp, (f"sseg{hb}",), (f"eseg{hb}",))
                v_tt(mtb[hb][:], eseg[hb][:], gmask[:], ALU.mult,
                     (f"eseg{hb}", "gmask"), (f"mtb{hb}",))
                if KDBG and sc_i == 0 and ci == 0 and h == 2:
                    sch.dma("sync", dbg_h["mtbh2"], mtb[hb][:], (f"mtb{hb}",), (), chan="oo")
                    sch.dma("sync", dbg_h["ssegh2"], sseg[hb][:], (f"sseg{hb}",), (), chan="oo")
                    sch.dma("sync", dbg_h["csbh2"], csbq[qb][:], (f"csbq{qb}",), (), chan="oo")
                    sch.dma("sync", dbg_h["csTd"], csT[:], ("csT",), (), chan="oo")
                v_tt(cmsb[hb][:], xbcc[:, 17 * SC + ci * L: 17 * SC + (ci + 1) * L],
                     expq[qb][:, (h % 4) * 128:(h % 4 + 1) * 128], ALU.mult,
                     ("xbcc17", f"expq{qb}"), (f"cmsb{hb}",))
                bank = PB[3 + (q % 2)]
                bkey = KPB[3 + (q % 2)]
                pp, ff = (h % 2) * 64, ((h // 2) % 4) * 128
                t_mm(bank[pp:pp + 64, ff:ff + 128], xdt[:, h * P:(h + 1) * P], mtb[hb][:],
                     True, ch == 0, ("xdt", f"mtb{hb}"), (bkey,), tag="ydiag",
                     tile_position=(0, pp))
                if ch > 0:
                    t_mm(bank[pp:pp + 64, ff:ff + 128], sall[:, h * P:(h + 1) * P], cmsb[hb][:],
                         False, True, (f"S{q}", f"cmsb{hb}"), (bkey,), tag="yoff",
                         tile_position=(0, pp))
                if hh == 7:
                    g = q
                    for jj in range(4):
                        j = g * 4 + jj
                        v_ts(dxsb[:, jj * 128:(jj + 1) * 128],
                             xbcc[:, j * SC + ci * L: j * SC + (ci + 1) * L],
                             dfm[:, j:j + 1], None, ALU.mult, ALU.bypass,
                             (f"xbcc{j}", "dfm"), ("dxsb",))
                    v_tt(yfmsc[:, g * 512:(g + 1) * 512],
                         bank[:, 0:512], dxsb[:], ALU.add,
                         (bkey, "dxsb"), (f"yfm{g}",), tag="yevac")
                    v_tt(xdwg[:].rearrange("p (hq q) -> p hq q", hq=8),
                         xdt[:, g * 512:(g + 1) * 512].rearrange("p (hq q) -> p hq q", hq=8),
                         dsT16[:, g * 8:(g + 1) * 8].unsqueeze(2).broadcast_to((128, 8, 64)),
                         ALU.mult, ("xdt", "dsT16"), ("xdwg",))
                    t_mm(PB[5][:, 0:512], bmtok[:], xdwg[:], True, True,
                         ("bmtok", "xdwg"), (KPB[5],), tag="states")
                    v_tt(stmp[:].rearrange("p (hq q) -> p hq q", hq=8),
                         sall[:, g * 512:(g + 1) * 512].rearrange("p (hq q) -> p hq q", hq=8),
                         lamb[:, g * 8:(g + 1) * 8].unsqueeze(2).broadcast_to((128, 8, 64)),
                         ALU.mult, (f"S{g}", "lamb"), ("stmp",), tag="supd")
                    v_tt(sall[:, g * 512:(g + 1) * 512], stmp[:], PB[5][:, 0:512], ALU.add,
                         ("stmp", KPB[5]), (f"S{g}",), tag="supd2")

            yzb = xdt
            yop = xnf
            yview = yfmsc[:].rearrange("p (j l) -> p j l", l=L)
            zview = sz[:].rearrange("p (j l) -> p j l", l=SC)[:, :, ci * L:(ci + 1) * L]
            yzview = yzb[:].rearrange("p (j l) -> p j l", l=L)
            ykeys = tuple(f"yfm{g}" for g in range(4))
            zkeys = tuple(f"sz{m}" for m in range(16))
            v_tt(yzview, yview, zview, ALU.mult, ykeys + zkeys, ("yz", "xdt"), tag="yz")
            for j in range(16):
                sq = sqb[:, (j % 2) * L:(j % 2 + 1) * L]
                v_tt(sq, yzb[:, j * L:(j + 1) * L], yzb[:, j * L:(j + 1) * L], ALU.mult,
                     ("yz", "xdt"), (f"sq{j % 2}",), tag="sq")
                t_mm(PB[2][0:1, 128:256], onescol[:], sq, j == 0, j == 15,
                     (f"sq{j % 2}", "onescol"), (KPB[2],), tag="sqmm")
            v_ts(rsr1[:], PB[2][0:1, 128:256], cst(C_I2048, 1), cst(C_EPS, 1),
                 ALU.mult, ALU.add, (KPB[2], "consts"), ("rsr1",))
            s_act(rsr2[:], rsr1[:], ACT.Ln, ("rsr1",), ("rsr2",))
            s_act(rsrow[:], rsr2[:], ACT.Exp, ("rsr2", "consts"), ("rsrow",),
                  scale=cst(C_NEGH, 1))
            bcast(rsb[:], rsrow[:], 128, ("rsrow",), ("rsb",))
            v_tt(yop[:].rearrange("p (j l) -> p j l", l=128), yzview,
                 rsb[:].unsqueeze(1).broadcast_to((128, 16, 128)), ALU.mult,
                 ("yz", "xdt", "rsb"), ("yop",) + tuple(f"xnf{k}" for k in range(8)), tag="yop")
            if KDBG and sc_i == 0 and ci == 0:
                sch.dma("sync", dbg_h["statc"], statc[:, 0:20], ("lnAb", "lnAh"), (), chan="oo")
                for nm, ap_, keys in [("xntok", xntok[:], ("xn1",)),
                                      ("sz", sz[:], tuple(f"sz{m}" for m in range(16))),
                                      ("xbcc", xbcc[:], tuple(f"xbcc{j}" for j in range(18))),
                                      ("dtv", dtv[:], ("dtv",)), ("csv", csv[:, 0:L], ("cs0",)), 
                                      ("xdt", xdt[:], ("yz", "xdt")), ("yfmsc", yfmsc[:], tuple(f"yfm{g}" for g in range(4))),
                                      ("yop", yop[:], ("yop",)), ("dsv", dsv[:], ("dsv",)),
                                      ("expcs32", expcs32[:], ("ecsv",)), ("sall", sall[:], ("S0", "S1", "S2", "S3")),
                                      ("gmask", gmask[:], ("gmask",)), ("mtb0", mtb[1][:], ("mtb1",))]:
                    sch.dma("sync", dbg_h[nm], ap_, keys, (), chan="oo")
            for half in range(2):
                pbo, pkey = PB[half], KPB[half]
                for j in range(16):
                    t_mm(pbo[:, 0:512], yop[:, j * 128:(j + 1) * 128],
                         cwo[:, j * 1024 + half * 512: j * 1024 + (half + 1) * 512],
                         j == 0, j == 15,
                         ("yop", f"xnf{j // 2}", f"cwo_{j}"), (pkey,), tag="outproj")
                v_copy(ytok[:, half * 512:(half + 1) * 512],
                       pbo[:, 0:512], (pkey,), ("ytokk",))
            sch.dma("sync", y_spill[ch, :, :], ytok[:],
                    ("ytokk",), (), chan="ys")
            for k in range(8):
                t_trf(PB[5][0:128, 96:224], ytok[:, k * 128:(k + 1) * 128],
                      128, ("ytokk",), (KPB[5],))
                v_copy(yfm16[:, k * 128:(k + 1) * 128],
                       PB[5][0:128, 96:224], (KPB[5],), ("yfm16k",))
            sch.dma("sync", yfm_spill[ch, :, :], yfm16[:],
                    ("yfm16k",), (), chan="ys")
            for m2 in range(2):
                for k in range(8):
                    t_mm(PB[2][:, 256:384], csc1[:, k * HID + m2 * 128: k * HID + (m2 + 1) * 128],
                         yfm16[:, k * 128:(k + 1) * 128],
                         k == 0, k == 7, (f"csc1_{k}", "yfm16k"), (KPB[2],), tag="sc1")
                v_ts(s1b[:, m2 * 128:(m2 + 1) * 128], PB[2][:, 256:384], 0.0, None,
                     ALU.max, ALU.bypass, (KPB[2],), (f"s1b{m2}",), tag="relu")
            for k2 in range(2):
                t_mm(PB[2][0:1, 384:512], csc2[:, k2:k2 + 1], s1b[:, k2 * 128:(k2 + 1) * 128],
                     k2 == 0, k2 == 1, (f"csc2_{k2}", "s1b0", "s1b1"), (KPB[2],), tag="sc2")
            s_act(rsr1[:], PB[2][0:1, 384:512], ACT.Exp, (KPB[2], "consts", "rsr1"), ("rsr1",),
                  scale=cst(C_NEG1, 1))
            v_ts(rsr2[:], rsr1[:], cst(C_ONE, 1), None, ALU.add, ALU.bypass,
                 ("rsr1", "consts", "rsr2"), ("rsr2",))
            v_recip(scrow[:], rsr2[:], ("rsr2",), ("scrow",))
            sch.dma("sync", sm_in.ap()[T + 2 * ch: T + 2 * ch + 2, :],
                    scrow[:].rearrange("o (a b) -> o a b", a=2),
                    ("scrow",), ("sc_in",), chan="ys")
            for k in range(8):
                t_mm(PB[5][0:64, 224:352], csum[:, k * SD:(k + 1) * SD],
                     yfm16[:, k * 128:(k + 1) * 128],
                     k == 0, k == 7, (f"csum_{k}", "yfm16k"), (KPB[5],), tag="summ")
            v_copy(summT[:], PB[5][0:64, 224:352], (KPB[5],), ("summT",))
            t_tr(PT[1][0:128, 128:192], summT[:], 64, ("summT",), (KPT[1],))
            v_copy(summtk[:], PT[1][0:128, 128:192], (KPT[1],), ("summtk",))
            sch.dma("sync", sm_in.ap()[ch * 128:(ch + 1) * 128, :], summtk[:],
                    ("summtk",), (), chan="ys")

    if KDBG2:
        d3 = {}
        for nm, shp, dt_ in [("csc2", [128, 2], BF16), ("csum", [128, 8 * SD], BF16),
                             ("s1b", [128, 2 * HID], BF16), ("yfm16", [128, 1024], BF16),
                             ("summT", [64, 128], BF16), ("scrow", [1, 128], F32),
                             ("summtk", [128, SD], F32)]:
            d3[nm] = nc.dram_tensor("dbg3_" + nm, shp, dt_, kind="ExternalOutput").ap()
        sch.dma("sync", d3["csc2"], csc2[:], ("csc2_0", "csc2_1"), (), chan="oo")
        sch.dma("sync", d3["csum"], csum[:], tuple(f"csum_{k}" for k in range(8)), (), chan="oo")
        sch.dma("sync", d3["s1b"], s1b[:], ("s1b0", "s1b1"), (), chan="oo")
        sch.dma("sync", d3["yfm16"], yfm16[:], ("yfm16k",), (), chan="oo")
        sch.dma("sync", d3["summT"], summT[:], ("summT",), (), chan="oo")
        sch.dma("sync", d3["scrow"], scrow[:], ("scrow",), (), chan="oo")
        sch.dma("sync", d3["summtk"], summtk[:], ("summtk",), (), chan="oo")
        d3y = nc.dram_tensor("dbg3_ysp", [4 * 128, D], F32, kind="ExternalOutput").ap()
        sch.dma("sync", d3y, y_spill[0:4, :, :].rearrange("a b c -> (a b) c"), (), (), chan="oo")
        d3yf = nc.dram_tensor("dbg3_yfsp", [2 * 128, 1024], BF16, kind="ExternalOutput").ap()
        sch.dma("sync", d3yf, yfm_spill[0:2, :, :].rearrange("a b c -> (a b) c"), (), (), chan="oo")
    sch.oplimit = None
    while gamma_i[0] < len(GAMMA_TILES):
        emit_gamma_tile()
    for ci3 in range(2):
        last_col = ci3 + 2 * gacc_par[ci3]
        t_mm(PB[2][0:1, ci3:ci3 + 1], gacc[:, last_col:last_col + 1], onescol[:],
             True, True, ("gacc", "onescol"), (KPB[2],), tag="gfin")
    v_ts(grow_sb[0:1, 0:2], PB[2][0:1, 0:2], cst(C_IFC, 1), cst(C_1E6M, 1),
         ALU.mult, ALU.add, (KPB[2], "consts"), ("grow",))
    v_ts(grow_sb[0:1, 2:4], grow_sb[0:1, 0:2], cst(C_HALF, 1), None, ALU.mult, ALU.bypass,
         ("grow", "consts"), ("ghrow",))
    bcast(ghalf[:], grow_sb[0:1, 2:4], 2, ("ghrow",), ("ghalf",))
    v_ts(gnegh[:], ghalf[:], cst(C_NEG1), None, ALU.mult, ALU.bypass,
         ("ghalf", "consts"), ("gnegh",))

    sch.barrier()
    stackA.close()

    if KPHASE <= 1:
        dummy = nc.alloc_sbuf_tensor("dummy1", [128, D], F32)
        sch.op("vector", lambda e: e.memset(dummy[:], 0.0), (), ("dummyk",), "dm")
        for chd in range(NCH):
            sch.dma("sync", out_h[chd * 128:(chd + 1) * 128, :], dummy[:], ("dummyk",), (), chan="oo")
        sch.finalize()
        with contextlib.ExitStack() as es:
            sems = {}
            for e in ENGS:
                sems[e] = es.enter_context(nc.semaphore(f"sem_{e}"))
            dma_sems = [es.enter_context(nc.semaphore(f"dsem{i}")) for i in range(sch.n_dma_slots)]
            block = es.enter_context(nc.Block())
            stats = sch.emit(block, sems, dma_sems)
        return nc, stats

    if KSELCUT < 99:
        sch.oplimit = KSELCUT
    if os.environ.get("KVFIRST", "0") == "1":
        pass
    stackS = contextlib.ExitStack()
    SBs = lambda name, shape, dt: stackS.enter_context(nc.sbuf_tensor(name, shape, dt))
    ssc = SBs("ssc", [128, 256], F32)
    sscraw = SBs("sscraw", [128, 256], F32)
    fhib = SBs("fhib", [128, 8], F32)
    flob = SBs("flob", [128, 8], F32)
    vcmp = SBs("vcmp", [128, 256], F32)
    redc = SBs("redc", [128, 2], F32)
    scal2 = SBs("scal2", [1, 8], F32)
    mx8 = SBs("mx8", [128, 8], F32)
    cnd8 = SBs("cnd8", [8, 128], F32)
    cand = SBs("cand", [1, 1024], F32)
    candB = SBs("candB", [1, 1024], F32)
    th8b = SBs("th8b", [128, 8], F32)
    mi8 = SBs("mi8", [128, 8], U32)
    mif = SBs("mif", [128, 8], F32)
    gpos = SBs("gpos", [128, 8], F32)
    gpos2 = SBs("gpos2", [128, 8], F32)
    gtr = SBs("gtr", [8, 128], F32)
    gall = SBs("gall", [8, 8], F32)
    gcolf = SBs("gcolf", [NSEL, 1], F32)
    gcolu = SBs("gcolu", [NSEL, 1], U32)
    gcolf2 = SBs("gcolf2", [NSEL, 1], F32)
    radj = SBs("radj", [NSEL, 1], F32)
    radj2 = SBs("radj2", [NSEL, 1], F32)
    rc1 = SBs("rc1", [NSEL, 1], F32)
    poolraw = SBs("poolraw", [NSEL, SD], F32)
    poolfm = SBs("poolfm", [64, NSEL], BF16)
    vmrow = SBs("vmrow", [1, NPOOL], F32)
    mrow = SBs("mrow", [1, NPOOL], F32)
    brow = SBs("brow", [1, NPOOL], F32)
    ckw = SBs("ckw", [64, 64], BF16)
    cvw = SBs("cvw", [64, 1024], BF16)
    wk_f = SBs("wk_f", [64, 64], F32)
    wv_f = SBs("wv_f", [64, 1024], F32)

    ccsm = sch.dma_op("gpsimd", lambda e: e.collective_compute(
        "AllGather", ALU.bypass, replica_groups=[list(range(NCORES))],
        ins=[sm_in.ap().opt()], outs=[sm_out.ap().opt()]),
        ("sc_in",), ("sm_out_cc",), chan="cc", nslots=1, tag="ccsm", amt=1)
    for s in sch.chan_slots.get("ys", []):
        if sch.slot_last[s] is not None and sch.slot_last[s] is not ccsm:
            ccsm.deps.append((sch.slot_last[s], False))

    for rr in range(8):
        sch.dma("sync", sscraw[rr * 16:(rr + 1) * 16, :],
                sm_out.ap()[rr * (T + 64) + T: (rr + 1) * (T + 64), :]
                .rearrange("a b -> (a b)").rearrange("(p f) -> p f", p=16),
                ("sm_out_cc",), ("sscraw",), chan="e")
    for hblk in range(2):
        t_trf(PB[3][0:128, 0:128], sscraw[:, hblk * 128:(hblk + 1) * 128], 128,
              ("sscraw",), (KPB[3],))
        v_copy(ssc[:, hblk * 128:(hblk + 1) * 128], PB[3][0:128, 0:128], (KPB[3],), ("ssc",))
    v_ts(vcmp[:], ssc[:], cst(C_HALF), None, ALU.is_gt, ALU.bypass,
         ("ssc", "consts"), ("vcmp",))
    v_red(redc[:, 0:1], vcmp[:], ALU.add, ("vcmp",), ("redc",))
    v_red(redc[:, 1:2], ssc[:], ALU.add, ("ssc",), ("redc",))
    t_mm(PB[2][0:1, 0:1], redc[:, 0:1], onescol[:], True, True,
         ("redc", "onescol"), (KPB[2],), tag="cnt")
    t_mm(PB[2][0:1, 1:2], redc[:, 1:2], onescol[:], True, True,
         ("redc", "onescol"), (KPB[2],), tag="sum")
    v_copy(scal2[0:1, 0:2], PB[2][0:1, 0:2], (KPB[2],), ("scal2",))
    v_ts(scal2[0:1, 2:3], scal2[0:1, 1:2], cst(C_TAUS, 1), None, ALU.is_gt, ALU.bypass,
         ("scal2", "consts"), ("sc3",))
    v_ts(scal2[0:1, 3:4], scal2[0:1, 0:1], cst(C_ZERO, 1), None, ALU.is_gt, ALU.bypass,
         ("scal2", "consts"), ("sc4",))
    v_tt(scal2[0:1, 4:5], scal2[0:1, 2:3], scal2[0:1, 3:4], ALU.mult,
         ("sc3", "sc4"), ("cond1",))
    bcast(condcol[:], scal2[0:1, 4:5], 1, ("cond1",), ("condcol",))
    v_max(mx8[:], ssc[:], ("ssc",), ("mx8",))
    t_trf(PB[2][0:8, 128:256], mx8[:], 128, ("mx8",), (KPB[2],))
    v_copy(cnd8[:], PB[2][0:8, 128:256], (KPB[2],), ("cnd8",))
    sch.dma("gpsimd", cand[0:1, :].rearrange("o (j p) -> o j p", j=8), cnd8[:],
            ("cnd8",), ("cand",), chan="ec", nslots=1)
    for r in range(7):
        src, skey = (cand, "cand") if r % 2 == 0 else (candB, "candB")
        dst, dkey = (candB, "candB") if r % 2 == 0 else (cand, "cand")
        v_max(thetas[0:1, r * 8:(r + 1) * 8], src[0:1, :], (skey,), ("thetas",))
        v_matchrep(dst[0:1, :], thetas[0:1, r * 8:(r + 1) * 8], src[0:1, :], -1e30,
                   ("thetas", skey), (dkey,))
    for r in range(7):
        bcast(th8b[:], thetas[0:1, r * 8:(r + 1) * 8], 8, ("thetas",), ("th8b",))
        v_maxidx(mi8[:], th8b[:], ssc[:], ("th8b", "ssc"), ("mi8",))
        v_copy(mif[:], mi8[:], ("mi8",), ("mif",))
        v_ts(fhib[:], mif[:], cst(C_128), None, ALU.is_ge, ALU.bypass,
             ("mif", "consts"), ("fhib",))
        v_stt(flob[:], fhib[:], cst(C_N128), mif[:], ALU.mult, ALU.add,
              ("fhib", "mif", "consts"), ("flob",))
        v_ts(gpos2[:], flob[:], cst(C_256), piota_f[:], ALU.mult, ALU.add,
             ("flob", "consts", "piota_f"), ("gpos1",))
        v_stt(gpos[:], fhib[:], cst(C_128), gpos2[:], ALU.mult, ALU.add,
              ("gpos1", "fhib", "consts"), ("gpos",))
        t_trf(PB[2][0:8, 256:384], gpos[:], 128, ("gpos",), (KPB[2],))
        v_copy(gtr[:], PB[2][0:8, 256:384], (KPB[2],), ("gtr",))
        v_red(gall[:, r:r + 1], gtr[:], ALU.min, ("gtr",), ("gall",))
    for r in range(7):
        sch.dma("gpsimd", gcolf[r * 8:(r + 1) * 8, 0:1], gall[0:8, r:r + 1],
                ("gall",), ("gcolf",), chan=f"eg{r}", nslots=1)
    v_memset(radj[:], 0.0, ("radj",))
    for kk7 in range(1, 8):
        v_ts(rc1[:], gcolf[:], consts[0:NSEL, C_T1 + kk7 - 1: C_T1 + kk7], None,
             ALU.is_ge, ALU.bypass, ("gcolf", "consts"), ("rc1",))
        v_tt(radj2[:], radj[:], rc1[:], ALU.add, ("radj", "rc1"), ("radj",), tag="radd")
        radj, radj2 = radj2, radj
    v_stt(gcolf2[:], radj[:], cst(C_64, NSEL), gcolf[:], ALU.mult, ALU.add,
          ("radj", "gcolf", "consts"), ("gcolf2",))
    v_copy(gcolu[:], gcolf2[:], ("gcolf2",), ("gcolu",))
    sch.dma_op("gpsimd", lambda e: e.indirect_dma_start(
        out=poolraw[:], out_offset=None, in_=sm_out.ap(),
        in_offset=bass.IndirectOffsetOnAxis(ap=gcolu[:], axis=0)),
        ("gcolu", "sm_out_cc"), ("poolraw",), chan="ig", tag="gather")
    t_trf(PB[3][0:64, 128:128 + NSEL], poolraw[:], 56, ("poolraw",), (KPB[3],))
    v_copy(poolfm[:], PB[3][0:64, 128:128 + NSEL], (KPB[3],), ("poolfm",))
    v_ts(vmrow[:], thetas[0:1, 0:NPOOL], cst(C_HALF, 1), None, ALU.is_gt, ALU.bypass,
         ("thetas", "consts"), ("vmrow",))
    v_ts(mrow[:], vmrow[:], cst(C_QUART, 1), None, ALU.mult, ALU.bypass,
         ("vmrow", "consts"), ("mrow",))
    v_ts(brow[:], vmrow[:], cst(C_1E9, 1), cst(C_N1E9, 1), ALU.mult, ALU.add,
         ("vmrow", "consts"), ("brow",))
    bcast(mb128[:], mrow[:], NPOOL, ("mrow",), ("mb128",))
    bcast(bb128[:], brow[:], NPOOL, ("brow",), ("bb128",))
    sch.dma("sync", wk_f[:], wk_h, (), ("wk_f",), chan="e")
    sch.dma("sync", wv_f[:], wv_h, (), ("wv_f",), chan="e")
    v_copy(ckw[:], wk_f[:], ("wk_f",), ("ckw",))
    v_copy(cvw[:], wv_f[:], ("wv_f",), ("cvw",))
    t_mm(PB[3][0:64, 0:NSEL], ckw[:], poolfm[:], True, True, ("ckw", "poolfm"), (KPB[3],))
    v_copy(kkT[:], PB[3][0:64, 0:NSEL], (KPB[3],), ("kkT",))
    for half in range(2):
        t_mm(PB[4][0:NSEL, 0:512], poolfm[:], cvw[:, half * 512:(half + 1) * 512],
             True, True, ("poolfm", "cvw"), (KPB[4],))
        v_copy(vvtok[:, half * 512:(half + 1) * 512], PB[4][0:NSEL, 0:512],
               (KPB[4],), ("vvtok",))

    sch.oplimit = None
    if KDBG2:
        d2 = {}
        smin_dbg = nc.dram_tensor("dbg2_smin", [64, 64], F32, kind="ExternalOutput").ap()
        smsum_dbg = nc.dram_tensor("dbg2_smsum", [128, 64], F32, kind="ExternalOutput").ap()
        smout_dbg = nc.dram_tensor("dbg2_smout", [64, 64], F32, kind="ExternalOutput").ap()
        sscraw_dbg = nc.dram_tensor("dbg2_sscraw", [128, 256], F32, kind="ExternalOutput").ap()
        sch.dma("sync", smin_dbg, sm_in.ap()[T:T + 64, :], ("sc_in",), (), chan="oo")
        sch.dma("sync", smsum_dbg, sm_in.ap()[0:128, :], (), (), chan="oo")
        dbgcp = sch.dma("sync", smout_dbg, sm_out.ap()[T:T + 64, :], ("sm_out_cc",), (), chan="oo")
        sch.dma("sync", sscraw_dbg, sscraw[:], ("sscraw",), (), chan="oo")
        for nm, shp, dt_ in [("ssc", [128, 256], F32), ("cand", [1, 1024], F32),
                             ("thetas", [1, 64], F32), ("gcolf", [NSEL, 1], F32),
                             ("gcolf2", [NSEL, 1], F32), ("poolraw", [NSEL, SD], F32),
                             ("kkT", [64, NSEL], BF16), ("vvtok", [NSEL, D], BF16),
                             ("scal2", [1, 8], F32), ("mrow", [1, NPOOL], F32),
                             ("gall", [8, 8], F32)]:
            d2[nm] = nc.dram_tensor("dbg2_" + nm, shp, dt_, kind="ExternalOutput").ap()
        for nm, ap_, keys in [("ssc", ssc[:], ("ssc",)), ("cand", cand[0:1, :], ("cand",)),
                              ("thetas", thetas[:], ("thetas",)), ("gcolf", gcolf[:], ("gcolf",)),
                              ("gcolf2", gcolf2[:], ("gcolf2",)), ("poolraw", poolraw[:], ("poolraw",)),
                              ("kkT", kkT[:], ("kkT",)), ("vvtok", vvtok[:], ("vvtok",)),
                              ("scal2", scal2[:], ("scal2", "cond1", "sc3", "sc4")),
                              ("mrow", mrow[:], ("mrow",)), ("gall", gall[:], ("gall",))]:
            sch.dma("sync", d2[nm], ap_, keys, (), chan="oo")
    sch.barrier()
    stackS.close()

    if KPHASE <= 2:
        dummy = nc.alloc_sbuf_tensor("dummy2", [128, D], F32)
        sch.op("vector", lambda e: e.memset(dummy[:], 0.0), (), ("dummyk",), "dm")
        for chd in range(NCH):
            sch.dma("sync", out_h[chd * 128:(chd + 1) * 128, :], dummy[:], ("dummyk",), (), chan="oo")
        sch.finalize()
        with contextlib.ExitStack() as es:
            sems = {}
            for e in ENGS:
                sems[e] = es.enter_context(nc.semaphore(f"sem_{e}"))
            dma_sems = [es.enter_context(nc.semaphore(f"dsem{i}")) for i in range(sch.n_dma_slots)]
            block = es.enter_context(nc.Block())
            stats = sch.emit(block, sems, dma_sems)
        return nc, stats

    stackB1 = contextlib.ExitStack()
    SB1 = lambda name, shape, dt: stackB1.enter_context(nc.sbuf_tensor(name, shape, dt))
    cgate = SB1("cgate", [128, 16 * 1024], BF16)
    cq = SB1("cq", [128, 8 * SD], BF16)
    ytokB = SB1("ytokB", [128, 4 * D], F32)
    yfmB = SB1("yfmB", [128, 4 * 1024], BF16)
    xB = SB1("xB", [128, 4 * D], F32)
    qfm = SB1("qfm", [64, 512], BF16)
    lgb = SB1("lgb", [128, 2 * NPOOL], F32)
    ebuf = SB1("ebuf", [128, NPOOL], F32)
    esum = SB1("esum", [128, 2], F32)
    smb = SB1("smb", [128, NPOOL], BF16)
    smTb = SB1("smTb", [64, 4 * 128], BF16)
    retB = SB1("retB", [128, 8 * 512], BF16)
    gb = SB1("gb", [128, 512], BF16)
    t2m = SB1("t2m", [128, 512], BF16)
    t2B = SB1("t2B", [128, 8 * 512], BF16)
    t2tok = SB1("t2tok", [128, 4 * 1024], BF16)
    statB = SB1("statB", [128, 24], F32)
    xpxB = SB1("xpxB", [128, D], F32)
    ysumB = SB1("ysumB", [128, D], F32)
    xmoB = SB1("xmoB", [128, D], F32)
    wtq = SB1("wtq", [128, 4096], F32)
    wqt = SB1("wqt", [128, 4096], BF16)
    wqneg = SB1("wqneg", [128, 4096], BF16)
    wqt2 = SB1("wqt2", [128, 4096], BF16)
    posb = SB1("posb", [128, 4096], F32)
    wtmpB = SB1("wtmpB", [128, SD], F32)
    lnbw2 = SB1("lnbw2", [128, 64], BF16)

    for k in range(16):
        sch.dma("sync", wtq[:, 0:1024], wg_h[k * 128:(k + 1) * 128, :], (), ("wtq",), chan="w")
        v_copy(cgate[:, k * 1024:(k + 1) * 1024], wtq[:, 0:1024], ("wtq",), (f"cg_{k}",))
    for k in range(8):
        sch.dma("sync", wtmpB[:], wq_h[k * 128:(k + 1) * 128, :], (), ("wtqq",), chan="w")
        v_copy(cq[:, k * SD:(k + 1) * SD], wtmpB[:], ("wtqq",), (f"cq_{k}",))
    for k in range(32):
        v_copy(lnbw2[:, 2 * k:2 * k + 1], lnb16[:, k:k + 1], ("lnb16",), ("lnbw2",))
        v_copy(lnbw2[:, 2 * k + 1:2 * k + 2], lnw16[:, k:k + 1], ("lnw16",), ("lnbw2",))

    QUANT_TILES = [("fc", k) for k in range(8)] + [("pr", k) for k in range(32)]
    quant_i = [0]

    def emit_quant_tile():
        if quant_i[0] >= len(QUANT_TILES):
            return
        which, k = QUANT_TILES[quant_i[0]]
        quant_i[0] += 1
        n = 4096 if which == "fc" else 1024
        src = wfcT_h if which == "fc" else wprT_h
        dst = wqfc_hbm if which == "fc" else wqpr_hbm
        gc = 0 if which == "fc" else 1
        sch.dma("sync", wtq[:, 0:n], src[k * 128:(k + 1) * 128, :], (), ("wtq",),
                chan="g", nslots=2)
        v_ts(posb[:, 0:n], wtq[:, 0:n], ghalf[:, gc:gc + 1], None, ALU.is_gt, ALU.bypass,
             ("wtq", "ghalf"), ("posb",))
        v_ts(wqneg[:, 0:n], wtq[:, 0:n], gnegh[:, gc:gc + 1], None, ALU.is_lt, ALU.bypass,
             ("wtq", "gnegh"), ("wqnegk",))
        v_tt(wqt[:, 0:n], posb[:, 0:n], wqneg[:, 0:n], ALU.subtract,
             ("posb", "wqnegk"), ("wqt",))
        if which == "pr":
            for half in range(2):
                t_mm(PB[4 + half][0:2, 0:512], lnbw2[:, 2 * k:2 * k + 2],
                     wqt[:, half * 512:(half + 1) * 512], k == 0, k == 31,
                     ("wqt", "lnbw2"), (KPB[4 + half],), tag="cmm")
            v_ts(wqt2[:, 0:n], wqt[:, 0:n], lnwf32[:, k:k + 1], None, ALU.mult, ALU.bypass,
                 ("wqt", "lnwf32"), ("wqt2",))
            sch.dma("sync", dst[k, :, :], wqt2[:, 0:n], ("wqt2",), (), chan="qo")
        else:
            sch.dma("sync", dst[k, :, :], wqt[:, 0:n], ("wqt",), (), chan="qo")

    for i2 in range(min(8, NSC_LIM)):
        ch0 = i2 * 4
        for tt in range(4):
            ch = ch0 + tt
            sch.dma("sync", ytokB[:, tt * D:(tt + 1) * D], y_spill[ch, :, :], (), (f"ytB{tt}",), chan="x")
            sch.dma("sync", yfmB[:, tt * 1024:(tt + 1) * 1024], yfm_spill[ch, :, :], (), (f"yfB{tt}",), chan="x")
            sch.dma("sync", xB[:, tt * D:(tt + 1) * D], x_h[ch * 128:(ch + 1) * 128, :], (), (f"xB{tt}",), chan="x")
        ykeys = tuple(f"yfB{tt}" for tt in range(4))
        for k in range(8):
            t_mm(PB[2][0:64, 0:512], cq[:, k * SD:(k + 1) * SD],
                 yfmB[:].rearrange("p (t4 n) -> p t4 n", t4=4)[:, :, k * 128:(k + 1) * 128],
                 k == 0, k == 7, (f"cq_{k}",) + ykeys, (KPB[2],), tag="q")
        v_copy(qfm[:], PB[2][0:64, 0:512], (KPB[2],), ("qfm",))
        for tt in range(4):
            t_mm(PB[3][:, 0:NPOOL], qfm[:, tt * 128:(tt + 1) * 128], kkT[0:64, 0:NPOOL],
                 True, True, ("qfm", "kkT"), (KPB[3],), tag="lg")
            v_tt(lgb[:, 0:NPOOL], PB[3][:, 0:NPOOL], mb128[:], ALU.mult,
                 (KPB[3], "mb128"), ("lg1",))
            v_tt(lgb[:, NPOOL:2 * NPOOL], lgb[:, 0:NPOOL], bb128[:], ALU.add,
                 ("lg1", "bb128"), ("lg2",))
            v_red(statB[:, 0:1], lgb[:, NPOOL:2 * NPOOL], ALU.max, ("lg2",), ("negmx",), negate=True)
            s_act(ebuf[:], lgb[:, NPOOL:2 * NPOOL], ACT.Exp, ("lg2", "negmx"), ("ebuf",),
                  bias=statB[:, 0:1], accum=esum[:, 0:1])
            v_recip(esum[:, 1:2], esum[:, 0:1], ("ebuf",), ("esr",))
            v_ts(smb[:], ebuf[:], esum[:, 1:2], None, ALU.mult, ALU.bypass,
                 ("ebuf", "esr"), ("smb",))
            t_tr(PT[0][0:NPOOL, 0:128], smb[:], 128, ("smb",), (KPT[0],))
            v_copy(smTb[0:NPOOL, tt * 128:(tt + 1) * 128], PT[0][0:NPOOL, 0:128],
                   (KPT[0],), ("smTb",))
        for k in range(8):
            t_mm(PB[k % 2][:, 0:512], vvtok[0:NPOOL, k * 128:(k + 1) * 128],
                 smTb[0:NPOOL, 0:512], True, True, ("vvtok", "smTb"), (KPB[k % 2],), tag="ret")
            v_copy(retB[:, k * 512:(k + 1) * 512], PB[k % 2][:, 0:512],
                   (KPB[k % 2],), (f"ret{k}",))
        for m in range(8):
            pb, pkey = PB[m % 2], KPB[m % 2]
            for k in range(8):
                t_mm(pb[:, 0:512], cgate[:, k * 1024 + m * 128: k * 1024 + m * 128 + 128],
                     yfmB[:].rearrange("p (t4 n) -> p t4 n", t4=4)[:, :, k * 128:(k + 1) * 128],
                     k == 0, False, (f"cg_{k}",) + ykeys, (pkey,), tag="gate")
            for k in range(8):
                t_mm(pb[:, 0:512], cgate[:, (8 + k) * 1024 + m * 128: (8 + k) * 1024 + m * 128 + 128],
                     retB[:, k * 512:(k + 1) * 512], False, k == 7,
                     (f"cg_{8 + k}", f"ret{k}"), (pkey,), tag="gate2")
            s_act(gb[:], pb[:, 0:512], ACT.Sigmoid, (pkey,), ("gb",))
            v_tt(t2m[:], gb[:], retB[:, m * 512:(m + 1) * 512], ALU.mult,
                 ("gb", f"ret{m}"), ("t2m",))
            v_ts(t2B[:, m * 512:(m + 1) * 512], t2m[:], condcol[:], None, ALU.mult, ALU.bypass,
                 ("t2m", "condcol"), (f"t2B{m}",))
        for m in range(8):
            for tt in range(4):
                t_tr(PT[1][0:128, 0:128], t2B[:, m * 512 + tt * 128: m * 512 + (tt + 1) * 128],
                     128, (f"t2B{m}",), (KPT[1],))
                v_copy(t2tok[:, tt * 1024 + m * 128: tt * 1024 + (m + 1) * 128],
                       PT[1][0:128, 0:128], (KPT[1],), (f"t2tok{tt}",))
        for tt in range(4):
            ch = ch0 + tt
            sch.op("vector", lambda e, tt=tt: e.bn_stats(statB[:, 2:8], xB[:, tt * D: tt * D + 512]),
                   (f"xB{tt}",), ("stB",))
            sch.op("vector", lambda e, tt=tt: e.bn_stats(statB[:, 8:14], xB[:, tt * D + 512:(tt + 1) * D]),
                   (f"xB{tt}",), ("stB",))
            sch.op("vector", lambda e: e.bn_aggr(statB[:, 14:16], statB[:, 2:14]),
                   ("stB",), ("stB",))
            mean, var = statB[:, 14:15], statB[:, 15:16]
            veps, inv = statB[:, 16:17], statB[:, 17:18]
            q1, q2, lnscr = statB[:, 18:19], statB[:, 19:20], statB[:, 20:21]
            v_ts(veps, var, cst(C_EPS), None, ALU.add, ALU.bypass, ("stB", "consts"), ("stB1",))
            rsqrt2(inv, veps, lnscr, 128, ("stB1",), ("stB2",))
            v_ts(q1, inv, cst(C_ONE), None, ALU.add, ALU.bypass, ("stB2", "consts"), ("stB3",))
            v_ts(q2, mean, inv, None, ALU.mult, ALU.bypass, ("stB", "stB2"), ("stB4",))
            v_ts(xpxB[:], xB[:, tt * D:(tt + 1) * D], q1, q2,
                 ALU.mult, ALU.subtract, (f"xB{tt}", "stB3", "stB4"), ("xpxB",))
            v_tt(ysumB[:], ytokB[:, tt * D:(tt + 1) * D], t2tok[:, tt * 1024:(tt + 1) * 1024],
                 ALU.add, (f"ytB{tt}", f"t2tok{tt}"), ("ysumB",))
            v_tt(xmoB[:], xpxB[:], ysumB[:], ALU.add, ("xpxB", "ysumB"), ("xmoB",))
            sch.dma("sync", xmid_spill[ch, :, :], xmoB[:], ("xmoB",), (), chan="xo")
        for _ in range(5):
            emit_quant_tile()

    while quant_i[0] < len(QUANT_TILES):
        emit_quant_tile()
    crow_st = SB1("crow_st", [2, 1024], F32)
    v_copy(crow_st[0:2, 0:512], PB[4][0:2, 0:512], (KPB[4],), ("c1row",))
    v_copy(crow_st[0:2, 512:1024], PB[5][0:2, 0:512], (KPB[5],), ("c2row",))
    sch.dma("sync", c_hbm[:, :], crow_st[:], ("c1row", "c2row"), ("c_hbm",), chan="xo")

    sch.barrier()
    stackB1.close()

    if KPHASE <= 3:
        dummy = nc.alloc_sbuf_tensor("dummy3", [128, D], F32)
        sch.op("vector", lambda e: e.memset(dummy[:], 0.0), (), ("dummyk",), "dm")
        for chd in range(NCH):
            sch.dma("sync", out_h[chd * 128:(chd + 1) * 128, :], dummy[:], ("dummyk",), (), chan="oo")
        sch.finalize()
        with contextlib.ExitStack() as es:
            sems = {}
            for e in ENGS:
                sems[e] = es.enter_context(nc.semaphore(f"sem_{e}"))
            dma_sems = [es.enter_context(nc.semaphore(f"dsem{i}")) for i in range(sch.n_dma_slots)]
            block = es.enter_context(nc.Block())
            stats = sch.emit(block, sems, dma_sems)
        return nc, stats

    stackB2 = contextlib.ExitStack()
    SB2 = lambda name, shape, dt: stackB2.enter_context(nc.sbuf_tensor(name, shape, dt))
    wqfcS = SB2("wqfcS", [128, 8 * 4096], BF16)
    wqprS = SB2("wqprS", [128, 32 * 1024], BF16)
    c1b = SB2("c1b", [128, 1024], F32)
    c2b = SB2("c2b", [128, 1024], F32)
    xmB2 = SB2("xmB2", [128, 2 * D], F32)
    x2tok = SB2("x2tok", [128, 2 * D], BF16)
    x2fm = SB2("x2fm", [128, 8 * 256], BF16)
    gbuf = SB2("gbuf", [128, 32 * 256], BF16)
    sqbB = SB2("sqbB", [128, 2 * 256], BF16)
    atmp = SB2("atmp", [128, 2 * 256], BF16)
    statB2 = SB2("statB2", [128, 24], F32)
    murow = SB2("murow", [1, 256], F32)
    m2row = SB2("m2row", [1, 256], F32)
    varrow = SB2("varrow", [1, 256], F32)
    var2row = SB2("var2row", [1, 256], F32)
    lnsrow = SB2("lnsrow", [1, 256], F32)
    rsrow2 = SB2("rsrow2", [1, 256], F32)
    muarow = SB2("muarow", [1, 256], F32)
    nmuarow = SB2("nmuarow", [1, 256], F32)
    rsb2 = SB2("rsb2", [128, 256], F32)
    nmacol = SB2("nmacol", [128, 2], F32)
    corrb = SB2("corrb", [128, 512], F32)
    corr2b = SB2("corr2b", [128, 512], F32)
    outsb = SB2("outsb", [128, 2 * D], F32)

    for k in range(8):
        sch.dma("sync", wqfcS[:, k * 4096:(k + 1) * 4096], wqfc_hbm[k, :, :], (), (f"wf_{k}",), chan="w")
    for k in range(32):
        sch.dma("sync", wqprS[:, k * 1024:(k + 1) * 1024], wqpr_hbm[k, :, :], (), (f"wp_{k}",), chan="w")
    sch.dma("gpsimd", c1b[:], c_hbm[0:1, :].unsqueeze(1).broadcast_to((1, 128, 1024)),
            ("c_hbm",), ("c1b",), chan="f")
    sch.dma("gpsimd", c2b[:], c_hbm[1:2, :].unsqueeze(1).broadcast_to((1, 128, 1024)),
            ("c_hbm",), ("c2b",), chan="f")

    for i3 in range(min(16, NSC_LIM)):
        ch0 = i3 * 2
        for tt2 in range(2):
            sch.dma("sync", xmB2[:, tt2 * D:(tt2 + 1) * D], xmid_spill[ch0 + tt2, :, :],
                    (), (f"xm2_{tt2}",), chan="x")
            sch.op("vector", lambda e, tt2=tt2: e.bn_stats(statB2[:, 0:6],
                   xmB2[:, tt2 * D: tt2 * D + 512]), (f"xm2_{tt2}",), ("st2",))
            sch.op("vector", lambda e, tt2=tt2: e.bn_stats(statB2[:, 6:12],
                   xmB2[:, tt2 * D + 512:(tt2 + 1) * D]), (f"xm2_{tt2}",), ("st2",))
            sch.op("vector", lambda e: e.bn_aggr(statB2[:, 12:14], statB2[:, 0:12]),
                   ("st2",), ("st2",))
            mean, var = statB2[:, 12:13], statB2[:, 13:14]
            veps, inv, lnscr = statB2[:, 14:15], statB2[:, 15:16], statB2[:, 16:17]
            v_ts(veps, var, cst(C_EPS), None, ALU.add, ALU.bypass, ("st2", "consts"), ("st2a",))
            rsqrt2(inv, veps, lnscr, 128, ("st2a",), ("st2b",))
            v_ts(x2tok[:, tt2 * D:(tt2 + 1) * D], xmB2[:, tt2 * D:(tt2 + 1) * D], mean, inv,
                 ALU.subtract, ALU.mult, (f"xm2_{tt2}", "st2", "st2b"), (f"x2_{tt2}",))
            for k in range(8):
                t_tr(PT[0][0:128, 0:128], x2tok[:, tt2 * D + k * 128: tt2 * D + (k + 1) * 128],
                     128, (f"x2_{tt2}",), (KPT[0],))
                v_copy(x2fm[:, k * 256 + tt2 * 128: k * 256 + (tt2 + 1) * 128],
                       PT[0][0:128, 0:128], (KPT[0],), (f"x2f{k}",))
        x2keys = tuple(f"x2f{k}" for k in range(8))
        for j in range(32):
            pb, pkey = PB[j % 2], KPB[j % 2]
            for k in range(8):
                t_mm(pb[:, 0:256], wqfcS[:, k * 4096 + j * 128: k * 4096 + (j + 1) * 128],
                     x2fm[:, k * 256:(k + 1) * 256], k == 0, k == 7,
                     (f"wf_{k}",) + x2keys, (pkey,), tag="hmm")
            s_act(gbuf[:, j * 256:(j + 1) * 256], pb[:, 0:256], ACT.Gelu, (pkey,), (f"g{j}",))
            sq = sqbB[:, (j % 2) * 256:(j % 2 + 1) * 256]
            v_tt(sq, gbuf[:, j * 256:(j + 1) * 256], gbuf[:, j * 256:(j + 1) * 256],
                 ALU.mult, (f"g{j}",), (f"sqB{j % 2}",), tag="sqB")
            t_mm(PB[4][0:1, 0:256], onescol16[:], gbuf[:, j * 256:(j + 1) * 256],
                 j == 0, j == 31, (f"g{j}", "onescol16"), (KPB[4],), tag="sgmm")
            t_mm(PB[5][0:1, 0:256], onescol16[:], sq, j == 0, j == 31,
                 (f"sqB{j % 2}", "onescol16"), (KPB[5],), tag="sqmm2")
        v_ts(murow[:], PB[4][0:1, 0:256], cst(C_I4096, 1), None, ALU.mult, ALU.bypass,
             (KPB[4], "consts"), ("murow",))
        v_tt(m2row[:], murow[:], murow[:], ALU.mult, ("murow",), ("m2row",))
        v_stt(varrow[:], PB[5][0:1, 0:256], cst(C_I4096, 1), m2row[:], ALU.mult, ALU.subtract,
              (KPB[5], "m2row", "consts"), ("varrow",))
        v_ts(var2row[:], varrow[:], cst(C_EPS, 1), None, ALU.add, ALU.bypass,
             ("varrow", "consts"), ("var2row",))
        s_act(lnsrow[:], var2row[:], ACT.Ln, ("var2row",), ("lnsrow",))
        s_act(rsrow2[:], lnsrow[:], ACT.Exp, ("lnsrow", "consts"), ("rsrow2",),
              scale=cst(C_NEGH, 1))
        v_tt(muarow[:], murow[:], rsrow2[:], ALU.mult, ("murow", "rsrow2"), ("muarow",))
        v_ts(nmuarow[:], muarow[:], cst(C_NEG1, 1), None, ALU.mult, ALU.bypass,
             ("muarow", "consts"), ("nmuarow",))
        bcast(rsb2[:], rsrow2[:], 256, ("rsrow2",), ("rsb2",))
        for tt2 in range(2):
            t_trf(PB[2][0:128, 0:1], nmuarow[0:1, tt2 * 128:(tt2 + 1) * 128], 1,
                  ("nmuarow",), (KPB[2],))
            v_copy(nmacol[:, tt2:tt2 + 1], PB[2][0:128, 0:1], (KPB[2],), ("nmacol",))
        for j in range(32):
            at = atmp[:, (j % 2) * 256:(j % 2 + 1) * 256]
            v_tt(at, gbuf[:, j * 256:(j + 1) * 256], rsb2[:], ALU.mult,
                 (f"g{j}", "rsb2"), (f"at{j % 2}",), tag="aA")
            for tt2 in range(2):
                for dh in range(2):
                    pb, pkey = PB[tt2 * 2 + dh], KPB[tt2 * 2 + dh]
                    t_mm(pb[:, 0:512], at[:, tt2 * 128:(tt2 + 1) * 128],
                         wqprS[:, j * 1024 + dh * 512: j * 1024 + (dh + 1) * 512],
                         j == 0, j == 31, (f"at{j % 2}", f"wp_{j}"), (pkey,), tag="pmm")
        for tt2 in range(2):
            for dh in range(2):
                pb, pkey = PB[tt2 * 2 + dh], KPB[tt2 * 2 + dh]
                v_stt(corrb[:], c2b[:, dh * 512:(dh + 1) * 512], nmacol[:, tt2:tt2 + 1],
                      c1b[:, dh * 512:(dh + 1) * 512], ALU.mult, ALU.add,
                      ("c2b", "c1b", "nmacol"), ("corrb",))
                v_tt(corr2b[:], corrb[:], pb[:, 0:512], ALU.add, ("corrb", pkey), ("corr2b",))
                v_tt(outsb[:, tt2 * D + dh * 512: tt2 * D + (dh + 1) * 512], corr2b[:],
                     xmB2[:, tt2 * D + dh * 512: tt2 * D + (dh + 1) * 512], ALU.add,
                     ("corr2b", f"xm2_{tt2}"), (f"osb{tt2}",))
            sch.dma("sync", out_h[(ch0 + tt2) * 128:(ch0 + tt2 + 1) * 128, :],
                    outsb[:, tt2 * D:(tt2 + 1) * D], (f"osb{tt2}",), (), chan="oo")

    sch.finalize()
    stackB2.close()

    with contextlib.ExitStack() as es:
        sems = {}
        for e in ENGS:
            sems[e] = es.enter_context(nc.semaphore(f"sem_{e}"))
        dma_sems = [es.enter_context(nc.semaphore(f"dsem{i}")) for i in range(sch.n_dma_slots)]
        block = es.enter_context(nc.Block())
        stats = sch.emit(block, sems, dma_sems)
    return nc, stats


_CACHE = {}


def _get_program():
    if "nc" not in _CACHE:
        nc, stats = build_program()
        _CACHE["nc"] = nc
        _CACHE["stats"] = stats
    return _CACHE["nc"]


def stage_inputs(**inputs):
    x = np.asarray(inputs["L0_attn_x"], dtype=np.float32)
    conv_w = np.asarray(inputs["L0_attn_conv_w"], dtype=np.float32)
    conv_b = np.asarray(inputs["L0_attn_conv_b"], dtype=np.float32)
    norm_w = np.asarray(inputs["L0_attn_norm_w"], dtype=np.float32)
    ssm_w = np.asarray(inputs["L0_attn_ssm_norm_w"], dtype=np.float32)
    ln_w = np.asarray(inputs["L0_ffn_ln_w"], dtype=np.float32)
    ln_b = np.asarray(inputs["L0_ffn_ln_b"], dtype=np.float32)
    Dh = np.asarray(inputs["L0_attn_D"], dtype=np.float32)

    common = {
        "w_inproj": np.ascontiguousarray(inputs["L0_attn_in_proj_w"], dtype=np.float32),
        "w_out": np.ascontiguousarray(inputs["L0_attn_out_proj_w"], dtype=np.float32),
        "w_sc1": np.ascontiguousarray(inputs["L0_attn_scorer_w1"], dtype=np.float32),
        "w_sc2": np.ascontiguousarray(inputs["L0_attn_scorer_w2"], dtype=np.float32),
        "w_summ": np.ascontiguousarray(inputs["L0_attn_summ_w"], dtype=np.float32),
        "w_q": np.ascontiguousarray(inputs["L0_attn_q_w"], dtype=np.float32),
        "w_k": np.ascontiguousarray(inputs["L0_attn_k_w"], dtype=np.float32),
        "w_v": np.ascontiguousarray(inputs["L0_attn_v_w"], dtype=np.float32),
        "w_gate": np.ascontiguousarray(inputs["L0_attn_gate_w"], dtype=np.float32),
        "w_fcT": np.ascontiguousarray(np.asarray(inputs["L0_ffn_W_fc"], dtype=np.float32).T),
        "w_projT": np.ascontiguousarray(np.asarray(inputs["L0_ffn_W_proj"], dtype=np.float32).T),
        "conv_w_t": np.ascontiguousarray(
            conv_w.reshape(18, 128, 4).transpose(1, 0, 2).reshape(128, 72)),
        "conv_b_t": np.ascontiguousarray(conv_b.reshape(18, 128).T),
        "norm_w_t": np.ascontiguousarray(norm_w.reshape(8, 128).T),
        "ssm_w_t": np.ascontiguousarray(ssm_w.reshape(16, 128).T),
        "ln_w_t": np.ascontiguousarray(ln_w.reshape(32, 128).T),
        "ln_b_t": np.ascontiguousarray(ln_b.reshape(32, 128).T),
        "dt_bias_t": np.ascontiguousarray(
            np.asarray(inputs["L0_attn_dt_bias"], dtype=np.float32).reshape(32, 1)),
        "a_log_t": np.ascontiguousarray(
            np.asarray(inputs["L0_attn_A_log"], dtype=np.float32).reshape(32, 1)),
        "d_fm_t": np.ascontiguousarray(np.concatenate(
            [np.broadcast_to(Dh[0::2], (64, 16)), np.broadcast_to(Dh[1::2], (64, 16))], axis=0)),
        "ident": np.eye(128, dtype=ml_dtypes.bfloat16),
        "identf": np.eye(128, dtype=np.float32),
        "mask_sl": np.ascontiguousarray(np.triu(np.ones((128, 128), dtype=np.float32))),
    }
    in_maps = [dict(common, x=np.ascontiguousarray(x[b])) for b in range(NCORES)]
    return in_maps


def kernel(**inputs):
    in_maps = stage_inputs(**inputs)
    nc = _get_program()
    res = run_bass_kernel_spmd(nc, in_maps, core_ids=list(range(NCORES)))
    out = np.stack([np.asarray(r["out"], dtype=np.float32) for r in res.results], axis=0)
    return out


# version 13
# speedup vs baseline: 1.3728x; 1.0028x over previous
import os
import contextlib
import numpy as np
import ml_dtypes
import concourse.bass as bass
import concourse.mybir as mybir
from concourse.bass_utils import run_bass_kernel_spmd

F32 = mybir.dt.float32
BF16 = mybir.dt.bfloat16
U32 = mybir.dt.uint32
ALU = mybir.AluOpType
ACT = mybir.ActivationFunctionType
AX = mybir.AxisListType

import os
NSC_LIM = int(os.environ.get("KNSC", "16"))
NO_CC = os.environ.get("KNOCC", "0") == "1"
NO_IG = os.environ.get("KNOIG", "0") == "1"
NO_MX = os.environ.get("KNOMX", "0") == "1"
KPHASE = int(os.environ.get("KPHASE", "4"))
KSELCUT = int(os.environ.get("KSELCUT", "99"))
KDBG = os.environ.get("KDBG", "0") == "1"
KDBG2 = os.environ.get("KDBG2", "0") == "1"
T = 4096
D = 1024
L = 128
NCH = T // L
SC = 256
NSC = T // SC
H = 32
P = 64
DIN = 2048
DIP = 4384
HID = 256
SD = 64
NPOOL = 50
NSEL = 56
NCORES = 8
ENGS = ("tensor", "vector", "scalar", "gpsimd", "sync")


class Sched:

    class Op:
        __slots__ = ("eng", "fn", "reads", "writes", "idx", "deps", "inc",
                     "is_dma", "dma_sem", "dma_seq", "dma_amt", "tag",
                     "need_drain", "need_local", "drain_before")

        def __init__(self, eng, fn, reads, writes, idx, tag):
            self.eng, self.fn = eng, fn
            self.reads, self.writes = reads, writes
            self.idx = idx
            self.deps = []
            self.inc = False
            self.need_drain = False
            self.need_local = False
            self.drain_before = False
            self.is_dma = False
            self.dma_sem = None
            self.dma_seq = 0
            self.dma_amt = 16
            self.tag = tag

    def __init__(self, n_dma_slots=None):
        if n_dma_slots is None:
            n_dma_slots = int(os.environ.get('KSLOTS', '64'))
        self.ops = {e: [] for e in ENGS}
        self.last_writer = {}
        self.readers = {}
        self.n_dma_slots = n_dma_slots
        self.slot_seq = [0] * n_dma_slots
        self.slot_last = [None] * n_dma_slots
        self.chan_slots = {}
        self.chan_uses = {}
        self._next_slot = 0
        self.oplimit = None

    def _muted(self):
        if self.oplimit is None:
            return False
        self.oplimit -= 1
        if self.oplimit < 0:
            return True
        return False

    def _record(self, eng, fn, reads, writes, tag):
        if self._muted():
            return Sched.Op(eng, None, (), (), -1, "muted")
        op = Sched.Op(eng, fn, tuple(reads), tuple(writes), len(self.ops[eng]), tag)
        deps = []
        for k in op.reads:
            w = self.last_writer.get(k)
            if w is not None:
                deps.append((w, 0))
        for k in op.writes:
            w = self.last_writer.get(k)
            if w is not None:
                deps.append((w, 1))
            for r in self.readers.get(k, ()):
                deps.append((r, 2))
        seen = {}
        for d, kind in deps:
            key = (d.eng, d.idx)
            if key not in seen:
                seen[key] = len(op.deps)
                op.deps.append((d, kind))
            else:
                i = seen[key]
                if kind < op.deps[i][1]:
                    op.deps[i] = (d, kind)
        for k in op.writes:
            self.last_writer[k] = op
            self.readers[k] = []
        for k in op.reads:
            self.readers.setdefault(k, []).append(op)
        self.ops[eng].append(op)
        return op

    def op(self, eng, fn, reads=(), writes=(), tag=""):
        return self._record(eng, fn, reads, writes, tag)

    def dma_op(self, eng, fn, reads=(), writes=(), chan="d", nslots=2, tag="dmaop", amt=16):
        slots = self.chan_slots.setdefault(chan, [])
        if len(slots) < nslots and self._next_slot < self.n_dma_slots:
            slots.append(self._next_slot)
            self._next_slot += 1
        use = self.chan_uses.get(chan, 0)
        self.chan_uses[chan] = use + 1
        slot = slots[use % len(slots)]
        op = self._record(eng, fn, reads, writes, tag)
        if op.idx < 0:
            return op
        op.is_dma = True
        op.dma_sem = slot
        op.dma_amt = amt
        prev = self.slot_last[slot]
        if prev is not None:
            op.deps.append((prev, False))
        self.slot_seq[slot] += 1
        op.dma_seq = self.slot_seq[slot]
        self.slot_last[slot] = op
        return op

    def dma(self, eng, out_ap, in_ap, reads=(), writes=(), chan="d", nslots=4, tag="dma"):
        nslots = min(nslots, int(os.environ.get('KNSLOT', '4')))
        slots = self.chan_slots.setdefault(chan, [])
        if len(slots) < nslots and self._next_slot < self.n_dma_slots:
            slots.append(self._next_slot)
            self._next_slot += 1
        use = self.chan_uses.get(chan, 0)
        self.chan_uses[chan] = use + 1
        slot = slots[use % len(slots)]

        def fn(e, out_ap=out_ap, in_ap=in_ap):
            return e.dma_start(out=out_ap, in_=in_ap)

        op = self._record(eng, fn, reads, writes, tag)
        if op.idx < 0:
            return op
        op.is_dma = True
        op.dma_sem = slot
        prev = self.slot_last[slot]
        if prev is not None:
            op.deps.append((prev, False))
        self.slot_seq[slot] += 1
        op.dma_seq = self.slot_seq[slot]
        self.slot_last[slot] = op
        return op

    def barrier(self):
        marks = {}
        for e in ENGS:
            m = None
            for op in reversed(self.ops[e]):
                if op.fn is not None and not op.is_dma:
                    m = op
                    break
            if m is not None:
                m.inc = True
                m.need_drain = True
            marks[e] = m
        for e in ENGS:
            w = self._record(e, None, (), (), "barrier_w")
            for o in ENGS:
                if o != e and marks[o] is not None:
                    w.deps.append((marks[o], False))
            for s in range(self.n_dma_slots):
                if self.slot_last[s] is not None:
                    w.deps.append((self.slot_last[s], False))

    def finalize(self):
        w = self._record("sync", (lambda en: en.nop()), (), (), "final")
        for s in range(self.n_dma_slots):
            if self.slot_last[s] is not None:
                w.deps.append((self.slot_last[s], False))

    def emit(self, block, sems, dma_sems):
        paranoid = os.environ.get("KPAR", "0") == "1"
        audit = os.environ.get("KAUDIT", "0") == "1"
        if audit:
            paranoid = True
        local_edges = {e: [] for e in ENGS}
        for e in ENGS:
            for op in self.ops[e]:
                if paranoid and not op.is_dma and op.fn is not None:
                    op.inc = True
                    if audit:
                        op.need_drain = True
                for d, kind in op.deps:
                    cross = d.eng != e or op.is_dma
                    if (not d.is_dma) and (cross or paranoid):
                        d.inc = True
                    if kind == 2 or d.is_dma:
                        continue
                    if cross:
                        d.need_drain = True
                    elif kind == 0:
                        local_edges[e].append((d.idx, op.idx))
        consumers = {}
        for e in ENGS:
            for op in self.ops[e]:
                for d, kind in op.deps:
                    if kind != 2 and not d.is_dma and (d.eng != op.eng or op.is_dma):
                        consumers.setdefault((d.eng, d.idx), []).append((op, d.idx))
        if not audit:
            for e in ENGS:
                for op in reversed(self.ops[e]):
                    if not op.need_drain:
                        continue
                    ok = True
                    for c, _pi in consumers.get((e, op.idx), ()):
                        if not any((d.eng == e and not d.is_dma and kind != 2
                                    and d.idx > op.idx and d.need_drain)
                                   for d, kind in c.deps):
                            ok = False
                            break
                    if ok and consumers.get((e, op.idx)):
                        op.need_drain = False
        nodrain = set(os.environ.get("KND", "tensor,scalar").split(",")) - {""}
        for e in ENGS:
            pts = sorted(o.idx + 0.5 for o in self.ops[e]
                         if (o.need_drain and e not in nodrain)
                         or (audit and o.inc))
            import bisect as _bi
            for p, c in sorted(local_edges[e], key=lambda pc: pc[1]):
                i = _bi.bisect_right(pts, p)
                if i < len(pts) and pts[i] < c:
                    continue
                self.ops[e][c].drain_before = True
                _bi.insort(pts, c - 0.5)
        inccount = {}
        for e in ENGS:
            c = 0
            for op in self.ops[e]:
                if op.inc and not op.is_dma:
                    c += 1
                inccount[(e, op.idx)] = c

        def make_section(ename):
            ops_list = self.ops[ename]
            eng_nodrain = ename in nodrain

            def section(eng):
                waited = {}
                for op in ops_list:
                    want = {}
                    for d, kind in op.deps:
                        if d.is_dma:
                            k = ("dma", d.dma_sem)
                            want[k] = max(want.get(k, 0), d.dma_amt * d.dma_seq)
                        elif d.eng != ename or paranoid or op.is_dma:
                            want[d.eng] = max(want.get(d.eng, 0), inccount[(d.eng, d.idx)])
                    for k, v in sorted(want.items(), key=str):
                        if waited.get(k, -1) < v:
                            if isinstance(k, tuple):
                                eng.wait_ge(dma_sems[k[1]], v)
                            else:
                                eng.wait_ge(sems[k], v)
                            waited[k] = v
                    if op.fn is None:
                        continue
                    if op.drain_before:
                        eng.drain(fusable=True)
                    instr = op.fn(eng)
                    if op.is_dma:
                        instr.then_inc(dma_sems[op.dma_sem], op.dma_amt)
                    elif op.inc:
                        if op.need_drain and not eng_nodrain:
                            eng.maybe_drain_then_inc((sems[ename], 1), fusable=True)
                        else:
                            if instr is None or ename == "gpsimd":
                                try:
                                    instr = eng.engine_nop()
                                except Exception:
                                    instr = eng.nop()
                            instr.then_inc(sems[ename], 1)
            return section

        for ename in ENGS:
            getattr(block, ename)(make_section(ename))
        return {e: len(self.ops[e]) for e in ENGS} | {
            "incs": sum(1 for e in ENGS for o in self.ops[e] if o.inc)}


def build_program():
    nc = bass.Bass("TRN2", target_bir_lowering=False,
                   detect_race_conditions=os.environ.get("KRACE", "0") == "1")
    sch = Sched()

    def dramin(name, shape, dt=F32):
        return nc.dram_tensor(name, list(shape), dt, kind="ExternalInput").ap()

    x_h = dramin("x", (T, D))
    w1_h = dramin("w_inproj", (D, DIP))
    wo_h = dramin("w_out", (DIN, D))
    wsc1_h = dramin("w_sc1", (D, HID))
    wsc2_h = dramin("w_sc2", (HID, 1))
    wsm_h = dramin("w_summ", (D, SD))
    wq_h = dramin("w_q", (D, SD))
    wk_h = dramin("w_k", (SD, SD))
    wv_h = dramin("w_v", (SD, D))
    wg_h = dramin("w_gate", (DIN, D))
    wfcT_h = dramin("w_fcT", (D, 4096))
    wprT_h = dramin("w_projT", (4096, D))
    cwv_h = dramin("conv_w_t", (128, 72))
    ccb_h = dramin("conv_b_t", (128, 18))
    nw_h = dramin("norm_w_t", (128, 8))
    sw_h = dramin("ssm_w_t", (128, 16))
    lnw_h = dramin("ln_w_t", (128, 32))
    lnb_h = dramin("ln_b_t", (128, 32))
    dtb_h = dramin("dt_bias_t", (32, 1))
    alog_h = dramin("a_log_t", (32, 1))
    dfm_h = dramin("d_fm_t", (128, 16))
    ident_h = dramin("ident", (128, 128), BF16)
    identf_h = dramin("identf", (128, 128), F32)
    mask_h = dramin("mask_sl", (128, 128))

    out_h = nc.dram_tensor("out", [T, D], F32, kind="ExternalOutput").ap()

    y_spill = nc.dram_tensor("y_spill", [NCH, 128, D], F32).ap()
    dbg_h = {}
    if KDBG:
        for nm, shp, dt_ in [("xntok", [128, D], BF16), ("xnf", [128, 2048], BF16),
                             ("sz", [128, 16 * SC], BF16), ("xbcc", [128, 18 * SC], BF16),
                             ("dtv", [32, SC], F32), ("csv", [32, L], F32), ("statc", [128, 20], F32),
                             ("xdt", [128, 2048], BF16), ("yfmsc", [128, 2048], BF16),
                             ("yop", [128, 2048], BF16), ("dsv", [32, L], F32),
                             ("expcs32", [32, L], F32), ("sall", [128, 2048], BF16),
                             ("gmask", [128, 128], BF16), ("mtb0", [128, 128], BF16),
                             ("xdt2", [128, 2048], BF16), ("dtT16", [128, 32], BF16),
                             ("mtbh2", [128, 128], BF16), ("ssegh2", [128, 128], F32),
                             ("identd", [128, 128], BF16), ("identfd", [128, 128], F32),
                             ("constsd", [128, 28], F32),
                             ("csbh2", [128, 512], F32), ("csTd", [128, 32], F32)]:
            dbg_h[nm] = nc.dram_tensor("dbg_" + nm, shp, dt_, kind="ExternalOutput").ap()
    yfm_spill = nc.dram_tensor("yfm_spill", [NCH, 128, 1024], BF16).ap()
    xmid_spill = nc.dram_tensor("xmid_spill", [NCH, 128, D], F32).ap()
    wqfc_hbm = nc.dram_tensor("wqfc_hbm", [8, 128, 4096], BF16).ap()
    wqpr_hbm = nc.dram_tensor("wqpr_hbm", [32, 128, D], BF16).ap()
    sm_in = nc.dram_tensor("sm_in", [T + 64, SD], F32)
    sm_out = nc.dram_tensor("sm_out", [8 * (T + 64), SD], F32)
    c_hbm = nc.dram_tensor("c_hbm", [2, 1024], F32).ap()

    A = nc.alloc_sbuf_tensor
    consts = A("constssb", [128, 28], F32)
    (C_HALF, C_EPS, C_NEG1, C_NEGH, C_I2048, C_I4096, C_ONE, C_TAUS,
     C_1E9, C_N1E9, C_128, C_QUART, C_IFC, C_ZERO, C_1E6M, C_256, C_N128,
     C_T1, C_T2, C_T3, C_T4, C_T5, C_T6, C_T7, C_64) = range(25)
    CONST_VALS = {C_HALF: 0.5, C_EPS: 1e-5, C_NEG1: -1.0, C_NEGH: -0.5,
                  C_I2048: 1.0 / 2048, C_I4096: 1.0 / 4096, C_ONE: 1.0,
                  C_TAUS: 0.3 * 32768.0, C_1E9: 1e9, C_N1E9: -1e9,
                  C_128: 128.0, C_QUART: 0.25, C_IFC: float(2.0 ** -22),
                  C_ZERO: 0.0, C_1E6M: 1e-6, C_256: 256.0, C_N128: -128.0,
                  C_T1: 4096.0, C_T2: 8192.0, C_T3: 12288.0, C_T4: 16384.0,
                  C_T5: 20480.0, C_T6: 24576.0, C_T7: 28672.0, C_64: 64.0}
    ident = A("identsb", [128, 128], BF16)
    identf = A("identfsb", [128, 128], F32)
    masksl = A("masksl", [128, 128], F32)
    onescol = A("onescol", [128, 1], F32)
    onescol16 = A("onescol16", [128, 1], BF16)
    ones32 = A("ones32", [32, 128], F32)
    onecol32 = A("onecol32", [32, 1], F32)
    piota_f = A("piota_f", [128, 1], F32)
    piota_u = A("piota_u", [128, 1], U32)
    scrow = A("scrow", [1, 128], F32)
    thetas = A("thetas", [1, 64], F32)
    kkT = A("kkT", [64, NSEL], BF16)
    vvtok = A("vvtok", [NSEL, D], BF16)
    mb128 = A("mb128", [128, NPOOL], F32)
    bb128 = A("bb128", [128, NPOOL], F32)
    condcol = A("condcol", [128, 1], F32)
    ghalf = A("ghalf", [128, 2], F32)
    gnegh = A("gnegh", [128, 2], F32)
    lnw16 = A("lnw16", [128, 32], BF16)
    lnwf32 = A("lnwf32", [128, 32], F32)
    lnb16 = A("lnb16", [128, 32], BF16)
    gacc = A("gacc", [128, 4], F32)
    grow_sb = A("grow_sb", [1, 4], F32)

    PB = [nc.alloc_psum_tensor(f"pb{i}", [128, 512], F32) for i in range(6)]
    PT = [nc.alloc_psum_tensor(f"pt{i}", [128, 1024], BF16) for i in range(2)]
    KPB = [f"PB{i}" for i in range(6)]
    KPT = [f"PT{i}" for i in range(2)]

    def cst(i, p=128):
        return consts[0:p, i:i + 1]

    def v_tt(out, a, b, op, reads, writes, tag="tt"):
        sch.op("vector", lambda e: e.tensor_tensor(out, a, b, op), reads, writes, tag)

    def v_ts(out, in0, s1, s2, op0, op1, reads, writes, tag="ts"):
        sch.op("vector", lambda e: e.tensor_scalar(out, in0, s1, s2, op0, op1), reads, writes, tag)

    def v_stt(out, in0, s, in1, op0, op1, reads, writes, tag="stt"):
        sch.op("vector", lambda e: e.scalar_tensor_tensor(out, in0, s, in1, op0, op1), reads, writes, tag)

    def v_copy(out, in_, reads, writes, tag="cp"):
        sch.op("vector", lambda e: e.tensor_copy(out, in_), reads, writes, tag)

    def v_red(out, in_, op, reads, writes, negate=False, absval=False, tag="red"):
        sch.op("vector", lambda e: e.tensor_reduce(out, in_, AX.X, op,
               apply_absolute_value=absval, negate=negate), reads, writes, tag)

    def v_scan(out, d0, d1, init, op0, op1, reads, writes, tag="scan"):
        sch.op("vector", lambda e: e.tensor_tensor_scan(out, d0, d1, init, op0, op1), reads, writes, tag)

    def v_memset(ap_, val, writes, tag="memset"):
        sch.op("vector", lambda e: e.memset(ap_, val), (), writes, tag)

    def v_recip(out, in_, reads, writes, tag="recip"):
        sch.op("vector", lambda e: e.reciprocal(out, in_), reads, writes, tag)

    def v_max(out, in_, reads, writes, tag="max"):
        sch.op("vector", lambda e: e.max(out, in_), reads, writes, tag)

    def v_maxidx(out, in_max, in_vals, reads, writes, tag="mxi"):
        sch.op("vector", lambda e: e.max_index(out, in_max, in_vals), reads, writes, tag)

    def v_matchrep(out, rep, vals, imm, reads, writes, tag="mr"):
        sch.op("vector", lambda e: e.match_replace(out, in_to_replace=rep, in_values=vals,
               imm_value=imm), reads, writes, tag)

    def s_act(out, in_, func, reads, writes, bias=None, scale=None, accum=None, tag="act"):
        kw = {}
        if bias is not None:
            kw["bias"] = bias
        if scale is not None:
            kw["scale"] = scale
        if accum is not None:
            kw["accum_out"] = accum
        sch.op("scalar", lambda e: e.activation(out, in_, func, **kw), reads, writes, tag)

    def s_copy(out, in_, reads, writes, tag="scp"):
        sch.op("scalar", lambda e: e.copy(out, in_), reads, writes, tag)

    def t_mm(out, lhsT, rhs, start, stop, reads, writes, tag="mm", tile_position=None):
        sch.op("tensor", lambda e: e.matmul(out, lhsT, rhs, start=start, stop=stop,
               tile_position=tile_position), reads, writes, tag)

    def t_tr(out, in_, pin, reads, writes, tag="tr"):
        sch.op("tensor", lambda e: e.transpose(out, in_, ident[0:pin, 0:pin]),
               tuple(reads) + ("ident",), writes, tag)

    def t_trf(out, in_, pin, reads, writes, tag="trf"):
        sch.op("tensor", lambda e: e.transpose(out, in_, identf[0:pin, 0:pin]),
               tuple(reads) + ("identf",), writes, tag)

    def bcast(dst, src_row, n, reads, writes, chan="f"):
        sch.dma("gpsimd", dst, src_row.unsqueeze(1).broadcast_to((1, 128, n)),
                reads, writes, chan=chan, tag="bcast")

    def rsqrt2(out, in_, scratch, p, reads, writes):
        s_act(scratch, in_, ACT.Ln, reads, ("rsq_t",) , tag="rln")
        s_act(out, scratch, ACT.Exp, ("rsq_t", "consts"), writes,
              scale=cst(C_NEGH, p), tag="rexp")

    v_memset(consts[:], 0.0, ("consts",), tag="c0")
    for ci2, v in CONST_VALS.items():
        v_memset(consts[:, ci2:ci2 + 1], v, ("consts",), tag="c")
    v_memset(onescol[:], 1.0, ("onescol",))
    v_memset(onescol16[:], 1.0, ("onescol16",))
    v_memset(ones32[:], 1.0, ("ones32",))
    v_memset(onecol32[:], 1.0, ("onecol32",))
    v_memset(gacc[:], 0.0, ("gacc",))
    sch.op("gpsimd", lambda e: e.iota(piota_u[:], pattern=[[0, 1]], base=0,
           channel_multiplier=1), (), ("piota_u",), "iota")
    v_copy(piota_f[:], piota_u[:], ("piota_u",), ("piota_f",))
    sch.dma("sync", ident[:], ident_h, (), ("ident",), chan="w")
    sch.dma("sync", identf[:], identf_h, (), ("identf",), chan="w")
    sch.dma("sync", masksl[:], mask_h, (), ("masksl",), chan="w")

    stackA = contextlib.ExitStack()
    SB = lambda name, shape, dt: stackA.enter_context(nc.sbuf_tensor(name, shape, dt))
    cw1 = SB("cw1", [128, 8 * DIP], BF16)
    cwo = SB("cwo", [128, 16 * 1024], BF16)
    csc1 = SB("csc1", [128, 8 * HID], BF16)
    csc2 = SB("csc2", [128, 2], BF16)
    csum = SB("csum", [128, 8 * SD], BF16)
    cwcv = SB("cwcv", [128, 72], F32)
    ccb = SB("ccb", [128, 18], F32)
    nwcols = SB("nwcols", [128, 8], F32)
    swcols = SB("swcols", [128, 16], F32)
    dtbcol = SB("dtbcol", [32, 1], F32)
    acol = SB("acol", [32, 1], F32)
    acol2 = SB("acol2", [32, 1], F32)
    dfm = SB("dfm", [128, 16], F32)
    stackP0 = contextlib.ExitStack()
    wtmp = stackP0.enter_context(nc.sbuf_tensor("wtmp", [128, DIP], F32, side="right"))
    wtmpG = SB("wtmpG", [128, 1024], F32)
    lnwf = SB("lnwf", [128, 32], F32)
    lnbf = SB("lnbf", [128, 32], F32)

    sch.dma("sync", cwcv[:], cwv_h, (), ("cwcv",), chan="w")
    sch.dma("sync", ccb[:], ccb_h, (), ("ccb",), chan="w")
    sch.dma("sync", nwcols[:], nw_h, (), ("nwcols",), chan="w")
    sch.dma("sync", swcols[:], sw_h, (), ("swcols",), chan="w")
    sch.dma("sync", dtbcol[:], dtb_h, (), ("dtbcol",), chan="w")
    sch.dma("sync", dfm[:], dfm_h, (), ("dfm",), chan="w")
    sch.dma("sync", lnwf[:], lnw_h, (), ("lnwf",), chan="w")
    sch.dma("sync", lnbf[:], lnb_h, (), ("lnbf",), chan="w")
    v_copy(lnw16[:], lnwf[:], ("lnwf",), ("lnw16",))
    v_copy(lnwf32[:], lnwf[:], ("lnwf",), ("lnwf32",))
    v_copy(lnb16[:], lnbf[:], ("lnbf",), ("lnb16",))
    sch.dma("sync", acol2[:], alog_h, (), ("acol_r",), chan="w")
    s_act(acol[:], acol2[:], ACT.Exp, ("acol_r",), ("acol_e",))
    v_ts(acol2[:], acol[:], cst(C_NEG1, 32), None, ALU.mult, ALU.bypass,
         ("acol_e", "consts"), ("acol",))
    acol_ap = acol2

    for k in range(8):
        wt, wkey = wtmp, "wtmpA"
        sch.dma("sync", wt[:], w1_h[k * 128:(k + 1) * 128, :], (), (wkey,), chan="w")
        v_ts(cw1[:, k * DIP:(k + 1) * DIP], wt[:], nwcols[:, k:k + 1], None,
             ALU.mult, ALU.bypass, (wkey, "nwcols"), (f"cw1_{k}",))
    for k in range(16):
        wt, wkey = wtmp, "wtmpA"
        sch.dma("sync", wt[:, 0:1024], wo_h[k * 128:(k + 1) * 128, :], (), (wkey,), chan="w")
        v_ts(cwo[:, k * 1024:(k + 1) * 1024], wt[:, 0:1024], swcols[:, k:k + 1], None,
             ALU.mult, ALU.bypass, (wkey, "swcols"), (f"cwo_{k}",))
    for k in range(8):
        wt, wkey = wtmp, "wtmpA"
        sch.dma("sync", wt[:, 0:HID], wsc1_h[k * 128:(k + 1) * 128, :], (), (wkey,), chan="w")
        v_copy(csc1[:, k * HID:(k + 1) * HID], wt[:, 0:HID], (wkey,), (f"csc1_{k}",))
    for k in range(2):
        wt, wkey = wtmp, "wtmpA"
        sch.dma("sync", wt[:, 0:1], wsc2_h[k * 128:(k + 1) * 128, :], (), (wkey,), chan="w")
        v_copy(csc2[:, k:k + 1], wt[:, 0:1], (wkey,), (f"csc2_{k}",))
    for k in range(8):
        wt, wkey = wtmp, "wtmpA"
        sch.dma("sync", wt[:, 0:SD], wsm_h[k * 128:(k + 1) * 128, :], (), (wkey,), chan="w")
        v_copy(csum[:, k * SD:(k + 1) * SD], wt[:, 0:SD], (wkey,), (f"csum_{k}",))
    stackP0.close()

    xtok = SB("xtok", [128, D], F32)
    xntok = SB("xntok", [128, D], BF16)
    xnf = SB("xnf", [128, 2048], BF16)
    statc = SB("statc", [128, 32], F32)
    sz = SB("sz", [128, 16 * SC], BF16)
    xbcr = SB("xbcr", [128, 18 * (SC + 4)], BF16)
    xbcc = SB("xbcc", [128, 18 * SC], BF16)
    cacc = SB("cacc", [128, 2 * SC], F32)
    dte = SB("dte", [32, SC], F32)
    dtv = SB("dtv", [32, SC], F32)
    dav = SB("dav", [32, SC], F32)
    csv = SB("csv", [32, SC], F32)
    ndsv = SB("ndsv", [32, L], F32)
    dsv = SB("dsv", [32, L], F32)
    expcs32 = SB("expcs32", [32, L], F32)
    expcs16 = SB("expcs16", [32, L], BF16)
    csflat4 = [SB(f"csflat4_{i}", [1, 4 * L], F32) for i in range(2)]
    expflat4 = [SB(f"expflat4_{i}", [1, 4 * L], BF16) for i in range(2)]
    lamrow = SB("lamrow", [1, 32], F32)
    lamraw = SB("lamraw", [1, 32], F32)
    lamb = SB("lamb", [128, 32], F32)
    csT = SB("csT", [128, 32], F32)
    dtT16 = SB("dtT16", [128, 32], BF16)
    dsT16 = SB("dsT16", [128, 32], BF16)
    csbq = [SB(f"csbq{i}", [128, 512], F32) for i in range(2)]
    expq = [SB(f"expq{i}", [128, 512], BF16) for i in range(2)]
    xdt = SB("xdt", [128, 2048], BF16)
    xdwg = SB("xdwg", [128, 512], BF16)
    bmtok = SB("bmtok", [128, 128], BF16)
    gmask2 = SB("gmask2", [128, 256], BF16)
    xbcc2 = SB("xbcc2", [128, 256], BF16)
    ssegP = [SB(f"ssegP{i}", [128, 256], F32) for i in range(2)]
    esegP = [SB(f"esegP{i}", [128, 256], BF16) for i in range(2)]
    mtbP = [SB(f"mtbP{i}", [128, 256], BF16) for i in range(2)]
    cmsbP = [SB(f"cmsbP{i}", [128, 256], BF16) for i in range(2)]
    sall = SB("sall", [128, 2048], BF16)
    stmp = SB("stmp", [128, 512], BF16)
    dxsb = SB("dxsb", [128, 512], BF16)
    yfmsc = SB("yfmsc", [128, 16 * L], BF16)
    sqb = SB("sqb", [128, 2 * L], F32)
    rsr1 = SB("rsr1", [1, L], F32)
    rsr2 = SB("rsr2", [1, L], F32)
    rsrow = SB("rsrow", [1, L], F32)
    rsb = SB("rsb", [128, L], F32)
    ytok = SB("ytok", [128, D], F32)
    yfm16 = SB("yfm16", [128, 1024], BF16)
    s1b = SB("s1b", [128, 2 * HID], BF16)
    summT = SB("summT", [64, 128], BF16)
    summtk = SB("summtk", [128, SD], F32)

    v_memset(sall[:], 0.0, ("S0", "S1", "S2", "S3"))
    v_memset(xbcr[:], 0.0, tuple(f"tail{j}" for j in range(18)))

    GAMMA_TILES = [("fc", k) for k in range(8)] + [("pr", k) for k in range(32)]
    gamma_i = [0]
    gacc_par = [0, 0]

    def emit_gamma_tile():
        if gamma_i[0] >= len(GAMMA_TILES):
            return
        which, k = GAMMA_TILES[gamma_i[0]]
        gamma_i[0] += 1
        npiece = 4 if which == "fc" else 1
        srcw = wfcT_h if which == "fc" else wprT_h
        ci3 = 0 if which == "fc" else 1
        for pc in range(npiece):
            sch.dma("sync", wtmpG[:],
                    srcw[k * 128:(k + 1) * 128, pc * 1024:(pc + 1) * 1024], (), ("wtmpG",),
                    chan="g", nslots=2)
            v_red(statc[:, 30:31], wtmpG[:], ALU.add, ("wtmpG",), ("gred",), absval=True)
            par = gacc_par[ci3]
            v_tt(gacc[:, ci3 + 2 * (1 - par): ci3 + 2 * (1 - par) + 1],
                 gacc[:, ci3 + 2 * par: ci3 + 2 * par + 1], statc[:, 30:31], ALU.add,
                 ("gacc", "gred"), ("gacc",), tag="gacc")
            gacc_par[ci3] = 1 - par

    def ln_tile(ti, xkey, xview, outview, outkey, stat, skeys):
        k0 = skeys
        sch.op("vector", lambda e: e.bn_stats(stat[:, 0:6], xview[:, 0:512]), (xkey,), (k0 + "a",))
        sch.op("vector", lambda e: e.bn_stats(stat[:, 6:12], xview[:, 512:1024]), (xkey,), (k0 + "a",))
        sch.op("vector", lambda e: e.bn_aggr(stat[:, 12:14], stat[:, 0:12]), (k0 + "a",), (k0 + "b",))
        mean = stat[:, 12:13]
        var = stat[:, 13:14]
        veps = stat[:, 14:15]
        inv = stat[:, 15:16]
        v2 = stat[:, 16:17]
        r2 = stat[:, 17:18]
        stot = stat[:, 18:19]
        lnscr = stat[:, 19:20]
        v_ts(veps, var, cst(C_EPS), None, ALU.add, ALU.bypass, (k0 + "b", "consts"), (k0 + "c",))
        rsqrt2(inv, veps, lnscr, 128, (k0 + "c",), (k0 + "d",))
        v_ts(v2, var, inv, inv, ALU.mult, ALU.mult, (k0 + "b", k0 + "d"), (k0 + "e",))
        v_ts(r2, v2, cst(C_EPS), None, ALU.add, ALU.bypass, (k0 + "e", "consts"), (k0 + "f",))
        rsqrt2(v2, r2, lnscr, 128, (k0 + "f",), (k0 + "g",))
        v_ts(stot, inv, v2, None, ALU.mult, ALU.bypass, (k0 + "d", k0 + "g"), (k0 + "h",))
        v_ts(outview, xview, mean, stot, ALU.subtract, ALU.mult,
             (xkey, k0 + "b", k0 + "h"), (outkey,))

    sch.barrier()

    for sc_i in range(min(NSC, NSC_LIM)):
        t0 = sc_i * SC
        for ti in range(2):
            xk = "xtokA"
            sch.dma("sync", xtok[:],
                    x_h[t0 + ti * 128: t0 + (ti + 1) * 128, :], (), (xk,), chan="x")
            ln_tile(ti, xk, xtok[:], xntok[:], "xn1", statc, "lnA")
            for k in range(8):
                t_tr(PT[0][0:128, 0:128], xntok[:, k * 128:(k + 1) * 128],
                     128, ("xn1",), (KPT[0],))
                v_copy(xnf[:, k * 256 + ti * 128: k * 256 + (ti + 1) * 128],
                       PT[0][0:128, 0:128], (KPT[0],), (f"xnf{k}",))
        if KDBG and sc_i == 0:
            sch.dma("sync", dbg_h["xnf"], xnf[:], tuple(f"xnf{k}" for k in range(8)), (), chan="oo")
            sch.dma("sync", dbg_h["identd"], ident[:], ("ident",), (), chan="oo")
            sch.dma("sync", dbg_h["identfd"], identf[:], ("identf",), (), chan="oo")
            sch.dma("sync", dbg_h["constsd"], consts[:], ("consts",), (), chan="oo")
        for m in range(34):
            pb, pkey = (PB[m % 2], KPB[m % 2])
            for k in range(8):
                t_mm(pb[:, 0:SC], cw1[:, k * DIP + m * 128: k * DIP + (m + 1) * 128],
                     xnf[:, k * 256:(k + 1) * 256], k == 0, k == 7,
                     (f"cw1_{k}", f"xnf{k}"), (pkey,))
            if m < 16:
                s_act(sz[:, m * SC:(m + 1) * SC], pb[:, 0:SC], ACT.Silu, (pkey,), (f"sz{m}",))
            else:
                j = m - 16
                s_copy(xbcr[:, j * (SC + 4) + 4: (j + 1) * (SC + 4)], pb[:, 0:SC],
                       (pkey, f"tail{j}"), (f"xbcr{j}",))
        pbdt = PB[2]
        for k in range(8):
            t_mm(pbdt[0:32, 0:SC], cw1[:, k * DIP + 34 * 128: k * DIP + 34 * 128 + 32],
                 xnf[:, k * 256:(k + 1) * 256], k == 0, k == 7,
                 (f"cw1_{k}", f"xnf{k}"), (KPB[2],))
        for j in range(18):
            base = j * (SC + 4)
            a0, a1 = cacc[:, 0:SC], cacc[:, SC:2 * SC]
            v_ts(a0, xbcr[:, base + 1: base + 1 + SC], cwcv[:, j * 4: j * 4 + 1], None,
                 ALU.mult, ALU.bypass, (f"xbcr{j}", f"tail{j}", "cwcv"), ("cacc0",))
            v_stt(a1, xbcr[:, base + 2: base + 2 + SC], cwcv[:, j * 4 + 1: j * 4 + 2], a0,
                  ALU.mult, ALU.add, (f"xbcr{j}", f"tail{j}", "cacc0", "cwcv"), ("cacc1",))
            v_stt(a0, xbcr[:, base + 3: base + 3 + SC], cwcv[:, j * 4 + 2: j * 4 + 3], a1,
                  ALU.mult, ALU.add, (f"xbcr{j}", f"tail{j}", "cacc1", "cwcv"), ("cacc0",))
            v_stt(a1, xbcr[:, base + 4: base + 4 + SC], cwcv[:, j * 4 + 3: j * 4 + 4], a0,
                  ALU.mult, ALU.add, (f"xbcr{j}", "cacc0", "cwcv"), ("cacc1",))
            s_act(xbcc[:, j * SC:(j + 1) * SC], a1, ACT.Silu, ("cacc1", "ccb"), (f"xbcc{j}",),
                  bias=ccb[:, j:j + 1])
            v_copy(xbcr[:, base + 1: base + 4], xbcr[:, base + SC + 1: base + SC + 4],
                   (f"xbcr{j}",), (f"tail{j}",))
        s_act(dte[:], pbdt[0:32, 0:SC], ACT.Exp, (KPB[2], "dtbcol"), ("dte",), bias=dtbcol[:])
        s_act(dtv[:], dte[:], ACT.Ln, ("dte", "onecol32"), ("dtv",), bias=onecol32[:])
        v_ts(dav[:], dtv[:], acol_ap[:], None, ALU.mult, ALU.bypass, ("dtv", "acol"), ("dav",))

        emit_gamma_tile()
        emit_gamma_tile()
        emit_gamma_tile()

        for ci in range(2):
            ch = sc_i * 2 + ci
            cs_ = csv[:, ci * L:(ci + 1) * L]
            v_scan(cs_, ones32[:], dav[:, ci * L:(ci + 1) * L], 0.0, ALU.mult, ALU.add,
                   ("dav", "ones32"), (f"cs{ci}",))
            nds_ = ndsv[:]
            v_ts(nds_, cs_, cs_[:, L - 1:L], None, ALU.subtract, ALU.bypass,
                 (f"cs{ci}",), ("ndsv",))
            s_act(dsv[:], nds_, ACT.Exp, ("ndsv", "consts"),
                  ("dsv",), scale=cst(C_NEG1, 32))
            s_act(expcs32[:], cs_, ACT.Exp, (f"cs{ci}",), ("ecsv",))
            v_copy(expcs16[:], expcs32[:], ("ecsv",), ("ecs16",))
            t_trf(PB[5][0:128, 0:32], cs_, 32, (f"cs{ci}",), (KPB[5],))
            v_copy(csT[:], PB[5][0:128, 0:32], (KPB[5],), ("csT",))
            t_trf(PB[5][0:128, 32:64], dtv[:, ci * L:(ci + 1) * L], 32, ("dtv",), (KPB[5],))
            v_copy(dtT16[:], PB[5][0:128, 32:64], (KPB[5],), ("dtT16",))
            t_trf(PB[5][0:128, 64:96], dsv[:], 32, ("dsv",), (KPB[5],))
            v_copy(dsT16[:], PB[5][0:128, 64:96], (KPB[5],), ("dsT16",))

            for j in range(16):
                t_tr(PT[1][0:128, 0:128], xbcc[:, j * SC + ci * L: j * SC + (ci + 1) * L],
                     128, (f"xbcc{j}",), (KPT[1],))
                v_tt(xdt[:, j * 128:(j + 1) * 128].rearrange("p (h q) -> p h q", h=2),
                     PT[1][0:128, 0:128].rearrange("p (h q) -> p h q", h=2),
                     dtT16[:, 2 * j:2 * j + 2].unsqueeze(2).broadcast_to((128, 2, 64)),
                     ALU.mult, (KPT[1], "dtT16"), ("xdt",))
            t_tr(PT[1][0:128, 0:128], xbcc[:, 16 * SC + ci * L: 16 * SC + (ci + 1) * L],
                 128, ("xbcc16",), (KPT[1],))
            v_copy(bmtok[:], PT[1][0:128, 0:128], (KPT[1],), ("bmtok",))

            if KDBG and sc_i == 0 and ci == 0:
                sch.dma("sync", dbg_h["xdt2"], xdt[:], ("xdt",), (), chan="oo")
                sch.dma("sync", dbg_h["dtT16"], dtT16[:], ("dtT16",), (), chan="oo")
            t_mm(PB[2][:, 0:128], xbcc[:, 16 * SC + ci * L: 16 * SC + (ci + 1) * L],
                 xbcc[:, 17 * SC + ci * L: 17 * SC + (ci + 1) * L], True, True,
                 ("xbcc16", "xbcc17"), (KPB[2],))
            v_tt(gmask2[:, 0:128], PB[2][:, 0:128], masksl[:], ALU.mult,
                 (KPB[2], "masksl"), ("gmask2",))
            v_tt(gmask2[:, 128:256], PB[2][:, 0:128], masksl[:], ALU.mult,
                 (KPB[2], "masksl"), ("gmask2",))
            v_copy(xbcc2[:, 0:128], xbcc[:, 17 * SC + ci * L: 17 * SC + (ci + 1) * L],
                   ("xbcc17",), ("xbcc2",))
            v_copy(xbcc2[:, 128:256], xbcc[:, 17 * SC + ci * L: 17 * SC + (ci + 1) * L],
                   ("xbcc17",), ("xbcc2",))

            for h in range(H):
                q, hb, hh = h // 8, h % 2, h % 8
                qb = (h // 4) % 2
                if h % 4 == 0:
                    h4 = h // 4
                    sch.dma("gpsimd", csflat4[qb][0:1, :].rearrange("o (h2 l) -> o h2 l", h2=4),
                            cs_[h4 * 4:(h4 + 1) * 4, :], (f"cs{ci}",), (f"csflat{qb}",),
                            chan="f", tag="csflat")
                    sch.dma("gpsimd", expflat4[qb][0:1, :].rearrange("o (h2 l) -> o h2 l", h2=4),
                            expcs16[h4 * 4:(h4 + 1) * 4, :], ("ecs16",), (f"expflat{qb}",),
                            chan="f", tag="expflat")
                    s_act(lamrow[0:1, h4 * 4:(h4 + 1) * 4], csflat4[qb][0:1, L - 1::L],
                          ACT.Exp, (f"csflat{qb}",), ("lamrow",))
                    sch.dma("gpsimd", csbq[qb][:],
                            csflat4[qb][0:1, :].unsqueeze(1).broadcast_to((1, 128, 512)),
                            (f"csflat{qb}",), (f"csbq{qb}",), chan="b")
                    sch.dma("gpsimd", expq[qb][:],
                            expflat4[qb][0:1, :].unsqueeze(1).broadcast_to((1, 128, 512)),
                            (f"expflat{qb}",), (f"expq{qb}",), chan="b")
                if h % 8 == 4:
                    g8 = (h // 8) * 8
                    bcast(lamb[:, g8:g8 + 8], lamrow[0:1, g8:g8 + 8], 8,
                          ("lamrow",), ("lamb",))
                pb = (h // 2) % 2
                hw = h % 2
                v_ts(ssegP[pb][:, hw * 128:(hw + 1) * 128],
                     csbq[qb][:, (h % 4) * 128:(h % 4 + 1) * 128], csT[:, h:h + 1],
                     cst(C_ZERO), ALU.subtract, ALU.min,
                     (f"csbq{qb}", "csT", "consts"), (f"ssegP{pb}",))
                if hw == 1:
                    s_act(esegP[pb][:], ssegP[pb][:], ACT.Exp,
                          (f"ssegP{pb}",), (f"esegP{pb}",))
                    e0 = ((h - 1) % 4) * 128
                    v_tt(cmsbP[pb][:], xbcc2[:], expq[qb][:, e0:e0 + 256], ALU.mult,
                         ("xbcc2", f"expq{qb}"), (f"cmsbP{pb}",))
                    v_tt(mtbP[pb][:], esegP[pb][:], gmask2[:], ALU.mult,
                         (f"esegP{pb}", "gmask2"), (f"mtbP{pb}",))
                    if KDBG and sc_i == 0 and ci == 0 and h == 3:
                        sch.dma("sync", dbg_h["mtbh2"], mtbP[pb][:, 0:128], (f"mtbP{pb}",), (), chan="oo")
                        sch.dma("sync", dbg_h["ssegh2"], ssegP[pb][:, 0:128], (f"ssegP{pb}",), (), chan="oo")
                        sch.dma("sync", dbg_h["csbh2"], csbq[qb][:], (f"csbq{qb}",), (), chan="oo")
                        sch.dma("sync", dbg_h["csTd"], csT[:], ("csT",), (), chan="oo")
                    bank = PB[3 + (q % 2)]
                    bkey = KPB[3 + (q % 2)]
                    for h2 in (h - 1, h):
                        pp, ff = (h2 % 2) * 64, ((h2 // 2) % 4) * 128
                        sl = (h2 % 2) * 128
                        t_mm(bank[pp:pp + 64, ff:ff + 128], xdt[:, h2 * P:(h2 + 1) * P],
                             mtbP[pb][:, sl:sl + 128],
                             True, ch == 0, ("xdt", f"mtbP{pb}"), (bkey,), tag="ydiag",
                             tile_position=(0, pp))
                        if ch > 0:
                            t_mm(bank[pp:pp + 64, ff:ff + 128], sall[:, h2 * P:(h2 + 1) * P],
                                 cmsbP[pb][:, sl:sl + 128],
                                 False, True, (f"S{q}", f"cmsbP{pb}", f"mtbP{pb}"), (bkey,),
                                 tag="yoff", tile_position=(0, pp))
                if hh == 7:
                    g = q
                    for jj in range(4):
                        j = g * 4 + jj
                        v_ts(dxsb[:, jj * 128:(jj + 1) * 128],
                             xbcc[:, j * SC + ci * L: j * SC + (ci + 1) * L],
                             dfm[:, j:j + 1], None, ALU.mult, ALU.bypass,
                             (f"xbcc{j}", "dfm"), ("dxsb",))
                    v_tt(yfmsc[:, g * 512:(g + 1) * 512],
                         bank[:, 0:512], dxsb[:], ALU.add,
                         (bkey, "dxsb"), (f"yfm{g}",), tag="yevac")
                    v_tt(xdwg[:].rearrange("p (hq q) -> p hq q", hq=8),
                         xdt[:, g * 512:(g + 1) * 512].rearrange("p (hq q) -> p hq q", hq=8),
                         dsT16[:, g * 8:(g + 1) * 8].unsqueeze(2).broadcast_to((128, 8, 64)),
                         ALU.mult, ("xdt", "dsT16"), ("xdwg",))
                    t_mm(PB[5][:, 0:512], bmtok[:], xdwg[:], True, True,
                         ("bmtok", "xdwg"), (KPB[5],), tag="states")
                    v_tt(stmp[:].rearrange("p (hq q) -> p hq q", hq=8),
                         sall[:, g * 512:(g + 1) * 512].rearrange("p (hq q) -> p hq q", hq=8),
                         lamb[:, g * 8:(g + 1) * 8].unsqueeze(2).broadcast_to((128, 8, 64)),
                         ALU.mult, (f"S{g}", "lamb"), ("stmp",), tag="supd")
                    v_tt(sall[:, g * 512:(g + 1) * 512], stmp[:], PB[5][:, 0:512], ALU.add,
                         ("stmp", KPB[5]), (f"S{g}",), tag="supd2")

            yzb = xdt
            yop = xnf
            yview = yfmsc[:].rearrange("p (j l) -> p j l", l=L)
            zview = sz[:].rearrange("p (j l) -> p j l", l=SC)[:, :, ci * L:(ci + 1) * L]
            yzview = yzb[:].rearrange("p (j l) -> p j l", l=L)
            ykeys = tuple(f"yfm{g}" for g in range(4))
            zkeys = tuple(f"sz{m}" for m in range(16))
            v_tt(yzview, yview, zview, ALU.mult, ykeys + zkeys, ("yz", "xdt"), tag="yz")
            for j in range(16):
                sq = sqb[:, (j % 2) * L:(j % 2 + 1) * L]
                v_tt(sq, yzb[:, j * L:(j + 1) * L], yzb[:, j * L:(j + 1) * L], ALU.mult,
                     ("yz", "xdt"), (f"sq{j % 2}",), tag="sq")
                t_mm(PB[2][0:1, 128:256], onescol[:], sq, j == 0, j == 15,
                     (f"sq{j % 2}", "onescol"), (KPB[2],), tag="sqmm")
            v_ts(rsr1[:], PB[2][0:1, 128:256], cst(C_I2048, 1), cst(C_EPS, 1),
                 ALU.mult, ALU.add, (KPB[2], "consts"), ("rsr1",))
            s_act(rsr2[:], rsr1[:], ACT.Ln, ("rsr1",), ("rsr2",))
            s_act(rsrow[:], rsr2[:], ACT.Exp, ("rsr2", "consts"), ("rsrow",),
                  scale=cst(C_NEGH, 1))
            bcast(rsb[:], rsrow[:], 128, ("rsrow",), ("rsb",))
            v_tt(yop[:].rearrange("p (j l) -> p j l", l=128), yzview,
                 rsb[:].unsqueeze(1).broadcast_to((128, 16, 128)), ALU.mult,
                 ("yz", "xdt", "rsb"), ("yop",) + tuple(f"xnf{k}" for k in range(8)), tag="yop")
            if KDBG and sc_i == 0 and ci == 0:
                sch.dma("sync", dbg_h["statc"], statc[:, 0:20], ("lnAb", "lnAh"), (), chan="oo")
                for nm, ap_, keys in [("xntok", xntok[:], ("xn1",)),
                                      ("sz", sz[:], tuple(f"sz{m}" for m in range(16))),
                                      ("xbcc", xbcc[:], tuple(f"xbcc{j}" for j in range(18))),
                                      ("dtv", dtv[:], ("dtv",)), ("csv", csv[:, 0:L], ("cs0",)), 
                                      ("xdt", xdt[:], ("yz", "xdt")), ("yfmsc", yfmsc[:], tuple(f"yfm{g}" for g in range(4))),
                                      ("yop", yop[:], ("yop",)), ("dsv", dsv[:], ("dsv",)),
                                      ("expcs32", expcs32[:], ("ecsv",)), ("sall", sall[:], ("S0", "S1", "S2", "S3")),
                                      ("gmask", gmask2[:, 0:128], ("gmask2",)),
                                      ("mtb0", mtbP[1][:, 128:256], ("mtbP1",))]:
                    sch.dma("sync", dbg_h[nm], ap_, keys, (), chan="oo")
            for half in range(2):
                pbo, pkey = PB[half], KPB[half]
                for j in range(16):
                    t_mm(pbo[:, 0:512], yop[:, j * 128:(j + 1) * 128],
                         cwo[:, j * 1024 + half * 512: j * 1024 + (half + 1) * 512],
                         j == 0, j == 15,
                         ("yop", f"xnf{j // 2}", f"cwo_{j}"), (pkey,), tag="outproj")
                v_copy(ytok[:, half * 512:(half + 1) * 512],
                       pbo[:, 0:512], (pkey,), ("ytokk",))
            sch.dma("sync", y_spill[ch, :, :], ytok[:],
                    ("ytokk",), (), chan="ys")
            for k in range(8):
                t_trf(PB[5][0:128, 96:224], ytok[:, k * 128:(k + 1) * 128],
                      128, ("ytokk",), (KPB[5],))
                v_copy(yfm16[:, k * 128:(k + 1) * 128],
                       PB[5][0:128, 96:224], (KPB[5],), ("yfm16k",))
            sch.dma("sync", yfm_spill[ch, :, :], yfm16[:],
                    ("yfm16k",), (), chan="ys")
            for m2 in range(2):
                for k in range(8):
                    t_mm(PB[2][:, 256:384], csc1[:, k * HID + m2 * 128: k * HID + (m2 + 1) * 128],
                         yfm16[:, k * 128:(k + 1) * 128],
                         k == 0, k == 7, (f"csc1_{k}", "yfm16k"), (KPB[2],), tag="sc1")
                v_ts(s1b[:, m2 * 128:(m2 + 1) * 128], PB[2][:, 256:384], 0.0, None,
                     ALU.max, ALU.bypass, (KPB[2],), (f"s1b{m2}",), tag="relu")
            for k2 in range(2):
                t_mm(PB[2][0:1, 384:512], csc2[:, k2:k2 + 1], s1b[:, k2 * 128:(k2 + 1) * 128],
                     k2 == 0, k2 == 1, (f"csc2_{k2}", "s1b0", "s1b1"), (KPB[2],), tag="sc2")
            s_act(rsr1[:], PB[2][0:1, 384:512], ACT.Exp, (KPB[2], "consts", "rsr1"), ("rsr1",),
                  scale=cst(C_NEG1, 1))
            v_ts(rsr2[:], rsr1[:], cst(C_ONE, 1), None, ALU.add, ALU.bypass,
                 ("rsr1", "consts", "rsr2"), ("rsr2",))
            v_recip(scrow[:], rsr2[:], ("rsr2",), ("scrow",))
            sch.dma("sync", sm_in.ap()[T + 2 * ch: T + 2 * ch + 2, :],
                    scrow[:].rearrange("o (a b) -> o a b", a=2),
                    ("scrow",), ("sc_in",), chan="ys")
            for k in range(8):
                t_mm(PB[5][0:64, 224:352], csum[:, k * SD:(k + 1) * SD],
                     yfm16[:, k * 128:(k + 1) * 128],
                     k == 0, k == 7, (f"csum_{k}", "yfm16k"), (KPB[5],), tag="summ")
            v_copy(summT[:], PB[5][0:64, 224:352], (KPB[5],), ("summT",))
            t_tr(PT[1][0:128, 128:192], summT[:], 64, ("summT",), (KPT[1],))
            v_copy(summtk[:], PT[1][0:128, 128:192], (KPT[1],), ("summtk",))
            sch.dma("sync", sm_in.ap()[ch * 128:(ch + 1) * 128, :], summtk[:],
                    ("summtk",), (), chan="ys")

    if KDBG2:
        d3 = {}
        for nm, shp, dt_ in [("csc2", [128, 2], BF16), ("csum", [128, 8 * SD], BF16),
                             ("s1b", [128, 2 * HID], BF16), ("yfm16", [128, 1024], BF16),
                             ("summT", [64, 128], BF16), ("scrow", [1, 128], F32),
                             ("summtk", [128, SD], F32)]:
            d3[nm] = nc.dram_tensor("dbg3_" + nm, shp, dt_, kind="ExternalOutput").ap()
        sch.dma("sync", d3["csc2"], csc2[:], ("csc2_0", "csc2_1"), (), chan="oo")
        sch.dma("sync", d3["csum"], csum[:], tuple(f"csum_{k}" for k in range(8)), (), chan="oo")
        sch.dma("sync", d3["s1b"], s1b[:], ("s1b0", "s1b1"), (), chan="oo")
        sch.dma("sync", d3["yfm16"], yfm16[:], ("yfm16k",), (), chan="oo")
        sch.dma("sync", d3["summT"], summT[:], ("summT",), (), chan="oo")
        sch.dma("sync", d3["scrow"], scrow[:], ("scrow",), (), chan="oo")
        sch.dma("sync", d3["summtk"], summtk[:], ("summtk",), (), chan="oo")
        d3y = nc.dram_tensor("dbg3_ysp", [4 * 128, D], F32, kind="ExternalOutput").ap()
        sch.dma("sync", d3y, y_spill[0:4, :, :].rearrange("a b c -> (a b) c"), (), (), chan="oo")
        d3yf = nc.dram_tensor("dbg3_yfsp", [2 * 128, 1024], BF16, kind="ExternalOutput").ap()
        sch.dma("sync", d3yf, yfm_spill[0:2, :, :].rearrange("a b c -> (a b) c"), (), (), chan="oo")
    sch.oplimit = None
    while gamma_i[0] < len(GAMMA_TILES):
        emit_gamma_tile()
    for ci3 in range(2):
        last_col = ci3 + 2 * gacc_par[ci3]
        t_mm(PB[2][0:1, ci3:ci3 + 1], gacc[:, last_col:last_col + 1], onescol[:],
             True, True, ("gacc", "onescol"), (KPB[2],), tag="gfin")
    v_ts(grow_sb[0:1, 0:2], PB[2][0:1, 0:2], cst(C_IFC, 1), cst(C_1E6M, 1),
         ALU.mult, ALU.add, (KPB[2], "consts"), ("grow",))
    v_ts(grow_sb[0:1, 2:4], grow_sb[0:1, 0:2], cst(C_HALF, 1), None, ALU.mult, ALU.bypass,
         ("grow", "consts"), ("ghrow",))
    bcast(ghalf[:], grow_sb[0:1, 2:4], 2, ("ghrow",), ("ghalf",))
    v_ts(gnegh[:], ghalf[:], cst(C_NEG1), None, ALU.mult, ALU.bypass,
         ("ghalf", "consts"), ("gnegh",))

    sch.barrier()
    stackA.close()

    if KPHASE <= 1:
        dummy = nc.alloc_sbuf_tensor("dummy1", [128, D], F32)
        sch.op("vector", lambda e: e.memset(dummy[:], 0.0), (), ("dummyk",), "dm")
        for chd in range(NCH):
            sch.dma("sync", out_h[chd * 128:(chd + 1) * 128, :], dummy[:], ("dummyk",), (), chan="oo")
        sch.finalize()
        with contextlib.ExitStack() as es:
            sems = {}
            for e in ENGS:
                sems[e] = es.enter_context(nc.semaphore(f"sem_{e}"))
            dma_sems = [es.enter_context(nc.semaphore(f"dsem{i}")) for i in range(sch.n_dma_slots)]
            block = es.enter_context(nc.Block())
            stats = sch.emit(block, sems, dma_sems)
        return nc, stats

    if KSELCUT < 99:
        sch.oplimit = KSELCUT
    if os.environ.get("KVFIRST", "0") == "1":
        pass
    stackS = contextlib.ExitStack()
    SBs = lambda name, shape, dt: stackS.enter_context(nc.sbuf_tensor(name, shape, dt))
    ssc = SBs("ssc", [128, 256], F32)
    sscraw = SBs("sscraw", [128, 256], F32)
    fhib = SBs("fhib", [128, 8], F32)
    flob = SBs("flob", [128, 8], F32)
    vcmp = SBs("vcmp", [128, 256], F32)
    redc = SBs("redc", [128, 2], F32)
    scal2 = SBs("scal2", [1, 8], F32)
    mx8 = SBs("mx8", [128, 8], F32)
    cnd8 = SBs("cnd8", [8, 128], F32)
    cand = SBs("cand", [1, 1024], F32)
    candB = SBs("candB", [1, 1024], F32)
    th8b = SBs("th8b", [128, 8], F32)
    mi8 = SBs("mi8", [128, 8], U32)
    mif = SBs("mif", [128, 8], F32)
    gpos = SBs("gpos", [128, 8], F32)
    gpos2 = SBs("gpos2", [128, 8], F32)
    gtr = SBs("gtr", [8, 128], F32)
    gall = SBs("gall", [8, 8], F32)
    gcolf = SBs("gcolf", [NSEL, 1], F32)
    gcolu = SBs("gcolu", [NSEL, 1], U32)
    gcolf2 = SBs("gcolf2", [NSEL, 1], F32)
    radj = SBs("radj", [NSEL, 1], F32)
    radj2 = SBs("radj2", [NSEL, 1], F32)
    rc1 = SBs("rc1", [NSEL, 1], F32)
    poolraw = SBs("poolraw", [NSEL, SD], F32)
    poolfm = SBs("poolfm", [64, NSEL], BF16)
    vmrow = SBs("vmrow", [1, NPOOL], F32)
    mrow = SBs("mrow", [1, NPOOL], F32)
    brow = SBs("brow", [1, NPOOL], F32)
    ckw = SBs("ckw", [64, 64], BF16)
    cvw = SBs("cvw", [64, 1024], BF16)
    wk_f = SBs("wk_f", [64, 64], F32)
    wv_f = SBs("wv_f", [64, 1024], F32)

    ccsm = sch.dma_op("gpsimd", lambda e: e.collective_compute(
        "AllGather", ALU.bypass, replica_groups=[list(range(NCORES))],
        ins=[sm_in.ap().opt()], outs=[sm_out.ap().opt()]),
        ("sc_in",), ("sm_out_cc",), chan="cc", nslots=1, tag="ccsm", amt=1)
    for s in sch.chan_slots.get("ys", []):
        if sch.slot_last[s] is not None and sch.slot_last[s] is not ccsm:
            ccsm.deps.append((sch.slot_last[s], False))

    for rr in range(8):
        sch.dma("sync", sscraw[rr * 16:(rr + 1) * 16, :],
                sm_out.ap()[rr * (T + 64) + T: (rr + 1) * (T + 64), :]
                .rearrange("a b -> (a b)").rearrange("(p f) -> p f", p=16),
                ("sm_out_cc",), ("sscraw",), chan="e")
    for hblk in range(2):
        t_trf(PB[3][0:128, 0:128], sscraw[:, hblk * 128:(hblk + 1) * 128], 128,
              ("sscraw",), (KPB[3],))
        v_copy(ssc[:, hblk * 128:(hblk + 1) * 128], PB[3][0:128, 0:128], (KPB[3],), ("ssc",))
    v_ts(vcmp[:], ssc[:], cst(C_HALF), None, ALU.is_gt, ALU.bypass,
         ("ssc", "consts"), ("vcmp",))
    v_red(redc[:, 0:1], vcmp[:], ALU.add, ("vcmp",), ("redc",))
    v_red(redc[:, 1:2], ssc[:], ALU.add, ("ssc",), ("redc",))
    t_mm(PB[2][0:1, 0:1], redc[:, 0:1], onescol[:], True, True,
         ("redc", "onescol"), (KPB[2],), tag="cnt")
    t_mm(PB[2][0:1, 1:2], redc[:, 1:2], onescol[:], True, True,
         ("redc", "onescol"), (KPB[2],), tag="sum")
    v_copy(scal2[0:1, 0:2], PB[2][0:1, 0:2], (KPB[2],), ("scal2",))
    v_ts(scal2[0:1, 2:3], scal2[0:1, 1:2], cst(C_TAUS, 1), None, ALU.is_gt, ALU.bypass,
         ("scal2", "consts"), ("sc3",))
    v_ts(scal2[0:1, 3:4], scal2[0:1, 0:1], cst(C_ZERO, 1), None, ALU.is_gt, ALU.bypass,
         ("scal2", "consts"), ("sc4",))
    v_tt(scal2[0:1, 4:5], scal2[0:1, 2:3], scal2[0:1, 3:4], ALU.mult,
         ("sc3", "sc4"), ("cond1",))
    bcast(condcol[:], scal2[0:1, 4:5], 1, ("cond1",), ("condcol",))
    v_max(mx8[:], ssc[:], ("ssc",), ("mx8",))
    t_trf(PB[2][0:8, 128:256], mx8[:], 128, ("mx8",), (KPB[2],))
    v_copy(cnd8[:], PB[2][0:8, 128:256], (KPB[2],), ("cnd8",))
    sch.dma("gpsimd", cand[0:1, :].rearrange("o (j p) -> o j p", j=8), cnd8[:],
            ("cnd8",), ("cand",), chan="ec", nslots=1)
    for r in range(7):
        src, skey = (cand, "cand") if r % 2 == 0 else (candB, "candB")
        dst, dkey = (candB, "candB") if r % 2 == 0 else (cand, "cand")
        v_max(thetas[0:1, r * 8:(r + 1) * 8], src[0:1, :], (skey,), ("thetas",))
        v_matchrep(dst[0:1, :], thetas[0:1, r * 8:(r + 1) * 8], src[0:1, :], -1e30,
                   ("thetas", skey), (dkey,))
    for r in range(7):
        bcast(th8b[:], thetas[0:1, r * 8:(r + 1) * 8], 8, ("thetas",), ("th8b",))
        v_maxidx(mi8[:], th8b[:], ssc[:], ("th8b", "ssc"), ("mi8",))
        v_copy(mif[:], mi8[:], ("mi8",), ("mif",))
        v_ts(fhib[:], mif[:], cst(C_128), None, ALU.is_ge, ALU.bypass,
             ("mif", "consts"), ("fhib",))
        v_stt(flob[:], fhib[:], cst(C_N128), mif[:], ALU.mult, ALU.add,
              ("fhib", "mif", "consts"), ("flob",))
        v_ts(gpos2[:], flob[:], cst(C_256), piota_f[:], ALU.mult, ALU.add,
             ("flob", "consts", "piota_f"), ("gpos1",))
        v_stt(gpos[:], fhib[:], cst(C_128), gpos2[:], ALU.mult, ALU.add,
              ("gpos1", "fhib", "consts"), ("gpos",))
        t_trf(PB[2][0:8, 256:384], gpos[:], 128, ("gpos",), (KPB[2],))
        v_copy(gtr[:], PB[2][0:8, 256:384], (KPB[2],), ("gtr",))
        v_red(gall[:, r:r + 1], gtr[:], ALU.min, ("gtr",), ("gall",))
    for r in range(7):
        sch.dma("gpsimd", gcolf[r * 8:(r + 1) * 8, 0:1], gall[0:8, r:r + 1],
                ("gall",), ("gcolf",), chan=f"eg{r}", nslots=1)
    v_memset(radj[:], 0.0, ("radj",))
    for kk7 in range(1, 8):
        v_ts(rc1[:], gcolf[:], consts[0:NSEL, C_T1 + kk7 - 1: C_T1 + kk7], None,
             ALU.is_ge, ALU.bypass, ("gcolf", "consts"), ("rc1",))
        v_tt(radj2[:], radj[:], rc1[:], ALU.add, ("radj", "rc1"), ("radj",), tag="radd")
        radj, radj2 = radj2, radj
    v_stt(gcolf2[:], radj[:], cst(C_64, NSEL), gcolf[:], ALU.mult, ALU.add,
          ("radj", "gcolf", "consts"), ("gcolf2",))
    v_copy(gcolu[:], gcolf2[:], ("gcolf2",), ("gcolu",))
    sch.dma_op("gpsimd", lambda e: e.indirect_dma_start(
        out=poolraw[:], out_offset=None, in_=sm_out.ap(),
        in_offset=bass.IndirectOffsetOnAxis(ap=gcolu[:], axis=0)),
        ("gcolu", "sm_out_cc"), ("poolraw",), chan="ig", tag="gather")
    t_trf(PB[3][0:64, 128:128 + NSEL], poolraw[:], 56, ("poolraw",), (KPB[3],))
    v_copy(poolfm[:], PB[3][0:64, 128:128 + NSEL], (KPB[3],), ("poolfm",))
    v_ts(vmrow[:], thetas[0:1, 0:NPOOL], cst(C_HALF, 1), None, ALU.is_gt, ALU.bypass,
         ("thetas", "consts"), ("vmrow",))
    v_ts(mrow[:], vmrow[:], cst(C_QUART, 1), None, ALU.mult, ALU.bypass,
         ("vmrow", "consts"), ("mrow",))
    v_ts(brow[:], vmrow[:], cst(C_1E9, 1), cst(C_N1E9, 1), ALU.mult, ALU.add,
         ("vmrow", "consts"), ("brow",))
    bcast(mb128[:], mrow[:], NPOOL, ("mrow",), ("mb128",))
    bcast(bb128[:], brow[:], NPOOL, ("brow",), ("bb128",))
    sch.dma("sync", wk_f[:], wk_h, (), ("wk_f",), chan="e")
    sch.dma("sync", wv_f[:], wv_h, (), ("wv_f",), chan="e")
    v_copy(ckw[:], wk_f[:], ("wk_f",), ("ckw",))
    v_copy(cvw[:], wv_f[:], ("wv_f",), ("cvw",))
    t_mm(PB[3][0:64, 0:NSEL], ckw[:], poolfm[:], True, True, ("ckw", "poolfm"), (KPB[3],))
    v_copy(kkT[:], PB[3][0:64, 0:NSEL], (KPB[3],), ("kkT",))
    for half in range(2):
        t_mm(PB[4][0:NSEL, 0:512], poolfm[:], cvw[:, half * 512:(half + 1) * 512],
             True, True, ("poolfm", "cvw"), (KPB[4],))
        v_copy(vvtok[:, half * 512:(half + 1) * 512], PB[4][0:NSEL, 0:512],
               (KPB[4],), ("vvtok",))

    sch.oplimit = None
    if KDBG2:
        d2 = {}
        smin_dbg = nc.dram_tensor("dbg2_smin", [64, 64], F32, kind="ExternalOutput").ap()
        smsum_dbg = nc.dram_tensor("dbg2_smsum", [128, 64], F32, kind="ExternalOutput").ap()
        smout_dbg = nc.dram_tensor("dbg2_smout", [64, 64], F32, kind="ExternalOutput").ap()
        sscraw_dbg = nc.dram_tensor("dbg2_sscraw", [128, 256], F32, kind="ExternalOutput").ap()
        sch.dma("sync", smin_dbg, sm_in.ap()[T:T + 64, :], ("sc_in",), (), chan="oo")
        sch.dma("sync", smsum_dbg, sm_in.ap()[0:128, :], (), (), chan="oo")
        dbgcp = sch.dma("sync", smout_dbg, sm_out.ap()[T:T + 64, :], ("sm_out_cc",), (), chan="oo")
        sch.dma("sync", sscraw_dbg, sscraw[:], ("sscraw",), (), chan="oo")
        for nm, shp, dt_ in [("ssc", [128, 256], F32), ("cand", [1, 1024], F32),
                             ("thetas", [1, 64], F32), ("gcolf", [NSEL, 1], F32),
                             ("gcolf2", [NSEL, 1], F32), ("poolraw", [NSEL, SD], F32),
                             ("kkT", [64, NSEL], BF16), ("vvtok", [NSEL, D], BF16),
                             ("scal2", [1, 8], F32), ("mrow", [1, NPOOL], F32),
                             ("gall", [8, 8], F32)]:
            d2[nm] = nc.dram_tensor("dbg2_" + nm, shp, dt_, kind="ExternalOutput").ap()
        for nm, ap_, keys in [("ssc", ssc[:], ("ssc",)), ("cand", cand[0:1, :], ("cand",)),
                              ("thetas", thetas[:], ("thetas",)), ("gcolf", gcolf[:], ("gcolf",)),
                              ("gcolf2", gcolf2[:], ("gcolf2",)), ("poolraw", poolraw[:], ("poolraw",)),
                              ("kkT", kkT[:], ("kkT",)), ("vvtok", vvtok[:], ("vvtok",)),
                              ("scal2", scal2[:], ("scal2", "cond1", "sc3", "sc4")),
                              ("mrow", mrow[:], ("mrow",)), ("gall", gall[:], ("gall",))]:
            sch.dma("sync", d2[nm], ap_, keys, (), chan="oo")
    sch.barrier()
    stackS.close()

    if KPHASE <= 2:
        dummy = nc.alloc_sbuf_tensor("dummy2", [128, D], F32)
        sch.op("vector", lambda e: e.memset(dummy[:], 0.0), (), ("dummyk",), "dm")
        for chd in range(NCH):
            sch.dma("sync", out_h[chd * 128:(chd + 1) * 128, :], dummy[:], ("dummyk",), (), chan="oo")
        sch.finalize()
        with contextlib.ExitStack() as es:
            sems = {}
            for e in ENGS:
                sems[e] = es.enter_context(nc.semaphore(f"sem_{e}"))
            dma_sems = [es.enter_context(nc.semaphore(f"dsem{i}")) for i in range(sch.n_dma_slots)]
            block = es.enter_context(nc.Block())
            stats = sch.emit(block, sems, dma_sems)
        return nc, stats

    stackB1 = contextlib.ExitStack()
    SB1 = lambda name, shape, dt: stackB1.enter_context(nc.sbuf_tensor(name, shape, dt))
    cgate = SB1("cgate", [128, 16 * 1024], BF16)
    cq = SB1("cq", [128, 8 * SD], BF16)
    ytokB = SB1("ytokB", [128, 4 * D], F32)
    yfmB = SB1("yfmB", [128, 4 * 1024], BF16)
    xB = SB1("xB", [128, 4 * D], F32)
    qfm = SB1("qfm", [64, 512], BF16)
    lgb = SB1("lgb", [128, 2 * NPOOL], F32)
    ebuf = SB1("ebuf", [128, NPOOL], F32)
    esum = SB1("esum", [128, 2], F32)
    smb = SB1("smb", [128, NPOOL], BF16)
    smTb = SB1("smTb", [64, 4 * 128], BF16)
    retB = SB1("retB", [128, 8 * 512], BF16)
    gb = SB1("gb", [128, 512], BF16)
    t2m = SB1("t2m", [128, 512], BF16)
    t2B = SB1("t2B", [128, 8 * 512], BF16)
    t2tok = SB1("t2tok", [128, 4 * 1024], BF16)
    statB = SB1("statB", [128, 24], F32)
    xpxB = SB1("xpxB", [128, D], F32)
    ysumB = SB1("ysumB", [128, D], F32)
    xmoB = SB1("xmoB", [128, D], F32)
    wtq = SB1("wtq", [128, 4096], F32)
    wqt = SB1("wqt", [128, 4096], BF16)
    wqneg = SB1("wqneg", [128, 4096], BF16)
    wqt2 = SB1("wqt2", [128, 4096], BF16)
    posb = SB1("posb", [128, 4096], F32)
    wtmpB = SB1("wtmpB", [128, SD], F32)
    lnbw2 = SB1("lnbw2", [128, 64], BF16)

    for k in range(16):
        sch.dma("sync", wtq[:, 0:1024], wg_h[k * 128:(k + 1) * 128, :], (), ("wtq",), chan="w")
        v_copy(cgate[:, k * 1024:(k + 1) * 1024], wtq[:, 0:1024], ("wtq",), (f"cg_{k}",))
    for k in range(8):
        sch.dma("sync", wtmpB[:], wq_h[k * 128:(k + 1) * 128, :], (), ("wtqq",), chan="w")
        v_copy(cq[:, k * SD:(k + 1) * SD], wtmpB[:], ("wtqq",), (f"cq_{k}",))
    for k in range(32):
        v_copy(lnbw2[:, 2 * k:2 * k + 1], lnb16[:, k:k + 1], ("lnb16",), ("lnbw2",))
        v_copy(lnbw2[:, 2 * k + 1:2 * k + 2], lnw16[:, k:k + 1], ("lnw16",), ("lnbw2",))

    QUANT_TILES = [("fc", k) for k in range(8)] + [("pr", k) for k in range(32)]
    quant_i = [0]

    def emit_quant_tile():
        if quant_i[0] >= len(QUANT_TILES):
            return
        which, k = QUANT_TILES[quant_i[0]]
        quant_i[0] += 1
        n = 4096 if which == "fc" else 1024
        src = wfcT_h if which == "fc" else wprT_h
        dst = wqfc_hbm if which == "fc" else wqpr_hbm
        gc = 0 if which == "fc" else 1
        sch.dma("sync", wtq[:, 0:n], src[k * 128:(k + 1) * 128, :], (), ("wtq",),
                chan="g", nslots=2)
        v_ts(posb[:, 0:n], wtq[:, 0:n], ghalf[:, gc:gc + 1], None, ALU.is_gt, ALU.bypass,
             ("wtq", "ghalf"), ("posb",))
        v_ts(wqneg[:, 0:n], wtq[:, 0:n], gnegh[:, gc:gc + 1], None, ALU.is_lt, ALU.bypass,
             ("wtq", "gnegh"), ("wqnegk",))
        v_tt(wqt[:, 0:n], posb[:, 0:n], wqneg[:, 0:n], ALU.subtract,
             ("posb", "wqnegk"), ("wqt",))
        if which == "pr":
            for half in range(2):
                t_mm(PB[4 + half][0:2, 0:512], lnbw2[:, 2 * k:2 * k + 2],
                     wqt[:, half * 512:(half + 1) * 512], k == 0, k == 31,
                     ("wqt", "lnbw2"), (KPB[4 + half],), tag="cmm")
            v_ts(wqt2[:, 0:n], wqt[:, 0:n], lnwf32[:, k:k + 1], None, ALU.mult, ALU.bypass,
                 ("wqt", "lnwf32"), ("wqt2",))
            sch.dma("sync", dst[k, :, :], wqt2[:, 0:n], ("wqt2",), (), chan="qo")
        else:
            sch.dma("sync", dst[k, :, :], wqt[:, 0:n], ("wqt",), (), chan="qo")

    for i2 in range(min(8, NSC_LIM)):
        ch0 = i2 * 4
        for tt in range(4):
            ch = ch0 + tt
            sch.dma("sync", ytokB[:, tt * D:(tt + 1) * D], y_spill[ch, :, :], (), (f"ytB{tt}",), chan="x")
            sch.dma("sync", yfmB[:, tt * 1024:(tt + 1) * 1024], yfm_spill[ch, :, :], (), (f"yfB{tt}",), chan="x")
            sch.dma("sync", xB[:, tt * D:(tt + 1) * D], x_h[ch * 128:(ch + 1) * 128, :], (), (f"xB{tt}",), chan="x")
        ykeys = tuple(f"yfB{tt}" for tt in range(4))
        for k in range(8):
            t_mm(PB[2][0:64, 0:512], cq[:, k * SD:(k + 1) * SD],
                 yfmB[:].rearrange("p (t4 n) -> p t4 n", t4=4)[:, :, k * 128:(k + 1) * 128],
                 k == 0, k == 7, (f"cq_{k}",) + ykeys, (KPB[2],), tag="q")
        v_copy(qfm[:], PB[2][0:64, 0:512], (KPB[2],), ("qfm",))
        for tt in range(4):
            t_mm(PB[3][:, 0:NPOOL], qfm[:, tt * 128:(tt + 1) * 128], kkT[0:64, 0:NPOOL],
                 True, True, ("qfm", "kkT"), (KPB[3],), tag="lg")
            v_tt(lgb[:, 0:NPOOL], PB[3][:, 0:NPOOL], mb128[:], ALU.mult,
                 (KPB[3], "mb128"), ("lg1",))
            v_tt(lgb[:, NPOOL:2 * NPOOL], lgb[:, 0:NPOOL], bb128[:], ALU.add,
                 ("lg1", "bb128"), ("lg2",))
            v_red(statB[:, 0:1], lgb[:, NPOOL:2 * NPOOL], ALU.max, ("lg2",), ("negmx",), negate=True)
            s_act(ebuf[:], lgb[:, NPOOL:2 * NPOOL], ACT.Exp, ("lg2", "negmx"), ("ebuf",),
                  bias=statB[:, 0:1], accum=esum[:, 0:1])
            v_recip(esum[:, 1:2], esum[:, 0:1], ("ebuf",), ("esr",))
            v_ts(smb[:], ebuf[:], esum[:, 1:2], None, ALU.mult, ALU.bypass,
                 ("ebuf", "esr"), ("smb",))
            t_tr(PT[0][0:NPOOL, 0:128], smb[:], 128, ("smb",), (KPT[0],))
            v_copy(smTb[0:NPOOL, tt * 128:(tt + 1) * 128], PT[0][0:NPOOL, 0:128],
                   (KPT[0],), ("smTb",))
        for k in range(8):
            t_mm(PB[k % 2][:, 0:512], vvtok[0:NPOOL, k * 128:(k + 1) * 128],
                 smTb[0:NPOOL, 0:512], True, True, ("vvtok", "smTb"), (KPB[k % 2],), tag="ret")
            v_copy(retB[:, k * 512:(k + 1) * 512], PB[k % 2][:, 0:512],
                   (KPB[k % 2],), (f"ret{k}",))
        for m in range(8):
            pb, pkey = PB[m % 2], KPB[m % 2]
            for k in range(8):
                t_mm(pb[:, 0:512], cgate[:, k * 1024 + m * 128: k * 1024 + m * 128 + 128],
                     yfmB[:].rearrange("p (t4 n) -> p t4 n", t4=4)[:, :, k * 128:(k + 1) * 128],
                     k == 0, False, (f"cg_{k}",) + ykeys, (pkey,), tag="gate")
            for k in range(8):
                t_mm(pb[:, 0:512], cgate[:, (8 + k) * 1024 + m * 128: (8 + k) * 1024 + m * 128 + 128],
                     retB[:, k * 512:(k + 1) * 512], False, k == 7,
                     (f"cg_{8 + k}", f"ret{k}"), (pkey,), tag="gate2")
            s_act(gb[:], pb[:, 0:512], ACT.Sigmoid, (pkey,), ("gb",))
            v_tt(t2m[:], gb[:], retB[:, m * 512:(m + 1) * 512], ALU.mult,
                 ("gb", f"ret{m}"), ("t2m",))
            v_ts(t2B[:, m * 512:(m + 1) * 512], t2m[:], condcol[:], None, ALU.mult, ALU.bypass,
                 ("t2m", "condcol"), (f"t2B{m}",))
        for m in range(8):
            for tt in range(4):
                t_tr(PT[1][0:128, 0:128], t2B[:, m * 512 + tt * 128: m * 512 + (tt + 1) * 128],
                     128, (f"t2B{m}",), (KPT[1],))
                v_copy(t2tok[:, tt * 1024 + m * 128: tt * 1024 + (m + 1) * 128],
                       PT[1][0:128, 0:128], (KPT[1],), (f"t2tok{tt}",))
        for tt in range(4):
            ch = ch0 + tt
            sch.op("vector", lambda e, tt=tt: e.bn_stats(statB[:, 2:8], xB[:, tt * D: tt * D + 512]),
                   (f"xB{tt}",), ("stB",))
            sch.op("vector", lambda e, tt=tt: e.bn_stats(statB[:, 8:14], xB[:, tt * D + 512:(tt + 1) * D]),
                   (f"xB{tt}",), ("stB",))
            sch.op("vector", lambda e: e.bn_aggr(statB[:, 14:16], statB[:, 2:14]),
                   ("stB",), ("stB",))
            mean, var = statB[:, 14:15], statB[:, 15:16]
            veps, inv = statB[:, 16:17], statB[:, 17:18]
            q1, q2, lnscr = statB[:, 18:19], statB[:, 19:20], statB[:, 20:21]
            v_ts(veps, var, cst(C_EPS), None, ALU.add, ALU.bypass, ("stB", "consts"), ("stB1",))
            rsqrt2(inv, veps, lnscr, 128, ("stB1",), ("stB2",))
            v_ts(q1, inv, cst(C_ONE), None, ALU.add, ALU.bypass, ("stB2", "consts"), ("stB3",))
            v_ts(q2, mean, inv, None, ALU.mult, ALU.bypass, ("stB", "stB2"), ("stB4",))
            v_ts(xpxB[:], xB[:, tt * D:(tt + 1) * D], q1, q2,
                 ALU.mult, ALU.subtract, (f"xB{tt}", "stB3", "stB4"), ("xpxB",))
            v_tt(ysumB[:], ytokB[:, tt * D:(tt + 1) * D], t2tok[:, tt * 1024:(tt + 1) * 1024],
                 ALU.add, (f"ytB{tt}", f"t2tok{tt}"), ("ysumB",))
            v_tt(xmoB[:], xpxB[:], ysumB[:], ALU.add, ("xpxB", "ysumB"), ("xmoB",))
            sch.dma("sync", xmid_spill[ch, :, :], xmoB[:], ("xmoB",), (), chan="xo")
        for _ in range(5):
            emit_quant_tile()

    while quant_i[0] < len(QUANT_TILES):
        emit_quant_tile()
    crow_st = SB1("crow_st", [2, 1024], F32)
    v_copy(crow_st[0:2, 0:512], PB[4][0:2, 0:512], (KPB[4],), ("c1row",))
    v_copy(crow_st[0:2, 512:1024], PB[5][0:2, 0:512], (KPB[5],), ("c2row",))
    sch.dma("sync", c_hbm[:, :], crow_st[:], ("c1row", "c2row"), ("c_hbm",), chan="xo")

    sch.barrier()
    stackB1.close()

    if KPHASE <= 3:
        dummy = nc.alloc_sbuf_tensor("dummy3", [128, D], F32)
        sch.op("vector", lambda e: e.memset(dummy[:], 0.0), (), ("dummyk",), "dm")
        for chd in range(NCH):
            sch.dma("sync", out_h[chd * 128:(chd + 1) * 128, :], dummy[:], ("dummyk",), (), chan="oo")
        sch.finalize()
        with contextlib.ExitStack() as es:
            sems = {}
            for e in ENGS:
                sems[e] = es.enter_context(nc.semaphore(f"sem_{e}"))
            dma_sems = [es.enter_context(nc.semaphore(f"dsem{i}")) for i in range(sch.n_dma_slots)]
            block = es.enter_context(nc.Block())
            stats = sch.emit(block, sems, dma_sems)
        return nc, stats

    stackB2 = contextlib.ExitStack()
    SB2 = lambda name, shape, dt: stackB2.enter_context(nc.sbuf_tensor(name, shape, dt))
    wqfcS = SB2("wqfcS", [128, 8 * 4096], BF16)
    wqprS = SB2("wqprS", [128, 32 * 1024], BF16)
    c1b = SB2("c1b", [128, 1024], F32)
    c2b = SB2("c2b", [128, 1024], F32)
    xmB2 = SB2("xmB2", [128, 2 * D], F32)
    x2tok = SB2("x2tok", [128, 2 * D], BF16)
    x2fm = SB2("x2fm", [128, 8 * 256], BF16)
    gbuf = SB2("gbuf", [128, 32 * 256], BF16)
    sqbB = SB2("sqbB", [128, 2 * 256], BF16)
    atmp = SB2("atmp", [128, 2 * 256], BF16)
    statB2 = SB2("statB2", [128, 24], F32)
    murow = SB2("murow", [1, 256], F32)
    m2row = SB2("m2row", [1, 256], F32)
    varrow = SB2("varrow", [1, 256], F32)
    var2row = SB2("var2row", [1, 256], F32)
    lnsrow = SB2("lnsrow", [1, 256], F32)
    rsrow2 = SB2("rsrow2", [1, 256], F32)
    muarow = SB2("muarow", [1, 256], F32)
    nmuarow = SB2("nmuarow", [1, 256], F32)
    rsb2 = SB2("rsb2", [128, 256], F32)
    nmacol = SB2("nmacol", [128, 2], F32)
    corrb = SB2("corrb", [128, 512], F32)
    corr2b = SB2("corr2b", [128, 512], F32)
    outsb = SB2("outsb", [128, 2 * D], F32)

    for k in range(8):
        sch.dma("sync", wqfcS[:, k * 4096:(k + 1) * 4096], wqfc_hbm[k, :, :], (), (f"wf_{k}",), chan="w")
    for k in range(32):
        sch.dma("sync", wqprS[:, k * 1024:(k + 1) * 1024], wqpr_hbm[k, :, :], (), (f"wp_{k}",), chan="w")
    sch.dma("gpsimd", c1b[:], c_hbm[0:1, :].unsqueeze(1).broadcast_to((1, 128, 1024)),
            ("c_hbm",), ("c1b",), chan="f")
    sch.dma("gpsimd", c2b[:], c_hbm[1:2, :].unsqueeze(1).broadcast_to((1, 128, 1024)),
            ("c_hbm",), ("c2b",), chan="f")

    for i3 in range(min(16, NSC_LIM)):
        ch0 = i3 * 2
        for tt2 in range(2):
            sch.dma("sync", xmB2[:, tt2 * D:(tt2 + 1) * D], xmid_spill[ch0 + tt2, :, :],
                    (), (f"xm2_{tt2}",), chan="x")
            sch.op("vector", lambda e, tt2=tt2: e.bn_stats(statB2[:, 0:6],
                   xmB2[:, tt2 * D: tt2 * D + 512]), (f"xm2_{tt2}",), ("st2",))
            sch.op("vector", lambda e, tt2=tt2: e.bn_stats(statB2[:, 6:12],
                   xmB2[:, tt2 * D + 512:(tt2 + 1) * D]), (f"xm2_{tt2}",), ("st2",))
            sch.op("vector", lambda e: e.bn_aggr(statB2[:, 12:14], statB2[:, 0:12]),
                   ("st2",), ("st2",))
            mean, var = statB2[:, 12:13], statB2[:, 13:14]
            veps, inv, lnscr = statB2[:, 14:15], statB2[:, 15:16], statB2[:, 16:17]
            v_ts(veps, var, cst(C_EPS), None, ALU.add, ALU.bypass, ("st2", "consts"), ("st2a",))
            rsqrt2(inv, veps, lnscr, 128, ("st2a",), ("st2b",))
            v_ts(x2tok[:, tt2 * D:(tt2 + 1) * D], xmB2[:, tt2 * D:(tt2 + 1) * D], mean, inv,
                 ALU.subtract, ALU.mult, (f"xm2_{tt2}", "st2", "st2b"), (f"x2_{tt2}",))
            for k in range(8):
                t_tr(PT[0][0:128, 0:128], x2tok[:, tt2 * D + k * 128: tt2 * D + (k + 1) * 128],
                     128, (f"x2_{tt2}",), (KPT[0],))
                v_copy(x2fm[:, k * 256 + tt2 * 128: k * 256 + (tt2 + 1) * 128],
                       PT[0][0:128, 0:128], (KPT[0],), (f"x2f{k}",))
        x2keys = tuple(f"x2f{k}" for k in range(8))
        for j in range(32):
            pb, pkey = PB[j % 2], KPB[j % 2]
            for k in range(8):
                t_mm(pb[:, 0:256], wqfcS[:, k * 4096 + j * 128: k * 4096 + (j + 1) * 128],
                     x2fm[:, k * 256:(k + 1) * 256], k == 0, k == 7,
                     (f"wf_{k}",) + x2keys, (pkey,), tag="hmm")
            s_act(gbuf[:, j * 256:(j + 1) * 256], pb[:, 0:256], ACT.Gelu, (pkey,), (f"g{j}",))
            sq = sqbB[:, (j % 2) * 256:(j % 2 + 1) * 256]
            v_tt(sq, gbuf[:, j * 256:(j + 1) * 256], gbuf[:, j * 256:(j + 1) * 256],
                 ALU.mult, (f"g{j}",), (f"sqB{j % 2}",), tag="sqB")
            t_mm(PB[4][0:1, 0:256], onescol16[:], gbuf[:, j * 256:(j + 1) * 256],
                 j == 0, j == 31, (f"g{j}", "onescol16"), (KPB[4],), tag="sgmm")
            t_mm(PB[5][0:1, 0:256], onescol16[:], sq, j == 0, j == 31,
                 (f"sqB{j % 2}", "onescol16"), (KPB[5],), tag="sqmm2")
        v_ts(murow[:], PB[4][0:1, 0:256], cst(C_I4096, 1), None, ALU.mult, ALU.bypass,
             (KPB[4], "consts"), ("murow",))
        v_tt(m2row[:], murow[:], murow[:], ALU.mult, ("murow",), ("m2row",))
        v_stt(varrow[:], PB[5][0:1, 0:256], cst(C_I4096, 1), m2row[:], ALU.mult, ALU.subtract,
              (KPB[5], "m2row", "consts"), ("varrow",))
        v_ts(var2row[:], varrow[:], cst(C_EPS, 1), None, ALU.add, ALU.bypass,
             ("varrow", "consts"), ("var2row",))
        s_act(lnsrow[:], var2row[:], ACT.Ln, ("var2row",), ("lnsrow",))
        s_act(rsrow2[:], lnsrow[:], ACT.Exp, ("lnsrow", "consts"), ("rsrow2",),
              scale=cst(C_NEGH, 1))
        v_tt(muarow[:], murow[:], rsrow2[:], ALU.mult, ("murow", "rsrow2"), ("muarow",))
        v_ts(nmuarow[:], muarow[:], cst(C_NEG1, 1), None, ALU.mult, ALU.bypass,
             ("muarow", "consts"), ("nmuarow",))
        bcast(rsb2[:], rsrow2[:], 256, ("rsrow2",), ("rsb2",))
        for tt2 in range(2):
            t_trf(PB[2][0:128, 0:1], nmuarow[0:1, tt2 * 128:(tt2 + 1) * 128], 1,
                  ("nmuarow",), (KPB[2],))
            v_copy(nmacol[:, tt2:tt2 + 1], PB[2][0:128, 0:1], (KPB[2],), ("nmacol",))
        for j in range(32):
            at = atmp[:, (j % 2) * 256:(j % 2 + 1) * 256]
            v_tt(at, gbuf[:, j * 256:(j + 1) * 256], rsb2[:], ALU.mult,
                 (f"g{j}", "rsb2"), (f"at{j % 2}",), tag="aA")
            for tt2 in range(2):
                for dh in range(2):
                    pb, pkey = PB[tt2 * 2 + dh], KPB[tt2 * 2 + dh]
                    t_mm(pb[:, 0:512], at[:, tt2 * 128:(tt2 + 1) * 128],
                         wqprS[:, j * 1024 + dh * 512: j * 1024 + (dh + 1) * 512],
                         j == 0, j == 31, (f"at{j % 2}", f"wp_{j}"), (pkey,), tag="pmm")
        for tt2 in range(2):
            for dh in range(2):
                pb, pkey = PB[tt2 * 2 + dh], KPB[tt2 * 2 + dh]
                v_stt(corrb[:], c2b[:, dh * 512:(dh + 1) * 512], nmacol[:, tt2:tt2 + 1],
                      c1b[:, dh * 512:(dh + 1) * 512], ALU.mult, ALU.add,
                      ("c2b", "c1b", "nmacol"), ("corrb",))
                v_tt(corr2b[:], corrb[:], pb[:, 0:512], ALU.add, ("corrb", pkey), ("corr2b",))
                v_tt(outsb[:, tt2 * D + dh * 512: tt2 * D + (dh + 1) * 512], corr2b[:],
                     xmB2[:, tt2 * D + dh * 512: tt2 * D + (dh + 1) * 512], ALU.add,
                     ("corr2b", f"xm2_{tt2}"), (f"osb{tt2}",))
            sch.dma("sync", out_h[(ch0 + tt2) * 128:(ch0 + tt2 + 1) * 128, :],
                    outsb[:, tt2 * D:(tt2 + 1) * D], (f"osb{tt2}",), (), chan="oo")

    sch.finalize()
    stackB2.close()

    with contextlib.ExitStack() as es:
        sems = {}
        for e in ENGS:
            sems[e] = es.enter_context(nc.semaphore(f"sem_{e}"))
        dma_sems = [es.enter_context(nc.semaphore(f"dsem{i}")) for i in range(sch.n_dma_slots)]
        block = es.enter_context(nc.Block())
        stats = sch.emit(block, sems, dma_sems)
    return nc, stats


_CACHE = {}


def _get_program():
    if "nc" not in _CACHE:
        nc, stats = build_program()
        _CACHE["nc"] = nc
        _CACHE["stats"] = stats
    return _CACHE["nc"]


def stage_inputs(**inputs):
    x = np.asarray(inputs["L0_attn_x"], dtype=np.float32)
    conv_w = np.asarray(inputs["L0_attn_conv_w"], dtype=np.float32)
    conv_b = np.asarray(inputs["L0_attn_conv_b"], dtype=np.float32)
    norm_w = np.asarray(inputs["L0_attn_norm_w"], dtype=np.float32)
    ssm_w = np.asarray(inputs["L0_attn_ssm_norm_w"], dtype=np.float32)
    ln_w = np.asarray(inputs["L0_ffn_ln_w"], dtype=np.float32)
    ln_b = np.asarray(inputs["L0_ffn_ln_b"], dtype=np.float32)
    Dh = np.asarray(inputs["L0_attn_D"], dtype=np.float32)

    common = {
        "w_inproj": np.ascontiguousarray(inputs["L0_attn_in_proj_w"], dtype=np.float32),
        "w_out": np.ascontiguousarray(inputs["L0_attn_out_proj_w"], dtype=np.float32),
        "w_sc1": np.ascontiguousarray(inputs["L0_attn_scorer_w1"], dtype=np.float32),
        "w_sc2": np.ascontiguousarray(inputs["L0_attn_scorer_w2"], dtype=np.float32),
        "w_summ": np.ascontiguousarray(inputs["L0_attn_summ_w"], dtype=np.float32),
        "w_q": np.ascontiguousarray(inputs["L0_attn_q_w"], dtype=np.float32),
        "w_k": np.ascontiguousarray(inputs["L0_attn_k_w"], dtype=np.float32),
        "w_v": np.ascontiguousarray(inputs["L0_attn_v_w"], dtype=np.float32),
        "w_gate": np.ascontiguousarray(inputs["L0_attn_gate_w"], dtype=np.float32),
        "w_fcT": np.ascontiguousarray(np.asarray(inputs["L0_ffn_W_fc"], dtype=np.float32).T),
        "w_projT": np.ascontiguousarray(np.asarray(inputs["L0_ffn_W_proj"], dtype=np.float32).T),
        "conv_w_t": np.ascontiguousarray(
            conv_w.reshape(18, 128, 4).transpose(1, 0, 2).reshape(128, 72)),
        "conv_b_t": np.ascontiguousarray(conv_b.reshape(18, 128).T),
        "norm_w_t": np.ascontiguousarray(norm_w.reshape(8, 128).T),
        "ssm_w_t": np.ascontiguousarray(ssm_w.reshape(16, 128).T),
        "ln_w_t": np.ascontiguousarray(ln_w.reshape(32, 128).T),
        "ln_b_t": np.ascontiguousarray(ln_b.reshape(32, 128).T),
        "dt_bias_t": np.ascontiguousarray(
            np.asarray(inputs["L0_attn_dt_bias"], dtype=np.float32).reshape(32, 1)),
        "a_log_t": np.ascontiguousarray(
            np.asarray(inputs["L0_attn_A_log"], dtype=np.float32).reshape(32, 1)),
        "d_fm_t": np.ascontiguousarray(np.concatenate(
            [np.broadcast_to(Dh[0::2], (64, 16)), np.broadcast_to(Dh[1::2], (64, 16))], axis=0)),
        "ident": np.eye(128, dtype=ml_dtypes.bfloat16),
        "identf": np.eye(128, dtype=np.float32),
        "mask_sl": np.ascontiguousarray(np.triu(np.ones((128, 128), dtype=np.float32))),
    }
    in_maps = [dict(common, x=np.ascontiguousarray(x[b])) for b in range(NCORES)]
    return in_maps


def kernel(**inputs):
    in_maps = stage_inputs(**inputs)
    nc = _get_program()
    res = run_bass_kernel_spmd(nc, in_maps, core_ids=list(range(NCORES)))
    out = np.stack([np.asarray(r["out"], dtype=np.float32) for r in res.results], axis=0)
    return out


# version 14
# speedup vs baseline: 1.5988x; 1.1646x over previous
import os
import contextlib
import numpy as np
import ml_dtypes
import concourse.bass as bass
import concourse.mybir as mybir
from concourse.bass_utils import run_bass_kernel_spmd

F32 = mybir.dt.float32
BF16 = mybir.dt.bfloat16
U32 = mybir.dt.uint32
ALU = mybir.AluOpType
ACT = mybir.ActivationFunctionType
AX = mybir.AxisListType

import os
NSC_LIM = int(os.environ.get("KNSC", "16"))
NO_CC = os.environ.get("KNOCC", "0") == "1"
NO_IG = os.environ.get("KNOIG", "0") == "1"
NO_MX = os.environ.get("KNOMX", "0") == "1"
KPHASE = int(os.environ.get("KPHASE", "4"))
KSELCUT = int(os.environ.get("KSELCUT", "99"))
KDBG = os.environ.get("KDBG", "0") == "1"
KDBG2 = os.environ.get("KDBG2", "0") == "1"
T = 4096
D = 1024
L = 128
NCH = T // L
SC = 256
NSC = T // SC
H = 32
P = 64
DIN = 2048
DIP = 4384
HID = 256
SD = 64
NPOOL = 50
NSEL = 56
NCORES = 8
ENGS = ("tensor", "vector", "scalar", "gpsimd", "sync")


class Sched:

    class Op:
        __slots__ = ("eng", "fn", "reads", "writes", "idx", "deps", "inc",
                     "is_dma", "dma_sem", "dma_seq", "dma_amt", "tag",
                     "need_drain", "need_local", "drain_before")

        def __init__(self, eng, fn, reads, writes, idx, tag):
            self.eng, self.fn = eng, fn
            self.reads, self.writes = reads, writes
            self.idx = idx
            self.deps = []
            self.inc = False
            self.need_drain = False
            self.need_local = False
            self.drain_before = False
            self.is_dma = False
            self.dma_sem = None
            self.dma_seq = 0
            self.dma_amt = 16
            self.tag = tag

    def __init__(self, n_dma_slots=None):
        if n_dma_slots is None:
            n_dma_slots = int(os.environ.get('KSLOTS', '64'))
        self.ops = {e: [] for e in ENGS}
        self.last_writer = {}
        self.readers = {}
        self.n_dma_slots = n_dma_slots
        self.slot_seq = [0] * n_dma_slots
        self.slot_last = [None] * n_dma_slots
        self.chan_slots = {}
        self.chan_uses = {}
        self._next_slot = 0
        self.oplimit = None

    def _muted(self):
        if self.oplimit is None:
            return False
        self.oplimit -= 1
        if self.oplimit < 0:
            return True
        return False

    def _record(self, eng, fn, reads, writes, tag):
        if self._muted():
            return Sched.Op(eng, None, (), (), -1, "muted")
        op = Sched.Op(eng, fn, tuple(reads), tuple(writes), len(self.ops[eng]), tag)
        deps = []
        for k in op.reads:
            w = self.last_writer.get(k)
            if w is not None:
                deps.append((w, 0))
        for k in op.writes:
            w = self.last_writer.get(k)
            if w is not None:
                deps.append((w, 1))
            for r in self.readers.get(k, ()):
                deps.append((r, 2))
        seen = {}
        for d, kind in deps:
            key = (d.eng, d.idx)
            if key not in seen:
                seen[key] = len(op.deps)
                op.deps.append((d, kind))
            else:
                i = seen[key]
                if kind < op.deps[i][1]:
                    op.deps[i] = (d, kind)
        for k in op.writes:
            self.last_writer[k] = op
            self.readers[k] = []
        for k in op.reads:
            self.readers.setdefault(k, []).append(op)
        self.ops[eng].append(op)
        return op

    def op(self, eng, fn, reads=(), writes=(), tag=""):
        return self._record(eng, fn, reads, writes, tag)

    def dma_op(self, eng, fn, reads=(), writes=(), chan="d", nslots=2, tag="dmaop", amt=16):
        slots = self.chan_slots.setdefault(chan, [])
        if len(slots) < nslots and self._next_slot < self.n_dma_slots:
            slots.append(self._next_slot)
            self._next_slot += 1
        use = self.chan_uses.get(chan, 0)
        self.chan_uses[chan] = use + 1
        slot = slots[use % len(slots)]
        op = self._record(eng, fn, reads, writes, tag)
        if op.idx < 0:
            return op
        op.is_dma = True
        op.dma_sem = slot
        op.dma_amt = amt
        prev = self.slot_last[slot]
        if prev is not None:
            op.deps.append((prev, False))
        self.slot_seq[slot] += 1
        op.dma_seq = self.slot_seq[slot]
        self.slot_last[slot] = op
        return op

    def dma(self, eng, out_ap, in_ap, reads=(), writes=(), chan="d", nslots=4, tag="dma"):
        nslots = min(nslots, int(os.environ.get('KNSLOT', '4')))
        slots = self.chan_slots.setdefault(chan, [])
        if len(slots) < nslots and self._next_slot < self.n_dma_slots:
            slots.append(self._next_slot)
            self._next_slot += 1
        use = self.chan_uses.get(chan, 0)
        self.chan_uses[chan] = use + 1
        slot = slots[use % len(slots)]

        def fn(e, out_ap=out_ap, in_ap=in_ap):
            return e.dma_start(out=out_ap, in_=in_ap)

        op = self._record(eng, fn, reads, writes, tag)
        if op.idx < 0:
            return op
        op.is_dma = True
        op.dma_sem = slot
        prev = self.slot_last[slot]
        if prev is not None:
            op.deps.append((prev, False))
        self.slot_seq[slot] += 1
        op.dma_seq = self.slot_seq[slot]
        self.slot_last[slot] = op
        return op

    def barrier(self):
        marks = {}
        for e in ENGS:
            m = None
            for op in reversed(self.ops[e]):
                if op.fn is not None and not op.is_dma:
                    m = op
                    break
            if m is not None:
                m.inc = True
                m.need_drain = True
            marks[e] = m
        for e in ENGS:
            w = self._record(e, None, (), (), "barrier_w")
            for o in ENGS:
                if o != e and marks[o] is not None:
                    w.deps.append((marks[o], False))
            for s in range(self.n_dma_slots):
                if self.slot_last[s] is not None:
                    w.deps.append((self.slot_last[s], False))

    def finalize(self):
        w = self._record("sync", (lambda en: en.nop()), (), (), "final")
        for s in range(self.n_dma_slots):
            if self.slot_last[s] is not None:
                w.deps.append((self.slot_last[s], False))

    def emit(self, block, sems, dma_sems):
        paranoid = os.environ.get("KPAR", "0") == "1"
        audit = os.environ.get("KAUDIT", "0") == "1"
        if audit:
            paranoid = True
        local_edges = {e: [] for e in ENGS}
        for e in ENGS:
            for op in self.ops[e]:
                if paranoid and not op.is_dma and op.fn is not None:
                    op.inc = True
                    if audit:
                        op.need_drain = True
                for d, kind in op.deps:
                    cross = d.eng != e or op.is_dma
                    if (not d.is_dma) and (cross or paranoid):
                        d.inc = True
                    if kind == 2 or d.is_dma:
                        continue
                    if cross:
                        d.need_drain = True
                    elif kind == 0:
                        local_edges[e].append((d.idx, op.idx))
        consumers = {}
        for e in ENGS:
            for op in self.ops[e]:
                for d, kind in op.deps:
                    if kind != 2 and not d.is_dma and (d.eng != op.eng or op.is_dma):
                        consumers.setdefault((d.eng, d.idx), []).append((op, d.idx))
        if not audit:
            for e in ENGS:
                for op in reversed(self.ops[e]):
                    if not op.need_drain:
                        continue
                    ok = True
                    for c, _pi in consumers.get((e, op.idx), ()):
                        if not any((d.eng == e and not d.is_dma and kind != 2
                                    and d.idx > op.idx and d.need_drain)
                                   for d, kind in c.deps):
                            ok = False
                            break
                    if ok and consumers.get((e, op.idx)):
                        op.need_drain = False
        nodrain = set(os.environ.get("KND", "tensor,scalar").split(",")) - {""}
        for e in ENGS:
            pts = sorted(o.idx + 0.5 for o in self.ops[e]
                         if (o.need_drain and e not in nodrain)
                         or (audit and o.inc))
            import bisect as _bi
            for p, c in sorted(local_edges[e], key=lambda pc: pc[1]):
                i = _bi.bisect_right(pts, p)
                if i < len(pts) and pts[i] < c:
                    continue
                self.ops[e][c].drain_before = True
                _bi.insort(pts, c - 0.5)
        inccount = {}
        for e in ENGS:
            c = 0
            for op in self.ops[e]:
                if op.inc and not op.is_dma:
                    c += 1
                inccount[(e, op.idx)] = c

        def make_section(ename):
            ops_list = self.ops[ename]
            eng_nodrain = ename in nodrain

            def section(eng):
                waited = {}
                for op in ops_list:
                    want = {}
                    for d, kind in op.deps:
                        if d.is_dma:
                            k = ("dma", d.dma_sem)
                            want[k] = max(want.get(k, 0), d.dma_amt * d.dma_seq)
                        elif d.eng != ename or paranoid or op.is_dma:
                            want[d.eng] = max(want.get(d.eng, 0), inccount[(d.eng, d.idx)])
                    for k, v in sorted(want.items(), key=str):
                        if waited.get(k, -1) < v:
                            if isinstance(k, tuple):
                                eng.wait_ge(dma_sems[k[1]], v)
                            else:
                                eng.wait_ge(sems[k], v)
                            waited[k] = v
                    if op.fn is None:
                        continue
                    if op.drain_before:
                        eng.drain(fusable=True)
                    instr = op.fn(eng)
                    if op.is_dma:
                        instr.then_inc(dma_sems[op.dma_sem], op.dma_amt)
                    elif op.inc:
                        if op.need_drain and not eng_nodrain:
                            eng.maybe_drain_then_inc((sems[ename], 1), fusable=True)
                        else:
                            if instr is None or ename == "gpsimd":
                                try:
                                    instr = eng.engine_nop()
                                except Exception:
                                    instr = eng.nop()
                            instr.then_inc(sems[ename], 1)
            return section

        for ename in ENGS:
            getattr(block, ename)(make_section(ename))
        return {e: len(self.ops[e]) for e in ENGS} | {
            "incs": sum(1 for e in ENGS for o in self.ops[e] if o.inc)}


def build_program():
    nc = bass.Bass("TRN2", target_bir_lowering=False,
                   detect_race_conditions=os.environ.get("KRACE", "0") == "1")
    sch = Sched()

    def dramin(name, shape, dt=F32):
        return nc.dram_tensor(name, list(shape), dt, kind="ExternalInput").ap()

    x_h = dramin("x", (T, D))
    w1_h = dramin("w_inproj", (D, DIP))
    wo_h = dramin("w_out", (DIN, D))
    wsc1_h = dramin("w_sc1", (D, HID))
    wsc2_h = dramin("w_sc2", (HID, 1))
    wsm_h = dramin("w_summ", (D, SD))
    wq_h = dramin("w_q", (D, SD))
    wk_h = dramin("w_k", (SD, SD))
    wv_h = dramin("w_v", (SD, D))
    wg_h = dramin("w_gate", (DIN, D))
    wfcT_h = dramin("w_fcT", (D, 4096))
    wprT_h = dramin("w_projT", (4096, D))
    cwv_h = dramin("conv_w_t", (128, 72))
    ccb_h = dramin("conv_b_t", (128, 18))
    nw_h = dramin("norm_w_t", (128, 8))
    sw_h = dramin("ssm_w_t", (128, 16))
    lnw_h = dramin("ln_w_t", (128, 32))
    lnb_h = dramin("ln_b_t", (128, 32))
    dtb_h = dramin("dt_bias_t", (32, 1))
    alog_h = dramin("a_log_t", (32, 1))
    dfm_h = dramin("d_fm_t", (128, 16))
    ident_h = dramin("ident", (128, 128), BF16)
    identf_h = dramin("identf", (128, 128), F32)
    mask_h = dramin("mask_sl", (128, 128))

    out_h = nc.dram_tensor("out", [T, D], F32, kind="ExternalOutput").ap()

    y_spill = nc.dram_tensor("y_spill", [NCH, 128, D], F32).ap()
    dbg_h = {}
    if KDBG:
        for nm, shp, dt_ in [("xntok", [128, D], BF16), ("xnf", [128, 2048], BF16),
                             ("sz", [128, 16 * SC], BF16), ("xbcc", [128, 18 * SC], BF16),
                             ("dtv", [32, SC], F32), ("csv", [32, L], F32), ("statc", [128, 20], F32),
                             ("xdt", [128, 2048], BF16), ("yfmsc", [128, 2048], BF16),
                             ("yop", [128, 2048], BF16), ("dsv", [32, L], F32),
                             ("expcs32", [32, L], F32), ("sall", [128, 2048], BF16),
                             ("gmask", [128, 128], BF16), ("mtb0", [128, 128], BF16),
                             ("xdt2", [128, 2048], BF16), ("dtT16", [128, 32], BF16),
                             ("mtbh2", [128, 128], BF16), ("ssegh2", [128, 128], F32),
                             ("identd", [128, 128], BF16), ("identfd", [128, 128], F32),
                             ("constsd", [128, 28], F32),
                             ("csbh2", [128, 512], F32), ("csTd", [128, 32], F32)]:
            dbg_h[nm] = nc.dram_tensor("dbg_" + nm, shp, dt_, kind="ExternalOutput").ap()
    yfm_spill = nc.dram_tensor("yfm_spill", [NCH, 128, 1024], BF16).ap()
    xmid_spill = nc.dram_tensor("xmid_spill", [NCH, 128, D], F32).ap()
    wqfc_hbm = nc.dram_tensor("wqfc_hbm", [8, 128, 4096], BF16).ap()
    wqpr_hbm = nc.dram_tensor("wqpr_hbm", [32, 128, D], BF16).ap()
    sm_in = nc.dram_tensor("sm_in", [T + 64, SD], F32)
    sm_out = nc.dram_tensor("sm_out", [8 * (T + 64), SD], F32)
    c_hbm = nc.dram_tensor("c_hbm", [2, 1024], F32).ap()

    A = nc.alloc_sbuf_tensor
    consts = A("constssb", [128, 28], F32)
    (C_HALF, C_EPS, C_NEG1, C_NEGH, C_I2048, C_I4096, C_ONE, C_TAUS,
     C_1E9, C_N1E9, C_128, C_QUART, C_IFC, C_ZERO, C_1E6M, C_256, C_N128,
     C_T1, C_T2, C_T3, C_T4, C_T5, C_T6, C_T7, C_64) = range(25)
    CONST_VALS = {C_HALF: 0.5, C_EPS: 1e-5, C_NEG1: -1.0, C_NEGH: -0.5,
                  C_I2048: 1.0 / 2048, C_I4096: 1.0 / 4096, C_ONE: 1.0,
                  C_TAUS: 0.3 * 32768.0, C_1E9: 1e9, C_N1E9: -1e9,
                  C_128: 128.0, C_QUART: 0.25, C_IFC: float(2.0 ** -22),
                  C_ZERO: 0.0, C_1E6M: 1e-6, C_256: 256.0, C_N128: -128.0,
                  C_T1: 4096.0, C_T2: 8192.0, C_T3: 12288.0, C_T4: 16384.0,
                  C_T5: 20480.0, C_T6: 24576.0, C_T7: 28672.0, C_64: 64.0}
    ident = A("identsb", [128, 128], BF16)
    identf = A("identfsb", [128, 128], F32)
    masksl = A("masksl", [128, 128], F32)
    onescol = A("onescol", [128, 1], F32)
    onescol16 = A("onescol16", [128, 1], BF16)
    ones32 = A("ones32", [32, 128], F32)
    onecol32 = A("onecol32", [32, 1], F32)
    piota_f = A("piota_f", [128, 1], F32)
    piota_u = A("piota_u", [128, 1], U32)
    scrow = A("scrow", [1, 128], F32)
    thetas = A("thetas", [1, 64], F32)
    kkT = A("kkT", [64, NSEL], BF16)
    vvtok = A("vvtok", [NSEL, D], BF16)
    mb128 = A("mb128", [128, NPOOL], F32)
    bb128 = A("bb128", [128, NPOOL], F32)
    condcol = A("condcol", [128, 1], F32)
    ghalf = A("ghalf", [128, 2], F32)
    gnegh = A("gnegh", [128, 2], F32)
    lnw16 = A("lnw16", [128, 32], BF16)
    lnwf32 = A("lnwf32", [128, 32], F32)
    lnb16 = A("lnb16", [128, 32], BF16)
    gacc = A("gacc", [128, 4], F32)
    grow_sb = A("grow_sb", [1, 4], F32)

    PB = [nc.alloc_psum_tensor(f"pb{i}", [128, 512], F32) for i in range(6)]
    PT = [nc.alloc_psum_tensor(f"pt{i}", [128, 1024], BF16) for i in range(2)]
    KPB = [f"PB{i}" for i in range(6)]
    KPT = [f"PT{i}" for i in range(2)]

    def cst(i, p=128):
        return consts[0:p, i:i + 1]

    def v_tt(out, a, b, op, reads, writes, tag="tt"):
        sch.op("vector", lambda e: e.tensor_tensor(out, a, b, op), reads, writes, tag)

    def v_ts(out, in0, s1, s2, op0, op1, reads, writes, tag="ts"):
        sch.op("vector", lambda e: e.tensor_scalar(out, in0, s1, s2, op0, op1), reads, writes, tag)

    def v_stt(out, in0, s, in1, op0, op1, reads, writes, tag="stt"):
        sch.op("vector", lambda e: e.scalar_tensor_tensor(out, in0, s, in1, op0, op1), reads, writes, tag)

    def v_copy(out, in_, reads, writes, tag="cp"):
        sch.op("vector", lambda e: e.tensor_copy(out, in_), reads, writes, tag)

    def v_red(out, in_, op, reads, writes, negate=False, absval=False, tag="red"):
        sch.op("vector", lambda e: e.tensor_reduce(out, in_, AX.X, op,
               apply_absolute_value=absval, negate=negate), reads, writes, tag)

    def v_scan(out, d0, d1, init, op0, op1, reads, writes, tag="scan"):
        sch.op("vector", lambda e: e.tensor_tensor_scan(out, d0, d1, init, op0, op1), reads, writes, tag)

    def v_memset(ap_, val, writes, tag="memset"):
        sch.op("vector", lambda e: e.memset(ap_, val), (), writes, tag)

    def v_recip(out, in_, reads, writes, tag="recip"):
        sch.op("vector", lambda e: e.reciprocal(out, in_), reads, writes, tag)

    def v_max(out, in_, reads, writes, tag="max"):
        sch.op("vector", lambda e: e.max(out, in_), reads, writes, tag)

    def v_maxidx(out, in_max, in_vals, reads, writes, tag="mxi"):
        sch.op("vector", lambda e: e.max_index(out, in_max, in_vals), reads, writes, tag)

    def v_matchrep(out, rep, vals, imm, reads, writes, tag="mr"):
        sch.op("vector", lambda e: e.match_replace(out, in_to_replace=rep, in_values=vals,
               imm_value=imm), reads, writes, tag)

    def s_act(out, in_, func, reads, writes, bias=None, scale=None, accum=None, tag="act"):
        kw = {}
        if bias is not None:
            kw["bias"] = bias
        if scale is not None:
            kw["scale"] = scale
        if accum is not None:
            kw["accum_out"] = accum
        sch.op("scalar", lambda e: e.activation(out, in_, func, **kw), reads, writes, tag)

    def s_copy(out, in_, reads, writes, tag="scp"):
        sch.op("scalar", lambda e: e.copy(out, in_), reads, writes, tag)

    def t_mm(out, lhsT, rhs, start, stop, reads, writes, tag="mm", tile_position=None):
        sch.op("tensor", lambda e: e.matmul(out, lhsT, rhs, start=start, stop=stop,
               tile_position=tile_position), reads, writes, tag)

    def t_tr(out, in_, pin, reads, writes, tag="tr"):
        sch.op("tensor", lambda e: e.transpose(out, in_, ident[0:pin, 0:pin]),
               tuple(reads) + ("ident",), writes, tag)

    def t_trf(out, in_, pin, reads, writes, tag="trf"):
        sch.op("tensor", lambda e: e.transpose(out, in_, identf[0:pin, 0:pin]),
               tuple(reads) + ("identf",), writes, tag)

    def bcast(dst, src_row, n, reads, writes, chan="f"):
        sch.dma("gpsimd", dst, src_row.unsqueeze(1).broadcast_to((1, 128, n)),
                reads, writes, chan=chan, tag="bcast")

    def rsqrt2(out, in_, scratch, p, reads, writes):
        s_act(scratch, in_, ACT.Ln, reads, ("rsq_t",) , tag="rln")
        s_act(out, scratch, ACT.Exp, ("rsq_t", "consts"), writes,
              scale=cst(C_NEGH, p), tag="rexp")

    v_memset(consts[:], 0.0, ("consts",), tag="c0")
    for ci2, v in CONST_VALS.items():
        v_memset(consts[:, ci2:ci2 + 1], v, ("consts",), tag="c")
    v_memset(onescol[:], 1.0, ("onescol",))
    v_memset(onescol16[:], 1.0, ("onescol16",))
    v_memset(ones32[:], 1.0, ("ones32",))
    v_memset(onecol32[:], 1.0, ("onecol32",))
    v_memset(gacc[:], 0.0, ("gacc",))
    sch.op("gpsimd", lambda e: e.iota(piota_u[:], pattern=[[0, 1]], base=0,
           channel_multiplier=1), (), ("piota_u",), "iota")
    v_copy(piota_f[:], piota_u[:], ("piota_u",), ("piota_f",))
    sch.dma("sync", ident[:], ident_h, (), ("ident",), chan="w")
    sch.dma("sync", identf[:], identf_h, (), ("identf",), chan="w")
    sch.dma("sync", masksl[:], mask_h, (), ("masksl",), chan="w")

    stackA = contextlib.ExitStack()
    SB = lambda name, shape, dt: stackA.enter_context(nc.sbuf_tensor(name, shape, dt))
    cw1 = SB("cw1", [128, 8 * DIP], BF16)
    cwo = SB("cwo", [128, 16 * 1024], BF16)
    csc1 = SB("csc1", [128, 8 * HID], BF16)
    csc2 = SB("csc2", [128, 2], BF16)
    csum = SB("csum", [128, 8 * SD], BF16)
    cwcv = SB("cwcv", [128, 72], F32)
    ccb = SB("ccb", [128, 18], F32)
    nwcols = SB("nwcols", [128, 8], F32)
    swcols = SB("swcols", [128, 16], F32)
    dtbcol = SB("dtbcol", [32, 1], F32)
    acol = SB("acol", [32, 1], F32)
    acol2 = SB("acol2", [32, 1], F32)
    dfm = SB("dfm", [128, 16], F32)
    stackP0 = contextlib.ExitStack()
    wtmp = stackP0.enter_context(nc.sbuf_tensor("wtmp", [128, DIP], F32, side="right"))
    wtmpG = SB("wtmpG", [128, 1024], F32)
    lnwf = SB("lnwf", [128, 32], F32)
    lnbf = SB("lnbf", [128, 32], F32)

    sch.dma("sync", cwcv[:], cwv_h, (), ("cwcv",), chan="w")
    sch.dma("sync", ccb[:], ccb_h, (), ("ccb",), chan="w")
    sch.dma("sync", nwcols[:], nw_h, (), ("nwcols",), chan="w")
    sch.dma("sync", swcols[:], sw_h, (), ("swcols",), chan="w")
    sch.dma("sync", dtbcol[:], dtb_h, (), ("dtbcol",), chan="w")
    sch.dma("sync", dfm[:], dfm_h, (), ("dfm",), chan="w")
    sch.dma("sync", lnwf[:], lnw_h, (), ("lnwf",), chan="w")
    sch.dma("sync", lnbf[:], lnb_h, (), ("lnbf",), chan="w")
    v_copy(lnw16[:], lnwf[:], ("lnwf",), ("lnw16",))
    v_copy(lnwf32[:], lnwf[:], ("lnwf",), ("lnwf32",))
    v_copy(lnb16[:], lnbf[:], ("lnbf",), ("lnb16",))
    sch.dma("sync", acol2[:], alog_h, (), ("acol_r",), chan="w")
    s_act(acol[:], acol2[:], ACT.Exp, ("acol_r",), ("acol_e",))
    v_ts(acol2[:], acol[:], cst(C_NEG1, 32), None, ALU.mult, ALU.bypass,
         ("acol_e", "consts"), ("acol",))
    acol_ap = acol2

    for k in range(8):
        wt, wkey = wtmp, "wtmpA"
        sch.dma("sync", wt[:], w1_h[k * 128:(k + 1) * 128, :], (), (wkey,), chan="w")
        v_ts(cw1[:, k * DIP:(k + 1) * DIP], wt[:], nwcols[:, k:k + 1], None,
             ALU.mult, ALU.bypass, (wkey, "nwcols"), (f"cw1_{k}",))
    for k in range(16):
        wt, wkey = wtmp, "wtmpA"
        sch.dma("sync", wt[:, 0:1024], wo_h[k * 128:(k + 1) * 128, :], (), (wkey,), chan="w")
        v_ts(cwo[:, k * 1024:(k + 1) * 1024], wt[:, 0:1024], swcols[:, k:k + 1], None,
             ALU.mult, ALU.bypass, (wkey, "swcols"), (f"cwo_{k}",))
    for k in range(8):
        wt, wkey = wtmp, "wtmpA"
        sch.dma("sync", wt[:, 0:HID], wsc1_h[k * 128:(k + 1) * 128, :], (), (wkey,), chan="w")
        v_copy(csc1[:, k * HID:(k + 1) * HID], wt[:, 0:HID], (wkey,), (f"csc1_{k}",))
    for k in range(2):
        wt, wkey = wtmp, "wtmpA"
        sch.dma("sync", wt[:, 0:1], wsc2_h[k * 128:(k + 1) * 128, :], (), (wkey,), chan="w")
        v_copy(csc2[:, k:k + 1], wt[:, 0:1], (wkey,), (f"csc2_{k}",))
    for k in range(8):
        wt, wkey = wtmp, "wtmpA"
        sch.dma("sync", wt[:, 0:SD], wsm_h[k * 128:(k + 1) * 128, :], (), (wkey,), chan="w")
        v_copy(csum[:, k * SD:(k + 1) * SD], wt[:, 0:SD], (wkey,), (f"csum_{k}",))
    stackP0.close()

    xtok = SB("xtok", [128, D], F32)
    xntok = SB("xntok", [128, D], BF16)
    xnf = SB("xnf", [128, 2048], BF16)
    statc = SB("statc", [128, 32], F32)
    sz = SB("sz", [128, 16 * SC], BF16)
    xbcr = SB("xbcr", [128, 18 * (SC + 4)], BF16)
    xbcc = SB("xbcc", [128, 18 * SC], BF16)
    cacc = SB("cacc", [128, 2 * SC], F32)
    dte = SB("dte", [32, SC], F32)
    dtv = SB("dtv", [32, SC], F32)
    dav = SB("dav", [32, SC], F32)
    csv = SB("csv", [32, SC], F32)
    ndsv = SB("ndsv", [32, L], F32)
    dsv = SB("dsv", [32, L], F32)
    expcs32 = SB("expcs32", [32, L], F32)
    expcs16 = SB("expcs16", [32, L], BF16)
    csflat4 = [SB(f"csflat4_{i}", [1, 4 * L], F32) for i in range(2)]
    onesrow = SB("onesrow", [1, 128], F32)
    v_memset(onesrow[:], 1.0, ("onesrow",))
    lamrow = SB("lamrow", [1, 32], F32)
    lamraw = SB("lamraw", [1, 32], F32)
    lamb = SB("lamb", [128, 32], F32)
    csT = SB("csT", [128, 32], F32)
    dtT16 = SB("dtT16", [128, 32], BF16)
    dsT16 = SB("dsT16", [128, 32], BF16)
    ebq = [SB(f"ebq{i}", [128, 256], BF16) for i in range(2)]
    xdt = SB("xdt", [128, 2048], BF16)
    xdwg = SB("xdwg", [128, 512], BF16)
    bmtok = SB("bmtok", [128, 128], BF16)
    gmask2 = SB("gmask2", [128, 256], BF16)
    xbcc2 = SB("xbcc2", [128, 256], BF16)
    ssegP = [SB(f"ssegP{i}", [128, 256], F32) for i in range(2)]
    esegP = [SB(f"esegP{i}", [128, 256], BF16) for i in range(2)]
    mtbP = [SB(f"mtbP{i}", [128, 256], BF16) for i in range(2)]
    cmsbP = [SB(f"cmsbP{i}", [128, 256], BF16) for i in range(2)]
    sall = SB("sall", [128, 2048], BF16)
    stmp = SB("stmp", [128, 512], BF16)
    dxsb = SB("dxsb", [128, 512], BF16)
    yfmsc = SB("yfmsc", [128, 16 * L], BF16)
    sqb = SB("sqb", [128, 2 * L], F32)
    rsr1 = SB("rsr1", [1, L], F32)
    rsr2 = SB("rsr2", [1, L], F32)
    rsrow = SB("rsrow", [1, L], F32)
    rsb = SB("rsb", [128, L], F32)
    ytok = SB("ytok", [128, D], F32)
    yfm16 = SB("yfm16", [128, 1024], BF16)
    s1b = SB("s1b", [128, 2 * HID], BF16)
    summT = SB("summT", [64, 128], BF16)
    summtk = SB("summtk", [128, SD], F32)

    v_memset(sall[:], 0.0, ("S0", "S1", "S2", "S3"))
    v_memset(xbcr[:], 0.0, tuple(f"tail{j}" for j in range(18)))

    GAMMA_TILES = [("fc", k) for k in range(8)] + [("pr", k) for k in range(32)]
    gamma_i = [0]
    gacc_par = [0, 0]

    def emit_gamma_tile():
        if gamma_i[0] >= len(GAMMA_TILES):
            return
        which, k = GAMMA_TILES[gamma_i[0]]
        gamma_i[0] += 1
        npiece = 4 if which == "fc" else 1
        srcw = wfcT_h if which == "fc" else wprT_h
        ci3 = 0 if which == "fc" else 1
        for pc in range(npiece):
            sch.dma("sync", wtmpG[:],
                    srcw[k * 128:(k + 1) * 128, pc * 1024:(pc + 1) * 1024], (), ("wtmpG",),
                    chan="g", nslots=2)
            v_red(statc[:, 30:31], wtmpG[:], ALU.add, ("wtmpG",), ("gred",), absval=True)
            par = gacc_par[ci3]
            v_tt(gacc[:, ci3 + 2 * (1 - par): ci3 + 2 * (1 - par) + 1],
                 gacc[:, ci3 + 2 * par: ci3 + 2 * par + 1], statc[:, 30:31], ALU.add,
                 ("gacc", "gred"), ("gacc",), tag="gacc")
            gacc_par[ci3] = 1 - par

    def ln_tile(ti, xkey, xview, outview, outkey, stat, skeys):
        k0 = skeys
        sch.op("vector", lambda e: e.bn_stats(stat[:, 0:6], xview[:, 0:512]), (xkey,), (k0 + "a",))
        sch.op("vector", lambda e: e.bn_stats(stat[:, 6:12], xview[:, 512:1024]), (xkey,), (k0 + "a",))
        sch.op("vector", lambda e: e.bn_aggr(stat[:, 12:14], stat[:, 0:12]), (k0 + "a",), (k0 + "b",))
        mean = stat[:, 12:13]
        var = stat[:, 13:14]
        veps = stat[:, 14:15]
        inv = stat[:, 15:16]
        v2 = stat[:, 16:17]
        r2 = stat[:, 17:18]
        stot = stat[:, 18:19]
        lnscr = stat[:, 19:20]
        v_ts(veps, var, cst(C_EPS), None, ALU.add, ALU.bypass, (k0 + "b", "consts"), (k0 + "c",))
        rsqrt2(inv, veps, lnscr, 128, (k0 + "c",), (k0 + "d",))
        v_ts(v2, var, inv, inv, ALU.mult, ALU.mult, (k0 + "b", k0 + "d"), (k0 + "e",))
        v_ts(r2, v2, cst(C_EPS), None, ALU.add, ALU.bypass, (k0 + "e", "consts"), (k0 + "f",))
        rsqrt2(v2, r2, lnscr, 128, (k0 + "f",), (k0 + "g",))
        v_ts(stot, inv, v2, None, ALU.mult, ALU.bypass, (k0 + "d", k0 + "g"), (k0 + "h",))
        v_ts(outview, xview, mean, stot, ALU.subtract, ALU.mult,
             (xkey, k0 + "b", k0 + "h"), (outkey,))

    sch.barrier()

    for sc_i in range(min(NSC, NSC_LIM)):
        t0 = sc_i * SC
        for ti in range(2):
            xk = "xtokA"
            sch.dma("sync", xtok[:],
                    x_h[t0 + ti * 128: t0 + (ti + 1) * 128, :], (), (xk,), chan="x")
            ln_tile(ti, xk, xtok[:], xntok[:], "xn1", statc, "lnA")
            for k in range(8):
                t_tr(PT[0][0:128, 0:128], xntok[:, k * 128:(k + 1) * 128],
                     128, ("xn1",), (KPT[0],))
                v_copy(xnf[:, k * 256 + ti * 128: k * 256 + (ti + 1) * 128],
                       PT[0][0:128, 0:128], (KPT[0],), (f"xnf{k}",))
        if KDBG and sc_i == 0:
            sch.dma("sync", dbg_h["xnf"], xnf[:], tuple(f"xnf{k}" for k in range(8)), (), chan="oo")
            sch.dma("sync", dbg_h["identd"], ident[:], ("ident",), (), chan="oo")
            sch.dma("sync", dbg_h["identfd"], identf[:], ("identf",), (), chan="oo")
            sch.dma("sync", dbg_h["constsd"], consts[:], ("consts",), (), chan="oo")
        for m in range(34):
            pb, pkey = (PB[m % 2], KPB[m % 2])
            for k in range(8):
                t_mm(pb[:, 0:SC], cw1[:, k * DIP + m * 128: k * DIP + (m + 1) * 128],
                     xnf[:, k * 256:(k + 1) * 256], k == 0, k == 7,
                     (f"cw1_{k}", f"xnf{k}"), (pkey,))
            if m < 16:
                s_act(sz[:, m * SC:(m + 1) * SC], pb[:, 0:SC], ACT.Silu, (pkey,), (f"sz{m}",))
            else:
                j = m - 16
                s_copy(xbcr[:, j * (SC + 4) + 4: (j + 1) * (SC + 4)], pb[:, 0:SC],
                       (pkey, f"tail{j}"), (f"xbcr{j}",))
        pbdt = PB[2]
        for k in range(8):
            t_mm(pbdt[0:32, 0:SC], cw1[:, k * DIP + 34 * 128: k * DIP + 34 * 128 + 32],
                 xnf[:, k * 256:(k + 1) * 256], k == 0, k == 7,
                 (f"cw1_{k}", f"xnf{k}"), (KPB[2],))
        for j in range(18):
            base = j * (SC + 4)
            a0, a1 = cacc[:, 0:SC], cacc[:, SC:2 * SC]
            v_ts(a0, xbcr[:, base + 1: base + 1 + SC], cwcv[:, j * 4: j * 4 + 1], None,
                 ALU.mult, ALU.bypass, (f"xbcr{j}", f"tail{j}", "cwcv"), ("cacc0",))
            v_stt(a1, xbcr[:, base + 2: base + 2 + SC], cwcv[:, j * 4 + 1: j * 4 + 2], a0,
                  ALU.mult, ALU.add, (f"xbcr{j}", f"tail{j}", "cacc0", "cwcv"), ("cacc1",))
            v_stt(a0, xbcr[:, base + 3: base + 3 + SC], cwcv[:, j * 4 + 2: j * 4 + 3], a1,
                  ALU.mult, ALU.add, (f"xbcr{j}", f"tail{j}", "cacc1", "cwcv"), ("cacc0",))
            v_stt(a1, xbcr[:, base + 4: base + 4 + SC], cwcv[:, j * 4 + 3: j * 4 + 4], a0,
                  ALU.mult, ALU.add, (f"xbcr{j}", "cacc0", "cwcv"), ("cacc1",))
            s_act(xbcc[:, j * SC:(j + 1) * SC], a1, ACT.Silu, ("cacc1", "ccb"), (f"xbcc{j}",),
                  bias=ccb[:, j:j + 1])
            v_copy(xbcr[:, base + 1: base + 4], xbcr[:, base + SC + 1: base + SC + 4],
                   (f"xbcr{j}",), (f"tail{j}",))
        s_act(dte[:], pbdt[0:32, 0:SC], ACT.Exp, (KPB[2], "dtbcol"), ("dte",), bias=dtbcol[:])
        s_act(dtv[:], dte[:], ACT.Ln, ("dte", "onecol32"), ("dtv",), bias=onecol32[:])
        v_ts(dav[:], dtv[:], acol_ap[:], None, ALU.mult, ALU.bypass, ("dtv", "acol"), ("dav",))

        emit_gamma_tile()
        emit_gamma_tile()
        emit_gamma_tile()

        for ci in range(2):
            ch = sc_i * 2 + ci
            cs_ = csv[:, ci * L:(ci + 1) * L]
            v_scan(cs_, ones32[:], dav[:, ci * L:(ci + 1) * L], 0.0, ALU.mult, ALU.add,
                   ("dav", "ones32"), (f"cs{ci}",))
            nds_ = ndsv[:]
            v_ts(nds_, cs_, cs_[:, L - 1:L], None, ALU.subtract, ALU.bypass,
                 (f"cs{ci}",), ("ndsv",))
            s_act(dsv[:], nds_, ACT.Exp, ("ndsv", "consts"),
                  ("dsv",), scale=cst(C_NEG1, 32))
            if KDBG:
                s_act(expcs32[:], cs_, ACT.Exp, (f"cs{ci}",), ("ecsv",))
            t_trf(PB[5][0:128, 0:32], cs_, 32, (f"cs{ci}",), (KPB[5],))
            v_copy(csT[:], PB[5][0:128, 0:32], (KPB[5],), ("csT",))
            t_trf(PB[5][0:128, 32:64], dtv[:, ci * L:(ci + 1) * L], 32, ("dtv",), (KPB[5],))
            v_copy(dtT16[:], PB[5][0:128, 32:64], (KPB[5],), ("dtT16",))
            t_trf(PB[5][0:128, 64:96], dsv[:], 32, ("dsv",), (KPB[5],))
            v_copy(dsT16[:], PB[5][0:128, 64:96], (KPB[5],), ("dsT16",))

            for j in range(16):
                t_tr(PT[1][0:128, 0:128], xbcc[:, j * SC + ci * L: j * SC + (ci + 1) * L],
                     128, (f"xbcc{j}",), (KPT[1],))
                v_tt(xdt[:, j * 128:(j + 1) * 128].rearrange("p (h q) -> p h q", h=2),
                     PT[1][0:128, 0:128].rearrange("p (h q) -> p h q", h=2),
                     dtT16[:, 2 * j:2 * j + 2].unsqueeze(2).broadcast_to((128, 2, 64)),
                     ALU.mult, (KPT[1], "dtT16"), ("xdt",))
            t_tr(PT[1][0:128, 0:128], xbcc[:, 16 * SC + ci * L: 16 * SC + (ci + 1) * L],
                 128, ("xbcc16",), (KPT[1],))
            v_copy(bmtok[:], PT[1][0:128, 0:128], (KPT[1],), ("bmtok",))

            if KDBG and sc_i == 0 and ci == 0:
                sch.dma("sync", dbg_h["xdt2"], xdt[:], ("xdt",), (), chan="oo")
                sch.dma("sync", dbg_h["dtT16"], dtT16[:], ("dtT16",), (), chan="oo")
            t_mm(PB[2][:, 0:128], xbcc[:, 16 * SC + ci * L: 16 * SC + (ci + 1) * L],
                 xbcc[:, 17 * SC + ci * L: 17 * SC + (ci + 1) * L], True, True,
                 ("xbcc16", "xbcc17"), (KPB[2],))
            v_tt(gmask2[:, 0:128], PB[2][:, 0:128], masksl[:], ALU.mult,
                 (KPB[2], "masksl"), ("gmask2",))
            v_tt(gmask2[:, 128:256], PB[2][:, 0:128], masksl[:], ALU.mult,
                 (KPB[2], "masksl"), ("gmask2",))
            v_copy(xbcc2[:, 0:128], xbcc[:, 17 * SC + ci * L: 17 * SC + (ci + 1) * L],
                   ("xbcc17",), ("xbcc2",))
            v_copy(xbcc2[:, 128:256], xbcc[:, 17 * SC + ci * L: 17 * SC + (ci + 1) * L],
                   ("xbcc17",), ("xbcc2",))

            for h in range(H):
                q, hb, hh = h // 8, h % 2, h % 8
                qb = (h // 4) % 2
                if h % 4 == 0:
                    h4 = h // 4
                    sch.dma("gpsimd", csflat4[qb][0:1, :].rearrange("o (h2 l) -> o h2 l", h2=4),
                            cs_[h4 * 4:(h4 + 1) * 4, :], (f"cs{ci}",), (f"csflat{qb}",),
                            chan="f", tag="csflat")
                    s_act(lamrow[0:1, h4 * 4:(h4 + 1) * 4], csflat4[qb][0:1, L - 1::L],
                          ACT.Exp, (f"csflat{qb}",), ("lamrow",))
                    t_mm(PB[5][:, 0:512], onesrow[:], csflat4[qb][0:1, :], True, True,
                         (f"csflat{qb}", "onesrow"), (KPB[5],), tag="csbmm")
                if h % 8 == 4:
                    g8 = (h // 8) * 8
                    bcast(lamb[:, g8:g8 + 8], lamrow[0:1, g8:g8 + 8], 8,
                          ("lamrow",), ("lamb",))
                pb = (h // 2) % 2
                hw = h % 2
                v_ts(ssegP[pb][:, hw * 128:(hw + 1) * 128],
                     PB[5][:, (h % 4) * 128:(h % 4 + 1) * 128], csT[:, h:h + 1],
                     cst(C_ZERO), ALU.subtract, ALU.min,
                     (KPB[5], "csT", "consts"), (f"ssegP{pb}",))
                if hw == 1:
                    s_act(esegP[pb][:], ssegP[pb][:], ACT.Exp,
                          (f"ssegP{pb}",), (f"esegP{pb}",))
                    e0 = ((h - 1) % 4) * 128
                    s_act(ebq[pb][:], PB[5][:, e0:e0 + 256], ACT.Exp,
                          (KPB[5],), (f"ebq{pb}",), tag="ebq")
                    v_tt(cmsbP[pb][:], xbcc2[:], ebq[pb][:], ALU.mult,
                         ("xbcc2", f"ebq{pb}"), (f"cmsbP{pb}",))
                    v_tt(mtbP[pb][:], esegP[pb][:], gmask2[:], ALU.mult,
                         (f"esegP{pb}", "gmask2"), (f"mtbP{pb}",))
                    if KDBG and sc_i == 0 and ci == 0 and h == 3:
                        sch.dma("sync", dbg_h["mtbh2"], mtbP[pb][:, 0:128], (f"mtbP{pb}",), (), chan="oo")
                        sch.dma("sync", dbg_h["ssegh2"], ssegP[pb][:, 0:128], (f"ssegP{pb}",), (), chan="oo")
                        sch.dma("sync", dbg_h["csTd"], csT[:], ("csT",), (), chan="oo")
                    bank = PB[3 + (q % 2)]
                    bkey = KPB[3 + (q % 2)]
                    for h2 in (h - 1, h):
                        pp, ff = (h2 % 2) * 64, ((h2 // 2) % 4) * 128
                        sl = (h2 % 2) * 128
                        t_mm(bank[pp:pp + 64, ff:ff + 128], xdt[:, h2 * P:(h2 + 1) * P],
                             mtbP[pb][:, sl:sl + 128],
                             True, ch == 0, ("xdt", f"mtbP{pb}"), (bkey,), tag="ydiag",
                             tile_position=(0, pp))
                        if ch > 0:
                            t_mm(bank[pp:pp + 64, ff:ff + 128], sall[:, h2 * P:(h2 + 1) * P],
                                 cmsbP[pb][:, sl:sl + 128],
                                 False, True, (f"S{q}", f"cmsbP{pb}", f"mtbP{pb}"), (bkey,),
                                 tag="yoff", tile_position=(0, pp))
                if hh == 7:
                    g = q
                    for jj in range(4):
                        j = g * 4 + jj
                        v_ts(dxsb[:, jj * 128:(jj + 1) * 128],
                             xbcc[:, j * SC + ci * L: j * SC + (ci + 1) * L],
                             dfm[:, j:j + 1], None, ALU.mult, ALU.bypass,
                             (f"xbcc{j}", "dfm"), ("dxsb",))
                    v_tt(yfmsc[:, g * 512:(g + 1) * 512],
                         bank[:, 0:512], dxsb[:], ALU.add,
                         (bkey, "dxsb"), (f"yfm{g}",), tag="yevac")
                    v_tt(xdwg[:].rearrange("p (hq q) -> p hq q", hq=8),
                         xdt[:, g * 512:(g + 1) * 512].rearrange("p (hq q) -> p hq q", hq=8),
                         dsT16[:, g * 8:(g + 1) * 8].unsqueeze(2).broadcast_to((128, 8, 64)),
                         ALU.mult, ("xdt", "dsT16"), ("xdwg",))
                    t_mm(PB[2][:, 0:512], bmtok[:], xdwg[:], True, True,
                         ("bmtok", "xdwg"), (KPB[2],), tag="states")
                    v_tt(stmp[:].rearrange("p (hq q) -> p hq q", hq=8),
                         sall[:, g * 512:(g + 1) * 512].rearrange("p (hq q) -> p hq q", hq=8),
                         lamb[:, g * 8:(g + 1) * 8].unsqueeze(2).broadcast_to((128, 8, 64)),
                         ALU.mult, (f"S{g}", "lamb"), ("stmp",), tag="supd")
                    v_tt(sall[:, g * 512:(g + 1) * 512], stmp[:], PB[2][:, 0:512], ALU.add,
                         ("stmp", KPB[2]), (f"S{g}",), tag="supd2")

            yzb = xdt
            yop = xnf
            yview = yfmsc[:].rearrange("p (j l) -> p j l", l=L)
            zview = sz[:].rearrange("p (j l) -> p j l", l=SC)[:, :, ci * L:(ci + 1) * L]
            yzview = yzb[:].rearrange("p (j l) -> p j l", l=L)
            ykeys = tuple(f"yfm{g}" for g in range(4))
            zkeys = tuple(f"sz{m}" for m in range(16))
            v_tt(yzview, yview, zview, ALU.mult, ykeys + zkeys, ("yz", "xdt"), tag="yz")
            for j in range(16):
                sq = sqb[:, (j % 2) * L:(j % 2 + 1) * L]
                v_tt(sq, yzb[:, j * L:(j + 1) * L], yzb[:, j * L:(j + 1) * L], ALU.mult,
                     ("yz", "xdt"), (f"sq{j % 2}",), tag="sq")
                t_mm(PB[2][0:1, 128:256], onescol[:], sq, j == 0, j == 15,
                     (f"sq{j % 2}", "onescol"), (KPB[2],), tag="sqmm")
            v_ts(rsr1[:], PB[2][0:1, 128:256], cst(C_I2048, 1), cst(C_EPS, 1),
                 ALU.mult, ALU.add, (KPB[2], "consts"), ("rsr1",))
            s_act(rsr2[:], rsr1[:], ACT.Ln, ("rsr1",), ("rsr2",))
            s_act(rsrow[:], rsr2[:], ACT.Exp, ("rsr2", "consts"), ("rsrow",),
                  scale=cst(C_NEGH, 1))
            bcast(rsb[:], rsrow[:], 128, ("rsrow",), ("rsb",))
            v_tt(yop[:].rearrange("p (j l) -> p j l", l=128), yzview,
                 rsb[:].unsqueeze(1).broadcast_to((128, 16, 128)), ALU.mult,
                 ("yz", "xdt", "rsb"), ("yop",) + tuple(f"xnf{k}" for k in range(8)), tag="yop")
            if KDBG and sc_i == 0 and ci == 0:
                sch.dma("sync", dbg_h["statc"], statc[:, 0:20], ("lnAb", "lnAh"), (), chan="oo")
                for nm, ap_, keys in [("xntok", xntok[:], ("xn1",)),
                                      ("sz", sz[:], tuple(f"sz{m}" for m in range(16))),
                                      ("xbcc", xbcc[:], tuple(f"xbcc{j}" for j in range(18))),
                                      ("dtv", dtv[:], ("dtv",)), ("csv", csv[:, 0:L], ("cs0",)), 
                                      ("xdt", xdt[:], ("yz", "xdt")), ("yfmsc", yfmsc[:], tuple(f"yfm{g}" for g in range(4))),
                                      ("yop", yop[:], ("yop",)), ("dsv", dsv[:], ("dsv",)),
                                      ("expcs32", expcs32[:], ("ecsv",)), ("sall", sall[:], ("S0", "S1", "S2", "S3")),
                                      ("gmask", gmask2[:, 0:128], ("gmask2",)),
                                      ("mtb0", mtbP[1][:, 128:256], ("mtbP1",))]:
                    sch.dma("sync", dbg_h[nm], ap_, keys, (), chan="oo")
            for half in range(2):
                pbo, pkey = PB[half], KPB[half]
                for j in range(16):
                    t_mm(pbo[:, 0:512], yop[:, j * 128:(j + 1) * 128],
                         cwo[:, j * 1024 + half * 512: j * 1024 + (half + 1) * 512],
                         j == 0, j == 15,
                         ("yop", f"xnf{j // 2}", f"cwo_{j}"), (pkey,), tag="outproj")
                v_copy(ytok[:, half * 512:(half + 1) * 512],
                       pbo[:, 0:512], (pkey,), ("ytokk",))
            sch.dma("sync", y_spill[ch, :, :], ytok[:],
                    ("ytokk",), (), chan="ys")
            for k in range(8):
                t_trf(PB[5][0:128, 96:224], ytok[:, k * 128:(k + 1) * 128],
                      128, ("ytokk",), (KPB[5],))
                v_copy(yfm16[:, k * 128:(k + 1) * 128],
                       PB[5][0:128, 96:224], (KPB[5],), ("yfm16k",))
            sch.dma("sync", yfm_spill[ch, :, :], yfm16[:],
                    ("yfm16k",), (), chan="ys")
            for m2 in range(2):
                for k in range(8):
                    t_mm(PB[2][:, 256:384], csc1[:, k * HID + m2 * 128: k * HID + (m2 + 1) * 128],
                         yfm16[:, k * 128:(k + 1) * 128],
                         k == 0, k == 7, (f"csc1_{k}", "yfm16k"), (KPB[2],), tag="sc1")
                v_ts(s1b[:, m2 * 128:(m2 + 1) * 128], PB[2][:, 256:384], 0.0, None,
                     ALU.max, ALU.bypass, (KPB[2],), (f"s1b{m2}",), tag="relu")
            for k2 in range(2):
                t_mm(PB[2][0:1, 384:512], csc2[:, k2:k2 + 1], s1b[:, k2 * 128:(k2 + 1) * 128],
                     k2 == 0, k2 == 1, (f"csc2_{k2}", "s1b0", "s1b1"), (KPB[2],), tag="sc2")
            s_act(rsr1[:], PB[2][0:1, 384:512], ACT.Exp, (KPB[2], "consts", "rsr1"), ("rsr1",),
                  scale=cst(C_NEG1, 1))
            v_ts(rsr2[:], rsr1[:], cst(C_ONE, 1), None, ALU.add, ALU.bypass,
                 ("rsr1", "consts", "rsr2"), ("rsr2",))
            v_recip(scrow[:], rsr2[:], ("rsr2",), ("scrow",))
            sch.dma("sync", sm_in.ap()[T + 2 * ch: T + 2 * ch + 2, :],
                    scrow[:].rearrange("o (a b) -> o a b", a=2),
                    ("scrow",), ("sc_in",), chan="ys")
            for k in range(8):
                t_mm(PB[5][0:64, 224:352], csum[:, k * SD:(k + 1) * SD],
                     yfm16[:, k * 128:(k + 1) * 128],
                     k == 0, k == 7, (f"csum_{k}", "yfm16k"), (KPB[5],), tag="summ")
            v_copy(summT[:], PB[5][0:64, 224:352], (KPB[5],), ("summT",))
            t_tr(PT[1][0:128, 128:192], summT[:], 64, ("summT",), (KPT[1],))
            v_copy(summtk[:], PT[1][0:128, 128:192], (KPT[1],), ("summtk",))
            sch.dma("sync", sm_in.ap()[ch * 128:(ch + 1) * 128, :], summtk[:],
                    ("summtk",), (), chan="ys")

    if KDBG2:
        d3 = {}
        for nm, shp, dt_ in [("csc2", [128, 2], BF16), ("csum", [128, 8 * SD], BF16),
                             ("s1b", [128, 2 * HID], BF16), ("yfm16", [128, 1024], BF16),
                             ("summT", [64, 128], BF16), ("scrow", [1, 128], F32),
                             ("summtk", [128, SD], F32)]:
            d3[nm] = nc.dram_tensor("dbg3_" + nm, shp, dt_, kind="ExternalOutput").ap()
        sch.dma("sync", d3["csc2"], csc2[:], ("csc2_0", "csc2_1"), (), chan="oo")
        sch.dma("sync", d3["csum"], csum[:], tuple(f"csum_{k}" for k in range(8)), (), chan="oo")
        sch.dma("sync", d3["s1b"], s1b[:], ("s1b0", "s1b1"), (), chan="oo")
        sch.dma("sync", d3["yfm16"], yfm16[:], ("yfm16k",), (), chan="oo")
        sch.dma("sync", d3["summT"], summT[:], ("summT",), (), chan="oo")
        sch.dma("sync", d3["scrow"], scrow[:], ("scrow",), (), chan="oo")
        sch.dma("sync", d3["summtk"], summtk[:], ("summtk",), (), chan="oo")
        d3y = nc.dram_tensor("dbg3_ysp", [4 * 128, D], F32, kind="ExternalOutput").ap()
        sch.dma("sync", d3y, y_spill[0:4, :, :].rearrange("a b c -> (a b) c"), (), (), chan="oo")
        d3yf = nc.dram_tensor("dbg3_yfsp", [2 * 128, 1024], BF16, kind="ExternalOutput").ap()
        sch.dma("sync", d3yf, yfm_spill[0:2, :, :].rearrange("a b c -> (a b) c"), (), (), chan="oo")
    sch.oplimit = None
    while gamma_i[0] < len(GAMMA_TILES):
        emit_gamma_tile()
    for ci3 in range(2):
        last_col = ci3 + 2 * gacc_par[ci3]
        t_mm(PB[2][0:1, ci3:ci3 + 1], gacc[:, last_col:last_col + 1], onescol[:],
             True, True, ("gacc", "onescol"), (KPB[2],), tag="gfin")
    v_ts(grow_sb[0:1, 0:2], PB[2][0:1, 0:2], cst(C_IFC, 1), cst(C_1E6M, 1),
         ALU.mult, ALU.add, (KPB[2], "consts"), ("grow",))
    v_ts(grow_sb[0:1, 2:4], grow_sb[0:1, 0:2], cst(C_HALF, 1), None, ALU.mult, ALU.bypass,
         ("grow", "consts"), ("ghrow",))
    bcast(ghalf[:], grow_sb[0:1, 2:4], 2, ("ghrow",), ("ghalf",))
    v_ts(gnegh[:], ghalf[:], cst(C_NEG1), None, ALU.mult, ALU.bypass,
         ("ghalf", "consts"), ("gnegh",))

    sch.barrier()
    stackA.close()

    if KPHASE <= 1:
        dummy = nc.alloc_sbuf_tensor("dummy1", [128, D], F32)
        sch.op("vector", lambda e: e.memset(dummy[:], 0.0), (), ("dummyk",), "dm")
        for chd in range(NCH):
            sch.dma("sync", out_h[chd * 128:(chd + 1) * 128, :], dummy[:], ("dummyk",), (), chan="oo")
        sch.finalize()
        with contextlib.ExitStack() as es:
            sems = {}
            for e in ENGS:
                sems[e] = es.enter_context(nc.semaphore(f"sem_{e}"))
            dma_sems = [es.enter_context(nc.semaphore(f"dsem{i}")) for i in range(sch.n_dma_slots)]
            block = es.enter_context(nc.Block())
            stats = sch.emit(block, sems, dma_sems)
        return nc, stats

    if KSELCUT < 99:
        sch.oplimit = KSELCUT
    if os.environ.get("KVFIRST", "0") == "1":
        pass
    stackS = contextlib.ExitStack()
    SBs = lambda name, shape, dt: stackS.enter_context(nc.sbuf_tensor(name, shape, dt))
    ssc = SBs("ssc", [128, 256], F32)
    sscraw = SBs("sscraw", [128, 256], F32)
    fhib = SBs("fhib", [128, 8], F32)
    flob = SBs("flob", [128, 8], F32)
    vcmp = SBs("vcmp", [128, 256], F32)
    redc = SBs("redc", [128, 2], F32)
    scal2 = SBs("scal2", [1, 8], F32)
    mx8 = SBs("mx8", [128, 8], F32)
    cnd8 = SBs("cnd8", [8, 128], F32)
    cand = SBs("cand", [1, 1024], F32)
    candB = SBs("candB", [1, 1024], F32)
    th8b = SBs("th8b", [128, 8], F32)
    mi8 = SBs("mi8", [128, 8], U32)
    mif = SBs("mif", [128, 8], F32)
    gpos = SBs("gpos", [128, 8], F32)
    gpos2 = SBs("gpos2", [128, 8], F32)
    gtr = SBs("gtr", [8, 128], F32)
    gall = SBs("gall", [8, 8], F32)
    gcolf = SBs("gcolf", [NSEL, 1], F32)
    gcolu = SBs("gcolu", [NSEL, 1], U32)
    gcolf2 = SBs("gcolf2", [NSEL, 1], F32)
    radj = SBs("radj", [NSEL, 1], F32)
    radj2 = SBs("radj2", [NSEL, 1], F32)
    rc1 = SBs("rc1", [NSEL, 1], F32)
    poolraw = SBs("poolraw", [NSEL, SD], F32)
    poolfm = SBs("poolfm", [64, NSEL], BF16)
    vmrow = SBs("vmrow", [1, NPOOL], F32)
    mrow = SBs("mrow", [1, NPOOL], F32)
    brow = SBs("brow", [1, NPOOL], F32)
    ckw = SBs("ckw", [64, 64], BF16)
    cvw = SBs("cvw", [64, 1024], BF16)
    wk_f = SBs("wk_f", [64, 64], F32)
    wv_f = SBs("wv_f", [64, 1024], F32)

    ccsm = sch.dma_op("gpsimd", lambda e: e.collective_compute(
        "AllGather", ALU.bypass, replica_groups=[list(range(NCORES))],
        ins=[sm_in.ap().opt()], outs=[sm_out.ap().opt()]),
        ("sc_in",), ("sm_out_cc",), chan="cc", nslots=1, tag="ccsm", amt=1)
    for s in sch.chan_slots.get("ys", []):
        if sch.slot_last[s] is not None and sch.slot_last[s] is not ccsm:
            ccsm.deps.append((sch.slot_last[s], False))

    for rr in range(8):
        sch.dma("sync", sscraw[rr * 16:(rr + 1) * 16, :],
                sm_out.ap()[rr * (T + 64) + T: (rr + 1) * (T + 64), :]
                .rearrange("a b -> (a b)").rearrange("(p f) -> p f", p=16),
                ("sm_out_cc",), ("sscraw",), chan="e")
    for hblk in range(2):
        t_trf(PB[3][0:128, 0:128], sscraw[:, hblk * 128:(hblk + 1) * 128], 128,
              ("sscraw",), (KPB[3],))
        v_copy(ssc[:, hblk * 128:(hblk + 1) * 128], PB[3][0:128, 0:128], (KPB[3],), ("ssc",))
    v_ts(vcmp[:], ssc[:], cst(C_HALF), None, ALU.is_gt, ALU.bypass,
         ("ssc", "consts"), ("vcmp",))
    v_red(redc[:, 0:1], vcmp[:], ALU.add, ("vcmp",), ("redc",))
    v_red(redc[:, 1:2], ssc[:], ALU.add, ("ssc",), ("redc",))
    t_mm(PB[2][0:1, 0:1], redc[:, 0:1], onescol[:], True, True,
         ("redc", "onescol"), (KPB[2],), tag="cnt")
    t_mm(PB[2][0:1, 1:2], redc[:, 1:2], onescol[:], True, True,
         ("redc", "onescol"), (KPB[2],), tag="sum")
    v_copy(scal2[0:1, 0:2], PB[2][0:1, 0:2], (KPB[2],), ("scal2",))
    v_ts(scal2[0:1, 2:3], scal2[0:1, 1:2], cst(C_TAUS, 1), None, ALU.is_gt, ALU.bypass,
         ("scal2", "consts"), ("sc3",))
    v_ts(scal2[0:1, 3:4], scal2[0:1, 0:1], cst(C_ZERO, 1), None, ALU.is_gt, ALU.bypass,
         ("scal2", "consts"), ("sc4",))
    v_tt(scal2[0:1, 4:5], scal2[0:1, 2:3], scal2[0:1, 3:4], ALU.mult,
         ("sc3", "sc4"), ("cond1",))
    bcast(condcol[:], scal2[0:1, 4:5], 1, ("cond1",), ("condcol",))
    v_max(mx8[:], ssc[:], ("ssc",), ("mx8",))
    t_trf(PB[2][0:8, 128:256], mx8[:], 128, ("mx8",), (KPB[2],))
    v_copy(cnd8[:], PB[2][0:8, 128:256], (KPB[2],), ("cnd8",))
    sch.dma("gpsimd", cand[0:1, :].rearrange("o (j p) -> o j p", j=8), cnd8[:],
            ("cnd8",), ("cand",), chan="ec", nslots=1)
    for r in range(7):
        src, skey = (cand, "cand") if r % 2 == 0 else (candB, "candB")
        dst, dkey = (candB, "candB") if r % 2 == 0 else (cand, "cand")
        v_max(thetas[0:1, r * 8:(r + 1) * 8], src[0:1, :], (skey,), ("thetas",))
        v_matchrep(dst[0:1, :], thetas[0:1, r * 8:(r + 1) * 8], src[0:1, :], -1e30,
                   ("thetas", skey), (dkey,))
    for r in range(7):
        bcast(th8b[:], thetas[0:1, r * 8:(r + 1) * 8], 8, ("thetas",), ("th8b",))
        v_maxidx(mi8[:], th8b[:], ssc[:], ("th8b", "ssc"), ("mi8",))
        v_copy(mif[:], mi8[:], ("mi8",), ("mif",))
        v_ts(fhib[:], mif[:], cst(C_128), None, ALU.is_ge, ALU.bypass,
             ("mif", "consts"), ("fhib",))
        v_stt(flob[:], fhib[:], cst(C_N128), mif[:], ALU.mult, ALU.add,
              ("fhib", "mif", "consts"), ("flob",))
        v_ts(gpos2[:], flob[:], cst(C_256), piota_f[:], ALU.mult, ALU.add,
             ("flob", "consts", "piota_f"), ("gpos1",))
        v_stt(gpos[:], fhib[:], cst(C_128), gpos2[:], ALU.mult, ALU.add,
              ("gpos1", "fhib", "consts"), ("gpos",))
        t_trf(PB[2][0:8, 256:384], gpos[:], 128, ("gpos",), (KPB[2],))
        v_copy(gtr[:], PB[2][0:8, 256:384], (KPB[2],), ("gtr",))
        v_red(gall[:, r:r + 1], gtr[:], ALU.min, ("gtr",), ("gall",))
    for r in range(7):
        sch.dma("gpsimd", gcolf[r * 8:(r + 1) * 8, 0:1], gall[0:8, r:r + 1],
                ("gall",), ("gcolf",), chan=f"eg{r}", nslots=1)
    v_memset(radj[:], 0.0, ("radj",))
    for kk7 in range(1, 8):
        v_ts(rc1[:], gcolf[:], consts[0:NSEL, C_T1 + kk7 - 1: C_T1 + kk7], None,
             ALU.is_ge, ALU.bypass, ("gcolf", "consts"), ("rc1",))
        v_tt(radj2[:], radj[:], rc1[:], ALU.add, ("radj", "rc1"), ("radj",), tag="radd")
        radj, radj2 = radj2, radj
    v_stt(gcolf2[:], radj[:], cst(C_64, NSEL), gcolf[:], ALU.mult, ALU.add,
          ("radj", "gcolf", "consts"), ("gcolf2",))
    v_copy(gcolu[:], gcolf2[:], ("gcolf2",), ("gcolu",))
    sch.dma_op("gpsimd", lambda e: e.indirect_dma_start(
        out=poolraw[:], out_offset=None, in_=sm_out.ap(),
        in_offset=bass.IndirectOffsetOnAxis(ap=gcolu[:], axis=0)),
        ("gcolu", "sm_out_cc"), ("poolraw",), chan="ig", tag="gather")
    t_trf(PB[3][0:64, 128:128 + NSEL], poolraw[:], 56, ("poolraw",), (KPB[3],))
    v_copy(poolfm[:], PB[3][0:64, 128:128 + NSEL], (KPB[3],), ("poolfm",))
    v_ts(vmrow[:], thetas[0:1, 0:NPOOL], cst(C_HALF, 1), None, ALU.is_gt, ALU.bypass,
         ("thetas", "consts"), ("vmrow",))
    v_ts(mrow[:], vmrow[:], cst(C_QUART, 1), None, ALU.mult, ALU.bypass,
         ("vmrow", "consts"), ("mrow",))
    v_ts(brow[:], vmrow[:], cst(C_1E9, 1), cst(C_N1E9, 1), ALU.mult, ALU.add,
         ("vmrow", "consts"), ("brow",))
    bcast(mb128[:], mrow[:], NPOOL, ("mrow",), ("mb128",))
    bcast(bb128[:], brow[:], NPOOL, ("brow",), ("bb128",))
    sch.dma("sync", wk_f[:], wk_h, (), ("wk_f",), chan="e")
    sch.dma("sync", wv_f[:], wv_h, (), ("wv_f",), chan="e")
    v_copy(ckw[:], wk_f[:], ("wk_f",), ("ckw",))
    v_copy(cvw[:], wv_f[:], ("wv_f",), ("cvw",))
    t_mm(PB[3][0:64, 0:NSEL], ckw[:], poolfm[:], True, True, ("ckw", "poolfm"), (KPB[3],))
    v_copy(kkT[:], PB[3][0:64, 0:NSEL], (KPB[3],), ("kkT",))
    for half in range(2):
        t_mm(PB[4][0:NSEL, 0:512], poolfm[:], cvw[:, half * 512:(half + 1) * 512],
             True, True, ("poolfm", "cvw"), (KPB[4],))
        v_copy(vvtok[:, half * 512:(half + 1) * 512], PB[4][0:NSEL, 0:512],
               (KPB[4],), ("vvtok",))

    sch.oplimit = None
    if KDBG2:
        d2 = {}
        smin_dbg = nc.dram_tensor("dbg2_smin", [64, 64], F32, kind="ExternalOutput").ap()
        smsum_dbg = nc.dram_tensor("dbg2_smsum", [128, 64], F32, kind="ExternalOutput").ap()
        smout_dbg = nc.dram_tensor("dbg2_smout", [64, 64], F32, kind="ExternalOutput").ap()
        sscraw_dbg = nc.dram_tensor("dbg2_sscraw", [128, 256], F32, kind="ExternalOutput").ap()
        sch.dma("sync", smin_dbg, sm_in.ap()[T:T + 64, :], ("sc_in",), (), chan="oo")
        sch.dma("sync", smsum_dbg, sm_in.ap()[0:128, :], (), (), chan="oo")
        dbgcp = sch.dma("sync", smout_dbg, sm_out.ap()[T:T + 64, :], ("sm_out_cc",), (), chan="oo")
        sch.dma("sync", sscraw_dbg, sscraw[:], ("sscraw",), (), chan="oo")
        for nm, shp, dt_ in [("ssc", [128, 256], F32), ("cand", [1, 1024], F32),
                             ("thetas", [1, 64], F32), ("gcolf", [NSEL, 1], F32),
                             ("gcolf2", [NSEL, 1], F32), ("poolraw", [NSEL, SD], F32),
                             ("kkT", [64, NSEL], BF16), ("vvtok", [NSEL, D], BF16),
                             ("scal2", [1, 8], F32), ("mrow", [1, NPOOL], F32),
                             ("gall", [8, 8], F32)]:
            d2[nm] = nc.dram_tensor("dbg2_" + nm, shp, dt_, kind="ExternalOutput").ap()
        for nm, ap_, keys in [("ssc", ssc[:], ("ssc",)), ("cand", cand[0:1, :], ("cand",)),
                              ("thetas", thetas[:], ("thetas",)), ("gcolf", gcolf[:], ("gcolf",)),
                              ("gcolf2", gcolf2[:], ("gcolf2",)), ("poolraw", poolraw[:], ("poolraw",)),
                              ("kkT", kkT[:], ("kkT",)), ("vvtok", vvtok[:], ("vvtok",)),
                              ("scal2", scal2[:], ("scal2", "cond1", "sc3", "sc4")),
                              ("mrow", mrow[:], ("mrow",)), ("gall", gall[:], ("gall",))]:
            sch.dma("sync", d2[nm], ap_, keys, (), chan="oo")
    sch.barrier()
    stackS.close()

    if KPHASE <= 2:
        dummy = nc.alloc_sbuf_tensor("dummy2", [128, D], F32)
        sch.op("vector", lambda e: e.memset(dummy[:], 0.0), (), ("dummyk",), "dm")
        for chd in range(NCH):
            sch.dma("sync", out_h[chd * 128:(chd + 1) * 128, :], dummy[:], ("dummyk",), (), chan="oo")
        sch.finalize()
        with contextlib.ExitStack() as es:
            sems = {}
            for e in ENGS:
                sems[e] = es.enter_context(nc.semaphore(f"sem_{e}"))
            dma_sems = [es.enter_context(nc.semaphore(f"dsem{i}")) for i in range(sch.n_dma_slots)]
            block = es.enter_context(nc.Block())
            stats = sch.emit(block, sems, dma_sems)
        return nc, stats

    stackB1 = contextlib.ExitStack()
    SB1 = lambda name, shape, dt: stackB1.enter_context(nc.sbuf_tensor(name, shape, dt))
    cgate = SB1("cgate", [128, 16 * 1024], BF16)
    cq = SB1("cq", [128, 8 * SD], BF16)
    ytokB = SB1("ytokB", [128, 4 * D], F32)
    yfmB = SB1("yfmB", [128, 4 * 1024], BF16)
    xB = SB1("xB", [128, 4 * D], F32)
    qfm = SB1("qfm", [64, 512], BF16)
    lgb = SB1("lgb", [128, 2 * NPOOL], F32)
    ebuf = SB1("ebuf", [128, NPOOL], F32)
    esum = SB1("esum", [128, 2], F32)
    smb = SB1("smb", [128, NPOOL], BF16)
    smTb = SB1("smTb", [64, 4 * 128], BF16)
    retB = SB1("retB", [128, 8 * 512], BF16)
    gb = SB1("gb", [128, 512], BF16)
    t2m = SB1("t2m", [128, 512], BF16)
    t2B = SB1("t2B", [128, 8 * 512], BF16)
    t2tok = SB1("t2tok", [128, 4 * 1024], BF16)
    statB = SB1("statB", [128, 24], F32)
    xpxB = SB1("xpxB", [128, D], F32)
    ysumB = SB1("ysumB", [128, D], F32)
    xmoB = SB1("xmoB", [128, D], F32)
    wtq = SB1("wtq", [128, 4096], F32)
    wqt = SB1("wqt", [128, 4096], BF16)
    wqneg = SB1("wqneg", [128, 4096], BF16)
    wqt2 = SB1("wqt2", [128, 4096], BF16)
    posb = SB1("posb", [128, 4096], F32)
    wtmpB = SB1("wtmpB", [128, SD], F32)
    lnbw2 = SB1("lnbw2", [128, 64], BF16)

    for k in range(16):
        sch.dma("sync", wtq[:, 0:1024], wg_h[k * 128:(k + 1) * 128, :], (), ("wtq",), chan="w")
        v_copy(cgate[:, k * 1024:(k + 1) * 1024], wtq[:, 0:1024], ("wtq",), (f"cg_{k}",))
    for k in range(8):
        sch.dma("sync", wtmpB[:], wq_h[k * 128:(k + 1) * 128, :], (), ("wtqq",), chan="w")
        v_copy(cq[:, k * SD:(k + 1) * SD], wtmpB[:], ("wtqq",), (f"cq_{k}",))
    for k in range(32):
        v_copy(lnbw2[:, 2 * k:2 * k + 1], lnb16[:, k:k + 1], ("lnb16",), ("lnbw2",))
        v_copy(lnbw2[:, 2 * k + 1:2 * k + 2], lnw16[:, k:k + 1], ("lnw16",), ("lnbw2",))

    QUANT_TILES = [("fc", k) for k in range(8)] + [("pr", k) for k in range(32)]
    quant_i = [0]

    def emit_quant_tile():
        if quant_i[0] >= len(QUANT_TILES):
            return
        which, k = QUANT_TILES[quant_i[0]]
        quant_i[0] += 1
        n = 4096 if which == "fc" else 1024
        src = wfcT_h if which == "fc" else wprT_h
        dst = wqfc_hbm if which == "fc" else wqpr_hbm
        gc = 0 if which == "fc" else 1
        sch.dma("sync", wtq[:, 0:n], src[k * 128:(k + 1) * 128, :], (), ("wtq",),
                chan="g", nslots=2)
        v_ts(posb[:, 0:n], wtq[:, 0:n], ghalf[:, gc:gc + 1], None, ALU.is_gt, ALU.bypass,
             ("wtq", "ghalf"), ("posb",))
        v_ts(wqneg[:, 0:n], wtq[:, 0:n], gnegh[:, gc:gc + 1], None, ALU.is_lt, ALU.bypass,
             ("wtq", "gnegh"), ("wqnegk",))
        v_tt(wqt[:, 0:n], posb[:, 0:n], wqneg[:, 0:n], ALU.subtract,
             ("posb", "wqnegk"), ("wqt",))
        if which == "pr":
            for half in range(2):
                t_mm(PB[4 + half][0:2, 0:512], lnbw2[:, 2 * k:2 * k + 2],
                     wqt[:, half * 512:(half + 1) * 512], k == 0, k == 31,
                     ("wqt", "lnbw2"), (KPB[4 + half],), tag="cmm")
            v_ts(wqt2[:, 0:n], wqt[:, 0:n], lnwf32[:, k:k + 1], None, ALU.mult, ALU.bypass,
                 ("wqt", "lnwf32"), ("wqt2",))
            sch.dma("sync", dst[k, :, :], wqt2[:, 0:n], ("wqt2",), (), chan="qo")
        else:
            sch.dma("sync", dst[k, :, :], wqt[:, 0:n], ("wqt",), (), chan="qo")

    for i2 in range(min(8, NSC_LIM)):
        ch0 = i2 * 4
        for tt in range(4):
            ch = ch0 + tt
            sch.dma("sync", ytokB[:, tt * D:(tt + 1) * D], y_spill[ch, :, :], (), (f"ytB{tt}",), chan="x")
            sch.dma("sync", yfmB[:, tt * 1024:(tt + 1) * 1024], yfm_spill[ch, :, :], (), (f"yfB{tt}",), chan="x")
            sch.dma("sync", xB[:, tt * D:(tt + 1) * D], x_h[ch * 128:(ch + 1) * 128, :], (), (f"xB{tt}",), chan="x")
        ykeys = tuple(f"yfB{tt}" for tt in range(4))
        for k in range(8):
            t_mm(PB[2][0:64, 0:512], cq[:, k * SD:(k + 1) * SD],
                 yfmB[:].rearrange("p (t4 n) -> p t4 n", t4=4)[:, :, k * 128:(k + 1) * 128],
                 k == 0, k == 7, (f"cq_{k}",) + ykeys, (KPB[2],), tag="q")
        v_copy(qfm[:], PB[2][0:64, 0:512], (KPB[2],), ("qfm",))
        for tt in range(4):
            t_mm(PB[3][:, 0:NPOOL], qfm[:, tt * 128:(tt + 1) * 128], kkT[0:64, 0:NPOOL],
                 True, True, ("qfm", "kkT"), (KPB[3],), tag="lg")
            v_tt(lgb[:, 0:NPOOL], PB[3][:, 0:NPOOL], mb128[:], ALU.mult,
                 (KPB[3], "mb128"), ("lg1",))
            v_tt(lgb[:, NPOOL:2 * NPOOL], lgb[:, 0:NPOOL], bb128[:], ALU.add,
                 ("lg1", "bb128"), ("lg2",))
            v_red(statB[:, 0:1], lgb[:, NPOOL:2 * NPOOL], ALU.max, ("lg2",), ("negmx",), negate=True)
            s_act(ebuf[:], lgb[:, NPOOL:2 * NPOOL], ACT.Exp, ("lg2", "negmx"), ("ebuf",),
                  bias=statB[:, 0:1], accum=esum[:, 0:1])
            v_recip(esum[:, 1:2], esum[:, 0:1], ("ebuf",), ("esr",))
            v_ts(smb[:], ebuf[:], esum[:, 1:2], None, ALU.mult, ALU.bypass,
                 ("ebuf", "esr"), ("smb",))
            t_tr(PT[0][0:NPOOL, 0:128], smb[:], 128, ("smb",), (KPT[0],))
            v_copy(smTb[0:NPOOL, tt * 128:(tt + 1) * 128], PT[0][0:NPOOL, 0:128],
                   (KPT[0],), ("smTb",))
        for k in range(8):
            t_mm(PB[k % 2][:, 0:512], vvtok[0:NPOOL, k * 128:(k + 1) * 128],
                 smTb[0:NPOOL, 0:512], True, True, ("vvtok", "smTb"), (KPB[k % 2],), tag="ret")
            v_copy(retB[:, k * 512:(k + 1) * 512], PB[k % 2][:, 0:512],
                   (KPB[k % 2],), (f"ret{k}",))
        for m in range(8):
            pb, pkey = PB[m % 2], KPB[m % 2]
            for k in range(8):
                t_mm(pb[:, 0:512], cgate[:, k * 1024 + m * 128: k * 1024 + m * 128 + 128],
                     yfmB[:].rearrange("p (t4 n) -> p t4 n", t4=4)[:, :, k * 128:(k + 1) * 128],
                     k == 0, False, (f"cg_{k}",) + ykeys, (pkey,), tag="gate")
            for k in range(8):
                t_mm(pb[:, 0:512], cgate[:, (8 + k) * 1024 + m * 128: (8 + k) * 1024 + m * 128 + 128],
                     retB[:, k * 512:(k + 1) * 512], False, k == 7,
                     (f"cg_{8 + k}", f"ret{k}"), (pkey,), tag="gate2")
            s_act(gb[:], pb[:, 0:512], ACT.Sigmoid, (pkey,), ("gb",))
            v_tt(t2m[:], gb[:], retB[:, m * 512:(m + 1) * 512], ALU.mult,
                 ("gb", f"ret{m}"), ("t2m",))
            v_ts(t2B[:, m * 512:(m + 1) * 512], t2m[:], condcol[:], None, ALU.mult, ALU.bypass,
                 ("t2m", "condcol"), (f"t2B{m}",))
        for m in range(8):
            for tt in range(4):
                t_tr(PT[1][0:128, 0:128], t2B[:, m * 512 + tt * 128: m * 512 + (tt + 1) * 128],
                     128, (f"t2B{m}",), (KPT[1],))
                v_copy(t2tok[:, tt * 1024 + m * 128: tt * 1024 + (m + 1) * 128],
                       PT[1][0:128, 0:128], (KPT[1],), (f"t2tok{tt}",))
        for tt in range(4):
            ch = ch0 + tt
            sch.op("vector", lambda e, tt=tt: e.bn_stats(statB[:, 2:8], xB[:, tt * D: tt * D + 512]),
                   (f"xB{tt}",), ("stB",))
            sch.op("vector", lambda e, tt=tt: e.bn_stats(statB[:, 8:14], xB[:, tt * D + 512:(tt + 1) * D]),
                   (f"xB{tt}",), ("stB",))
            sch.op("vector", lambda e: e.bn_aggr(statB[:, 14:16], statB[:, 2:14]),
                   ("stB",), ("stB",))
            mean, var = statB[:, 14:15], statB[:, 15:16]
            veps, inv = statB[:, 16:17], statB[:, 17:18]
            q1, q2, lnscr = statB[:, 18:19], statB[:, 19:20], statB[:, 20:21]
            v_ts(veps, var, cst(C_EPS), None, ALU.add, ALU.bypass, ("stB", "consts"), ("stB1",))
            rsqrt2(inv, veps, lnscr, 128, ("stB1",), ("stB2",))
            v_ts(q1, inv, cst(C_ONE), None, ALU.add, ALU.bypass, ("stB2", "consts"), ("stB3",))
            v_ts(q2, mean, inv, None, ALU.mult, ALU.bypass, ("stB", "stB2"), ("stB4",))
            v_ts(xpxB[:], xB[:, tt * D:(tt + 1) * D], q1, q2,
                 ALU.mult, ALU.subtract, (f"xB{tt}", "stB3", "stB4"), ("xpxB",))
            v_tt(ysumB[:], ytokB[:, tt * D:(tt + 1) * D], t2tok[:, tt * 1024:(tt + 1) * 1024],
                 ALU.add, (f"ytB{tt}", f"t2tok{tt}"), ("ysumB",))
            v_tt(xmoB[:], xpxB[:], ysumB[:], ALU.add, ("xpxB", "ysumB"), ("xmoB",))
            sch.dma("sync", xmid_spill[ch, :, :], xmoB[:], ("xmoB",), (), chan="xo")
        for _ in range(5):
            emit_quant_tile()

    while quant_i[0] < len(QUANT_TILES):
        emit_quant_tile()
    crow_st = SB1("crow_st", [2, 1024], F32)
    v_copy(crow_st[0:2, 0:512], PB[4][0:2, 0:512], (KPB[4],), ("c1row",))
    v_copy(crow_st[0:2, 512:1024], PB[5][0:2, 0:512], (KPB[5],), ("c2row",))
    sch.dma("sync", c_hbm[:, :], crow_st[:], ("c1row", "c2row"), ("c_hbm",), chan="xo")

    sch.barrier()
    stackB1.close()

    if KPHASE <= 3:
        dummy = nc.alloc_sbuf_tensor("dummy3", [128, D], F32)
        sch.op("vector", lambda e: e.memset(dummy[:], 0.0), (), ("dummyk",), "dm")
        for chd in range(NCH):
            sch.dma("sync", out_h[chd * 128:(chd + 1) * 128, :], dummy[:], ("dummyk",), (), chan="oo")
        sch.finalize()
        with contextlib.ExitStack() as es:
            sems = {}
            for e in ENGS:
                sems[e] = es.enter_context(nc.semaphore(f"sem_{e}"))
            dma_sems = [es.enter_context(nc.semaphore(f"dsem{i}")) for i in range(sch.n_dma_slots)]
            block = es.enter_context(nc.Block())
            stats = sch.emit(block, sems, dma_sems)
        return nc, stats

    stackB2 = contextlib.ExitStack()
    SB2 = lambda name, shape, dt: stackB2.enter_context(nc.sbuf_tensor(name, shape, dt))
    wqfcS = SB2("wqfcS", [128, 8 * 4096], BF16)
    wqprS = SB2("wqprS", [128, 32 * 1024], BF16)
    c1b = SB2("c1b", [128, 1024], F32)
    c2b = SB2("c2b", [128, 1024], F32)
    xmB2 = SB2("xmB2", [128, 2 * D], F32)
    x2tok = SB2("x2tok", [128, 2 * D], BF16)
    x2fm = SB2("x2fm", [128, 8 * 256], BF16)
    gbuf = SB2("gbuf", [128, 32 * 256], BF16)
    sqbB = SB2("sqbB", [128, 2 * 256], BF16)
    atmp = SB2("atmp", [128, 2 * 256], BF16)
    statB2 = SB2("statB2", [128, 24], F32)
    murow = SB2("murow", [1, 256], F32)
    m2row = SB2("m2row", [1, 256], F32)
    varrow = SB2("varrow", [1, 256], F32)
    var2row = SB2("var2row", [1, 256], F32)
    lnsrow = SB2("lnsrow", [1, 256], F32)
    rsrow2 = SB2("rsrow2", [1, 256], F32)
    muarow = SB2("muarow", [1, 256], F32)
    nmuarow = SB2("nmuarow", [1, 256], F32)
    rsb2 = SB2("rsb2", [128, 256], F32)
    nmacol = SB2("nmacol", [128, 2], F32)
    corrb = SB2("corrb", [128, 512], F32)
    corr2b = SB2("corr2b", [128, 512], F32)
    outsb = SB2("outsb", [128, 2 * D], F32)

    for k in range(8):
        sch.dma("sync", wqfcS[:, k * 4096:(k + 1) * 4096], wqfc_hbm[k, :, :], (), (f"wf_{k}",), chan="w")
    for k in range(32):
        sch.dma("sync", wqprS[:, k * 1024:(k + 1) * 1024], wqpr_hbm[k, :, :], (), (f"wp_{k}",), chan="w")
    sch.dma("gpsimd", c1b[:], c_hbm[0:1, :].unsqueeze(1).broadcast_to((1, 128, 1024)),
            ("c_hbm",), ("c1b",), chan="f")
    sch.dma("gpsimd", c2b[:], c_hbm[1:2, :].unsqueeze(1).broadcast_to((1, 128, 1024)),
            ("c_hbm",), ("c2b",), chan="f")

    for i3 in range(min(16, NSC_LIM)):
        ch0 = i3 * 2
        for tt2 in range(2):
            sch.dma("sync", xmB2[:, tt2 * D:(tt2 + 1) * D], xmid_spill[ch0 + tt2, :, :],
                    (), (f"xm2_{tt2}",), chan="x")
            sch.op("vector", lambda e, tt2=tt2: e.bn_stats(statB2[:, 0:6],
                   xmB2[:, tt2 * D: tt2 * D + 512]), (f"xm2_{tt2}",), ("st2",))
            sch.op("vector", lambda e, tt2=tt2: e.bn_stats(statB2[:, 6:12],
                   xmB2[:, tt2 * D + 512:(tt2 + 1) * D]), (f"xm2_{tt2}",), ("st2",))
            sch.op("vector", lambda e: e.bn_aggr(statB2[:, 12:14], statB2[:, 0:12]),
                   ("st2",), ("st2",))
            mean, var = statB2[:, 12:13], statB2[:, 13:14]
            veps, inv, lnscr = statB2[:, 14:15], statB2[:, 15:16], statB2[:, 16:17]
            v_ts(veps, var, cst(C_EPS), None, ALU.add, ALU.bypass, ("st2", "consts"), ("st2a",))
            rsqrt2(inv, veps, lnscr, 128, ("st2a",), ("st2b",))
            v_ts(x2tok[:, tt2 * D:(tt2 + 1) * D], xmB2[:, tt2 * D:(tt2 + 1) * D], mean, inv,
                 ALU.subtract, ALU.mult, (f"xm2_{tt2}", "st2", "st2b"), (f"x2_{tt2}",))
            for k in range(8):
                t_tr(PT[0][0:128, 0:128], x2tok[:, tt2 * D + k * 128: tt2 * D + (k + 1) * 128],
                     128, (f"x2_{tt2}",), (KPT[0],))
                v_copy(x2fm[:, k * 256 + tt2 * 128: k * 256 + (tt2 + 1) * 128],
                       PT[0][0:128, 0:128], (KPT[0],), (f"x2f{k}",))
        x2keys = tuple(f"x2f{k}" for k in range(8))
        for j in range(32):
            pb, pkey = PB[j % 2], KPB[j % 2]
            for k in range(8):
                t_mm(pb[:, 0:256], wqfcS[:, k * 4096 + j * 128: k * 4096 + (j + 1) * 128],
                     x2fm[:, k * 256:(k + 1) * 256], k == 0, k == 7,
                     (f"wf_{k}",) + x2keys, (pkey,), tag="hmm")
            s_act(gbuf[:, j * 256:(j + 1) * 256], pb[:, 0:256], ACT.Gelu, (pkey,), (f"g{j}",))
            sq = sqbB[:, (j % 2) * 256:(j % 2 + 1) * 256]
            v_tt(sq, gbuf[:, j * 256:(j + 1) * 256], gbuf[:, j * 256:(j + 1) * 256],
                 ALU.mult, (f"g{j}",), (f"sqB{j % 2}",), tag="sqB")
            t_mm(PB[4][0:1, 0:256], onescol16[:], gbuf[:, j * 256:(j + 1) * 256],
                 j == 0, j == 31, (f"g{j}", "onescol16"), (KPB[4],), tag="sgmm")
            t_mm(PB[5][0:1, 0:256], onescol16[:], sq, j == 0, j == 31,
                 (f"sqB{j % 2}", "onescol16"), (KPB[5],), tag="sqmm2")
        v_ts(murow[:], PB[4][0:1, 0:256], cst(C_I4096, 1), None, ALU.mult, ALU.bypass,
             (KPB[4], "consts"), ("murow",))
        v_tt(m2row[:], murow[:], murow[:], ALU.mult, ("murow",), ("m2row",))
        v_stt(varrow[:], PB[5][0:1, 0:256], cst(C_I4096, 1), m2row[:], ALU.mult, ALU.subtract,
              (KPB[5], "m2row", "consts"), ("varrow",))
        v_ts(var2row[:], varrow[:], cst(C_EPS, 1), None, ALU.add, ALU.bypass,
             ("varrow", "consts"), ("var2row",))
        s_act(lnsrow[:], var2row[:], ACT.Ln, ("var2row",), ("lnsrow",))
        s_act(rsrow2[:], lnsrow[:], ACT.Exp, ("lnsrow", "consts"), ("rsrow2",),
              scale=cst(C_NEGH, 1))
        v_tt(muarow[:], murow[:], rsrow2[:], ALU.mult, ("murow", "rsrow2"), ("muarow",))
        v_ts(nmuarow[:], muarow[:], cst(C_NEG1, 1), None, ALU.mult, ALU.bypass,
             ("muarow", "consts"), ("nmuarow",))
        bcast(rsb2[:], rsrow2[:], 256, ("rsrow2",), ("rsb2",))
        for tt2 in range(2):
            t_trf(PB[2][0:128, 0:1], nmuarow[0:1, tt2 * 128:(tt2 + 1) * 128], 1,
                  ("nmuarow",), (KPB[2],))
            v_copy(nmacol[:, tt2:tt2 + 1], PB[2][0:128, 0:1], (KPB[2],), ("nmacol",))
        for j in range(32):
            at = atmp[:, (j % 2) * 256:(j % 2 + 1) * 256]
            v_tt(at, gbuf[:, j * 256:(j + 1) * 256], rsb2[:], ALU.mult,
                 (f"g{j}", "rsb2"), (f"at{j % 2}",), tag="aA")
            for tt2 in range(2):
                for dh in range(2):
                    pb, pkey = PB[tt2 * 2 + dh], KPB[tt2 * 2 + dh]
                    t_mm(pb[:, 0:512], at[:, tt2 * 128:(tt2 + 1) * 128],
                         wqprS[:, j * 1024 + dh * 512: j * 1024 + (dh + 1) * 512],
                         j == 0, j == 31, (f"at{j % 2}", f"wp_{j}"), (pkey,), tag="pmm")
        for tt2 in range(2):
            for dh in range(2):
                pb, pkey = PB[tt2 * 2 + dh], KPB[tt2 * 2 + dh]
                v_stt(corrb[:], c2b[:, dh * 512:(dh + 1) * 512], nmacol[:, tt2:tt2 + 1],
                      c1b[:, dh * 512:(dh + 1) * 512], ALU.mult, ALU.add,
                      ("c2b", "c1b", "nmacol"), ("corrb",))
                v_tt(corr2b[:], corrb[:], pb[:, 0:512], ALU.add, ("corrb", pkey), ("corr2b",))
                v_tt(outsb[:, tt2 * D + dh * 512: tt2 * D + (dh + 1) * 512], corr2b[:],
                     xmB2[:, tt2 * D + dh * 512: tt2 * D + (dh + 1) * 512], ALU.add,
                     ("corr2b", f"xm2_{tt2}"), (f"osb{tt2}",))
            sch.dma("sync", out_h[(ch0 + tt2) * 128:(ch0 + tt2 + 1) * 128, :],
                    outsb[:, tt2 * D:(tt2 + 1) * D], (f"osb{tt2}",), (), chan="oo")

    sch.finalize()
    stackB2.close()

    with contextlib.ExitStack() as es:
        sems = {}
        for e in ENGS:
            sems[e] = es.enter_context(nc.semaphore(f"sem_{e}"))
        dma_sems = [es.enter_context(nc.semaphore(f"dsem{i}")) for i in range(sch.n_dma_slots)]
        block = es.enter_context(nc.Block())
        stats = sch.emit(block, sems, dma_sems)
    return nc, stats


_CACHE = {}


def _get_program():
    if "nc" not in _CACHE:
        nc, stats = build_program()
        _CACHE["nc"] = nc
        _CACHE["stats"] = stats
    return _CACHE["nc"]


def stage_inputs(**inputs):
    x = np.asarray(inputs["L0_attn_x"], dtype=np.float32)
    conv_w = np.asarray(inputs["L0_attn_conv_w"], dtype=np.float32)
    conv_b = np.asarray(inputs["L0_attn_conv_b"], dtype=np.float32)
    norm_w = np.asarray(inputs["L0_attn_norm_w"], dtype=np.float32)
    ssm_w = np.asarray(inputs["L0_attn_ssm_norm_w"], dtype=np.float32)
    ln_w = np.asarray(inputs["L0_ffn_ln_w"], dtype=np.float32)
    ln_b = np.asarray(inputs["L0_ffn_ln_b"], dtype=np.float32)
    Dh = np.asarray(inputs["L0_attn_D"], dtype=np.float32)

    common = {
        "w_inproj": np.ascontiguousarray(inputs["L0_attn_in_proj_w"], dtype=np.float32),
        "w_out": np.ascontiguousarray(inputs["L0_attn_out_proj_w"], dtype=np.float32),
        "w_sc1": np.ascontiguousarray(inputs["L0_attn_scorer_w1"], dtype=np.float32),
        "w_sc2": np.ascontiguousarray(inputs["L0_attn_scorer_w2"], dtype=np.float32),
        "w_summ": np.ascontiguousarray(inputs["L0_attn_summ_w"], dtype=np.float32),
        "w_q": np.ascontiguousarray(inputs["L0_attn_q_w"], dtype=np.float32),
        "w_k": np.ascontiguousarray(inputs["L0_attn_k_w"], dtype=np.float32),
        "w_v": np.ascontiguousarray(inputs["L0_attn_v_w"], dtype=np.float32),
        "w_gate": np.ascontiguousarray(inputs["L0_attn_gate_w"], dtype=np.float32),
        "w_fcT": np.ascontiguousarray(np.asarray(inputs["L0_ffn_W_fc"], dtype=np.float32).T),
        "w_projT": np.ascontiguousarray(np.asarray(inputs["L0_ffn_W_proj"], dtype=np.float32).T),
        "conv_w_t": np.ascontiguousarray(
            conv_w.reshape(18, 128, 4).transpose(1, 0, 2).reshape(128, 72)),
        "conv_b_t": np.ascontiguousarray(conv_b.reshape(18, 128).T),
        "norm_w_t": np.ascontiguousarray(norm_w.reshape(8, 128).T),
        "ssm_w_t": np.ascontiguousarray(ssm_w.reshape(16, 128).T),
        "ln_w_t": np.ascontiguousarray(ln_w.reshape(32, 128).T),
        "ln_b_t": np.ascontiguousarray(ln_b.reshape(32, 128).T),
        "dt_bias_t": np.ascontiguousarray(
            np.asarray(inputs["L0_attn_dt_bias"], dtype=np.float32).reshape(32, 1)),
        "a_log_t": np.ascontiguousarray(
            np.asarray(inputs["L0_attn_A_log"], dtype=np.float32).reshape(32, 1)),
        "d_fm_t": np.ascontiguousarray(np.concatenate(
            [np.broadcast_to(Dh[0::2], (64, 16)), np.broadcast_to(Dh[1::2], (64, 16))], axis=0)),
        "ident": np.eye(128, dtype=ml_dtypes.bfloat16),
        "identf": np.eye(128, dtype=np.float32),
        "mask_sl": np.ascontiguousarray(np.triu(np.ones((128, 128), dtype=np.float32))),
    }
    in_maps = [dict(common, x=np.ascontiguousarray(x[b])) for b in range(NCORES)]
    return in_maps


def kernel(**inputs):
    in_maps = stage_inputs(**inputs)
    nc = _get_program()
    res = run_bass_kernel_spmd(nc, in_maps, core_ids=list(range(NCORES)))
    out = np.stack([np.asarray(r["out"], dtype=np.float32) for r in res.results], axis=0)
    return out
